# version 03
# baseline (speedup 1.0000x reference)
import numpy as np
import ml_dtypes
import concourse.bass as bass
import concourse.mybir as mybir
from concourse.bass_utils import run_bass_kernel_spmd

F32 = mybir.dt.float32
BF16 = mybir.dt.bfloat16
AF = mybir.ActivationFunctionType
ALU = mybir.AluOpType
AX = mybir.AxisListType

NCORE = 8
B = 2
S = 8192
D = 1024
T = 2048
FC = D // 128
GROUPS = [[0, 1, 2, 3], [4, 5, 6, 7]]
ENGS = ("tensor", "vector", "scalar", "gpsimd", "sync")


def tk(*toks):
    return tuple(t for t in toks if t is not None)


class Sem:
    __slots__ = ("name", "count", "handle")

    def __init__(self, name):
        self.name = name
        self.count = 0
        self.handle = None


class Prog:
    STRICT_MODE = True

    def __init__(self, nc):
        self.nc = nc
        self.ops = {e: [] for e in ENGS}
        self.sems = []
        self.esem = {e: self.sem("s_" + e) for e in ENGS}
        self.qsem = {e: self.sem("q_" + e) for e in ENGS}
        self.csem = self.sem("s_coll")
        self.ser = {e: self.sem("ser_" + e) for e in Prog.STRICT + ("tensor",)}
        self.last_tok = {}
        self.bar = {}

    def sem(self, name):
        s = Sem(name)
        self.sems.append(s)
        return s

    def add(self, eng, fn, waits=(), inc=None, amt=1, is_dma=False):
        waits = tuple(waits) + self.bar.get(eng, ())
        strict = Prog.STRICT_MODE and fn is not None and (
            eng in Prog.STRICT or (Prog.STRICT_T and eng == "tensor"))
        if strict:
            waits = tuple(waits) + (self.last_tok.get(eng),)
            if inc is None:
                inc = self.ser[eng]
                amt = 16 if (is_dma or amt >= 16) else 1
        waits = tuple(w for w in waits if w is not None)
        tok = None
        if inc is not None:
            inc.count += amt
            tok = (inc, inc.count)
        if strict:
            self.last_tok[eng] = tok
        self.ops[eng].append((fn, waits, inc, amt, is_dma or amt >= 16))
        return tok

    def op(self, eng, fn, waits=()):
        return self.add(eng, fn, waits, inc=self.esem[eng], amt=1)

    def dma(self, queue, out, in_, waits=(), **kw):
        return self.add(queue, lambda e: e.dma_start(out=out, in_=in_, **kw),
                        waits, inc=self.qsem[queue], amt=16, is_dma=True)

    def wait(self, eng, waits):
        self.add(eng, None, waits)

    STRICT = ("vector", "scalar", "gpsimd")
    STRICT_T = False

    def emit(self):
        nc = self.nc
        for s in self.sems:
            s.handle = nc.alloc_semaphore(s.name)
        with nc.Block() as block:
            for e in ENGS:
                ops = self.ops[e]
                if not ops:
                    continue

                def body(eng, ops=ops):
                    hw = {}
                    for fn, waits, inc, amt, isd in ops:
                        for (ws, wv) in waits:
                            if hw.get(id(ws), 0) < wv:
                                eng.wait_ge(ws.handle, wv)
                                hw[id(ws)] = wv
                        if fn is None:
                            continue
                        inst = fn(eng)
                        if inc is not None:
                            inst.then_inc(inc.handle, amt)
                getattr(block, e)(body)


class Bank:
    __slots__ = ("t", "rd")

    def __init__(self, t):
        self.t = t
        self.rd = None


class Ctx:
    pass


def make_ctx(layers):
    nc = bass.Bass("TRN2", target_bir_lowering=False, debug=False, num_devices=NCORE)
    g = Ctx()
    g.nc = nc
    g.layers = layers
    g.dram = {}

    def din(name, shape, dtype=F32):
        g.dram[name] = nc.dram_tensor(name, list(shape), dtype, kind="ExternalInput").ap()
        return g.dram[name]

    g.din = din
    din("x_in", [D, T])
    din("identityf", [64, 64], F32)
    g.y_out = nc.dram_tensor("y_out", [D, T], F32, kind="ExternalOutput").ap()
    din("identity", [128, 128], BF16)

    if 0 in layers:
        din("l0_qp", [128, 1, 8, 256])
        din("l0_kp", [128, 1, 8, 256])
        din("l0_w1", [128, 16, 8, 256])
        din("l0_w2", [128, 16, 8, 256])
        din("l0_h1", [128, 32, 32, 128])
        din("l0_h2", [128, 32, 32, 128])
        din("l0_w3", [128, 8, 32, 128])
        din("l0_s1", [128, 32])
        din("l0_s2", [128, 32])
        din("l0_sh1", [128, 32])
        din("l0_sh2", [128, 32])
        din("l0_s3", [128, 8])
        g.kv_in = nc.dram_tensor("l0_kv_in", [256, D], F32, kind="Internal").ap()
        g.kv_out = nc.dram_tensor("l0_kv_out", [256, D], F32, kind="Internal").ap()

    if 1 in layers:
        din("l1_wq", [128, 2, 8, 512])
        din("l1_wk", [128, 2, 8, 512])
        din("l1_wv", [128, 2, 8, 512])
        din("l1_wo", [128, 2, 8, 512])
        din("l1_bq", [128, 8])
        din("l1_bk", [128, 8])
        din("l1_bv", [1, 1024])
        din("l1_bo", [128, 8])
        din("l1_g1", [128, 8])
        din("l1_b1", [128, 8])
        din("l1_g2", [128, 8])
        din("l1_b2", [128, 8])
        din("l1_w1", [128, 8, 8, 512])
        din("l1_bf1", [128, 32])
        din("l1_w2", [128, 8, 32, 128])
        din("l1_bf2", [128, 8])
        din("l1_wr", [128, 8])
        din("l1_W1", [128, 8, 8, 512])
        din("l1_b1m", [128, 32])
        din("l1_W2", [128, 8, 32, 128])
        din("l1_b2m", [128, 8])
        din("hmask", [128, 8, 16], BF16)
        din("selc", [16, 8, 128], BF16)
        g.nag_in = nc.dram_tensor("l1_nag_in", [T, 16], F32, kind="Internal").ap()
        g.nag_out = nc.dram_tensor("l1_nag_out", [S, 16], F32, kind="Internal").ap()
        g.sag_in = nc.dram_tensor("l1_sag_in", [T, 1], F32, kind="Internal").ap()
        g.sag_out = nc.dram_tensor("l1_sag_out", [S, 1], F32, kind="Internal").ap()
        g.bnc = nc.dram_tensor("l1_bnc", [128, 1], F32, kind="Internal").ap()

    if 2 in layers:
        din("l2_wq", [128, 2, 8, 512])
        din("l2_wk", [128, 2, 8, 512])
        din("l2_wv", [128, 2, 8, 512])
        din("l2_wo", [128, 2, 8, 512])
        din("l2_wck", [128, 32, 64])
        din("l2_wcv", [128, 32, 64])
        din("l2_bck", [64, 1])
        din("l2_bcv", [64, 1])
        din("l2_wgb", [128, 3])
        din("l2_bgr", [1, 48])
        din("l2_wr", [128, 1, 8, 8])
        din("l2_br", [1, 8])
        din("l2_W1", [8, 128, 8, 8, 512])
        din("l2_b1", [128, 256])
        din("l2_W2", [8, 128, 8, 32, 128])
        din("l2_b2", [128, 64])
        g.vfag_in = nc.dram_tensor("l2_vfag_in", [D, T], BF16, kind="Internal").ap()
        g.vfag_out = nc.dram_tensor("l2_vfag_out", [4 * D, T], BF16, kind="Internal").ap()
        g.ktm_in = nc.dram_tensor("l2_ktm_in", [T, D], BF16, kind="Internal").ap()
        g.ktm_out = nc.dram_tensor("l2_ktm_out", [S, D], BF16, kind="Internal").ap()
        g.ksel = nc.dram_tensor("l2_ksel", [D, 1024], BF16, kind="Internal").ap()
        g.vsel = nc.dram_tensor("l2_vsel", [1024, D], BF16, kind="Internal").ap()
        din("iota64", [128, 1])
        g.bi_in = nc.dram_tensor("l2_bi_in", [1, 511], F32, kind="Internal").ap()
        g.bi_out = nc.dram_tensor("l2_bi_out", [1, 511], F32, kind="Internal").ap()
        g.qs_in = nc.dram_tensor("l2_qs_in", [128, 8], F32, kind="Internal").ap()
        g.qs_out = nc.dram_tensor("l2_qs_out", [128, 8], F32, kind="Internal").ap()
        g.wrow_d = nc.dram_tensor("l2_wrow_d", [8, T], F32, kind="Internal").ap()
    if 3 in layers:
        din("l3_win", [128, 17, 8, 256])
        din("l3_wout", [128, 4, 16, 256])
        din("l3_convw", [128, 17, 4])
        din("l3_convb", [128, 17])
        din("l3_dtb", [32, 1])
        din("l3_alog", [32, 1])
        din("l3_dcol", [128, 16])
        din("l3_nw", [128, 16])
        din("l3_keys", [16, 128, 1, 8, 128])
        din("l3_vals", [16, 128, 1024])
        din("l3_selt", [128, 8, 16])
        din("l3_seg", [128, 3])
        din("l3_hmask", [128, 4])
        din("trineg", [128, 128], F32)
        g.ysc = nc.dram_tensor("l3_ysc", [2 * D, T], BF16, kind="Internal").ap()
        g.zsc = nc.dram_tensor("l3_zsc", [2 * D, T], BF16, kind="Internal").ap()
        g.cag = nc.dram_tensor("l3_cag", [32, T], F32, kind="Internal").ap()
        g.hag_in = nc.dram_tensor("l3_hag_in", [D, 3], BF16, kind="Internal").ap()
        g.hag_out = nc.dram_tensor("l3_hag_out", [4 * D, 3], BF16, kind="Internal").ap()
        g.sst_in = nc.dram_tensor("l3_sst_in", [64, 2048], F32, kind="Internal").ap()
        g.sst_out = nc.dram_tensor("l3_sst_out", [256, 2048], F32, kind="Internal").ap()
        g.a_in = nc.dram_tensor("l3_a_in", [1, 32], F32, kind="Internal").ap()
        g.a_out = nc.dram_tensor("l3_a_out", [4, 32], F32, kind="Internal").ap()
    if (1 in layers) or (2 in layers):
        g.kag_in = nc.dram_tensor("ag_kag_in", [D, T], BF16, kind="Internal").ap()
        g.kag_out = nc.dram_tensor("ag_kag_out", [4 * D, T], BF16, kind="Internal").ap()
        g.vag_in = nc.dram_tensor("ag_vag_in", [T, D], BF16, kind="Internal").ap()
        g.vag_out = nc.dram_tensor("ag_vag_out", [S, D], BF16, kind="Internal").ap()

    sb = nc.alloc_sbuf_tensor
    g.X = sb("X", [128, FC, T], F32)
    g.XN = sb("XN", [128, FC, T], BF16)
    g.SBIGf = sb("SBIG", [128, 16384], F32)
    g.SBIG = g.SBIGf.bitcast(BF16)
    g.SC = sb("SC", [128, 8192], BF16)
    g.SCf = g.SC.bitcast(F32)
    g.WB = sb("WB", [128, 2, 4096], BF16)
    g.MISC = sb("MISC", [128, 520], F32)
    g.MISCI = g.MISC.bitcast(mybir.dt.int32)
    g.STAT = sb("STAT", [1, 2048], F32)
    g.IDN = sb("IDN", [128, 128], BF16)
    g.IDNF = sb("IDNF", [64, 64], F32)
    if 3 in layers:
        g.TRINEG = g.SBIGf[:, 16256:16384]
    g.ONESB = sb("ONESB", [128, 1], BF16)
    g.ONESF = sb("ONESF", [128, 1], F32)
    g.ONESR = sb("ONESR", [1, 128], F32)
    g.ONESRB = sb("ONESRB", [1, 128], BF16)
    if 1 in layers:
        g.HM = sb("HM", [128, 8, 16], BF16)
        g.SELC = sb("SELC", [16, 8, 128], BF16)
    if (1 in layers) or (2 in layers):
        g.BROW = sb("BROW", [1, 1024], BF16)

    ps = nc.alloc_psum_tensor
    g.PS = [Bank(ps(f"ps{i}", [128, 512], F32)) for i in range(6)]
    g.PT = [Bank(ps("pst0", [128, 256], BF16)), Bank(ps("pst1", [128, 256], BF16))]

    g.dbg = {}
    if getattr(make_ctx, "DBG", False):
        for nm, shape, dt in [("d_xn", [128, FC, T], BF16), ("d_qp", [128, 4096], BF16),
                              ("d_kvb", [128, 2048], BF16), ("d_z", [1, 2048], F32)]:
            g.dbg[nm] = nc.dram_tensor(nm, shape, dt, kind="ExternalOutput").ap()
    if getattr(make_ctx, "DBG3", False):
        for nm, shape, dt in [("d_ysc", [2 * D, T], BF16), ("d_zsc", [2 * D, T], BF16),
                              ("d_cag", [32, T], F32), ("d_sst", [256, 2048], F32),
                              ("d_wq0", [128, 8192], BF16), ("d_nr0", [1, 512], F32)]:
            g.dbg[nm] = nc.dram_tensor(nm, shape, dt, kind="ExternalOutput").ap()
    if getattr(make_ctx, "DBG2", False):
        for nm, shape, dt in [("d_stat", [1, 2048], F32), ("d_misc", [128, 520], F32),
                              ("d_sbig", [128, 8192], F32), ("d_sc", [128, 4096], F32),
                              ("d_bi", [1, 511], F32), ("d_idx", [1, 16], F32)]:
            g.dbg[nm] = nc.dram_tensor(nm, shape, dt, kind="ExternalOutput").ap()

    g.p = Prog(nc)
    return g


def emit_prelude(g):
    p = g.p
    g.tok_idn = p.dma("sync", g.IDN[:], g.dram["identity"][:])
    g.tok_idnf = p.dma("sync", g.IDNF[:], g.dram["identityf"][:])
    p.op("vector", lambda e: e.memset(g.ONESB[:], 1.0))
    p.op("vector", lambda e: e.memset(g.ONESF[:], 1.0))
    p.op("vector", lambda e: e.memset(g.ONESR[:], 1.0))
    g.tok_ones = p.op("vector", lambda e: e.memset(g.ONESRB[:], 1.0))
    x = g.dram["x_in"].rearrange("(c p) t -> p c t", p=128)
    t = p.dma("sync", g.X[:], x)
    g.x_toks = [t]


def emit_store_x(g):
    p = g.p
    y = g.y_out.rearrange("(c p) t -> p c t", p=128)
    st = p.dma("sync", y, g.X[:], waits=tk(*g.x_toks))
    p.wait("sync", [st])


def emit_ln(g, src=None, dst=None, src_toks=None, gb=None, sqreg=None, tmpreg=None):
    p = g.p
    if src is None:
        src = g.X
    if dst is None:
        dst = g.XN
    if src_toks is None:
        src_toks = g.x_toks
    if sqreg is None:
        sqreg = g.SC
    if tmpreg is None:
        tmpreg = g.SBIGf
    is_f32 = (src.dtype == F32)
    ones = g.ONESF if is_f32 else g.ONESB
    ln_done = [[None] * 4 for _ in range(FC)]
    sq_guard = None
    bc_banks = [g.PS[4], g.PS[5]]
    for t in range(4):
        ts = slice(t * 512, (t + 1) * 512)
        s0 = g.PS[(2 * t) % 4]
        s1 = g.PS[(2 * t + 1) % 4]
        mm0 = mm1 = None
        for c in range(FC):
            slot = sqreg[:, c * 512:(c + 1) * 512]
            sqt = p.op("vector", lambda e, c=c, ts=ts, slot=slot: e.tensor_mul(
                slot, src[:, c, ts], src[:, c, ts]),
                waits=tk(*src_toks, sq_guard))
            mm0 = p.add("tensor", lambda e, c=c, ts=ts, s0=s0: e.matmul(
                s0.t[0:1, 0:512], ones[:], src[:, c, ts],
                start=(c == 0), stop=(c == FC - 1)),
                waits=tk(*(list(src_toks) + [g.tok_ones, s0.rd] if c == 0 else ())),
                inc=p.esem["tensor"] if c == FC - 1 else None)
            mm1 = p.add("tensor", lambda e, c=c, s1=s1, slot=slot: e.matmul(
                s1.t[0:1, 0:512], g.ONESB[:], slot,
                start=(c == 0), stop=(c == FC - 1)),
                waits=tk(sqt, s1.rd if c == 0 else None),
                inc=p.esem["tensor"] if c == FC - 1 else None)
        sq_guard = mm1
        mrow = g.STAT[0:1, 0:512]
        vrow = g.STAT[0:1, 512:1024]
        trow = g.STAT[0:1, 1024:1536]
        rrow = g.STAT[0:1, 1536:2048]
        t_m = p.op("vector", lambda e, s0=s0: e.tensor_scalar_mul(mrow, s0.t[0:1, 0:512], 1.0 / D), waits=tk(mm0))
        s0.rd = t_m
        t_v = p.op("vector", lambda e, s1=s1: e.tensor_scalar_mul(vrow, s1.t[0:1, 0:512], 1.0 / D), waits=tk(mm1))
        s1.rd = t_v
        t_m2 = p.op("vector", lambda e: e.tensor_mul(trow, mrow, mrow), waits=tk(t_m))
        t_v2 = p.op("vector", lambda e: e.tensor_sub(vrow, vrow, trow), waits=tk(t_v, t_m2))
        t_v3 = p.op("vector", lambda e: e.tensor_scalar_add(vrow, vrow, 1e-5), waits=tk(t_v2))
        t_v4 = p.op("vector", lambda e: e.reciprocal(vrow, vrow), waits=tk(t_v3))
        t_r = p.op("scalar", lambda e: e.activation(rrow, vrow, AF.Sqrt), waits=tk(t_v4))
        bm = bc_banks[0]
        br = bc_banks[1]
        t_bm = p.op("tensor", lambda e, bm=bm: e.matmul(
            bm.t[:, 0:512], g.ONESR[:], mrow, start=True, stop=True), waits=tk(t_m, bm.rd, g.tok_ones))
        t_br = p.op("tensor", lambda e, br=br: e.matmul(
            br.t[:, 0:512], g.ONESR[:], rrow, start=True, stop=True), waits=tk(t_r, br.rd))
        last_a = []
        for c in range(FC):
            tmp = tmpreg[:, (c % 2) * 512:(c % 2) * 512 + 512]
            a1 = p.op("vector", lambda e, c=c, ts=ts, tmp=tmp, bm=bm: e.tensor_sub(
                tmp, src[:, c, ts], bm.t[:, 0:512]), waits=tk(t_bm, ln_done[c - 2][t] if c >= 2 else None))
            if gb is None:
                a2 = p.op("vector", lambda e, c=c, ts=ts, tmp=tmp, br=br: e.tensor_mul(
                    dst[:, c, ts], tmp, br.t[:, 0:512]), waits=tk(a1, t_br))
            else:
                gcol, bcol = gb
                a2 = p.op("vector", lambda e, c=c, ts=ts, tmp=tmp, br=br, gcol=gcol: e.scalar_tensor_tensor(
                    dst[:, c, ts], tmp, gcol[:, c:c + 1], br.t[:, 0:512], ALU.mult, ALU.mult),
                    waits=tk(a1, t_br))
                a2 = p.op("vector", lambda e, c=c, ts=ts, bcol=bcol: e.tensor_scalar_add(
                    dst[:, c, ts], dst[:, c, ts], bcol[:, c:c + 1]), waits=tk(a2))
            ln_done[c][t] = a2
            last_a.append(a2)
        bm.rd = last_a[-1]
        br.rd = last_a[-1]
    return ln_done


def emit_l0(g, ln):
    p = g.p
    dr = g.dram
    QP = lambda e, ts: g.SC[:, e * 2048 + ts.start: e * 2048 + ts.stop]
    KTM = lambda ti: g.SC[:, 4096 + ti * 256: 4096 + (ti + 1) * 256]
    XNT = g.SBIG[:, 0:16384]
    KVP = g.SBIGf[:, 8192:10240]
    KVB = lambda e: g.SC[:, 4096 + e * 1024: 4096 + (e + 1) * 1024]

    wq = g.WB[:, 0, 0:2048].rearrange("p (k o) -> p k o", k=8)
    wk = g.WB[:, 1, 0:2048].rearrange("p (k o) -> p k o", k=8)
    t_wq = p.dma("gpsimd", wq, dr["l0_qp"][:, 0], waits=tk())
    t_wk = p.dma("gpsimd", wk, dr["l0_kp"][:, 0], waits=tk())

    tok_qfm = [[None] * 4 for _ in range(2)]
    for e2 in range(2):
        for t in range(4):
            ts = slice(t * 512, (t + 1) * 512)
            bank = g.PS[(e2 * 4 + t) % 4]
            mm = None
            for c in range(FC):
                mm = p.add("tensor", lambda e, c=c, e2=e2, ts=ts, bank=bank: e.matmul(
                    bank.t[:, 0:512], g.WB[:, 0, c * 256 + e2 * 128: c * 256 + e2 * 128 + 128],
                    g.XN[:, c, ts], start=(c == 0), stop=(c == FC - 1)),
                    waits=tk(ln[c][t], *( (t_wq, bank.rd) if c == 0 else ())),
                    inc=p.esem["tensor"] if c == FC - 1 else None)
            ev = p.op("scalar", lambda e, e2=e2, ts=ts, bank=bank: e.activation(
                QP(e2, ts), bank.t[:, 0:512], AF.Exp), waits=tk(mm))
            bank.rd = ev
            tok_qfm[e2][t] = ev

    tok_ktm = [None] * 16
    for ti in range(16):
        bank = g.PS[4 + (ti % 2)]
        mm = None
        for c in range(FC):
            mm = p.add("tensor", lambda e, c=c, ti=ti, bank=bank: e.matmul(
                bank.t[:, 0:256], g.XN[:, c, ti * 128:(ti + 1) * 128],
                g.WB[:, 1, c * 256: c * 256 + 256], start=(c == 0), stop=(c == FC - 1)),
                waits=tk(ln[c][ti // 4], *( (t_wk, bank.rd) if c == 0 else ())),
                inc=p.esem["tensor"] if c == FC - 1 else None)
        ev = p.op("scalar", lambda e, ti=ti, bank=bank: e.activation(
            KTM(ti), bank.t[:, 0:256], AF.Exp), waits=tk(mm))
        bank.rd = ev
        tok_ktm[ti] = ev

    tok_xnt = [None] * 16
    for ti in range(16):
        last = None
        for c in range(FC):
            pb = g.PT[(ti * FC + c) % 2]
            tt = p.op("tensor", lambda e, c=c, ti=ti, pb=pb: e.transpose(
                pb.t[:, 0:128], g.XN[:, c, ti * 128:(ti + 1) * 128], g.IDN[:]),
                waits=tk(ln[c][ti // 4], g.tok_idn, pb.rd))
            cp = p.op("vector", lambda e, c=c, ti=ti, pb=pb: e.tensor_copy(
                XNT[:, ti * 1024 + c * 128: ti * 1024 + (c + 1) * 128], pb.t[:, 0:128]),
                waits=tk(tt))
            pb.rd = cp
            last = cp
        tok_xnt[ti] = last

    tok_kvp = []
    for e2 in range(2):
        for dh in range(2):
            bank = g.PS[(e2 * 2 + dh) % 4]
            mm = None
            for ti in range(16):
                mm = p.add("tensor", lambda e, e2=e2, dh=dh, ti=ti, bank=bank: e.matmul(
                    bank.t[:, 0:512], KTM(ti)[:, e2 * 128:(e2 + 1) * 128],
                    XNT[:, ti * 1024 + dh * 512: ti * 1024 + dh * 512 + 512],
                    start=(ti == 0), stop=(ti == 15)),
                    waits=tk(tok_ktm[ti], tok_xnt[ti], bank.rd if ti == 0 else None),
                    inc=p.esem["tensor"] if ti == 15 else None)
            ev = p.op("vector", lambda e, e2=e2, dh=dh, bank=bank: e.tensor_copy(
                KVP[:, e2 * 1024 + dh * 512: e2 * 1024 + dh * 512 + 512], bank.t[:, 0:512]),
                waits=tk(mm))
            bank.rd = ev
            tok_kvp.append(ev)

    zrow = g.STAT[0:1, 0:2048]
    tok_z = []
    for t in range(4):
        bank = g.PS[4 + (t % 2)]
        mm = None
        for e2 in range(2):
            mm = p.add("tensor", lambda e, e2=e2, t=t, bank=bank: e.matmul(
                bank.t[0:1, 0:512], g.ONESB[:], QP(e2, slice(t * 512, (t + 1) * 512)),
                start=(e2 == 0), stop=(e2 == 1)),
                waits=tk(tok_qfm[e2][t], bank.rd if e2 == 0 else None),
                inc=p.esem["tensor"] if e2 == 1 else None)
        ev = p.op("vector", lambda e, t=t, bank=bank: e.tensor_scalar_add(
            zrow[0:1, t * 512:(t + 1) * 512], bank.t[0:1, 0:512], 1e-8), waits=tk(mm))
        bank.rd = ev
        tok_z.append(ev)
    tok_zr = p.op("vector", lambda e: e.reciprocal(zrow, zrow), waits=tk(*tok_z))

    t_kvst = p.dma("sync", g.kv_in.rearrange("(e p) d -> p e d", p=128),
                   KVP.rearrange("p (e d) -> p e d", e=2), waits=tk(*tok_kvp))
    t_coll = p.add("gpsimd", lambda e: e.collective_compute(
        "AllReduce", ALU.add, replica_groups=GROUPS, ins=[g.kv_in[:]], outs=[g.kv_out[:]]),
        waits=tk(t_kvst), inc=p.csem, amt=1)

    HA = g.SBIG[:, 0:16384].rearrange("p (o t) -> p o t", o=32)
    HB = g.SBIG[:, 16384:32768].rearrange("p (o t) -> p o t", o=32)
    t_s1 = p.dma("sync", g.MISC[:, 0:32], dr["l0_s1"][:])
    t_s2 = p.dma("sync", g.MISC[:, 32:64], dr["l0_s2"][:])
    t_sh1 = p.dma("sync", g.MISC[:, 64:96], dr["l0_sh1"][:])
    t_sh2 = p.dma("sync", g.MISC[:, 96:128], dr["l0_sh2"][:])
    t_s3 = p.dma("sync", g.MISC[:, 128:136], dr["l0_s3"][:])

    wbtok = [None, None]
    wbtok[0] = tok_qfm[1][3]
    wbtok[1] = tok_ktm[15]
    silu_rd = [None, None]
    ffn_x_toks = []
    for t in range(4):
        ts = slice(t * 512, (t + 1) * 512)
        for og in range(16):
            b01 = og % 2
            wdst1 = g.WB[:, b01, 0:2048].rearrange("p (k o) -> p k o", k=8)
            wdst2 = g.WB[:, b01, 2048:4096].rearrange("p (k o) -> p k o", k=8)
            tw1 = p.dma("gpsimd", wdst1, dr["l0_w1"][:, og], waits=tk(wbtok[b01]))
            tw2 = p.dma("gpsimd", wdst2, dr["l0_w2"][:, og], waits=tk(wbtok[b01]))
            for oc in range(2):
                oabs = og * 2 + oc
                ba = g.PS[(2 * oc) % 4]
                bb = g.PS[(2 * oc + 1) % 4]
                m1 = m2 = None
                for c in range(FC):
                    m1 = p.add("tensor", lambda e, c=c, b01=b01, oc=oc, ts=ts, ba=ba: e.matmul(
                        ba.t[:, 0:512], g.WB[:, b01, c * 256 + oc * 128: c * 256 + oc * 128 + 128],
                        g.XN[:, c, ts], start=(c == 0), stop=(c == FC - 1)),
                        waits=tk(ln[c][t], *( (tw1, ba.rd) if c == 0 else ())),
                        inc=p.esem["tensor"] if c == FC - 1 else None)
                for c in range(FC):
                    m2 = p.add("tensor", lambda e, c=c, b01=b01, oc=oc, ts=ts, bb=bb: e.matmul(
                        bb.t[:, 0:512], g.WB[:, b01, 2048 + c * 256 + oc * 128: 2048 + c * 256 + oc * 128 + 128],
                        g.XN[:, c, ts], start=(c == 0), stop=(c == FC - 1)),
                        waits=tk(*( (tw2, bb.rd) if c == 0 else ())),
                        inc=p.esem["tensor"] if c == FC - 1 else None)
                wbtok[b01] = m2
                slot = g.SC[:, 6144 + (oabs % 2) * 512: 6144 + (oabs % 2) * 512 + 512]
                sv = p.op("scalar", lambda e, oabs=oabs, ba=ba, slot=slot: e.activation(
                    slot, ba.t[:, 0:512], AF.Silu, scale=g.MISC[:, oabs:oabs + 1]),
                    waits=tk(m1, t_s1, silu_rd[oabs % 2]))
                ba.rd = sv
                hv = p.op("vector", lambda e, oabs=oabs, t=t, bb=bb, slot=slot: e.scalar_tensor_tensor(
                    HA[:, oabs, 0:512],
                    bb.t[:, 0:512], g.MISC[:, 32 + oabs:33 + oabs], slot, ALU.mult, ALU.mult),
                    waits=tk(m2, sv, t_s2,
                             tok_kvp[-1] if (t == 0 and og == 0 and oc == 0) else None))
                bb.rd = hv
                silu_rd[oabs % 2] = hv
        for og in range(32):
            b01 = og % 2
            wdst = g.WB[:, b01, :].rearrange("p (k o) -> p k o", k=32)
            twk = p.dma("gpsimd", wdst, dr["l0_h1"][:, og], waits=tk(wbtok[b01]))
            bank = g.PS[og % 4]
            mm = None
            for k in range(32):
                mm = p.add("tensor", lambda e, k=k, b01=b01, bank=bank: e.matmul(
                    bank.t[:, 0:512], g.WB[:, b01, k * 128:(k + 1) * 128],
                    HA[:, k, 0:512], start=(k == 0), stop=(k == 31)),
                    waits=tk(*( (twk, bank.rd, hv) if k == 0 else ())),
                    inc=p.esem["tensor"] if k == 31 else None)
            wbtok[b01] = mm
            gv = p.op("scalar", lambda e, og=og, bank=bank: e.activation(
                HB[:, og, 0:512], bank.t[:, 0:512], AF.Gelu, scale=g.MISC[:, 64 + og:65 + og]),
                waits=tk(mm, t_sh1, t_kvst if og == 0 else None))
            bank.rd = gv
            if og == 31:
                hb_last = gv
        for og in range(32):
            b01 = og % 2
            wdst = g.WB[:, b01, :].rearrange("p (k o) -> p k o", k=32)
            twk = p.dma("gpsimd", wdst, dr["l0_h2"][:, og], waits=tk(wbtok[b01]))
            bank = g.PS[og % 4]
            mm = None
            for k in range(32):
                mm = p.add("tensor", lambda e, k=k, b01=b01, bank=bank: e.matmul(
                    bank.t[:, 0:512], g.WB[:, b01, k * 128:(k + 1) * 128],
                    HB[:, k, 0:512], start=(k == 0), stop=(k == 31)),
                    waits=tk(*( (twk, bank.rd, hb_last) if k == 0 else ())),
                    inc=p.esem["tensor"] if k == 31 else None)
            wbtok[b01] = mm
            gv = p.op("scalar", lambda e, og=og, bank=bank: e.activation(
                HA[:, og, 0:512], bank.t[:, 0:512], AF.Gelu, scale=g.MISC[:, 96 + og:97 + og]),
                waits=tk(mm, t_sh2))
            bank.rd = gv
            if og == 31:
                hc_last = gv
        for og in range(FC):
            b01 = og % 2
            wdst = g.WB[:, b01, :].rearrange("p (k o) -> p k o", k=32)
            twk = p.dma("gpsimd", wdst, dr["l0_w3"][:, og], waits=tk(wbtok[b01]))
            bank = g.PS[og % 4]
            mm = None
            for k in range(32):
                mm = p.add("tensor", lambda e, k=k, b01=b01, bank=bank: e.matmul(
                    bank.t[:, 0:512], g.WB[:, b01, k * 128:(k + 1) * 128],
                    HA[:, k, 0:512], start=(k == 0), stop=(k == 31)),
                    waits=tk(*( (twk, bank.rd, hc_last) if k == 0 else ())),
                    inc=p.esem["tensor"] if k == 31 else None)
            wbtok[b01] = mm
            xv = p.op("vector", lambda e, og=og, ts=ts, bank=bank: e.scalar_tensor_tensor(
                g.X[:, og, ts], bank.t[:, 0:512], g.MISC[:, 128 + og:129 + og],
                g.X[:, og, ts], ALU.mult, ALU.add),
                waits=tk(mm, t_s3))
            bank.rd = xv
            ffn_x_toks.append(xv)

    t_kvb = p.dma("gpsimd", g.SC[:, 4096:6144].rearrange("p (e d) -> p e d", e=2),
                  g.kv_out.rearrange("(e p) d -> p e d", p=128),
                  waits=tk(t_coll, *tok_kvp))

    attn_x_toks = []
    zbc_rd = [None, None]
    tmp_rd = [None, None]
    for t in range(4):
        ts = slice(t * 512, (t + 1) * 512)
        bz = g.PS[4 + (t % 2)]
        t_zb = p.op("tensor", lambda e, t=t, bz=bz: e.matmul(
            bz.t[:, 0:512], g.ONESR[:], zrow[0:1, t * 512:(t + 1) * 512],
            start=True, stop=True), waits=tk(tok_zr, bz.rd))
        zslot = g.SBIGf[:, (t % 2) * 512:(t % 2) * 512 + 512]
        t_zs = p.op("vector", lambda e, bz=bz, zslot=zslot: e.tensor_copy(zslot, bz.t[:, 0:512]),
                    waits=tk(t_zb, zbc_rd[t % 2], *(ffn_x_toks[-1:] if t == 0 else ())))
        bz.rd = t_zs
        for dch in range(FC):
            bank = g.PS[dch % 4]
            mm = None
            for e2 in range(2):
                mm = p.add("tensor", lambda e, e2=e2, dch=dch, ts=ts, bank=bank: e.matmul(
                    bank.t[:, 0:512], KVB(e2)[:, dch * 128:(dch + 1) * 128],
                    QP(e2, ts), start=(e2 == 0), stop=(e2 == 1)),
                    waits=tk(*( (t_kvb, bank.rd) if e2 == 0 else ())),
                    inc=p.esem["tensor"] if e2 == 1 else None)
            tmp = g.SBIGf[:, 1024 + (dch % 2) * 512: 1024 + (dch % 2) * 512 + 512]
            m1 = p.op("vector", lambda e, tmp=tmp, bank=bank, zslot=zslot: e.tensor_mul(
                tmp, bank.t[:, 0:512], zslot), waits=tk(mm, t_zs, tmp_rd[dch % 2]))
            bank.rd = m1
            m2 = p.op("vector", lambda e, dch=dch, ts=ts, tmp=tmp: e.tensor_add(
                g.X[:, dch, ts], g.X[:, dch, ts], tmp), waits=tk(m1))
            tmp_rd[dch % 2] = m2
            attn_x_toks.append(m2)
        zbc_rd[t % 2] = attn_x_toks[-1]

    g.x_toks = ffn_x_toks + attn_x_toks


TEMP = 8.0
MXN = 1.0 - 1e-5


def _clip_heads(g, A, a_toks, wk_out=None):
    p = g.p
    done = [[None] * 4 for _ in range(FC)]
    rsb = g.SCf[0:16, 3584:4096]
    fsb = g.SC[0:16, 6656:7168]
    for t in range(4):
        ts = slice(t * 512, (t + 1) * 512)
        nb = g.PS[4 + (t % 2)]
        mm = None
        sq_toks = []
        for c in range(FC):
            slot = g.SC[:, c * 512:(c + 1) * 512]
            sq = p.op("vector", lambda e, c=c, ts=ts, slot=slot: e.tensor_mul(
                slot, A[:, c, ts], A[:, c, ts]),
                waits=tk(a_toks[c][t], _clip_heads_guard.get(id(g), None)))
            mm = p.add("tensor", lambda e, c=c, slot=slot, nb=nb: e.matmul(
                nb.t[0:16, 0:512], g.HM[:, c, :], slot,
                start=(c == 0), stop=(c == FC - 1)),
                waits=tk(sq, g.tok_hm if c == 0 else None, nb.rd if c == 0 else None),
                inc=p.esem["tensor"] if c == FC - 1 else None)
            sq_toks.append(sq)
        _clip_heads_guard[id(g)] = mm
        if wk_out is not None:
            nk = p.op("vector", lambda e, nb=nb: e.tensor_scalar(
                rsb, nb.t[0:16, 0:512], MXN * MXN, -1.0 / TEMP, ALU.min, ALU.mult),
                waits=tk(mm))
            for q in range(4):
                pb = g.PS[(t * 4 + q) % 4]
                tt = p.op("tensor", lambda e, q=q, pb=pb: e.transpose(
                    pb.t[:, 0:16], rsb[0:16, q * 128:(q + 1) * 128], g.IDNF[0:16, 0:16]),
                    waits=tk(nk, pb.rd, g.tok_idnf))
                cp = p.op("vector", lambda e, t=t, q=q, pb=pb: e.tensor_copy(
                    g.MISC[:, 256 + (t * 4 + q) * 16: 256 + (t * 4 + q) * 16 + 16], pb.t[:, 0:16]),
                    waits=tk(tt))
                pb.rd = cp
                _nk2_toks.append(cp)
        r1 = p.op("vector", lambda e, nb=nb: e.reciprocal(rsb, nb.t[0:16, 0:512]), waits=tk(mm, *( [_nk2_toks[-1]] if wk_out is not None else [] )))
        nb.rd = r1
        r2 = p.op("scalar", lambda e: e.activation(fsb, rsb, AF.Sqrt, scale=MXN * MXN), waits=tk(r1))
        r3 = p.op("vector", lambda e: e.tensor_scalar_min(fsb, fsb, 1.0), waits=tk(r2))
        for c in range(FC):
            bb = g.PS[(c % 2)]
            bc = p.op("tensor", lambda e, c=c, bb=bb: e.matmul(
                bb.t[:, 0:512], g.SELC[:, c, :], fsb, start=True, stop=True),
                waits=tk(r3, bb.rd, g.tok_selc))
            mu = p.op("vector", lambda e, c=c, ts=ts, bb=bb: e.tensor_mul(
                A[:, c, ts], A[:, c, ts], bb.t[:, 0:512]),
                waits=tk(bc, sq_toks[c]))
            bb.rd = mu
            done[c][t] = mu
    return done


_clip_heads_guard = {}
_nk2_toks = []


def _proj_fullvec(g, A, a_toks, sq_guard_key):
    p = g.p
    done = [[None] * 4 for _ in range(FC)]
    rrow = g.SCf[0:1, 3104:3616]
    for t in range(4):
        ts = slice(t * 512, (t + 1) * 512)
        nb = g.PS[4 + (t % 2)]
        mm = None
        sq_toks = []
        for c in range(FC):
            slot = g.SC[:, c * 512:(c + 1) * 512]
            sq = p.op("vector", lambda e, c=c, ts=ts, slot=slot: e.tensor_mul(
                slot, A[:, c, ts], A[:, c, ts]),
                waits=tk(a_toks[c][t] if a_toks else None, _clip_heads_guard.get(sq_guard_key, None)))
            mm = p.add("tensor", lambda e, c=c, slot=slot, nb=nb: e.matmul(
                nb.t[0:1, 0:512], g.ONESB[:], slot,
                start=(c == 0), stop=(c == FC - 1)),
                waits=tk(sq, nb.rd if c == 0 else None),
                inc=p.esem["tensor"] if c == FC - 1 else None)
            sq_toks.append(sq)
        _clip_heads_guard[sq_guard_key] = mm
        r1 = p.op("vector", lambda e, nb=nb: e.reciprocal(rrow, nb.t[0:1, 0:512]), waits=tk(mm))
        nb.rd = r1
        r2 = p.op("scalar", lambda e: e.activation(rrow, rrow, AF.Sqrt, scale=MXN * MXN), waits=tk(r1))
        r3 = p.op("vector", lambda e: e.tensor_scalar_min(rrow, rrow, 1.0), waits=tk(r2))
        bb = g.PS[(t % 2)]
        bc = p.op("tensor", lambda e, bb=bb: e.matmul(
            bb.t[:, 0:512], g.ONESR[:], rrow, start=True, stop=True),
            waits=tk(r3, bb.rd))
        last = None
        for c in range(FC):
            mu = p.op("vector", lambda e, c=c, ts=ts, bb=bb: e.tensor_mul(
                A[:, c, ts], A[:, c, ts], bb.t[:, 0:512]),
                waits=tk(bc, sq_toks[c]))
            done[c][t] = mu
            last = mu
        bb.rd = last
    return done


def emit_l1(g, ln):
    p = g.p
    dr = g.dram
    _clip_heads_guard.clear()
    _nk2_toks.clear()
    g.tok_hm = p.dma("sync", g.HM[:], dr["hmask"][:])
    g.tok_selc = p.dma("sync", g.SELC[:], dr["selc"][:])
    tok_brow = p.dma("gpsimd", g.BROW[:], dr["l1_bv"][:])
    misc_loads = []
    mcols = {"bq": (0, 8), "bk": (8, 8), "bo": (16, 8), "g1": (24, 8), "b1": (32, 8),
             "g2": (40, 8), "b2": (48, 8), "bf1": (56, 32), "bf2": (88, 8),
             "b1m": (96, 32), "b2m": (128, 8), "wr": (136, 8)}
    mtok = {}
    for nm, (o, w) in mcols.items():
        mtok[nm] = p.dma("sync", g.MISC[:, o:o + w], dr["l1_" + nm][:])
    MC = lambda nm, i: g.MISC[:, mcols[nm][0] + i: mcols[nm][0] + i + 1]

    XL = g.SBIG[:, 0:16384].rearrange("p (c t) -> p c t", c=8)
    ln1 = emit_ln(g, src=g.XN, dst=XL, src_toks=[ln[c][t] for c in range(FC) for t in range(4)] + [mtok["g1"], mtok["b1"]],
                  gb=(g.MISC[:, 24:32], g.MISC[:, 32:40]), sqreg=g.SC,
                  tmpreg=g.SBIGf[:, 8192:9216])

    srow = g.STAT[0:1, 0:2048]
    WRB = g.SC[:, 4096:4104]
    t_wrb = p.dma("gpsimd", WRB, dr["l1_wr"][:])
    sc_toks = []
    for t in range(4):
        nb = g.PS[4 + (t % 2)]
        mm = None
        for c in range(FC):
            mm = p.add("tensor", lambda e, c=c, t=t, nb=nb: e.matmul(
                nb.t[0:1, 0:512], WRB[:, c:c + 1], g.XN[:, c, t * 512:(t + 1) * 512],
                start=(c == 0), stop=(c == FC - 1)),
                waits=tk(ln[c][t], t_wrb if c == 0 else None, nb.rd if c == 0 else None),
                inc=p.esem["tensor"] if c == FC - 1 else None)
        ev = p.op("vector", lambda e, t=t, nb=nb: e.tensor_copy(
            srow[0:1, t * 512:(t + 1) * 512], nb.t[0:1, 0:512]), waits=tk(mm))
        nb.rd = ev
        sc_toks.append(ev)
    t_sst = p.dma("sync", g.sag_in.rearrange("t one -> one t"), srow[0:1, :], waits=tk(*sc_toks))
    t_sag = p.add("gpsimd", lambda e: e.collective_compute(
        "AllGather", ALU.bypass, replica_groups=GROUPS, ins=[g.sag_in[:]], outs=[g.sag_out[:]]),
        waits=tk(t_sst), inc=p.csem, amt=1)

    KV = g.SBIG[:, 16384:32768].rearrange("p (c t) -> p c t", c=8)
    wb_rd = [None, None]

    def proj_fm(wname, bname, out, waits_extra=()):
        toks = [[None] * 4 for _ in range(FC)]
        for og in range(2):
            b01 = og % 2
            wdst = g.WB[:, b01, :].rearrange("p (k o) -> p k o", k=8)
            tw = p.dma("gpsimd", wdst, dr[wname][:, og], waits=tk(wb_rd[b01]))
            last = None
            for oc in range(4):
                c_out = og * 4 + oc
                for t in range(4):
                    bk = g.PS[(oc * 4 + t) % 4]
                    mm = None
                    for c in range(FC):
                        mm = p.add("tensor", lambda e, c=c, oc=oc, t=t, b01=b01, bk=bk: e.matmul(
                            bk.t[:, 0:512], g.WB[:, b01, c * 512 + oc * 128: c * 512 + oc * 128 + 128],
                            XL[:, c, t * 512:(t + 1) * 512], start=(c == 0), stop=(c == FC - 1)),
                            waits=tk(ln1[c][t], *( (tw, bk.rd) + tuple(waits_extra) if c == 0 else ())),
                            inc=p.esem["tensor"] if c == FC - 1 else None)
                    if bname is None:
                        ev = p.op("vector", lambda e, c_out=c_out, t=t, bk=bk: e.tensor_copy(
                            out[:, c_out, t * 512:(t + 1) * 512], bk.t[:, 0:512]), waits=tk(mm))
                    else:
                        ev = p.op("vector", lambda e, c_out=c_out, t=t, bk=bk: e.tensor_scalar_add(
                            out[:, c_out, t * 512:(t + 1) * 512], bk.t[:, 0:512], MC(bname, c_out)),
                            waits=tk(mm, mtok[bname]))
                    bk.rd = ev
                    toks[c_out][t] = ev
                    last = ev
            wb_rd[b01] = last
        return toks

    k_toks = proj_fm("l1_wk", "bk", KV)
    k_toks = _clip_heads(g, KV, k_toks, wk_out=True)
    t_kst = p.dma("sync", g.kag_in.rearrange("(c p) t -> p c t", p=128), KV[:],
                  waits=tk(*[k_toks[c][t] for c in range(FC) for t in range(4)]))
    t_kag = p.add("gpsimd", lambda e: e.collective_compute(
        "AllGather", ALU.bypass, replica_groups=GROUPS, ins=[g.kag_in[:]], outs=[g.kag_out[:]]),
        waits=tk(t_kst), inc=p.csem, amt=1)
    t_nst = p.dma("sync", g.nag_in.rearrange("(tc p) h -> p tc h", p=128),
                  g.MISC[:, 256:512].rearrange("p (tc h) -> p tc h", h=16),
                  waits=tk(*_nk2_toks))
    t_nag = p.add("gpsimd", lambda e: e.collective_compute(
        "AllGather", ALU.bypass, replica_groups=GROUPS, ins=[g.nag_in[:]], outs=[g.nag_out[:]]),
        waits=tk(t_nst), inc=p.csem, amt=1)

    v_toks = [[None] * 4 for _ in range(2)]
    VV = KV
    VT = g.SBIG[:, 16384:32768].rearrange("p (tc d) -> p tc d", tc=16)
    vlast = None
    for og in range(2):
        b01 = og % 2
        wdst = g.WB[:, b01, :].rearrange("p (k o) -> p k o", k=8)
        tw = p.dma("gpsimd", wdst, dr["l1_wv"][:, og], waits=tk(wb_rd[b01]))
        for ti in range(16):
            bk = g.PS[(ti % 4)]
            mm = None
            for c in range(FC):
                mm = p.add("tensor", lambda e, c=c, ti=ti, og=og, b01=b01, bk=bk: e.matmul(
                    bk.t[:, 0:512], XL[:, c, ti * 128:(ti + 1) * 128],
                    g.WB[:, b01, c * 512: c * 512 + 512], start=(c == 0), stop=False),
                    waits=tk(ln1[c][ti // 4], *( (tw, bk.rd, t_kst) if c == 0 else ())),
                    inc=None)
            mm = p.add("tensor", lambda e, og=og, bk=bk: e.matmul(
                bk.t[:, 0:512], g.ONESRB[:], g.BROW[0:1, og * 512:(og + 1) * 512],
                start=False, stop=True),
                waits=tk(tok_brow), inc=p.esem["tensor"])
            ev = p.op("vector", lambda e, ti=ti, og=og, bk=bk: e.tensor_copy(
                VT[:, ti, og * 512:(og + 1) * 512], bk.t[:, 0:512]), waits=tk(mm))
            bk.rd = ev
            vlast = ev
        wb_rd[b01] = vlast
    t_vst = p.dma("sync", g.vag_in.rearrange("(tc p) d -> p tc d", p=128), VT[:], waits=tk(vlast))
    t_vag = p.add("gpsimd", lambda e: e.collective_compute(
        "AllGather", ALU.bypass, replica_groups=GROUPS, ins=[g.vag_in[:]], outs=[g.vag_out[:]]),
        waits=tk(t_vst), inc=p.csem, amt=1)

    q_toks = proj_fm("l1_wq", "bq", KV, waits_extra=(t_vst,))
    q_toks = _clip_heads(g, KV, q_toks, wk_out=None)
    QV = KV

    t_sld = p.dma("sync", g.MISC[:, 144:208], g.sag_out.rearrange("(p a) one -> p (a one)", p=128),
                  waits=tk(t_sag))
    SAL = g.MISC[:, 144:208]
    LO = g.MISC[:, 208:209]; HI = g.MISC[:, 209:210]; MID = g.MISC[:, 210:211]
    CB = g.MISC[:, 211:212]; CO = g.MISC[:, 212:213]; C64 = g.MISC[:, 213:214]
    TMP64 = g.SCf[:, 2048:2112]
    ROWT = g.SCf[0:1, 3104:3296]
    i1 = p.op("vector", lambda e: e.reduce_max(CB, SAL, axis=AX.X), waits=tk(t_sld))
    d1a = p.dma("gpsimd", g.bnc[:], CB, waits=tk(i1))
    d1 = p.dma("gpsimd", ROWT[0:1, 0:128], g.bnc.rearrange("p one -> one p"), waits=tk(d1a))
    i2 = p.op("vector", lambda e: e.reduce_max(ROWT[0:1, 128:129], ROWT[0:1, 0:128], axis=AX.X), waits=tk(d1))
    b1_ = g.PS[4]
    i3 = p.op("tensor", lambda e: e.matmul(b1_.t[:, 0:1], g.ONESR[:], ROWT[0:1, 128:129],
                                           start=True, stop=True), waits=tk(i2, b1_.rd))
    i4 = p.op("vector", lambda e: e.tensor_scalar_add(HI, b1_.t[:, 0:1], 1.0), waits=tk(i3))
    b1_.rd = i4
    i5 = p.op("vector", lambda e: e.tensor_reduce(CB, SAL, axis=AX.X, op=ALU.min), waits=tk(t_sld, i1))
    d2a = p.dma("gpsimd", g.bnc[:], CB, waits=tk(i5, i2))
    d2 = p.dma("gpsimd", ROWT[0:1, 0:128], g.bnc.rearrange("p one -> one p"), waits=tk(d2a))
    i6 = p.op("vector", lambda e: e.tensor_reduce(ROWT[0:1, 128:129], ROWT[0:1, 0:128], axis=AX.X, op=ALU.min), waits=tk(d2))
    i7 = p.op("tensor", lambda e: e.matmul(b1_.t[:, 0:1], g.ONESR[:], ROWT[0:1, 128:129],
                                           start=True, stop=True), waits=tk(i6, b1_.rd))
    i8 = p.op("vector", lambda e: e.tensor_scalar_add(LO, b1_.t[:, 0:1], -1.0), waits=tk(i7))
    b1_.rd = i8
    prev = i8
    for it in range(46):
        m1 = p.op("vector", lambda e: e.tensor_add(MID, LO, HI), waits=tk(prev))
        m2 = p.op("vector", lambda e: e.tensor_scalar_mul(MID, MID, 0.5), waits=tk(m1))
        c1 = p.op("vector", lambda e: e.tensor_scalar(TMP64, SAL, MID, None, ALU.is_ge), waits=tk(m2))
        c2 = p.op("vector", lambda e: e.reduce_sum(CB, TMP64, axis=AX.X), waits=tk(c1))
        c3 = p.op("tensor", lambda e: e.matmul(b1_.t[0:1, 0:1], g.ONESF[:], CB, start=True, stop=True),
                  waits=tk(c2, b1_.rd))
        c4 = p.op("vector", lambda e: e.tensor_copy(ROWT[0:1, 129:130], b1_.t[0:1, 0:1]), waits=tk(c3))
        c5 = p.op("tensor", lambda e: e.matmul(b1_.t[:, 0:1], g.ONESR[:], ROWT[0:1, 129:130],
                                               start=True, stop=True), waits=tk(c4))
        c6 = p.op("vector", lambda e: e.tensor_copy(C64, b1_.t[:, 0:1]), waits=tk(c5))
        b1_.rd = c6
        c7 = p.op("vector", lambda e: e.tensor_scalar(CO, C64, float(S // 2), None, ALU.is_ge), waits=tk(c6))
        TMA = g.MISC[:, 214:215]
        TMB = g.MISC[:, 215:216]
        s1 = p.op("vector", lambda e: e.tensor_sub(TMA, MID, LO), waits=tk(c7))
        s2 = p.op("vector", lambda e: e.tensor_sub(TMB, HI, MID), waits=tk(c7))
        s3 = p.op("vector", lambda e: e.tensor_mul(TMA, TMA, CO), waits=tk(s1))
        s4 = p.op("vector", lambda e: e.tensor_mul(TMB, TMB, CO), waits=tk(s2))
        s5 = p.op("vector", lambda e: e.tensor_add(LO, LO, TMA), waits=tk(s3))
        s6 = p.op("vector", lambda e: e.tensor_add(HI, MID, TMB), waits=tk(s4, s2))
        prev = s6
        prev2 = s5
    THR = g.MISC[0:1, 208:209]
    SGROW = g.SCf[0:1, 0:2048]
    sg1 = p.op("scalar", lambda e: e.activation(SGROW, srow, AF.Sigmoid), waits=tk(*sc_toks, prev))
    sg2 = p.op("vector", lambda e: e.tensor_single_scalar(srow, srow, THR, ALU.is_ge), waits=tk(sg1, prev, prev2))
    sg3 = p.op("vector", lambda e: e.tensor_mul(srow, srow, SGROW), waits=tk(sg2))

    HM_ = g.SBIG[:, 0:16384].rearrange("p (o t) -> p o t", o=32)
    mod_x_toks = []
    for t in range(4):
        ts = slice(t * 512, (t + 1) * 512)
        mb = g.PS[5]
        bcm = p.op("tensor", lambda e, t=t, mb=mb: e.matmul(
            mb.t[:, 0:512], g.ONESR[:], srow[0:1, t * 512:(t + 1) * 512], start=True, stop=True),
            waits=tk(sg3, mb.rd))
        mcp = p.op("vector", lambda e, t=t, mb=mb: e.tensor_copy(g.SCf[:, (t % 2) * 512:(t % 2) * 512 + 512], mb.t[:, 0:512]),
                   waits=tk(bcm, mod_x_toks[-1] if t >= 2 else None))
        mb.rd = mcp
        MBC = g.SCf[:, (t % 2) * 512:(t % 2) * 512 + 512]
        hlast = None
        for og in range(8):
            b01 = og % 2
            wdst = g.WB[:, b01, :].rearrange("p (k o) -> p k o", k=8)
            tw = p.dma("gpsimd", wdst, dr["l1_W1"][:, og], waits=tk(wb_rd[b01]))
            for oc in range(4):
                oabs = og * 4 + oc
                bk = g.PS[oabs % 4]
                mm = None
                for c in range(FC):
                    mm = p.add("tensor", lambda e, c=c, oc=oc, b01=b01, ts=ts, bk=bk: e.matmul(
                        bk.t[:, 0:512], g.WB[:, b01, c * 512 + oc * 128: c * 512 + oc * 128 + 128],
                        g.XN[:, c, ts], start=(c == 0), stop=(c == FC - 1)),
                        waits=tk(ln[c][t], *( (tw, bk.rd) if c == 0 else ())),
                        inc=p.esem["tensor"] if c == FC - 1 else None)
                wb_rd[b01] = mm
                gv = p.op("scalar", lambda e, oabs=oabs, bk=bk: e.activation(
                    HM_[:, oabs, 0:512], bk.t[:, 0:512], AF.Gelu, bias=MC("b1m", oabs)),
                    waits=tk(mm, mtok["b1m"], *( [q_toks[FC-1][3]] if (t == 0 and og == 0 and oc == 0) else [] )))
                bk.rd = gv
                hlast = gv
        for og in range(FC):
            b01 = og % 2
            wdst = g.WB[:, b01, :].rearrange("p (k o) -> p k o", k=32)
            tw = p.dma("gpsimd", wdst, dr["l1_W2"][:, og], waits=tk(wb_rd[b01]))
            bk = g.PS[og % 4]
            mm = None
            for k in range(32):
                mm = p.add("tensor", lambda e, k=k, b01=b01, bk=bk: e.matmul(
                    bk.t[:, 0:512], g.WB[:, b01, k * 128:(k + 1) * 128],
                    HM_[:, k, 0:512], start=(k == 0), stop=(k == 31)),
                    waits=tk(*( (tw, bk.rd, hlast) if k == 0 else ())),
                    inc=p.esem["tensor"] if k == 31 else None)
            wb_rd[b01] = mm
            x1_ = p.op("vector", lambda e, og=og, bk=bk, MBC=MBC: e.scalar_tensor_tensor(
                g.SCf[:, 1024 + (og % 2) * 512: 1024 + (og % 2) * 512 + 512],
                bk.t[:, 0:512], MC("b2m", og), MBC, ALU.add, ALU.mult),
                waits=tk(mm, mtok["b2m"], mcp, mod_x_toks[-2] if len(mod_x_toks) >= 2 else None))
            bk.rd = x1_
            x2_ = p.op("vector", lambda e, og=og, ts=ts: e.tensor_add(
                g.X[:, og, ts], g.X[:, og, ts],
                g.SCf[:, 1024 + (og % 2) * 512: 1024 + (og % 2) * 512 + 512]), waits=tk(x1_))
            x3_ = p.op("vector", lambda e, og=og, ts=ts: e.tensor_add(
                g.X[:, og, ts], g.X[:, og, ts], g.XN[:, og, ts]), waits=tk(x2_))
            mod_x_toks.append(x3_)

    g.l1_state = dict(QV=QV, q_toks=q_toks, t_kag=t_kag, t_vag=t_vag, t_nag=t_nag,
                      mod_x_toks=mod_x_toks, wb_rd=wb_rd, mtok=mtok, MC=MC, XL=XL)
    _emit_l1_flash(g, ln)


def _emit_l1_flash(g, ln):
    p = g.p
    dr = g.dram
    st = g.l1_state
    QV, q_toks = st["QV"], st["q_toks"]
    MC, mtok = st["MC"], st["mtok"]
    wb_rd = st["wb_rd"]
    ATT = g.SBIG[:, 0:16384].rearrange("p (c t) -> p c t", c=8)
    KCH = lambda s: g.SC[:, s * 1024: (s + 1) * 1024].rearrange("p (c t) -> p c t", c=8)
    VCH = lambda s: g.SC[:, 2048 + s * 1024: 2048 + (s + 1) * 1024]
    ECH = lambda s: g.SC[:, 4096 + s * 512: 4096 + (s + 1) * 512]
    NKC = lambda s: g.SCf[:, 2560 + s * 16: 2560 + (s + 1) * 16]
    RBC = g.SCf[:, 2592:3104]
    kch_rd = [st["mod_x_toks"][-1], st["mod_x_toks"][-1]]
    vch_rd = [st["mod_x_toks"][-1], st["mod_x_toks"][-1]]
    ech_rd = [None, None]
    nk_rd = [None, None]
    att_toks = []
    for qt in range(4):
        qs = slice(qt * 512, (qt + 1) * 512)
        for hg in range(4):
            nbank = [g.PS[i] for i in range(4)]
            sbank = [g.PS[4], g.PS[5]]
            den_end = [None] * 4
            for kc in range(64):
                s2 = kc % 2
                r = kc // 16
                tl = (kc % 16) * 128
                tkk = p.dma("sync", KCH(s2)[:],
                            g.kag_out[r * D:(r + 1) * D, tl:tl + 128].rearrange("(c p) t -> p c t", p=128),
                            waits=tk(st["t_kag"], kch_rd[s2]))
                tkv = p.dma("sync", VCH(s2), g.vag_out[kc * 128:(kc + 1) * 128, :],
                            waits=tk(st["t_vag"], vch_rd[s2]))
                tkn = p.dma("sync", NKC(s2), g.nag_out[kc * 128:(kc + 1) * 128, :],
                            waits=tk(st["t_nag"], nk_rd[s2]))
                for h4 in range(4):
                    h = hg * 4 + h4
                    hp = 64 * (h % 2)
                    hc = h // 2
                    sb_ = sbank[h4 % 2]
                    mm = p.op("tensor", lambda e, h=h, hp=hp, hc=hc, s2=s2, sb_=sb_, qs=qs: e.matmul(
                        sb_.t[:, 0:512], KCH(s2)[hp:hp + 64, hc, :], QV[hp:hp + 64, hc, qs],
                        start=True, stop=True),
                        waits=tk(tkk, q_toks[hc][qt] if kc == 0 else None, sb_.rd))
                    ex = p.op("scalar", lambda e, h=h, s2=s2, sb_=sb_: e.activation(
                        ECH(s2), sb_.t[:, 0:512], AF.Exp, bias=NKC(s2)[:, h:h + 1], scale=2.0 / TEMP),
                        waits=tk(mm, tkn, ech_rd[s2]))
                    sb_.rd = ex
                    nb = nbank[h4]
                    hm_ = h % 16
                    mmn = p.add("tensor", lambda e, hm_=hm_, s2=s2, nb=nb, kc=kc: e.matmul(
                        nb.t[0:64, 0:512], VCH(s2)[:, 64 * hm_:64 * hm_ + 64], ECH(s2),
                        start=(kc == 0), stop=(kc == 63)),
                        waits=tk(ex, tkv, nb.rd if kc == 0 else None), inc=None)
                    mmd = p.add("tensor", lambda e, s2=s2, nb=nb, kc=kc: e.matmul(
                        nb.t[64:65, 0:512], g.ONESB[:], ECH(s2),
                        start=(kc == 0), stop=(kc == 63), tile_position=(0, 64)),
                        waits=tk(), inc=p.esem["tensor"])
                    if kc == 63:
                        den_end[h4] = mmd
                    ech_rd[s2] = mmd
                    if h4 == 3:
                        kch_rd[s2] = mm
                        vch_rd[s2] = mmd
                        nk_rd[s2] = ex
            for h4 in range(4):
                h = hg * 4 + h4
                hp = 64 * (h % 2)
                hc = h // 2
                nb = nbank[h4]
                grp_end = den_end[h4]
                DEN = g.SCf[0:1, 3232:3744]
                r1 = p.op("vector", lambda e, nb=nb, DEN=DEN: e.reciprocal(DEN[0:1, 0:512], nb.t[64:65, 0:512]),
                          waits=tk(grp_end))
                bb = sbank[h4 % 2]
                r2 = p.op("tensor", lambda e, bb=bb, DEN=DEN: e.matmul(
                    bb.t[0:64, 0:512], g.ONESR[0:1, 0:64], DEN[0:1, 0:512],
                    start=True, stop=True), waits=tk(r1, bb.rd))
                r3 = p.op("vector", lambda e, bb=bb: e.tensor_copy(RBC[0:64, :], bb.t[0:64, 0:512]), waits=tk(r2))
                bb.rd = r3
                r4 = p.op("vector", lambda e, h=h, hp=hp, hc=hc, nb=nb, qs=qs: e.tensor_mul(
                    ATT[hp:hp + 64, hc, qs], nb.t[0:64, 0:512], RBC[0:64, :]), waits=tk(r3))
                nb.rd = r4
                att_toks.append(r4)

    U = g.SBIG[:, 16384:32768].rearrange("p (c t) -> p c t", c=8)
    u_toks = [[None] * 4 for _ in range(FC)]
    for og in range(2):
        b01 = og % 2
        wdst = g.WB[:, b01, :].rearrange("p (k o) -> p k o", k=8)
        tw = p.dma("gpsimd", wdst, dr["l1_wo"][:, og], waits=tk(wb_rd[b01]))
        for oc in range(4):
            c_out = og * 4 + oc
            for t in range(4):
                bk = g.PS[(oc * 4 + t) % 4]
                mm = None
                for c in range(FC):
                    mm = p.add("tensor", lambda e, c=c, oc=oc, t=t, b01=b01, bk=bk: e.matmul(
                        bk.t[:, 0:512], g.WB[:, b01, c * 512 + oc * 128: c * 512 + oc * 128 + 128],
                        ATT[:, c, t * 512:(t + 1) * 512], start=(c == 0), stop=(c == FC - 1)),
                        waits=tk(att_toks[-1] if (c == 0 and og == 0 and oc == 0 and t == 0) else None,
                                 *( (tw, bk.rd) if c == 0 else ())),
                        inc=p.esem["tensor"] if c == FC - 1 else None)
                ev = p.op("vector", lambda e, c_out=c_out, t=t, bk=bk: e.scalar_tensor_tensor(
                    U[:, c_out, t * 512:(t + 1) * 512], bk.t[:, 0:512], MC("bo", c_out),
                    g.XN[:, c_out, t * 512:(t + 1) * 512], ALU.add, ALU.add),
                    waits=tk(mm, mtok["bo"]))
                bk.rd = ev
                u_toks[c_out][t] = ev
        wb_rd[b01] = u_toks[og * 4 + 3][3]

    u_toks = _proj_fullvec(g, U, u_toks, "l1u")

    XN2 = ATT
    ln2 = emit_ln(g, src=U, dst=XN2,
                  src_toks=[u_toks[c][t] for c in range(FC) for t in range(4)],
                  gb=(g.MISC[:, 40:48], g.MISC[:, 48:56]),
                  sqreg=g.SC, tmpreg=g.SCf[:, 2048:3072])

    u2_toks = [[None] * 4 for _ in range(FC)]
    HH = g.SC.rearrange("p (o t) -> p o t", o=32)
    hh_rd = None
    for tt in range(8):
        ts = slice(tt * 256, (tt + 1) * 256)
        t4 = tt // 2
        hlast = None
        for og in range(8):
            b01 = og % 2
            wdst = g.WB[:, b01, :].rearrange("p (k o) -> p k o", k=8)
            tw = p.dma("gpsimd", wdst, dr["l1_w1"][:, og], waits=tk(wb_rd[b01]))
            for oc in range(4):
                oabs = og * 4 + oc
                bk = g.PS[oabs % 4]
                mm = None
                for c in range(FC):
                    mm = p.add("tensor", lambda e, c=c, oc=oc, b01=b01, ts=ts, bk=bk: e.matmul(
                        bk.t[:, 0:256], g.WB[:, b01, c * 512 + oc * 128: c * 512 + oc * 128 + 128],
                        XN2[:, c, ts], start=(c == 0), stop=(c == FC - 1)),
                        waits=tk(ln2[c][t4], *( (tw, bk.rd, hh_rd) if c == 0 else ())),
                        inc=p.esem["tensor"] if c == FC - 1 else None)
                wb_rd[b01] = mm
                gv = p.op("scalar", lambda e, oabs=oabs, bk=bk: e.activation(
                    HH[:, oabs, 0:256], bk.t[:, 0:256], AF.Gelu, bias=MC("bf1", oabs)),
                    waits=tk(mm, mtok["bf1"]))
                bk.rd = gv
                hlast = gv
        for og in range(FC):
            b01 = og % 2
            wdst = g.WB[:, b01, :].rearrange("p (k o) -> p k o", k=32)
            tw = p.dma("gpsimd", wdst, dr["l1_w2"][:, og], waits=tk(wb_rd[b01]))
            bk = g.PS[og % 4]
            mm = None
            for k in range(32):
                mm = p.add("tensor", lambda e, k=k, b01=b01, bk=bk: e.matmul(
                    bk.t[:, 0:256], g.WB[:, b01, k * 128:(k + 1) * 128],
                    HH[:, k, 0:256], start=(k == 0), stop=(k == 31)),
                    waits=tk(*( (tw, bk.rd, hlast) if k == 0 else ())),
                    inc=p.esem["tensor"] if k == 31 else None)
            wb_rd[b01] = mm
            hh_rd = mm
            ev = p.op("vector", lambda e, og=og, ts=ts, bk=bk: e.scalar_tensor_tensor(
                U[:, og, ts], bk.t[:, 0:256], MC("bf2", og), U[:, og, ts], ALU.add, ALU.add),
                waits=tk(mm, mtok["bf2"]))
            bk.rd = ev
            u2_toks[og][t4] = ev

    u2_toks = _proj_fullvec(g, U, u2_toks, "l1u2")
    x_toks = list(st["mod_x_toks"])
    for t in range(4):
        ts = slice(t * 512, (t + 1) * 512)
        for c in range(FC):
            xa = p.op("vector", lambda e, c=c, ts=ts: e.tensor_add(
                g.X[:, c, ts], g.X[:, c, ts], U[:, c, ts]), waits=tk(u2_toks[c][t]))
            x_toks.append(xa)
    g.x_toks = x_toks


def _proj_generic(g, wname, src3, src_toks, out3, wb_rd, token_major=False,
                  extra_first=()):
    p = g.p
    dr = g.dram
    toks = [[None] * 4 for _ in range(FC)]
    last = None
    for og in range(2):
        b01 = og % 2
        wdst = g.WB[:, b01, :].rearrange("p (k o) -> p k o", k=8)
        tw = p.dma("gpsimd", wdst, dr[wname][:, og], waits=tk(wb_rd[b01]))
        if token_major:
            for ti in range(16):
                bk = g.PS[ti % 4]
                mm = None
                for c in range(FC):
                    mm = p.add("tensor", lambda e, c=c, ti=ti, b01=b01, bk=bk: e.matmul(
                        bk.t[:, 0:512], src3[:, c, ti * 128:(ti + 1) * 128],
                        g.WB[:, b01, c * 512: c * 512 + 512], start=(c == 0), stop=(c == FC - 1)),
                        waits=tk(src_toks[c][ti // 4], *( (tw, bk.rd) + tuple(extra_first) if c == 0 else ())),
                        inc=p.esem["tensor"] if c == FC - 1 else None)
                ev = p.op("vector", lambda e, ti=ti, og=og, bk=bk: e.tensor_copy(
                    out3[:, ti, og * 512:(og + 1) * 512], bk.t[:, 0:512]), waits=tk(mm))
                bk.rd = ev
                last = ev
            wb_rd[b01] = last
        else:
            for oc in range(4):
                c_out = og * 4 + oc
                for t in range(4):
                    bk = g.PS[(oc * 4 + t) % 4]
                    mm = None
                    for c in range(FC):
                        mm = p.add("tensor", lambda e, c=c, oc=oc, t=t, b01=b01, bk=bk: e.matmul(
                            bk.t[:, 0:512], g.WB[:, b01, c * 512 + oc * 128: c * 512 + oc * 128 + 128],
                            src3[:, c, t * 512:(t + 1) * 512], start=(c == 0), stop=(c == FC - 1)),
                            waits=tk(src_toks[c][t], *( (tw, bk.rd) + tuple(extra_first) if c == 0 else ())),
                            inc=p.esem["tensor"] if c == FC - 1 else None)
                    ev = p.op("vector", lambda e, c_out=c_out, t=t, bk=bk: e.tensor_copy(
                        out3[:, c_out, t * 512:(t + 1) * 512], bk.t[:, 0:512]), waits=tk(mm))
                    bk.rd = ev
                    toks[c_out][t] = ev
                    last = ev
            wb_rd[b01] = last
    return toks, last


def emit_l2(g, ln):
    p = g.p
    dr = g.dram
    SCL = 0.125
    QV = g.SBIG[:, 0:16384].rearrange("p (c t) -> p c t", c=8)
    KV = g.SBIG[:, 16384:32768].rearrange("p (c t) -> p c t", c=8)
    VT = g.SBIG[:, 16384:32768].rearrange("p (tc d) -> p tc d", tc=16)
    wb_rd = [None, None]

    t_bck = p.dma("sync", g.MISC[0:64, 0:1], dr["l2_bck"][:])
    t_bcv = p.dma("sync", g.MISC[0:64, 1:2], dr["l2_bcv"][:])
    t_b1 = p.dma("sync", g.MISC[:, 10:266], dr["l2_b1"][:])
    t_b2 = p.dma("sync", g.MISC[:, 266:330], dr["l2_b2"][:])

    q_toks, _ = _proj_generic(g, "l2_wq", g.XN, ln, QV, wb_rd)
    k_toks, klast = _proj_generic(g, "l2_wk", g.XN, ln, KV, wb_rd)
    t_kst = p.dma("sync", g.kag_in.rearrange("(c p) t -> p c t", p=128), KV[:],
                  waits=tk(*[k_toks[c][t] for c in range(FC) for t in range(4)]))
    t_kag = p.add("gpsimd", lambda e: e.collective_compute(
        "AllGather", ALU.bypass, replica_groups=GROUPS, ins=[g.kag_in[:]], outs=[g.kag_out[:]]),
        waits=tk(t_kst), inc=p.csem, amt=1)
    _, ktlast = _proj_generic(g, "l2_wk", g.XN, ln, VT, wb_rd, token_major=True,
                              extra_first=(t_kst,))
    t_ktst = p.dma("sync", g.ktm_in.rearrange("(tc p) d -> p tc d", p=128), VT[:], waits=tk(ktlast))
    t_ktag = p.add("gpsimd", lambda e: e.collective_compute(
        "AllGather", ALU.bypass, replica_groups=GROUPS, ins=[g.ktm_in[:]], outs=[g.ktm_out[:]]),
        waits=tk(t_ktst), inc=p.csem, amt=1)
    _, vlast = _proj_generic(g, "l2_wv", g.XN, ln, VT, wb_rd, token_major=True,
                             extra_first=(t_ktst,))
    t_vst = p.dma("sync", g.vag_in.rearrange("(tc p) d -> p tc d", p=128), VT[:], waits=tk(vlast))
    t_vag = p.add("gpsimd", lambda e: e.collective_compute(
        "AllGather", ALU.bypass, replica_groups=GROUPS, ins=[g.vag_in[:]], outs=[g.vag_out[:]]),
        waits=tk(t_vst), inc=p.csem, amt=1)
    vf_toks, vflast = _proj_generic(g, "l2_wv", g.XN, ln, KV, wb_rd, extra_first=(t_vst,))
    t_vfst = p.dma("sync", g.vfag_in.rearrange("(c p) t -> p c t", p=128), KV[:], waits=tk(vflast))
    t_vfag = p.add("gpsimd", lambda e: e.collective_compute(
        "AllGather", ALU.bypass, replica_groups=GROUPS, ins=[g.vfag_in[:]], outs=[g.vfag_out[:]]),
        waits=tk(t_vfst), inc=p.csem, amt=1)

    qs_toks = []
    for c in range(FC):
        t1 = p.op("vector", lambda e, c=c: e.reduce_sum(
            g.MISC[:, 2 + c:3 + c], QV[:, c, :], axis=AX.X),
            waits=tk(*[q_toks[c][t] for t in range(4)]))
        qs_toks.append(t1)
    t_qst = p.dma("sync", g.qs_in[:], g.MISC[:, 2:10], waits=tk(*qs_toks))
    t_qar = p.add("gpsimd", lambda e: e.collective_compute(
        "AllReduce", ALU.add, replica_groups=GROUPS, ins=[g.qs_in[:]], outs=[g.qs_out[:]]),
        waits=tk(t_qst), inc=p.csem, amt=1)

    CK = g.SBIG[:, 16384:20480].rearrange("p (hp n) -> p hp n", hp=8)
    CVF = g.SBIG[:, 20480:24576].rearrange("p (hp n) -> p hp n", hp=8)
    CVT = g.SBIG[:, 24576:28672].rearrange("p (nc d) -> p nc d", nc=4)
    WCK = g.SBIG[:, 28672:30720].rearrange("p (j o) -> p j o", j=32)
    WCV = g.SBIG[:, 30720:32768].rearrange("p (j o) -> p j o", j=32)
    t_wck = p.dma("gpsimd", WCK[:], dr["l2_wck"][:], waits=tk(t_vfst))
    t_wcv = p.dma("gpsimd", WCV[:], dr["l2_wcv"][:], waits=tk(t_vfst))
    KWS = lambda s: g.SC[:, s * 2064:(s + 1) * 2064]
    kw_rd = [None, None]
    ck_toks = []
    for src_ag, src_tok, WCB, OUT, bcol, wtok in (
            (g.kag_out, t_kag, WCK, CK, g.MISC[0:64, 0:1], t_wck),
            (g.vfag_out, t_vfag, WCV, CVF, g.MISC[0:64, 1:2], t_wcv)):
        for w in range(4):
            nblk = 128 if w < 3 else 127
            for hp in range(8):
                s = (w * 8 + hp) % 2
                tkw = p.dma("sync", KWS(s)[:, 0:T], src_ag[w * D + hp * 128: w * D + (hp + 1) * 128, :],
                            waits=tk(kw_rd[s], src_tok))
                if w < 3:
                    tkw = p.dma("sync", KWS(s)[:, T:2064],
                                src_ag[(w + 1) * D + hp * 128: (w + 1) * D + (hp + 1) * 128, 0:16],
                                waits=tk(tkw))
                mm2 = None
                for h2 in range(2):
                    hb = 64 * h2
                    bk = g.PS[(hp * 2 + h2) % 4]
                    mm = None
                    for j in range(32):
                        mm = p.add("tensor", lambda e, j=j, h2=h2, hb=hb, s=s, nblk=nblk, bk=bk, WCB=WCB: e.matmul(
                            bk.t[0:64, 0:nblk], WCB[hb:hb + 64, j, :], KWS(s)[hb:hb + 64, j: j + 16 * (nblk - 1) + 1: 16],
                            start=(j == 0), stop=(j == 31)),
                            waits=tk(tkw, *( (wtok, bk.rd) if j == 0 else ())),
                            inc=p.esem["tensor"] if j == 31 else None)
                    ev = p.op("vector", lambda e, w=w, hp=hp, h2=h2, hb=hb, nblk=nblk, bk=bk, OUT=OUT, bcol=bcol: e.tensor_scalar_add(
                        OUT[hb:hb + 64, hp, w * 128: w * 128 + nblk], bk.t[0:64, 0:nblk], bcol),
                        waits=tk(mm, t_bck, t_bcv))
                    bk.rd = ev
                    mm2 = mm
                    ck_toks.append(ev)
                kw_rd[s] = mm2
    cvt_toks = []
    for h in range(16):
        hb = 64 * (h % 2)
        hp = h // 2
        for nc4 in range(4):
            nblk = 128 if nc4 < 3 else 127
            pb = g.PT[(h * 4 + nc4) % 2]
            tt = p.op("tensor", lambda e, h=h, hb=hb, hp=hp, nc4=nc4, nblk=nblk, pb=pb: e.transpose(
                pb.t[0:nblk, 0:64], CVF[hb:hb + 64, hp, nc4 * 128: nc4 * 128 + nblk], g.IDN[hb:hb + 64, hb:hb + 64]),
                waits=tk(ck_toks[-1] if (h == 0 and nc4 == 0) else None, pb.rd, g.tok_idn))
            cp = p.op("vector", lambda e, h=h, nc4=nc4, nblk=nblk, pb=pb: e.tensor_copy(
                CVT[0:nblk, nc4, h * 64:(h + 1) * 64], pb.t[0:nblk, 0:64]), waits=tk(tt))
            pb.rd = cp
            cvt_toks.append(cp)

    BIB = g.PS[5]
    EW = g.SC[:, 2568:3592]
    bi_last = None
    first = True
    for qt in range(16):
        for h in range(16):
            hb = 64 * (h % 2)
            hp = h // 2
            sb_ = g.PS[(qt + h) % 4]
            mm = p.op("tensor", lambda e, qt=qt, h=h, hb=hb, hp=hp, sb_=sb_: e.matmul(
                sb_.t[:, 0:511], QV[hb:hb + 64, hp, qt * 128:(qt + 1) * 128], CK[hb:hb + 64, hp, 0:511],
                start=True, stop=True),
                waits=tk(ck_toks[-1] if first else None, q_toks[hp][qt // 4] if first else None, sb_.rd))
            ex = p.op("scalar", lambda e, sb_=sb_: e.activation(
                EW[:, 0:511], sb_.t[:, 0:511], AF.Exp, scale=SCL), waits=tk(mm, bi_last))
            rd_ = p.op("vector", lambda e, sb_=sb_: e.reduce_sum(
                g.MISC[:, 378:379], EW[:, 0:511], axis=AX.X), waits=tk(ex))
            rc = p.op("vector", lambda e: e.reciprocal(g.MISC[:, 379:380], g.MISC[:, 378:379]), waits=tk(rd_))
            ts_ = p.op("vector", lambda e: e.tensor_scalar_mul(EW[:, 0:511], EW[:, 0:511], g.MISC[:, 379:380]),
                       waits=tk(rc))
            sb_.rd = ts_
            bi_last = p.add("tensor", lambda e, qt=qt, h=h: e.matmul(
                BIB.t[0:1, 0:511], g.ONESB[:], EW[:, 0:511],
                start=(qt == 0 and h == 0), stop=(qt == 15 and h == 15)),
                waits=tk(ts_, BIB.rd if first else None),
                inc=p.esem["tensor"])
            first = False
    bi_ev = p.op("vector", lambda e: e.tensor_copy(g.STAT[0:1, 0:511], BIB.t[0:1, 0:511]), waits=tk(bi_last))
    BIB.rd = bi_ev
    t_bst = p.dma("sync", g.bi_in[:], g.STAT[0:1, 0:511], waits=tk(bi_ev))
    t_bar = p.add("gpsimd", lambda e: e.collective_compute(
        "AllReduce", ALU.add, replica_groups=GROUPS, ins=[g.bi_in[:]], outs=[g.bi_out[:]]),
        waits=tk(t_bst), inc=p.csem, amt=1)
    t_bld = p.dma("sync", g.STAT[0:1, 0:511], g.bi_out[:], waits=tk(t_bar))

    TOPV = g.STAT[0:1, 512:528]
    STATU = g.STAT.bitcast(mybir.dt.uint32)
    IDXU = STATU[0:1, 528:544]
    BI2 = g.STAT[0:1, 544:1055]
    t_dbi = None
    if "d_bi" in g.dbg:
        t_dbi = p.dma("sync", g.dbg["d_bi"][:], g.STAT[0:1, 0:511], waits=tk(t_bld))
    tt1 = p.op("vector", lambda e: e.max(TOPV[0:1, 0:8], g.STAT[0:1, 0:511]), waits=tk(t_bld))
    tt2 = p.op("vector", lambda e: e.match_replace(BI2[0:1, 0:511], TOPV[0:1, 0:8], g.STAT[0:1, 0:511], -1.0),
               waits=tk(tt1))
    tt3 = p.op("vector", lambda e: e.max(TOPV[0:1, 8:16], BI2[0:1, 0:511]), waits=tk(tt2))
    it1 = p.op("vector", lambda e: e.max_index(IDXU[0:1, 0:8], TOPV[0:1, 0:8], g.STAT[0:1, 0:511]), waits=tk(tt3))
    cv2 = p.op("vector", lambda e: e.max_index(IDXU[0:1, 8:16], TOPV[0:1, 8:16], BI2[0:1, 0:511]), waits=tk(it1))
    IDXF2 = g.STAT[0:1, 1056:1072]
    cf1 = p.op("vector", lambda e: e.tensor_copy(IDXF2, IDXU), waits=tk(cv2))
    if "d_idx" in g.dbg:
        p.dma("sync", g.dbg["d_idx"][:], IDXF2, waits=tk(cf1))
    bbx = g.PS[4]
    bx1 = p.op("tensor", lambda e: e.matmul(bbx.t[:, 0:16], g.ONESR[:], IDXF2, start=True, stop=True),
               waits=tk(cf1, bbx.rd))
    BLKC = g.MISC[:, 380:396]
    bx2 = p.op("vector", lambda e: e.tensor_copy(BLKC, bbx.t[:, 0:16]), waits=tk(bx1))
    bbx.rd = bx2
    t_io = p.dma("sync", g.MISC[:, 396:397], dr["iota64"][:])
    SELF_ = g.MISC[:, 397:405]
    st_toks = []
    for ci in range(8):
        for half in (0, 64):
            bidx = 2 * ci + (half // 64)
            s1_ = p.op("vector", lambda e, ci=ci, half=half, bidx=bidx: e.scalar_tensor_tensor(
                SELF_[half:half + 64, ci:ci + 1], BLKC[half:half + 64, bidx:bidx + 1], 16.0,
                g.MISC[half:half + 64, 396:397], ALU.mult, ALU.add), waits=tk(bx2, t_io))
            s2_ = p.op("vector", lambda e, ci=ci, half=half: e.tensor_scalar_min(
                SELF_[half:half + 64, ci:ci + 1], SELF_[half:half + 64, ci:ci + 1], float(S - 1)),
                waits=tk(s1_))
            st_toks.append(s2_)
    SELTI = g.MISCI[:, 405:413]
    cvt_ = p.op("vector", lambda e: e.tensor_copy(SELTI, SELF_), waits=tk(*st_toks))
    from concourse.bass_core_rust import IndirectOffsetOnAxis as _IOA
    gather_toks = []
    KTCH = g.SC[:, 0:1024]
    KSF = g.SC[:, 1024:2048].rearrange("p (c t) -> p c t", c=8)
    ktch_rd = None
    for ci in range(8):
        gk = p.add("gpsimd", lambda e, ci=ci: e.indirect_dma_start(
            KTCH, None, g.ktm_out[0:128, :], _IOA(SELTI[:, ci:ci + 1], 0)),
            waits=tk(cvt_, t_ktag, ktch_rd, kw_rd[0], kw_rd[1]), inc=p.qsem["gpsimd"], amt=16)
        trs = []
        for fc in range(8):
            pb = g.PT[fc % 2]
            tt = p.op("tensor", lambda e, fc=fc, pb=pb: e.transpose(
                pb.t[:, 0:128], KTCH.rearrange("p (c t) -> p c t", c=8)[:, fc, :], g.IDN[:]),
                waits=tk(gk, pb.rd, g.tok_idn))
            cp = p.op("vector", lambda e, fc=fc, pb=pb: e.tensor_copy(KSF[:, fc, :], pb.t[:, 0:128]),
                      waits=tk(tt))
            pb.rd = cp
            trs.append(cp)
        ktch_rd = trs[-1]
        gks = p.dma("sync", g.ksel[:, ci * 128:(ci + 1) * 128].rearrange("(c p) t -> p c t", p=128),
                    KSF[:], waits=tk(*trs))
        VTCH = g.SC[:, 2048:3072]
        gv1 = p.add("gpsimd", lambda e, ci=ci, VTCH=VTCH: e.indirect_dma_start(
            VTCH, None, g.vag_out[0:128, :], _IOA(SELTI[:, ci:ci + 1], 0)),
            waits=tk(cvt_, t_vag, gather_toks[-1] if gather_toks else None), inc=p.qsem["gpsimd"], amt=16)
        gv = p.dma("sync", g.vsel[ci * 128:(ci + 1) * 128, :], VTCH, waits=tk(gv1))
        gather_toks += [gks, gv]

    t_qld = p.dma("sync", g.MISC[:, 2:10], g.qs_out[:], waits=tk(t_qar))
    t_wgb = p.dma("gpsimd", g.SC[:, 3592:3595], dr["l2_wgb"][:])
    t_bgr = p.dma("sync", g.STAT[0:1, 1100:1148], dr["l2_bgr"][:])
    QSB = g.SC[:, 3596:3604]
    cq = p.op("vector", lambda e: e.tensor_copy(QSB, g.MISC[:, 2:10]), waits=tk(t_qld))
    GROW = g.STAT[0:1, 1148:1196]
    g_toks = []
    for h in range(16):
        hb = 64 * (h % 2)
        bb = g.PS[4 + (h % 2)]
        m1 = p.op("tensor", lambda e, h=h, hb=hb, bb=bb: e.matmul(
            bb.t[0:1, 0:3], QSB[hb:hb + 64, h // 2: h // 2 + 1], g.SC[hb:hb + 64, 3592:3595],
            start=True, stop=True), waits=tk(cq, t_wgb, bb.rd))
        c1 = p.op("vector", lambda e, h=h, bb=bb: e.tensor_copy(GROW[0:1, 3 * h:3 * h + 3], bb.t[0:1, 0:3]),
                  waits=tk(m1))
        bb.rd = c1
        g_toks.append(c1)
    gs1 = p.op("vector", lambda e: e.tensor_scalar_mul(GROW, GROW, 1.0 / S), waits=tk(*g_toks))
    gs2 = p.op("vector", lambda e: e.tensor_add(GROW, GROW, g.STAT[0:1, 1100:1148]), waits=tk(gs1, t_bgr))
    gs3 = p.op("scalar", lambda e: e.activation(GROW, GROW, AF.Sigmoid), waits=tk(gs2))
    GC = g.MISC[0:64, 412:460]
    gc_toks = []
    for j in range(48):
        bb = g.PS[4 + (j % 2)]
        m1 = p.op("tensor", lambda e, j=j, bb=bb: e.matmul(
            bb.t[0:64, 0:1], g.ONESR[0:1, 0:64], GROW[0:1, j:j + 1], start=True, stop=True),
            waits=tk(gs3, bb.rd))
        c1 = p.op("vector", lambda e, j=j, bb=bb: e.tensor_copy(GC[:, j:j + 1], bb.t[0:64, 0:1]), waits=tk(m1))
        bb.rd = c1
        gc_toks.append(c1)
    gate_done = gc_toks[-1]

    ATT_A = g.SBIG[:, 20480:24576].rearrange("p (c t) -> p c t", c=4)
    ATT_B = g.SBIG[:, 28672:32768].rearrange("p (c t) -> p c t", c=4)
    def att_sl(c, qs_):
        return ATT_A[:, c, qs_] if c < 4 else ATT_B[:, c - 4, qs_]
    KCH = lambda s: g.SC[:, s * 1024: (s + 1) * 1024].rearrange("p (c t) -> p c t", c=8)
    VCH = lambda s: g.SC[:, 2048 + s * 1024: 2048 + (s + 1) * 1024]
    ECH = lambda s: g.SC[:, 4096 + s * 512: 4096 + (s + 1) * 512]
    RBC = g.SCf[:, 2592:3104]
    DENR = g.STAT[0:1, 0:512]
    TS_ = g.SCf[0:64, 3104:3616]
    kch_rd = [cvt_toks[-1], cvt_toks[-1]]
    vch_rd = [None, None]
    ech_rd = [None, None]
    x_toks = []
    for qh in range(2):
        att_toks = []
        for qt2 in range(2):
            q0 = qh * 1024 + qt2 * 512
            for hg in range(4):
                nbank = [g.PS[i] for i in range(4)]
                sbank = [g.PS[4], g.PS[5]]
                for src_i, nch in ((0, 4), (1, 8), (2, 4)):
                    den_end = [None] * 4
                    for kc in range(nch):
                        s2 = kc % 2
                        nblk = 128
                        if src_i == 0:
                            tkk = tkv = None
                            if kc == 3:
                                nblk = 127
                        elif src_i == 1:
                            tkk = p.dma("sync", KCH(s2)[:], g.ksel[:, kc * 128:(kc + 1) * 128].rearrange(
                                "(c p) t -> p c t", p=128), waits=tk(*gather_toks, kch_rd[s2]))
                            tkv = p.dma("sync", VCH(s2), g.vsel[kc * 128:(kc + 1) * 128, :],
                                        waits=tk(gather_toks[-1], vch_rd[s2]))
                        else:
                            w0 = T - 512 + kc * 128
                            tkk = p.dma("sync", KCH(s2)[:], g.kag_out[3 * D:4 * D, w0:w0 + 128].rearrange(
                                "(c p) t -> p c t", p=128), waits=tk(t_kag, kch_rd[s2]))
                            tkv = p.dma("sync", VCH(s2), g.vag_out[S - 512 + kc * 128: S - 512 + (kc + 1) * 128, :],
                                        waits=tk(t_vag, vch_rd[s2]))
                        for h4 in range(4):
                            h = hg * 4 + h4
                            hb = 64 * (h % 2)
                            hp = h // 2
                            sb_ = sbank[h4 % 2]
                            if src_i == 0:
                                mm = p.op("tensor", lambda e, hb=hb, hp=hp, kc=kc, nblk=nblk, sb_=sb_, q0=q0: e.matmul(
                                    sb_.t[0:nblk, 0:512], CK[hb:hb + 64, hp, kc * 128: kc * 128 + nblk],
                                    QV[hb:hb + 64, hp, q0:q0 + 512], start=True, stop=True),
                                    waits=tk(sb_.rd, *( (gate_done, t_dbi) if (qh == 0 and qt2 == 0 and hg == 0 and kc == 0 and h4 == 0) else ())))
                            else:
                                mm = p.op("tensor", lambda e, hb=hb, hp=hp, s2=s2, sb_=sb_, q0=q0: e.matmul(
                                    sb_.t[:, 0:512], KCH(s2)[hb:hb + 64, hp, :],
                                    QV[hb:hb + 64, hp, q0:q0 + 512], start=True, stop=True),
                                    waits=tk(tkk, sb_.rd))
                            ex = p.op("scalar", lambda e, s2=s2, sb_=sb_, nblk=nblk: e.activation(
                                ECH(s2)[0:nblk, :], sb_.t[0:nblk, 0:512], AF.Exp, scale=SCL),
                                waits=tk(mm, ech_rd[s2]))
                            sb_.rd = ex
                            nb = nbank[h4]
                            if src_i == 0:
                                lhsV = (lambda e, h=h, kc=kc, nblk=nblk: CVT[0:nblk, kc, 64 * h:64 * h + 64])
                            else:
                                lhsV = (lambda e, h=h, s2=s2: VCH(s2)[:, 64 * h:64 * h + 64])
                            mmn = p.add("tensor", lambda e, lhsV=lhsV, s2=s2, nb=nb, kc=kc, nblk=nblk, nch=nch: e.matmul(
                                nb.t[0:64, 0:512], lhsV(e), ECH(s2)[0:nblk, :],
                                start=(kc == 0), stop=(kc == nch - 1)),
                                waits=tk(ex, tkv, nb.rd if kc == 0 else None), inc=None)
                            mmd = p.add("tensor", lambda e, s2=s2, nb=nb, kc=kc, nblk=nblk, nch=nch: e.matmul(
                                nb.t[64:65, 0:512], g.ONESB[0:nblk, :], ECH(s2)[0:nblk, :],
                                start=(kc == 0), stop=(kc == nch - 1), tile_position=(0, 64)),
                                waits=tk(), inc=p.esem["tensor"])
                            if kc == nch - 1:
                                den_end[h4] = mmd
                            ech_rd[s2] = mmd
                            if h4 == 3:
                                kch_rd[s2] = mm
                                vch_rd[s2] = mmd
                    for h4 in range(4):
                        h = hg * 4 + h4
                        hb = 64 * (h % 2)
                        hc = h // 2
                        nb = nbank[h4]
                        r1 = p.op("vector", lambda e, nb=nb: e.reciprocal(DENR[0:1, 0:512], nb.t[64:65, 0:512]),
                                  waits=tk(den_end[h4]))
                        bb = sbank[h4 % 2]
                        r2 = p.op("tensor", lambda e, bb=bb: e.matmul(
                            bb.t[0:64, 0:512], g.ONESR[0:1, 0:64], DENR[0:1, 0:512],
                            start=True, stop=True), waits=tk(r1, bb.rd))
                        r3 = p.op("vector", lambda e, bb=bb: e.tensor_copy(RBC[0:64, :], bb.t[0:64, 0:512]), waits=tk(r2))
                        bb.rd = r3
                        qsl = slice(qt2 * 512, qt2 * 512 + 512)
                        TSH = g.SCf[hb:hb + 64, 3104:3616]
                        r4 = p.op("vector", lambda e, h=h, src_i=src_i, nb=nb, TSH=TSH: e.scalar_tensor_tensor(
                            TSH, nb.t[0:64, 0:512], GC[0:64, 3 * h + src_i: 3 * h + src_i + 1], RBC[0:64, :],
                            ALU.mult, ALU.mult), waits=tk(r3))
                        nb.rd = r4
                        if src_i == 0:
                            r5 = p.op("vector", lambda e, hb=hb, hc=hc, qsl=qsl, TSH=TSH: e.tensor_copy(
                                att_sl(hc, qsl)[hb:hb + 64, :], TSH), waits=tk(r4))
                        else:
                            r5 = p.op("vector", lambda e, hb=hb, hc=hc, qsl=qsl, TSH=TSH: e.tensor_add(
                                att_sl(hc, qsl)[hb:hb + 64, :], att_sl(hc, qsl)[hb:hb + 64, :], TSH), waits=tk(r4))
                        att_toks.append(r5)
        for og in range(2):
            b01 = og % 2
            wdst = g.WB[:, b01, :].rearrange("p (k o) -> p k o", k=8)
            tw = p.dma("gpsimd", wdst, dr["l2_wo"][:, og], waits=tk(wb_rd[b01]))
            for oc in range(4):
                c_out = og * 4 + oc
                for t2 in range(2):
                    qsl = slice(t2 * 512, t2 * 512 + 512)
                    bk = g.PS[(oc * 2 + t2) % 4]
                    mm = None
                    for c in range(FC):
                        mm = p.add("tensor", lambda e, c=c, oc=oc, b01=b01, qsl=qsl, bk=bk: e.matmul(
                            bk.t[:, 0:512], g.WB[:, b01, c * 512 + oc * 128: c * 512 + oc * 128 + 128],
                            att_sl(c, qsl), start=(c == 0), stop=(c == FC - 1)),
                            waits=tk(att_toks[-1] if (c == 0 and og == 0 and oc == 0 and t2 == 0) else None,
                                     *( (tw, bk.rd) if c == 0 else ())),
                            inc=p.esem["tensor"] if c == FC - 1 else None)
                    xa = p.op("vector", lambda e, c_out=c_out, qh=qh, qsl=qsl, bk=bk: e.tensor_add(
                        g.X[:, c_out, qh * 1024 + qsl.start: qh * 1024 + qsl.stop],
                        g.X[:, c_out, qh * 1024 + qsl.start: qh * 1024 + qsl.stop], bk.t[:, 0:512]),
                        waits=tk(mm))
                    bk.rd = xa
                    x_toks.append(xa)
            wb_rd[b01] = x_toks[-1]

    WRB2 = g.SC[:, 0:64].rearrange("p (k o) -> p k o", k=8)
    t_wr2 = p.dma("gpsimd", WRB2[:], dr["l2_wr"][:, 0], waits=tk(x_toks[-1] if x_toks else None, kch_rd[0], kch_rd[1]))
    t_br2 = p.dma("gpsimd", g.BROW[0:1, 0:8], dr["l2_br"][:], waits=tk())
    LG = g.SCf[:, 512:640].rearrange("p (tc e) -> p tc e", tc=16)
    lg_toks = []
    for ti in range(16):
        bk = g.PS[ti % 4]
        mm = None
        for c in range(FC):
            mm = p.add("tensor", lambda e, c=c, ti=ti, bk=bk: e.matmul(
                bk.t[0:128, 0:8], g.XN[:, c, ti * 128:(ti + 1) * 128], WRB2[:, c, :],
                start=(c == 0), stop=False),
                waits=tk(t_wr2 if c == 0 else None, bk.rd if c == 0 else None), inc=None)
        mm = p.add("tensor", lambda e, bk=bk: e.matmul(
            bk.t[0:128, 0:8], g.ONESRB[:], g.BROW[0:1, 0:8], start=False, stop=True),
            waits=tk(t_br2), inc=p.esem["tensor"])
        ev = p.op("vector", lambda e, ti=ti, bk=bk: e.tensor_copy(LG[:, ti, :], bk.t[0:128, 0:8]), waits=tk(mm))
        bk.rd = ev
        lg_toks.append(ev)
    WEX = g.SC[:, 1024:1152].rearrange("p (tc e) -> p tc e", tc=16)
    TC1 = g.MISC[:, 378:379]; TC2 = g.MISC[:, 379:380]
    EQ1 = g.SCf[:, 640:648]; EQ2 = g.SCf[:, 648:656]
    EXR = g.MISC[:, 396:397]; WA = g.MISC[:, 397:398]; WBC2 = g.MISC[:, 398:399]
    wex_toks = []
    for ti in range(16):
        m1 = p.op("vector", lambda e, ti=ti: e.reduce_max(TC1, LG[:, ti, :], axis=AX.X), waits=tk(lg_toks[ti], *wex_toks[-1:]))
        e1 = p.op("vector", lambda e, ti=ti: e.tensor_scalar(EQ1, LG[:, ti, :], TC1, None, ALU.is_ge), waits=tk(m1))
        s1_ = p.op("vector", lambda e, ti=ti: e.scalar_tensor_tensor(EQ2, EQ1, -1e30, LG[:, ti, :], ALU.mult, ALU.add), waits=tk(e1))
        m2 = p.op("vector", lambda e: e.reduce_max(TC2, EQ2, axis=AX.X), waits=tk(s1_))
        e2 = p.op("vector", lambda e, ti=ti: e.tensor_scalar(EQ2, LG[:, ti, :], TC2, None, ALU.is_ge), waits=tk(m2))
        d1 = p.op("vector", lambda e: e.tensor_sub(EXR, TC2, TC1), waits=tk(m2, e2))
        sg = p.op("scalar", lambda e: e.activation(WA, EXR, AF.Sigmoid), waits=tk(d1))
        w1_ = p.op("vector", lambda e: e.tensor_scalar(WBC2, WA, -1.0, 1.0, ALU.mult, ALU.add), waits=tk(sg))
        c1 = p.op("vector", lambda e, ti=ti: e.tensor_scalar_mul(EQ1, EQ1, WBC2), waits=tk(w1_, e1))
        c2 = p.op("vector", lambda e, ti=ti: e.tensor_scalar_mul(EQ2, EQ2, WA), waits=tk(sg, e2, c1))
        c3 = p.op("vector", lambda e, ti=ti: e.tensor_add(WEX[:, ti, :], EQ1, EQ2), waits=tk(c2))
        wex_toks.append(c3)
    wf_toks = []
    WFM = g.SCf[0:8, 656:2704]
    for ti in range(16):
        pb = g.PS[4 + ti % 2]
        tt = p.op("tensor", lambda e, ti=ti, pb=pb: e.matmul(
            pb.t[0:8, 0:128], WEX[:, ti, :], g.IDN[:], start=True, stop=True),
            waits=tk(wex_toks[ti], pb.rd, g.tok_idn))
        cp = p.op("vector", lambda e, ti=ti, pb=pb: e.tensor_copy(
            WFM[0:8, ti * 128:(ti + 1) * 128], pb.t[0:8, 0:128]), waits=tk(tt))
        pb.rd = cp
        wf_toks.append(cp)
    t_wfd = p.dma("sync", g.wrow_d[:], WFM, waits=tk(*wf_toks))

    HMO = g.SBIG[:, 0:16384].rearrange("p (o t) -> p o t", o=32)
    moe_x_toks = []
    wrow_rd = None
    for e_ in range(8):
        t_wr = p.dma("sync", g.STAT[0:1, 0:2048], g.wrow_d[e_:e_ + 1, :], waits=tk(t_wfd, wrow_rd))
        for t in range(4):
            ts = slice(t * 512, (t + 1) * 512)
            mb = g.PS[5]
            bcm = p.op("tensor", lambda e, t=t, mb=mb: e.matmul(
                mb.t[:, 0:512], g.ONESR[:], g.STAT[0:1, t * 512:(t + 1) * 512], start=True, stop=True),
                waits=tk(t_wr, mb.rd))
            WBC = g.SCf[:, 2704 + (t % 2) * 512: 2704 + (t % 2) * 512 + 512]
            mcp = p.op("vector", lambda e, mb=mb, WBC=WBC: e.tensor_copy(WBC, mb.t[:, 0:512]),
                       waits=tk(bcm, moe_x_toks[-15] if len(moe_x_toks) >= 15 else None))
            mb.rd = mcp
            wrow_rd = mcp
            hlast = None
            for og in range(8):
                b01 = og % 2
                wdst = g.WB[:, b01, :].rearrange("p (k o) -> p k o", k=8)
                tw = p.dma("gpsimd", wdst, dr["l2_W1"][e_][:, og], waits=tk(wb_rd[b01]))
                for oc in range(4):
                    oabs = og * 4 + oc
                    bk = g.PS[oabs % 4]
                    mm = None
                    for c in range(FC):
                        mm = p.add("tensor", lambda e, c=c, oc=oc, b01=b01, ts=ts, bk=bk: e.matmul(
                            bk.t[:, 0:512], g.WB[:, b01, c * 512 + oc * 128: c * 512 + oc * 128 + 128],
                            g.XN[:, c, ts], start=(c == 0), stop=(c == FC - 1)),
                            waits=tk(ln[c][t], *( (tw, bk.rd) if c == 0 else ())),
                            inc=p.esem["tensor"] if c == FC - 1 else None)
                    wb_rd[b01] = mm
                    gv = p.op("scalar", lambda e, e_=e_, oabs=oabs, bk=bk: e.activation(
                        HMO[:, oabs, 0:512], bk.t[:, 0:512], AF.Gelu, bias=g.MISC[:, 10 + e_ * 32 + oabs: 11 + e_ * 32 + oabs]),
                        waits=tk(mm, t_b1))
                    bk.rd = gv
                    hlast = gv
            for og in range(FC):
                b01 = og % 2
                wdst = g.WB[:, b01, :].rearrange("p (k o) -> p k o", k=32)
                tw = p.dma("gpsimd", wdst, dr["l2_W2"][e_][:, og], waits=tk(wb_rd[b01]))
                bk = g.PS[og % 4]
                mm = None
                for k in range(32):
                    mm = p.add("tensor", lambda e, k=k, b01=b01, bk=bk: e.matmul(
                        bk.t[:, 0:512], g.WB[:, b01, k * 128:(k + 1) * 128],
                        HMO[:, k, 0:512], start=(k == 0), stop=(k == 31)),
                        waits=tk(*( (tw, bk.rd, hlast) if k == 0 else ())),
                        inc=p.esem["tensor"] if k == 31 else None)
                wb_rd[b01] = mm
                x1_ = p.op("vector", lambda e, e_=e_, og=og, bk=bk, WBC=WBC: e.scalar_tensor_tensor(
                    g.SCf[:, 656 + (og % 2) * 512: 656 + (og % 2) * 512 + 512],
                    bk.t[:, 0:512], g.MISC[:, 266 + e_ * 8 + og: 267 + e_ * 8 + og], WBC, ALU.add, ALU.mult),
                    waits=tk(mm, t_b2, mcp, moe_x_toks[-2] if len(moe_x_toks) >= 2 else None))
                bk.rd = x1_
                x2_ = p.op("vector", lambda e, og=og, ts=ts: e.tensor_add(
                    g.X[:, og, ts], g.X[:, og, ts],
                    g.SCf[:, 656 + (og % 2) * 512: 656 + (og % 2) * 512 + 512]), waits=tk(x1_))
                moe_x_toks.append(x2_)
    g.x_toks = x_toks + moe_x_toks

def emit_l3(g, ln):
    p = g.p
    dr = g.dram
    EPS = 1e-5
    ZX = g.SBIG[:, 0:17408].rearrange("p (c t) -> p c t", c=34)
    XBCC = g.SBIG[:, 17408:26112].rearrange("p (c t) -> p c t", c=17)
    XT = g.SBIG[:, 26112:28160].rearrange("p (c f) -> p c f", c=16)
    YV = g.SBIG[:, 8192:16384].rearrange("p (c t) -> p c t", c=16)
    SSTB = g.SBIG[:, 28160:30208].rearrange("p (h n) -> p h n", h=32)
    CSAVE = g.SBIG[:, 30208:32256].rearrange("p (c t) -> p c t", c=16)
    sst_sl = lambda h: SSTB[0:64, h, :]
    TAIL = g.SC[:, 0:51].rearrange("p (c t) -> p c t", c=17)
    CALN = g.SC[:, 64:576]
    BTM = g.SC[:, 576:640]
    SMT = lambda s: g.SC[:, 640 + s * 128: 640 + (s + 1) * 128]
    GSB = g.SC[:, 896:1024]
    CESL = lambda s: g.SC[:, 1024 + s * 128: 1024 + (s + 1) * 128]
    BESL = g.SC[:, 1280:1344]
    AIH = g.SC[:, 1344:1408]
    DTT = g.SCf[:, 704:736]
    CAT = g.SCf[:, 736:768]
    EBT = g.SCf[:, 768:800]
    ACOL = g.SCf[:, 800:832]
    CA = [g.SCf[0:32, 832:960], g.SCf[0:32, 960:1088]]
    RUNC = g.SCf[0:32, 1088:1089]
    CAEND = g.SCf[0:1, 1089:1121]
    DTQ = g.SCf[0:32, 1152:1664]
    LDA = g.SCf[0:32, 1664:2176]
    CACC = g.SCf[:, 2176:2688]
    NROW = lambda i: g.SCf[0:1, 2688 + i * 128: 2688 + (i + 1) * 128]
    EROW = g.SCf[0:1, 2944:3072]

    t_convw = p.dma("sync", g.MISC[:, 51:119], dr["l3_convw"].rearrange("p c f -> p (c f)"))
    t_convb = p.dma("sync", g.MISC[:, 34:51], dr["l3_convb"][:])
    t_dtb = p.dma("sync", g.MISC[0:32, 0:1], dr["l3_dtb"][:])
    t_alog = p.dma("sync", g.MISC[0:32, 1:2], dr["l3_alog"][:])
    t_dcol = p.dma("sync", g.MISC[:, 2:18], dr["l3_dcol"][:])
    t_nw = p.dma("sync", g.MISC[:, 18:34], dr["l3_nw"][:])
    t_seg = p.dma("sync", g.MISC[:, 119:122], dr["l3_seg"][:])
    t_hm = p.dma("sync", g.MISC[:, 122:126], dr["l3_hmask"][:])
    t_trin = p.dma("sync", g.TRINEG[:], dr["trineg"][:])
    t_triu = t_trin
    SMTF_ = lambda s: g.SCf[:, 3232 + s * 128: 3232 + (s + 1) * 128]
    a1 = p.op("scalar", lambda e: e.activation(g.MISC[0:32, 1:2], g.MISC[0:32, 1:2], AF.Exp), waits=tk(t_alog))
    t_negA = p.op("vector", lambda e: e.tensor_scalar_mul(g.MISC[0:32, 1:2], g.MISC[0:32, 1:2], -1.0), waits=tk(a1))

    t_hst = p.dma("sync", g.hag_in.rearrange("(c p) t -> p c t", p=128), g.XN[:, :, T - 3:T],
                  waits=tk(*[ln[c][3] for c in range(FC)]))
    t_hag = p.add("gpsimd", lambda e: e.collective_compute(
        "AllGather", ALU.bypass, replica_groups=GROUPS, ins=[g.hag_in[:]], outs=[g.hag_out[:]]),
        waits=tk(t_hst), inc=p.csem, amt=1)
    HALL = g.SCf[:, 3072:3120].bitcast(BF16)
    t_hld = p.dma("sync", HALL.rearrange("p (r c t) -> p r c t", r=4, c=8),
                  g.hag_out.rearrange("(r c p) t -> p r c t", p=128, r=4), waits=tk(t_hag))
    HSUM = g.SCf[:, 3120:3132].bitcast(BF16)
    hz = p.op("vector", lambda e: e.memset(HSUM[:, 0:24], 0.0), waits=tk())
    hs_toks = []
    for r in range(4):
        hv = p.op("vector", lambda e, r=r: e.scalar_tensor_tensor(
            HSUM[:, 0:24], HALL[:, r * 24:(r + 1) * 24], g.MISC[:, 122 + r:123 + r],
            HSUM[:, 0:24], ALU.mult, ALU.add), waits=tk(t_hld, hz, t_hm))
        hs_toks.append(hv)

    wb_rd = [None, None]
    tail_toks = [None] * 17
    sst_tok = [p.op("vector", lambda e: e.memset(SSTB[0:64, :, :], 0.0), waits=tk())] * 32
    runc_tok = p.op("vector", lambda e: e.memset(RUNC[:], 0.0), waits=tk())
    ca_rd = None
    ca_bd = None
    row_rd = None
    erow_rd = None
    sm_rd = [None, None]
    smtf_rd = [None, None]
    nrow_rd = [None, None]
    ce_rd = [None, None]
    be_rd = None
    ai_rd = None
    ty = tz = None
    t_acore = None
    conv_toks = [None] * 17

    for q in range(4):
        ts = slice(q * 512, (q + 1) * 512)
        zx_toks = [None] * 34
        first_ev = True
        for og in range(17):
            b01 = og % 2
            wdst = g.WB[:, b01, 0:2048].rearrange("p (k o) -> p k o", k=8)
            tw = p.dma("gpsimd", wdst, dr["l3_win"][:, og], waits=tk(wb_rd[b01]))
            last_mm = None
            for oc in range(2):
                c_out = og * 2 + oc
                bk = g.PS[(og * 2 + oc) % 4]
                mm = None
                for c in range(FC):
                    mm = p.add("tensor", lambda e, c=c, oc=oc, b01=b01, ts=ts, bk=bk: e.matmul(
                        bk.t[:, 0:512], g.WB[:, b01, c * 256 + oc * 128: c * 256 + oc * 128 + 128],
                        g.XN[:, c, ts], start=(c == 0), stop=(c == FC - 1)),
                        waits=tk(ln[c][q], *( (tw, bk.rd) if c == 0 else ())),
                        inc=p.esem["tensor"] if c == FC - 1 else None)
                ev = p.op("vector", lambda e, c_out=c_out, bk=bk: e.tensor_copy(
                    ZX[:, c_out, 0:512], bk.t[:, 0:512]),
                    waits=tk(mm, *( (ty, tz) if (first_ev and q > 0) else ())))
                first_ev = False
                bk.rd = ev
                zx_toks[c_out] = ev
                last_mm = mm
            if q == 0 and 8 <= og <= 16:
                for oc in range(2):
                    c_out = og * 2 + oc
                    if not (16 <= c_out <= 32):
                        continue
                    ch = c_out - 16
                    bk = g.PS[4 + (ch % 2)]
                    mm = None
                    for c in range(FC):
                        mm = p.add("tensor", lambda e, c=c, oc=oc, b01=b01, bk=bk: e.matmul(
                            bk.t[:, 0:3], g.WB[:, b01, c * 256 + oc * 128: c * 256 + oc * 128 + 128],
                            HSUM[:, 0:24].rearrange("p (c t) -> p c t", c=8)[:, c, :],
                            start=(c == 0), stop=(c == FC - 1)),
                            waits=tk(*( (hs_toks[-1], bk.rd) if c == 0 else ())),
                            inc=p.esem["tensor"] if c == FC - 1 else None)
                    tv = p.op("vector", lambda e, ch=ch, bk=bk: e.tensor_copy(
                        TAIL[:, ch, :], bk.t[:, 0:3]), waits=tk(mm))
                    bk.rd = tv
                    tail_toks[ch] = tv
            wb_rd[b01] = last_mm
        d1 = p.op("vector", lambda e: e.tensor_scalar_add(DTQ, ZX[0:32, 33, 0:512], g.MISC[0:32, 0:1]),
                  waits=tk(zx_toks[33], t_dtb))
        d2 = p.op("scalar", lambda e: e.activation(LDA, DTQ, AF.Abs), waits=tk(d1))
        d3 = p.op("scalar", lambda e: e.activation(LDA, LDA, AF.Exp, scale=-1.0), waits=tk(d2))
        d4 = p.op("scalar", lambda e: e.activation(LDA, LDA, AF.Ln, bias=1.0), waits=tk(d3))
        d5 = p.op("vector", lambda e: e.tensor_scalar_max(DTQ, DTQ, 0.0), waits=tk(d2))
        d6 = p.op("vector", lambda e: e.tensor_add(DTQ, DTQ, LDA), waits=tk(d4, d5))
        d7 = p.op("vector", lambda e: e.tensor_scalar_mul(LDA, DTQ, g.MISC[0:32, 1:2]), waits=tk(d6, t_negA))
        conv_prev_q = conv_toks[16] if q else None
        conv_toks = [None] * 17
        for ch in range(17):
            wc = lambda d_, ch=ch: g.MISC[:, 51 + ch * 4 + d_: 52 + ch * 4 + d_]
            raw = ZX[:, 16 + ch, 0:512]
            cc = p.op("vector", lambda e, raw=raw, wc=wc: e.tensor_scalar_mul(CACC, raw, wc(3)),
                      waits=tk(zx_toks[16 + ch], t_convw, conv_toks[ch - 1] if ch else (conv_prev_q if q else None)))
            for s in range(1, 4):
                cc = p.op("vector", lambda e, s=s, raw=raw, wc=wc: e.scalar_tensor_tensor(
                    CACC[:, s:512], raw[:, 0:512 - s], wc(3 - s), CACC[:, s:512], ALU.mult, ALU.add),
                    waits=tk(cc))
                cc = p.op("vector", lambda e, s=s, ch=ch, wc=wc: e.scalar_tensor_tensor(
                    CACC[:, 0:s], TAIL[:, ch, 3 - s:3], wc(3 - s), CACC[:, 0:s], ALU.mult, ALU.add),
                    waits=tk(cc, tail_toks[ch]))
            tu = p.op("vector", lambda e, ch=ch, raw=raw: e.tensor_copy(TAIL[:, ch, :], raw[:, 509:512]),
                      waits=tk(cc))
            tail_toks[ch] = tu
            cs_ = p.op("scalar", lambda e, ch=ch: e.activation(
                XBCC[:, ch, 0:512], CACC, AF.Silu, bias=g.MISC[:, 34 + ch:35 + ch]),
                waits=tk(cc, t_convb))
            conv_toks[ch] = cs_
        t_cal = p.dma("sync", CALN[0:64, :], XBCC[64:128, 16, 0:512], waits=tk(conv_toks[16], ce_rd[0], ce_rd[1]))
        for c4 in range(4):
            p.op("vector", lambda e, q=q, c4=c4: e.tensor_copy(
                CSAVE[0:64, q * 4 + c4, :], CALN.rearrange("p (a t) -> p a t", a=4)[0:64, c4, :]),
                waits=tk(t_cal))
        for c4 in range(4):
            gidx = q * 4 + c4
            cs_sl = slice(c4 * 128, (c4 + 1) * 128)
            cur = 0
            lt = p.op("vector", lambda e, c4=c4: e.tensor_copy(CA[0], LDA[:, c4 * 128:(c4 + 1) * 128]),
                      waits=tk(d7, ca_rd, ca_bd))
            for s in (1, 2, 4, 8, 16, 32, 64):
                nxt = 1 - cur
                l1 = p.op("vector", lambda e, s=s, cur=cur, nxt=nxt: e.tensor_copy(
                    CA[nxt][:, 0:s], CA[cur][:, 0:s]), waits=tk(lt))
                lt = p.op("vector", lambda e, s=s, cur=cur, nxt=nxt: e.tensor_add(
                    CA[nxt][:, s:128], CA[cur][:, s:128], CA[cur][:, 0:128 - s]), waits=tk(l1))
                cur = nxt
            lt = p.op("vector", lambda e, cur=cur: e.tensor_scalar_add(CA[cur], CA[cur], RUNC),
                      waits=tk(lt, runc_tok))
            runc_tok = p.op("vector", lambda e, cur=cur: e.tensor_copy(RUNC, CA[cur][:, 127:128]), waits=tk(lt))
            bd = p.dma("sync", g.cag[:, gidx * 128:(gidx + 1) * 128], CA[cur], waits=tk(lt))
            ca_bd = bd
            pb = g.PS[4]
            ct1 = p.op("tensor", lambda e, cur=cur, pb=pb: e.transpose(
                pb.t[:, 0:32], CA[cur], g.IDNF[0:32, 0:32]), waits=tk(lt, pb.rd, g.tok_idnf))
            ct2 = p.op("vector", lambda e, pb=pb: e.tensor_copy(CAT, pb.t[:, 0:32]), waits=tk(ct1))
            pb.rd = ct2
            ca_rd = ct1
            pb2 = g.PS[5]
            dt1 = p.op("tensor", lambda e, c4=c4, pb2=pb2: e.transpose(
                pb2.t[:, 0:32], DTQ[:, c4 * 128:(c4 + 1) * 128], g.IDNF[0:32, 0:32]),
                waits=tk(d6, pb2.rd, g.tok_idnf))
            dt2 = p.op("vector", lambda e, pb2=pb2: e.tensor_copy(DTT, pb2.t[:, 0:32]), waits=tk(dt1))
            pb2.rd = dt2
            gb = g.PS[4]
            g1 = p.op("tensor", lambda e, c4=c4, gb=gb, cs_sl=cs_sl: e.matmul(
                gb.t[:, 0:128], XBCC[0:64, 16, cs_sl], CALN[0:64, cs_sl], start=True, stop=True),
                waits=tk(t_cal, conv_toks[16], gb.rd))
            g2 = p.op("vector", lambda e, gb=gb: e.tensor_copy(GSB, gb.t[:, 0:128]), waits=tk(g1))
            gb.rd = g2
            bb_ = g.PT[0]
            b1 = p.op("tensor", lambda e, c4=c4, bb_=bb_, cs_sl=cs_sl: e.transpose(
                bb_.t[:, 0:64], XBCC[0:64, 16, cs_sl], g.IDN[0:64, 0:64]),
                waits=tk(conv_toks[16], bb_.rd, g.tok_idn, be_rd))
            b2 = p.op("vector", lambda e, bb_=bb_: e.tensor_copy(BTM, bb_.t[:, 0:64]), waits=tk(b1))
            bb_.rd = b2
            xt_toks = []
            for zc in range(16):
                pbx = g.PT[zc % 2]
                x1 = p.op("tensor", lambda e, zc=zc, cs_sl=cs_sl, pbx=pbx: e.transpose(
                    pbx.t[:, 0:128], XBCC[:, zc, cs_sl], g.IDN[:]),
                    waits=tk(conv_toks[zc], pbx.rd, g.tok_idn))
                x2 = p.op("vector", lambda e, zc=zc, pbx=pbx: e.tensor_copy(
                    XT[:, zc, :], pbx.t[:, 0:128]), waits=tk(x1))
                pbx.rd = x2
                xt_toks.append(x2)
            for half in range(2):
                bl = p.dma("sync", g.STAT[0:1, 0:2048].rearrange("o (h t) -> o h t", h=16),
                           g.cag[half * 16:(half + 1) * 16, gidx * 128:(gidx + 1) * 128]
                           .rearrange("h t -> () h t" if False else "h t -> h t")[None, :, :] if False else
                           g.cag[half * 16:(half + 1) * 16, gidx * 128:(gidx + 1) * 128].rearrange("h t -> o h t", o=1) if False else
                           g.cag[half * 16:(half + 1) * 16, gidx * 128:(gidx + 1) * 128],
                           waits=tk(bd, row_rd))
                hsl = slice(half * 16, half * 16 + 16)
                ce1 = p.op("vector", lambda e, hsl=hsl: e.tensor_copy(
                    CAEND[0:1, hsl], g.STAT[0:1, 127:2048:128]), waits=tk(bl))
                pb3 = g.PS[4]
                eb1 = p.op("tensor", lambda e, hsl=hsl, pb3=pb3: e.matmul(
                    pb3.t[:, 0:16], g.ONESR[:], CAEND[0:1, hsl], start=True, stop=True),
                    waits=tk(ce1, pb3.rd))
                eb2 = p.op("vector", lambda e, hsl=hsl, pb3=pb3: e.tensor_sub(
                    EBT[:, hsl], pb3.t[:, 0:16], CAT[:, hsl]), waits=tk(eb1, ct2))
                pb3.rd = eb2
                eb3 = p.op("scalar", lambda e, hsl=hsl: e.activation(EBT[:, hsl], EBT[:, hsl], AF.Exp),
                           waits=tk(eb2))
                eb4 = p.op("vector", lambda e, hsl=hsl: e.tensor_mul(EBT[:, hsl], EBT[:, hsl], DTT[:, hsl]),
                           waits=tk(eb3, dt2))
                av1 = p.op("scalar", lambda e, hsl=hsl: e.activation(CAEND[0:1, hsl], CAEND[0:1, hsl], AF.Exp),
                           waits=tk(eb2))
                pb4 = g.PS[5]
                av2 = p.op("tensor", lambda e, hsl=hsl, pb4=pb4: e.matmul(
                    pb4.t[:, 0:16], g.ONESR[:], CAEND[0:1, hsl], start=True, stop=True),
                    waits=tk(av1, pb4.rd))
                av3 = p.op("vector", lambda e, hsl=hsl, pb4=pb4: e.tensor_copy(
                    ACOL[:, hsl], pb4.t[:, 0:16]), waits=tk(av2))
                pb4.rd = av3
                if gidx == 15 and half == 1:
                    t_acore = p.dma("sync", g.a_in[:], CAEND, waits=tk(av1, ce1))
                for hh in range(16):
                    h = half * 16 + hh
                    hb = 64 * (h % 2)
                    hc = h // 2
                    zc = hc
                    carow = g.STAT[0:1, hh * 128:(hh + 1) * 128]
                    nr = NROW(h % 2)
                    n1 = p.op("vector", lambda e, nr=nr, carow=carow: e.tensor_scalar_mul(nr, carow, -1.0),
                              waits=tk(bl, nrow_rd[h % 2]))
                    db = g.PS[h % 2]
                    m1 = p.op("tensor", lambda e, db=db, carow=carow: e.matmul(
                        db.t[:, 0:128], g.ONESR[:], carow, start=True, stop=False),
                        waits=tk(bl, db.rd))
                    m2 = p.add("tensor", lambda e, db=db, nr=nr: e.matmul(
                        db.t[:, 0:128], nr, g.ONESR[:], start=False, stop=True),
                        waits=tk(n1), inc=p.esem["tensor"])
                    nrow_rd[h % 2] = m2
                    sm = SMT(h % 2)
                    smf = SMTF_(h % 2)
                    s0 = p.op("vector", lambda e, db=db, smf=smf: e.tensor_add(smf, db.t[:, 0:128], g.TRINEG[:]),
                              waits=tk(m2, t_trin, smtf_rd[h % 2]))
                    db.rd = s0
                    s1 = p.op("scalar", lambda e, sm=sm, smf=smf: e.activation(sm, smf, AF.Exp),
                              waits=tk(s0, sm_rd[h % 2]))
                    smtf_rd[h % 2] = s1
                    s3 = p.op("vector", lambda e, sm=sm: e.tensor_mul(sm, sm, GSB), waits=tk(s1, g2))
                    s4 = p.op("vector", lambda e, sm=sm, h=h: e.tensor_scalar_mul(sm, sm, DTT[:, h:h + 1]),
                              waits=tk(s3, dt2))
                    er1 = p.op("scalar", lambda e, carow=carow: e.activation(EROW, carow, AF.Exp),
                               waits=tk(bl, erow_rd))
                    eb_ = g.PS[4 + (h % 2)]
                    er2 = p.op("tensor", lambda e, eb_=eb_: e.matmul(
                        eb_.t[0:64, 0:128], g.ONESR[0:1, 0:64], EROW, start=True, stop=True),
                        waits=tk(er1, eb_.rd))
                    erow_rd = er2
                    ce_ = CESL(h % 2)
                    er3 = p.op("vector", lambda e, eb_=eb_, ce_=ce_, cs_sl=cs_sl: e.tensor_mul(
                        ce_[0:64, :], CALN[0:64, cs_sl], eb_.t[0:64, 0:128]),
                        waits=tk(er2, t_cal, ce_rd[h % 2]))
                    eb_.rd = er3
                    py = g.PS[2 + (h % 2)]
                    y1 = p.op("tensor", lambda e, py=py, zc=zc, hb=hb, sm=sm: e.matmul(
                        py.t[0:64, 0:128], XT[:, zc, hb:hb + 64], sm, start=True, stop=False),
                        waits=tk(s4, xt_toks[zc], py.rd))
                    sm_rd[h % 2] = y1
                    y2 = p.add("tensor", lambda e, py=py, h=h, ce_=ce_: e.matmul(
                        py.t[0:64, 0:128], sst_sl(h), ce_[0:64, :], start=False, stop=True),
                        waits=tk(er3, sst_tok[h]), inc=p.esem["tensor"])
                    ce_rd[h % 2] = y2
                    yv = p.op("vector", lambda e, py=py, hb=hb, hc=hc, cs_sl=cs_sl: e.scalar_tensor_tensor(
                        YV[hb:hb + 64, hc, cs_sl], XBCC[hb:hb + 64, hc, cs_sl],
                        g.MISC[hb:hb + 64, 2 + hc:3 + hc], py.t[0:64, 0:128], ALU.mult, ALU.add),
                        waits=tk(y2, t_dcol, conv_toks[hc]))
                    py.rd = yv
                    ai1 = p.op("vector", lambda e, h=h: e.tensor_scalar_mul(
                        AIH[0:64, :], g.IDN[0:64, 0:64], ACOL[0:64, h:h + 1]), waits=tk(av3, ai_rd))
                    be1 = p.op("vector", lambda e, h=h: e.tensor_scalar_mul(BESL, BTM, EBT[:, h:h + 1]),
                               waits=tk(b2, eb4, be_rd))
                    st1 = p.op("tensor", lambda e, db=db, zc=zc, hb=hb: e.matmul(
                        db.t[0:64, 0:64], BESL, XT[:, zc, hb:hb + 64], start=True, stop=False),
                        waits=tk(be1, xt_toks[zc], s1))
                    be_rd = st1
                    st2 = p.add("tensor", lambda e, db=db, h=h: e.matmul(
                        db.t[0:64, 0:64], AIH[0:64, :], sst_sl(h), start=False, stop=True),
                        waits=tk(ai1, sst_tok[h]), inc=p.esem["tensor"])
                    ai_rd = st2
                    sv = p.op("vector", lambda e, db=db, h=h: e.tensor_copy(sst_sl(h), db.t[0:64, 0:64]),
                              waits=tk(st2))
                    db.rd = sv
                    sst_tok[h] = sv
                row_rd = er2
        sz_toks = []
        last_yv_tok = sst_tok[31]
        for zc in range(16):
            z1 = p.op("scalar", lambda e, zc=zc: e.activation(
                ZX[:, zc, 0:512], ZX[:, zc, 0:512], AF.Silu), waits=tk(zx_toks[zc]))
            w1 = p.op("vector", lambda e, zc=zc: e.tensor_mul(
                YV[:, zc, 0:512], YV[:, zc, 0:512], ZX[:, zc, 0:512]), waits=tk(z1, last_yv_tok))
            sz_toks.append(w1)
        ty = p.dma("sync", g.ysc[:, ts].rearrange("(c p) t -> p c t", p=128), YV[:, :, 0:512],
                   waits=tk(*sz_toks))
        tz = p.dma("sync", g.zsc[:, ts].rearrange("(c p) t -> p c t", p=128), ZX[:, 0:16, 0:512],
                   waits=tk(*sz_toks))

    STAGE = g.SBIGf[0:64, 6144:8192]
    t_sst_c = p.op("vector", lambda e: e.tensor_copy(
        STAGE.rearrange("p (h n) -> p h n", h=32), SSTB[0:64, :, :]), waits=tk(*sst_tok, ty, tz))
    t_sstst = p.dma("sync", g.sst_in[:], STAGE, waits=tk(t_sst_c))
    t_sag2 = p.add("gpsimd", lambda e: e.collective_compute(
        "AllGather", ALU.bypass, replica_groups=GROUPS, ins=[g.sst_in[:]], outs=[g.sst_out[:]]),
        waits=tk(t_sstst), inc=p.csem, amt=1)
    t_aag = p.add("gpsimd", lambda e: e.collective_compute(
        "AllGather", ALU.bypass, replica_groups=GROUPS, ins=[g.a_in[:]], outs=[g.a_out[:]]),
        waits=tk(t_acore), inc=p.csem, amt=1)

    SALL = g.SBIGf[0:64, 0:8192].rearrange("p (r h n) -> p r h n", r=4, h=32)
    t_sld2 = p.dma("sync", g.SBIGf[0:64, 0:8192].rearrange("p (r x) -> p r x", r=4),
                   g.sst_out.rearrange("(r p) x -> p r x", p=64), waits=tk(t_sag2, ty, tz))
    AROW4 = g.SCf[0:1, 3072:3200]
    t_ald = p.dma("sync", AROW4,
                  g.a_out.rearrange("r h -> (r h)").rearrange("(o x) -> o x", o=1), waits=tk(t_aag))
    TCUR = g.SBIGf[0:64, 8192:10240].rearrange("p (h n) -> p h n", h=32)
    SIN = g.SBIGf[0:64, 10240:12288].rearrange("p (h n) -> p h n", h=32)
    SINB = g.SBIG[0:64, 24576:26624].rearrange("p (h n) -> p h n", h=32)
    ACM = g.SCf[:, 3200:3232]
    tz1 = p.op("vector", lambda e: e.memset(TCUR[:, :, :], 0.0), waits=tk(t_sld2))
    tz2 = p.op("vector", lambda e: e.memset(SIN[:, :, :], 0.0), waits=tk(t_sld2, t_sst_c))
    tprev = tz1
    sprev = tz2
    for r in range(3):
        pb5 = g.PS[4]
        am1 = p.op("tensor", lambda e, r=r, pb5=pb5: e.matmul(
            pb5.t[:, 0:32], g.ONESR[:], AROW4[0:1, r * 32:(r + 1) * 32], start=True, stop=True),
            waits=tk(t_ald, pb5.rd))
        am2 = p.op("vector", lambda e, pb5=pb5: e.tensor_copy(ACM, pb5.t[:, 0:32]), waits=tk(am1))
        pb5.rd = am2
        for h in range(32):
            t1 = p.op("vector", lambda e, r=r, h=h: e.scalar_tensor_tensor(
                TCUR[:, h, :], TCUR[:, h, :], ACM[0:64, h:h + 1], SALL[:, r, h, :], ALU.mult, ALU.add),
                waits=tk(am2, tprev))
            t2 = p.op("vector", lambda e, r=r, h=h: e.scalar_tensor_tensor(
                SIN[:, h, :], TCUR[:, h, :], g.MISC[0:64, 119 + r:120 + r], SIN[:, h, :], ALU.mult, ALU.add),
                waits=tk(t1, t_seg, sprev))
            tprev = t1
            sprev = t2
    sinb = p.op("vector", lambda e: e.tensor_copy(SINB[:, :, :], SIN[:, :, :]), waits=tk(sprev))

    WQ = g.SBIG[:, 16384:24576].rearrange("p (c t) -> p c t", c=16)
    SZQ = g.SBIG[:, 24576:32768 - 2560].rearrange("p (c t) -> p c t", c=16) if False else None
    SZQ = g.SBIG[:, 0:8192].rearrange("p (c t) -> p c t", c=16)
    x_toks = []
    wq_rd = None
    sq_rd = None
    r4_prev = None
    for q in range(4):
        ts = slice(q * 512, (q + 1) * 512)
        t_wld = p.dma("sync", WQ[:, :, 0:512], g.ysc[:, ts].rearrange("(c p) t -> p c t", p=128),
                      waits=tk(wq_rd, sinb))
        t_szld = p.dma("sync", SZQ[:, :, 0:512], g.zsc[:, ts].rearrange("(c p) t -> p c t", p=128),
                       waits=tk(sq_rd, sinb, tz))
        corr_toks = []
        for half in range(2):
            for hh in range(16):
                if getattr(emit_l3, "NO_CORR", False):
                    continue
                h = half * 16 + hh
                hb = 64 * (h % 2)
                hc = h // 2
                erw = p.dma("sync", g.STAT[0:1, 0:512],
                            g.cag[h:h + 1, q * 512:(q + 1) * 512], waits=tk(row_rd))
                ex1 = p.op("scalar", lambda e: e.activation(g.STAT[0:1, 0:512], g.STAT[0:1, 0:512], AF.Exp),
                           waits=tk(erw))
                row_rd = ex1
                for c4 in range(4):
                    gidx = q * 4 + c4
                    cs_sl = slice(c4 * 128, (c4 + 1) * 128)
                    eb_ = g.PS[4 + (c4 % 2)]
                    er2 = p.op("tensor", lambda e, c4=c4, eb_=eb_: e.matmul(
                        eb_.t[0:64, 0:128], g.ONESR[0:1, 0:64], g.STAT[0:1, c4 * 128:(c4 + 1) * 128],
                        start=True, stop=True), waits=tk(ex1, eb_.rd))
                    row_rd = er2
                    ce_ = CESL(c4 % 2)
                    er3 = p.op("vector", lambda e, gidx=gidx, eb_=eb_, ce_=ce_: e.tensor_mul(
                        ce_[0:64, :], CSAVE[0:64, gidx, :], eb_.t[0:64, 0:128]),
                        waits=tk(er2, ce_rd[c4 % 2]))
                    eb_.rd = er3
                    pc = g.PS[c4 % 2]
                    yc = p.op("tensor", lambda e, h=h, pc=pc, ce_=ce_: e.matmul(
                        pc.t[0:64, 0:128], SINB[0:64, h, :], ce_[0:64, :],
                        start=True, stop=True), waits=tk(er3, sinb, pc.rd))
                    ce_rd[c4 % 2] = yc
                    TMPH = g.SC[hb:hb + 64, 1408 + (c4 % 2) * 128: 1408 + (c4 % 2) * 128 + 128]
                    c1 = p.op("vector", lambda e, hb=hb, hc=hc, cs_sl=cs_sl, pc=pc, TMPH=TMPH: e.tensor_mul(
                        TMPH, pc.t[0:64, 0:128], SZQ[hb:hb + 64, hc, cs_sl]),
                        waits=tk(yc, t_szld))
                    pc.rd = c1
                    c2 = p.op("vector", lambda e, hb=hb, hc=hc, cs_sl=cs_sl, TMPH=TMPH: e.tensor_add(
                        WQ[hb:hb + 64, hc, cs_sl], WQ[hb:hb + 64, hc, cs_sl], TMPH),
                        waits=tk(c1, t_wld))
                    corr_toks.append(c2)
        if q == 0 and "d_wq0" in g.dbg:
            dd1 = p.dma("sync", g.dbg["d_wq0"][:], g.SBIG[:, 16384:24576],
                        waits=tk(corr_toks[-1] if corr_toks else t_wld))
            p.wait("sync", [dd1])
        nb = g.PS[4]
        mm = None
        sqt_toks = []
        for chh in range(16):
            slot = g.SC[:, 1664 + (chh % 4) * 512: 1664 + (chh % 4) * 512 + 512]
            sq = p.op("vector", lambda e, chh=chh, slot=slot: e.tensor_mul(
                slot, WQ[:, chh, 0:512], WQ[:, chh, 0:512]),
                waits=tk(corr_toks[-1] if corr_toks else t_wld, sq_rd if chh < 4 else sqt_toks[chh - 4]))
            mm = p.add("tensor", lambda e, chh=chh, slot=slot, nb=nb: e.matmul(
                nb.t[0:1, 0:512], g.ONESB[:], slot, start=(chh == 0), stop=(chh == 15)),
                waits=tk(sq, nb.rd if chh == 0 else None),
                inc=p.esem["tensor"])
            sqt_toks.append(mm)
        r1 = p.op("vector", lambda e, nb=nb: e.tensor_scalar(
            g.SCf[0:1, 2944:3456][0:1, 0:512], nb.t[0:1, 0:512], 1.0 / (2 * D), EPS, ALU.mult, ALU.add),
            waits=tk(mm, r4_prev))
        nb.rd = r1
        r2 = p.op("vector", lambda e: e.reciprocal(g.SCf[0:1, 2944:3456][0:1, 0:512],
                                                   g.SCf[0:1, 2944:3456][0:1, 0:512]), waits=tk(r1))
        r3 = p.op("scalar", lambda e: e.activation(g.SCf[0:1, 2944:3456][0:1, 0:512],
                                                   g.SCf[0:1, 2944:3456][0:1, 0:512], AF.Sqrt), waits=tk(r2))
        nb2 = g.PS[5]
        r4 = p.op("tensor", lambda e, nb2=nb2: e.matmul(
            nb2.t[:, 0:512], g.ONESR[:], g.SCf[0:1, 2944:3456][0:1, 0:512], start=True, stop=True),
            waits=tk(r3, nb2.rd))
        r4_prev = r4
        appl_toks = []
        for chh in range(16):
            ap1 = p.op("vector", lambda e, chh=chh, nb2=nb2: e.scalar_tensor_tensor(
                WQ[:, chh, 0:512], WQ[:, chh, 0:512], g.MISC[:, 18 + chh:19 + chh], nb2.t[:, 0:512],
                ALU.mult, ALU.mult), waits=tk(r4, t_nw, corr_toks[-1] if corr_toks else t_wld))
            appl_toks.append(ap1)
        nb2.rd = appl_toks[-1]
        sq_rd = appl_toks[-1]

        for og in range(4):
            b01 = og % 2
            wdst = g.WB[:, b01, :].rearrange("p (k o) -> p k o", k=16)
            tww = p.dma("gpsimd", wdst, dr["l3_wout"][:, og], waits=tk(wb_rd[b01]))
            for oc in range(2):
                c_out = og * 2 + oc
                bk = g.PS[(og * 2 + oc) % 4]
                mm2 = None
                for k in range(16):
                    mm2 = p.add("tensor", lambda e, k=k, oc=oc, b01=b01, bk=bk: e.matmul(
                        bk.t[:, 0:512], g.WB[:, b01, k * 256 + oc * 128: k * 256 + oc * 128 + 128],
                        WQ[:, k, 0:512], start=(k == 0), stop=(k == 15)),
                        waits=tk(appl_toks[k], *( (tww, bk.rd) if k == 0 else ())),
                        inc=p.esem["tensor"] if k == 15 else None)
                xa = p.op("vector", lambda e, c_out=c_out, ts=ts, bk=bk: e.tensor_add(
                    g.X[:, c_out, ts], g.X[:, c_out, ts], bk.t[:, 0:512]), waits=tk(mm2))
                bk.rd = xa
                x_toks.append(xa)
            wb_rd[b01] = x_toks[-1]
        wq_rd = x_toks[-1]

    if getattr(emit_l3, "NO_PEER", False):
        g.x_toks = x_toks
        return
    VALS = g.SBIG[:, 0:16384].rearrange("p (e d) -> p e d", e=16)
    SELB = g.SBIG[:, 16384:16512].rearrange("p (c e) -> p c e", c=8)
    last_x = x_toks[-1]
    t_vals = p.dma("gpsimd", VALS[:], dr["l3_vals"].rearrange("e p d -> p e d"), waits=tk(last_x))
    t_selt = p.dma("gpsimd", SELB[:], dr["l3_selt"][:], waits=tk(last_x))
    SCW = g.SC.rearrange("p (e t) -> p e t", e=16)
    peer_tile_done = None
    selrow_rd = last_x
    scw_rd = [None] * 16
    for t in range(4):
        ts = slice(t * 512, (t + 1) * 512)
        for e_ in range(16):
            kb01 = e_ % 2
            wdst = g.WB[:, kb01, 0:1024].rearrange("p (k o) -> p k o", k=8)
            twk = p.dma("gpsimd", wdst, dr["l3_keys"][e_][:, 0], waits=tk(wb_rd[kb01]))
            bk = g.PS[e_ % 2]
            mm = None
            for c in range(FC):
                mm = p.add("tensor", lambda e, c=c, kb01=kb01, ts=ts, bk=bk: e.matmul(
                    bk.t[:, 0:512], g.WB[:, kb01, c * 128:(c + 1) * 128],
                    g.XN[:, c, ts], start=(c == 0), stop=(c == FC - 1)),
                    waits=tk(ln[c][t], *( (twk, bk.rd) if c == 0 else ())),
                    inc=p.esem["tensor"] if c == FC - 1 else None)
            wb_rd[kb01] = mm
            nbs = g.PS[4 + (e_ % 2)]
            mms = None
            for c in range(FC):
                mms = p.add("tensor", lambda e, c=c, e_=e_, ts=ts, nbs=nbs: e.matmul(
                    nbs.t[0:1, 0:512], SELB[:, c, e_:e_ + 1], g.XN[:, c, ts],
                    start=(c == 0), stop=(c == FC - 1)),
                    waits=tk(*( (t_selt, nbs.rd) if c == 0 else ())),
                    inc=p.esem["tensor"] if c == FC - 1 else None)
            srow2 = g.STAT[0:1, 0:512]
            sg = p.op("scalar", lambda e, nbs=nbs, srow2=srow2: e.activation(
                srow2, nbs.t[0:1, 0:512], AF.Sigmoid), waits=tk(mms, selrow_rd))
            nbs.rd = sg
            bcb = g.PS[2 + (e_ % 2)]
            sb1 = p.op("tensor", lambda e, bcb=bcb, srow2=srow2: e.matmul(
                bcb.t[:, 0:512], g.ONESR[:], srow2, start=True, stop=True),
                waits=tk(sg, bcb.rd))
            selrow_rd = sb1
            sc1 = p.op("vector", lambda e, e_=e_, bk=bk: e.tensor_scalar_max(
                SCW[:, e_, 0:512], bk.t[:, 0:512], 0.0), waits=tk(mm, scw_rd[e_], peer_tile_done))
            bk.rd = sc1
            if getattr(emit_l3, "NO_SEL", False):
                sc2 = p.op("vector", lambda e, e_=e_, bcb=bcb: e.tensor_copy(
                    SCW[:, e_, 0:512], SCW[:, e_, 0:512]), waits=tk(sc1, sb1))
            else:
                sc2 = p.op("vector", lambda e, e_=e_, bcb=bcb: e.tensor_mul(
                    SCW[:, e_, 0:512], SCW[:, e_, 0:512], bcb.t[:, 0:512]), waits=tk(sc1, sb1))
            bcb.rd = sc2
            scw_rd[e_] = sc2
        for dch in range(FC):
            bk = g.PS[dch % 2]
            mm = None
            for e_ in range(16):
                mm = p.add("tensor", lambda e, e_=e_, dch=dch, bk=bk: e.matmul(
                    bk.t[:, 0:512], VALS[:, e_, dch * 128:(dch + 1) * 128], SCW[:, e_, 0:512],
                    start=(e_ == 0), stop=(e_ == 15)),
                    waits=tk(scw_rd[e_], bk.rd if e_ == 0 else None,
                             t_vals if (dch == 0 and t == 0 and e_ == 0) else None),
                    inc=p.esem["tensor"] if e_ == 15 else None)
            xa = p.op("vector", lambda e, dch=dch, ts=ts, bk=bk: e.tensor_add(
                g.X[:, dch, ts], g.X[:, dch, ts], bk.t[:, 0:512]), waits=tk(mm))
            bk.rd = xa
            x_toks.append(xa)
            if dch == FC - 1:
                peer_tile_done = mm
    g.x_toks = x_toks



def tile_w(w, og):
    K, O = w.shape
    return np.ascontiguousarray(
        w.reshape(K // 128, 128, O // og, og).transpose(1, 2, 0, 3))


def col128(v):
    return np.ascontiguousarray(v.reshape(-1, 128).T)


def prep_weights(inputs, layers):
    d = {}
    f32 = lambda a: np.ascontiguousarray(a, dtype=np.float32)
    if 0 in layers:
        d["l0_qp"] = tile_w(f32(inputs["L0_attn_q_proj"]), 256)
        d["l0_kp"] = tile_w(f32(inputs["L0_attn_k_proj"]), 256)
        d["l0_w1"] = tile_w(f32(inputs["L0_ffn_w1_w"]).T, 256)
        d["l0_w2"] = tile_w(f32(inputs["L0_ffn_w2_w"]).T, 256)
        d["l0_h1"] = tile_w(f32(inputs["L0_ffn_h1_w"]).T, 128)
        d["l0_h2"] = tile_w(f32(inputs["L0_ffn_h2_w"]).T, 128)
        d["l0_w3"] = tile_w(f32(inputs["L0_ffn_w3_w"]).T, 128)
        d["l0_s1"] = col128(f32(inputs["L0_ffn_w1_s"]))
        d["l0_s2"] = col128(f32(inputs["L0_ffn_w2_s"]))
        d["l0_sh1"] = col128(f32(inputs["L0_ffn_h1_s"]))
        d["l0_sh2"] = col128(f32(inputs["L0_ffn_h2_s"]))
        d["l0_s3"] = col128(f32(inputs["L0_ffn_w3_s"]))
    if 2 in layers:
        d["l2_wq"] = tile_w(f32(inputs["L2_attn_Wq"]), 512)
        d["l2_wk"] = tile_w(f32(inputs["L2_attn_Wk"]), 512)
        d["l2_wv"] = tile_w(f32(inputs["L2_attn_Wv"]), 512)
        d["l2_wo"] = tile_w(f32(inputs["L2_attn_Wo"]), 512)
        wck = f32(inputs["L2_attn_Wck"]).reshape(32, 64, 64).transpose(1, 0, 2)
        d["l2_wck"] = np.ascontiguousarray(np.concatenate([wck, wck], axis=0))
        wcv = f32(inputs["L2_attn_Wcv"]).reshape(32, 64, 64).transpose(1, 0, 2)
        d["l2_wcv"] = np.ascontiguousarray(np.concatenate([wcv, wcv], axis=0))
        d["l2_bck"] = f32(inputs["L2_attn_bck"]).reshape(64, 1)
        d["l2_bcv"] = f32(inputs["L2_attn_bcv"]).reshape(64, 1)
        wg = f32(inputs["L2_attn_Wg"])
        d["l2_wgb"] = np.ascontiguousarray(np.concatenate([wg, wg], axis=0))
        d["l2_bgr"] = np.ascontiguousarray(np.tile(f32(inputs["L2_attn_bg"]), 16).reshape(1, 48))
        d["l2_wr"] = tile_w(f32(inputs["L2_ffn_Wr"]), 8)
        d["l2_br"] = f32(inputs["L2_ffn_br"]).reshape(1, 8)
        d["l2_W1"] = np.ascontiguousarray(np.stack([tile_w(f32(inputs["L2_ffn_W1"][e]), 512) for e in range(8)]))
        d["l2_b1"] = np.ascontiguousarray(np.concatenate([col128(f32(inputs["L2_ffn_b1"][e])) for e in range(8)], axis=1))
        d["l2_W2"] = np.ascontiguousarray(np.stack([tile_w(f32(inputs["L2_ffn_W2"][e]), 128) for e in range(8)]))
        d["l2_b2"] = np.ascontiguousarray(np.concatenate([col128(f32(inputs["L2_ffn_b2"][e])) for e in range(8)], axis=1))
        d["iota64"] = np.ascontiguousarray((np.arange(128) % 64).astype(np.float32).reshape(128, 1))
    if 3 in layers:
        win = np.zeros((1024, 4352), np.float32)
        win[:, 0:4256] = f32(inputs["L3_attn_W_in"])
        d["l3_win"] = tile_w(win, 256)
        d["l3_wout"] = tile_w(f32(inputs["L3_attn_W_out"]), 256)
        cw = f32(inputs["L3_attn_conv_w"]).reshape(17, 128, 4).transpose(1, 0, 2)
        d["l3_convw"] = np.ascontiguousarray(cw)
        d["l3_convb"] = col128(f32(inputs["L3_attn_conv_b"]))
        d["l3_dtb"] = f32(inputs["L3_attn_dt_bias"]).reshape(32, 1)
        d["l3_alog"] = f32(inputs["L3_attn_A_log"]).reshape(32, 1)
        dd = f32(inputs["L3_attn_D"])
        d["l3_dcol"] = np.ascontiguousarray(
            dd[2 * np.arange(16)[None, :] + (np.arange(128)[:, None] >= 64)].astype(np.float32))
        d["l3_nw"] = col128(f32(inputs["L3_attn_norm_weight"]))
        d["l3_keys"] = np.ascontiguousarray(np.stack([tile_w(f32(inputs["L3_ffn_keys"][e]), 128) for e in range(16)]))
        d["l3_vals"] = np.ascontiguousarray(f32(inputs["L3_ffn_values"]))
        d["l3_selt"] = tile_w(np.ascontiguousarray(f32(inputs["L3_ffn_expert_sel"]).T), 16)[:, 0]
        tri = np.zeros((128, 128), dtype=ml_dtypes.bfloat16)
        tri[np.tril_indices(128)] = 1
        d["trineg"] = np.ascontiguousarray(np.where(np.triu(np.ones((128, 128), np.bool_)), 0.0, -1e30).astype(np.float32))
    if 1 in layers:
        d["l1_wq"] = tile_w(f32(inputs["L1_attn_wq"]).T, 512)
        d["l1_wk"] = tile_w(f32(inputs["L1_attn_wk"]).T, 512)
        d["l1_wv"] = tile_w(f32(inputs["L1_attn_wv"]).T, 512)
        d["l1_wo"] = tile_w(f32(inputs["L1_attn_wo"]).T, 512)
        d["l1_bq"] = col128(f32(inputs["L1_attn_bq"]))
        d["l1_bk"] = col128(f32(inputs["L1_attn_bk"]))
        d["l1_bv"] = np.ascontiguousarray(f32(inputs["L1_attn_bv"]).reshape(1, 1024))
        d["l1_bo"] = col128(f32(inputs["L1_attn_bo"]))
        d["l1_g1"] = col128(f32(inputs["L1_attn_g1"]))
        d["l1_b1"] = col128(f32(inputs["L1_attn_b1"]))
        d["l1_g2"] = col128(f32(inputs["L1_attn_g2"]))
        d["l1_b2"] = col128(f32(inputs["L1_attn_b2"]))
        d["l1_w1"] = tile_w(f32(inputs["L1_attn_w1"]).T, 512)
        d["l1_bf1"] = col128(f32(inputs["L1_attn_bf1"]))
        d["l1_w2"] = tile_w(f32(inputs["L1_attn_w2"]).T, 128)
        d["l1_bf2"] = col128(f32(inputs["L1_attn_bf2"]))
        d["l1_wr"] = col128(f32(inputs["L1_ffn_w_router"]))
        d["l1_W1"] = tile_w(f32(inputs["L1_ffn_W1"]), 512)
        d["l1_b1m"] = col128(f32(inputs["L1_ffn_b1"]))
        d["l1_W2"] = tile_w(f32(inputs["L1_ffn_W2"]), 128)
        d["l1_b2m"] = col128(f32(inputs["L1_ffn_b2"]))
        hm = np.zeros((128, 8, 16), dtype=ml_dtypes.bfloat16)
        for p_ in range(128):
            for c_ in range(8):
                hm[p_, c_, 2 * c_ + p_ // 64] = 1
        d["hmask"] = hm
        sc_ = np.zeros((16, 8, 128), dtype=ml_dtypes.bfloat16)
        for h_ in range(16):
            for p_ in range(128):
                if h_ % 2 == p_ // 64:
                    sc_[h_, h_ // 2, p_] = 1
        d["selc"] = sc_
    d["identity"] = np.eye(128, dtype=ml_dtypes.bfloat16)
    d["identityf"] = np.eye(64, dtype=np.float32)
    return d


def build_nc(layers):
    Prog.STRICT_T = (3 in layers)
    g = make_ctx(layers)
    emit_prelude(g)
    for i, L in enumerate(sorted(layers)):
        if i > 0:
            g.p.bar = {e: tk(*g.x_toks) for e in ENGS}
        ln = emit_ln(g)
        if L == 0:
            emit_l0(g, ln)
        elif L == 1:
            emit_l1(g, ln)
        elif L == 2:
            emit_l2(g, ln)
        elif L == 3:
            emit_l3(g, ln)
        else:
            raise NotImplementedError(f"layer {L}")
    p = g.p
    if "d_stat" in g.dbg:
        w = tk(*g.x_toks)
        s1 = p.dma("sync", g.dbg["d_stat"][:], g.STAT[:], waits=w)
        s2 = p.dma("sync", g.dbg["d_misc"][:], g.MISC[:], waits=w)
        s3 = p.dma("sync", g.dbg["d_sbig"][:], g.SBIGf[:, 8192:16384], waits=w)
        s4 = p.dma("sync", g.dbg["d_sc"][:], g.SCf[:], waits=w)
        p.wait("sync", [s1, s2, s3, s4])
    if "d_ysc" in g.dbg:
        w = tk(*g.x_toks)
        s1 = p.dma("sync", g.dbg["d_ysc"][:], g.ysc[:], waits=w)
        s2 = p.dma("sync", g.dbg["d_zsc"][:], g.zsc[:], waits=w)
        s3 = p.dma("sync", g.dbg["d_cag"][:], g.cag[:], waits=w)
        s4 = p.dma("sync", g.dbg["d_sst"][:], g.sst_out[:], waits=w)
        p.wait("sync", [s1, s2, s3, s4])
    if "d_xn" in g.dbg:
        w = tk(*g.x_toks)
        st1 = p.dma("sync", g.dbg["d_xn"][:], g.XN[:], waits=w)
        st2 = p.dma("sync", g.dbg["d_qp"][:], g.SC[:, 0:4096], waits=w)
        st3 = p.dma("sync", g.dbg["d_kvb"][:], g.SC[:, 4096:6144], waits=w)
        st4 = p.dma("sync", g.dbg["d_z"][:], g.STAT[0:1, 0:2048], waits=w)
        p.wait("sync", [st1, st2, st3, st4])
    emit_store_x(g)
    g.p.emit()
    return g


def run_layers(x_full, inputs, layers):
    g = build_nc(layers)
    wd = prep_weights(inputs, layers)
    in_maps = []
    for c in range(NCORE):
        b, q = c // 4, c % 4
        m = dict(wd)
        m["x_in"] = np.ascontiguousarray(x_full[b, :, q * T:(q + 1) * T])
        if 3 in layers:
            seg = np.zeros((128, 3), np.float32)
            if q >= 1:
                seg[:, q - 1] = 1.0
            m["l3_seg"] = seg
            hm = np.zeros((128, 4), np.float32)
            if q >= 1:
                hm[:, q - 1] = 1.0
            m["l3_hmask"] = hm
        in_maps.append(m)
    res = run_bass_kernel_spmd(g.nc, in_maps, core_ids=list(range(NCORE)))
    out = np.empty_like(x_full)
    for c in range(NCORE):
        b, q = c // 4, c % 4
        out[b, :, q * T:(q + 1) * T] = res.results[c]["y_out"]
    if g.dbg:
        return out, res
    return out


def kernel(**inputs):
    x = np.ascontiguousarray(
        np.transpose(np.asarray(inputs["L0_attn_x"], dtype=np.float32), (0, 2, 1)))
    x = run_layers(x, inputs, [0, 1, 2, 3])
    return np.ascontiguousarray(np.transpose(x, (0, 2, 1)))


# version 04
# speedup vs baseline: 1.5054x; 1.5054x over previous
import numpy as np
import ml_dtypes
import concourse.bass as bass
import concourse.mybir as mybir
from concourse.bass_utils import run_bass_kernel_spmd

F32 = mybir.dt.float32
BF16 = mybir.dt.bfloat16
AF = mybir.ActivationFunctionType
ALU = mybir.AluOpType
AX = mybir.AxisListType

NCORE = 8
B = 2
S = 8192
D = 1024
T = 2048
FC = D // 128
GROUPS = [[0, 1, 2, 3], [4, 5, 6, 7]]
ENGS = ("tensor", "vector", "scalar", "gpsimd", "sync")


def tk(*toks):
    return tuple(t for t in toks if t is not None)


class Sem:
    __slots__ = ("name", "count", "handle")

    def __init__(self, name):
        self.name = name
        self.count = 0
        self.handle = None


class Prog:
    STRICT_MODE = True

    def __init__(self, nc):
        self.nc = nc
        self.ops = {e: [] for e in ENGS}
        self.sems = []
        self.esem = {e: self.sem("s_" + e) for e in ENGS}
        self.qsem = {e: self.sem("q_" + e) for e in ENGS}
        self.csem = self.sem("s_coll")
        self.ser = {e: self.sem("ser_" + e) for e in Prog.STRICT + ("tensor",)}
        self.last_tok = {}
        self.bar = {}

    def sem(self, name):
        s = Sem(name)
        self.sems.append(s)
        return s

    def add(self, eng, fn, waits=(), inc=None, amt=1, is_dma=False):
        waits = tuple(waits) + self.bar.get(eng, ())
        strict = Prog.STRICT_MODE and fn is not None and (
            eng in Prog.STRICT or (Prog.STRICT_T and eng == "tensor"))
        if strict:
            waits = tuple(waits) + (self.last_tok.get(eng),)
            if inc is None:
                inc = self.ser[eng]
                amt = 16 if (is_dma or amt >= 16) else 1
        waits = tuple(w for w in waits if w is not None)
        tok = None
        if inc is not None:
            inc.count += amt
            tok = (inc, inc.count)
        if strict:
            self.last_tok[eng] = tok
        self.ops[eng].append((fn, waits, inc, amt, is_dma or amt >= 16))
        return tok

    def op(self, eng, fn, waits=()):
        return self.add(eng, fn, waits, inc=self.esem[eng], amt=1)

    def dma(self, queue, out, in_, waits=(), **kw):
        return self.add(queue, lambda e: e.dma_start(out=out, in_=in_, **kw),
                        waits, inc=self.qsem[queue], amt=16, is_dma=True)

    def wait(self, eng, waits):
        self.add(eng, None, waits)

    STRICT = ("vector", "scalar", "gpsimd")
    STRICT_T = False

    def emit(self):
        nc = self.nc
        for s in self.sems:
            s.handle = nc.alloc_semaphore(s.name)
        with nc.Block() as block:
            for e in ENGS:
                ops = self.ops[e]
                if not ops:
                    continue

                def body(eng, ops=ops):
                    hw = {}
                    for fn, waits, inc, amt, isd in ops:
                        for (ws, wv) in waits:
                            if hw.get(id(ws), 0) < wv:
                                eng.wait_ge(ws.handle, wv)
                                hw[id(ws)] = wv
                        if fn is None:
                            continue
                        inst = fn(eng)
                        if inc is not None:
                            inst.then_inc(inc.handle, amt)
                getattr(block, e)(body)


class Bank:
    __slots__ = ("t", "rd")

    def __init__(self, t):
        self.t = t
        self.rd = None


class Ctx:
    pass


def make_ctx(layers):
    nc = bass.Bass("TRN2", target_bir_lowering=False, debug=False, num_devices=NCORE)
    g = Ctx()
    g.nc = nc
    g.layers = layers
    g.dram = {}

    def din(name, shape, dtype=F32):
        g.dram[name] = nc.dram_tensor(name, list(shape), dtype, kind="ExternalInput").ap()
        return g.dram[name]

    g.din = din
    din("x_in", [D, T])
    din("identityf", [64, 64], F32)
    g.y_out = nc.dram_tensor("y_out", [D, T], F32, kind="ExternalOutput").ap()
    din("identity", [128, 128], BF16)

    if 0 in layers:
        din("l0_qp", [128, 1, 8, 256])
        din("l0_kp", [128, 1, 8, 256])
        din("l0_w1", [128, 16, 8, 256])
        din("l0_w2", [128, 16, 8, 256])
        din("l0_h1", [128, 32, 32, 128])
        din("l0_h2", [128, 32, 32, 128])
        din("l0_w3", [128, 8, 32, 128])
        din("l0_s1", [128, 32])
        din("l0_s2", [128, 32])
        din("l0_sh1", [128, 32])
        din("l0_sh2", [128, 32])
        din("l0_s3", [128, 8])
        g.kv_in = nc.dram_tensor("l0_kv_in", [256, D], F32, kind="Internal").ap()
        g.kv_out = nc.dram_tensor("l0_kv_out", [256, D], F32, kind="Internal").ap()

    if 1 in layers:
        din("l1_wq", [128, 2, 8, 512])
        din("l1_wk", [128, 2, 8, 512])
        din("l1_wv", [128, 2, 8, 512])
        din("l1_wo", [128, 2, 8, 512])
        din("l1_bq", [128, 8])
        din("l1_bk", [128, 8])
        din("l1_bv", [1, 1024])
        din("l1_bo", [128, 8])
        din("l1_g1", [128, 8])
        din("l1_b1", [128, 8])
        din("l1_g2", [128, 8])
        din("l1_b2", [128, 8])
        din("l1_w1", [128, 8, 8, 512])
        din("l1_bf1", [128, 32])
        din("l1_w2", [128, 8, 32, 128])
        din("l1_bf2", [128, 8])
        din("l1_wr", [128, 8])
        din("l1_W1", [128, 8, 8, 512])
        din("l1_b1m", [128, 32])
        din("l1_W2", [128, 8, 32, 128])
        din("l1_b2m", [128, 8])
        din("hmask", [128, 8, 16], BF16)
        din("selc", [16, 8, 128], BF16)
        g.nag_in = nc.dram_tensor("l1_nag_in", [T, 16], F32, kind="Internal").ap()
        g.nag_out = nc.dram_tensor("l1_nag_out", [S, 16], F32, kind="Internal").ap()
        g.sag_in = nc.dram_tensor("l1_sag_in", [T, 1], F32, kind="Internal").ap()
        g.sag_out = nc.dram_tensor("l1_sag_out", [S, 1], F32, kind="Internal").ap()
        g.bnc = nc.dram_tensor("l1_bnc", [128, 1], F32, kind="Internal").ap()

    if 2 in layers:
        din("l2_wq", [128, 2, 8, 512])
        din("l2_wk", [128, 2, 8, 512])
        din("l2_wv", [128, 2, 8, 512])
        din("l2_wo", [128, 2, 8, 512])
        din("l2_wck", [128, 32, 64])
        din("l2_wcv", [128, 32, 64])
        din("l2_bck", [64, 1])
        din("l2_bcv", [64, 1])
        din("l2_wgb", [128, 3])
        din("l2_bgr", [1, 48])
        din("l2_wr", [128, 1, 8, 8])
        din("l2_br", [1, 8])
        din("l2_W1", [8, 128, 8, 8, 512])
        din("l2_b1", [128, 256])
        din("l2_W2", [8, 128, 8, 32, 128])
        din("l2_b2", [128, 64])
        g.vfag_in = nc.dram_tensor("l2_vfag_in", [D, T], BF16, kind="Internal").ap()
        g.vfag_out = nc.dram_tensor("l2_vfag_out", [4 * D, T], BF16, kind="Internal").ap()
        g.ktm_in = nc.dram_tensor("l2_ktm_in", [T, D], BF16, kind="Internal").ap()
        g.ktm_out = nc.dram_tensor("l2_ktm_out", [S, D], BF16, kind="Internal").ap()
        g.ksel = nc.dram_tensor("l2_ksel", [D, 1024], BF16, kind="Internal").ap()
        g.vsel = nc.dram_tensor("l2_vsel", [1024, D], BF16, kind="Internal").ap()
        din("iota64", [128, 1])
        g.bi_in = nc.dram_tensor("l2_bi_in", [1, 511], F32, kind="Internal").ap()
        g.bi_out = nc.dram_tensor("l2_bi_out", [1, 511], F32, kind="Internal").ap()
        g.qs_in = nc.dram_tensor("l2_qs_in", [128, 8], F32, kind="Internal").ap()
        g.qs_out = nc.dram_tensor("l2_qs_out", [128, 8], F32, kind="Internal").ap()
        g.wrow_d = nc.dram_tensor("l2_wrow_d", [8, T], F32, kind="Internal").ap()
    if 3 in layers:
        din("l3_win", [128, 17, 8, 256])
        din("l3_wout", [128, 4, 16, 256])
        din("l3_convw", [128, 17, 4])
        din("l3_convb", [128, 17])
        din("l3_dtb", [32, 1])
        din("l3_alog", [32, 1])
        din("l3_dcol", [128, 16])
        din("l3_nw", [128, 16])
        din("l3_keys", [16, 128, 1, 8, 128])
        din("l3_vals", [16, 128, 1024])
        din("l3_selt", [128, 8, 16])
        din("l3_seg", [128, 3])
        din("l3_hmask", [128, 4])
        din("trineg", [128, 128], F32)
        g.ysc = nc.dram_tensor("l3_ysc", [2 * D, T], BF16, kind="Internal").ap()
        g.zsc = nc.dram_tensor("l3_zsc", [2 * D, T], BF16, kind="Internal").ap()
        g.cag = nc.dram_tensor("l3_cag", [32, T], F32, kind="Internal").ap()
        g.hag_in = nc.dram_tensor("l3_hag_in", [D, 3], BF16, kind="Internal").ap()
        g.hag_out = nc.dram_tensor("l3_hag_out", [4 * D, 3], BF16, kind="Internal").ap()
        g.sst_in = nc.dram_tensor("l3_sst_in", [64, 2048], F32, kind="Internal").ap()
        g.sst_out = nc.dram_tensor("l3_sst_out", [256, 2048], F32, kind="Internal").ap()
        g.a_in = nc.dram_tensor("l3_a_in", [1, 32], F32, kind="Internal").ap()
        g.a_out = nc.dram_tensor("l3_a_out", [4, 32], F32, kind="Internal").ap()
    if (1 in layers) or (2 in layers):
        g.kag_in = nc.dram_tensor("ag_kag_in", [D, T], BF16, kind="Internal").ap()
        g.kag_out = nc.dram_tensor("ag_kag_out", [4 * D, T], BF16, kind="Internal").ap()
        g.vag_in = nc.dram_tensor("ag_vag_in", [T, D], BF16, kind="Internal").ap()
        g.vag_out = nc.dram_tensor("ag_vag_out", [S, D], BF16, kind="Internal").ap()

    sb = nc.alloc_sbuf_tensor
    g.X = sb("X", [128, FC, T], F32)
    g.XN = sb("XN", [128, FC, T], BF16)
    g.SBIGf = sb("SBIG", [128, 16384], F32)
    g.SBIG = g.SBIGf.bitcast(BF16)
    g.SC = sb("SC", [128, 8192], BF16)
    g.SCf = g.SC.bitcast(F32)
    g.WB = sb("WB", [128, 2, 4096], BF16)
    g.MISC = sb("MISC", [128, 520], F32)
    g.MISCI = g.MISC.bitcast(mybir.dt.int32)
    g.STAT = sb("STAT", [1, 2048], F32)
    g.IDN = sb("IDN", [128, 128], BF16)
    g.IDNF = sb("IDNF", [64, 64], F32)
    if 3 in layers:
        g.TRINEG = g.SBIGf[:, 16256:16384]
    g.ONESB = sb("ONESB", [128, 1], BF16)
    g.ONESF = sb("ONESF", [128, 1], F32)
    g.ONESR = sb("ONESR", [1, 128], F32)
    g.ONESRB = sb("ONESRB", [1, 128], BF16)
    if 1 in layers:
        g.HM = sb("HM", [128, 8, 16], BF16)
        g.SELC = sb("SELC", [16, 8, 128], BF16)
    if (1 in layers) or (2 in layers):
        g.BROW = sb("BROW", [1, 1024], BF16)

    ps = nc.alloc_psum_tensor
    g.PS = [Bank(ps(f"ps{i}", [128, 512], F32)) for i in range(6)]
    g.PT = [Bank(ps("pst0", [128, 256], BF16)), Bank(ps("pst1", [128, 256], BF16))]

    g.dbg = {}
    if getattr(make_ctx, "DBG", False):
        for nm, shape, dt in [("d_xn", [128, FC, T], BF16), ("d_qp", [128, 4096], BF16),
                              ("d_kvb", [128, 2048], BF16), ("d_z", [1, 2048], F32)]:
            g.dbg[nm] = nc.dram_tensor(nm, shape, dt, kind="ExternalOutput").ap()
    if getattr(make_ctx, "DBG3", False):
        for nm, shape, dt in [("d_ysc", [2 * D, T], BF16), ("d_zsc", [2 * D, T], BF16),
                              ("d_cag", [32, T], F32), ("d_sst", [256, 2048], F32),
                              ("d_wq0", [128, 8192], BF16), ("d_nr0", [1, 512], F32)]:
            g.dbg[nm] = nc.dram_tensor(nm, shape, dt, kind="ExternalOutput").ap()
    if getattr(make_ctx, "DBG2", False):
        for nm, shape, dt in [("d_stat", [1, 2048], F32), ("d_misc", [128, 520], F32),
                              ("d_sbig", [128, 8192], F32), ("d_sc", [128, 4096], F32),
                              ("d_bi", [1, 511], F32), ("d_idx", [1, 16], F32)]:
            g.dbg[nm] = nc.dram_tensor(nm, shape, dt, kind="ExternalOutput").ap()

    g.p = Prog(nc)
    return g


def emit_prelude(g):
    p = g.p
    g.tok_idn = p.dma("sync", g.IDN[:], g.dram["identity"][:])
    g.tok_idnf = p.dma("sync", g.IDNF[:], g.dram["identityf"][:])
    p.op("vector", lambda e: e.memset(g.ONESB[:], 1.0))
    p.op("vector", lambda e: e.memset(g.ONESF[:], 1.0))
    p.op("vector", lambda e: e.memset(g.ONESR[:], 1.0))
    g.tok_ones = p.op("vector", lambda e: e.memset(g.ONESRB[:], 1.0))
    x = g.dram["x_in"].rearrange("(c p) t -> p c t", p=128)
    t = p.dma("sync", g.X[:], x)
    g.x_toks = [t]


def emit_store_x(g):
    p = g.p
    y = g.y_out.rearrange("(c p) t -> p c t", p=128)
    st = p.dma("sync", y, g.X[:], waits=tk(*g.x_toks))
    p.wait("sync", [st])


def emit_ln(g, src=None, dst=None, src_toks=None, gb=None, sqreg=None, tmpreg=None):
    p = g.p
    if src is None:
        src = g.X
    if dst is None:
        dst = g.XN
    if src_toks is None:
        src_toks = g.x_toks
    if sqreg is None:
        sqreg = g.SC
    if tmpreg is None:
        tmpreg = g.SBIGf
    is_f32 = (src.dtype == F32)
    ones = g.ONESF if is_f32 else g.ONESB
    ln_done = [[None] * 4 for _ in range(FC)]
    sq_guard = None
    bc_banks = [g.PS[4], g.PS[5]]
    for t in range(4):
        ts = slice(t * 512, (t + 1) * 512)
        s0 = g.PS[(2 * t) % 4]
        s1 = g.PS[(2 * t + 1) % 4]
        mm0 = mm1 = None
        for c in range(FC):
            slot = sqreg[:, c * 512:(c + 1) * 512]
            sqt = p.op("vector", lambda e, c=c, ts=ts, slot=slot: e.tensor_mul(
                slot, src[:, c, ts], src[:, c, ts]),
                waits=tk(*src_toks, sq_guard))
            mm0 = p.add("tensor", lambda e, c=c, ts=ts, s0=s0: e.matmul(
                s0.t[0:1, 0:512], ones[:], src[:, c, ts],
                start=(c == 0), stop=(c == FC - 1)),
                waits=tk(*(list(src_toks) + [g.tok_ones, s0.rd] if c == 0 else ())),
                inc=p.esem["tensor"] if c == FC - 1 else None)
            mm1 = p.add("tensor", lambda e, c=c, s1=s1, slot=slot: e.matmul(
                s1.t[0:1, 0:512], g.ONESB[:], slot,
                start=(c == 0), stop=(c == FC - 1)),
                waits=tk(sqt, s1.rd if c == 0 else None),
                inc=p.esem["tensor"] if c == FC - 1 else None)
        sq_guard = mm1
        mrow = g.STAT[0:1, 0:512]
        vrow = g.STAT[0:1, 512:1024]
        trow = g.STAT[0:1, 1024:1536]
        rrow = g.STAT[0:1, 1536:2048]
        t_m = p.op("vector", lambda e, s0=s0: e.tensor_scalar_mul(mrow, s0.t[0:1, 0:512], 1.0 / D), waits=tk(mm0))
        s0.rd = t_m
        t_v = p.op("vector", lambda e, s1=s1: e.tensor_scalar_mul(vrow, s1.t[0:1, 0:512], 1.0 / D), waits=tk(mm1))
        s1.rd = t_v
        t_m2 = p.op("vector", lambda e: e.tensor_mul(trow, mrow, mrow), waits=tk(t_m))
        t_v2 = p.op("vector", lambda e: e.tensor_sub(vrow, vrow, trow), waits=tk(t_v, t_m2))
        t_v3 = p.op("vector", lambda e: e.tensor_scalar_add(vrow, vrow, 1e-5), waits=tk(t_v2))
        t_v4 = p.op("vector", lambda e: e.reciprocal(vrow, vrow), waits=tk(t_v3))
        t_r = p.op("scalar", lambda e: e.activation(rrow, vrow, AF.Sqrt), waits=tk(t_v4))
        bm = bc_banks[0]
        br = bc_banks[1]
        t_bm = p.op("tensor", lambda e, bm=bm: e.matmul(
            bm.t[:, 0:512], g.ONESR[:], mrow, start=True, stop=True), waits=tk(t_m, bm.rd, g.tok_ones))
        t_br = p.op("tensor", lambda e, br=br: e.matmul(
            br.t[:, 0:512], g.ONESR[:], rrow, start=True, stop=True), waits=tk(t_r, br.rd))
        last_a = []
        for c in range(FC):
            tmp = tmpreg[:, (c % 2) * 512:(c % 2) * 512 + 512]
            a1 = p.op("vector", lambda e, c=c, ts=ts, tmp=tmp, bm=bm: e.tensor_sub(
                tmp, src[:, c, ts], bm.t[:, 0:512]), waits=tk(t_bm, ln_done[c - 2][t] if c >= 2 else None))
            if gb is None:
                a2 = p.op("vector", lambda e, c=c, ts=ts, tmp=tmp, br=br: e.tensor_mul(
                    dst[:, c, ts], tmp, br.t[:, 0:512]), waits=tk(a1, t_br))
            else:
                gcol, bcol = gb
                a2 = p.op("vector", lambda e, c=c, ts=ts, tmp=tmp, br=br, gcol=gcol: e.scalar_tensor_tensor(
                    dst[:, c, ts], tmp, gcol[:, c:c + 1], br.t[:, 0:512], ALU.mult, ALU.mult),
                    waits=tk(a1, t_br))
                a2 = p.op("vector", lambda e, c=c, ts=ts, bcol=bcol: e.tensor_scalar_add(
                    dst[:, c, ts], dst[:, c, ts], bcol[:, c:c + 1]), waits=tk(a2))
            ln_done[c][t] = a2
            last_a.append(a2)
        bm.rd = last_a[-1]
        br.rd = last_a[-1]
    return ln_done


def emit_l0(g, ln):
    p = g.p
    dr = g.dram
    QP = lambda e, ts: g.SC[:, e * 2048 + ts.start: e * 2048 + ts.stop]
    KTM = lambda ti: g.SC[:, 4096 + ti * 256: 4096 + (ti + 1) * 256]
    XNT = g.SBIG[:, 0:16384]
    KVP = g.SBIGf[:, 8192:10240]
    KVB = lambda e: g.SC[:, 4096 + e * 1024: 4096 + (e + 1) * 1024]

    wq = g.WB[:, 0, 0:2048].rearrange("p (k o) -> p k o", k=8)
    wk = g.WB[:, 1, 0:2048].rearrange("p (k o) -> p k o", k=8)
    t_wq = p.dma("gpsimd", wq, dr["l0_qp"][:, 0], waits=tk())
    t_wk = p.dma("gpsimd", wk, dr["l0_kp"][:, 0], waits=tk())

    tok_qfm = [[None] * 4 for _ in range(2)]
    for e2 in range(2):
        for t in range(4):
            ts = slice(t * 512, (t + 1) * 512)
            bank = g.PS[(e2 * 4 + t) % 4]
            mm = None
            for c in range(FC):
                mm = p.add("tensor", lambda e, c=c, e2=e2, ts=ts, bank=bank: e.matmul(
                    bank.t[:, 0:512], g.WB[:, 0, c * 256 + e2 * 128: c * 256 + e2 * 128 + 128],
                    g.XN[:, c, ts], start=(c == 0), stop=(c == FC - 1)),
                    waits=tk(ln[c][t], *( (t_wq, bank.rd) if c == 0 else ())),
                    inc=p.esem["tensor"] if c == FC - 1 else None)
            ev = p.op("scalar", lambda e, e2=e2, ts=ts, bank=bank: e.activation(
                QP(e2, ts), bank.t[:, 0:512], AF.Exp), waits=tk(mm))
            bank.rd = ev
            tok_qfm[e2][t] = ev

    tok_ktm = [None] * 16
    for ti in range(16):
        bank = g.PS[4 + (ti % 2)]
        mm = None
        for c in range(FC):
            mm = p.add("tensor", lambda e, c=c, ti=ti, bank=bank: e.matmul(
                bank.t[:, 0:256], g.XN[:, c, ti * 128:(ti + 1) * 128],
                g.WB[:, 1, c * 256: c * 256 + 256], start=(c == 0), stop=(c == FC - 1)),
                waits=tk(ln[c][ti // 4], *( (t_wk, bank.rd) if c == 0 else ())),
                inc=p.esem["tensor"] if c == FC - 1 else None)
        ev = p.op("scalar", lambda e, ti=ti, bank=bank: e.activation(
            KTM(ti), bank.t[:, 0:256], AF.Exp), waits=tk(mm))
        bank.rd = ev
        tok_ktm[ti] = ev

    tok_xnt = [None] * 16
    for ti in range(16):
        last = None
        for c in range(FC):
            pb = g.PT[(ti * FC + c) % 2]
            tt = p.op("tensor", lambda e, c=c, ti=ti, pb=pb: e.transpose(
                pb.t[:, 0:128], g.XN[:, c, ti * 128:(ti + 1) * 128], g.IDN[:]),
                waits=tk(ln[c][ti // 4], g.tok_idn, pb.rd))
            cp = p.op("vector", lambda e, c=c, ti=ti, pb=pb: e.tensor_copy(
                XNT[:, ti * 1024 + c * 128: ti * 1024 + (c + 1) * 128], pb.t[:, 0:128]),
                waits=tk(tt))
            pb.rd = cp
            last = cp
        tok_xnt[ti] = last

    tok_kvp = []
    for e2 in range(2):
        for dh in range(2):
            bank = g.PS[(e2 * 2 + dh) % 4]
            mm = None
            for ti in range(16):
                mm = p.add("tensor", lambda e, e2=e2, dh=dh, ti=ti, bank=bank: e.matmul(
                    bank.t[:, 0:512], KTM(ti)[:, e2 * 128:(e2 + 1) * 128],
                    XNT[:, ti * 1024 + dh * 512: ti * 1024 + dh * 512 + 512],
                    start=(ti == 0), stop=(ti == 15)),
                    waits=tk(tok_ktm[ti], tok_xnt[ti], bank.rd if ti == 0 else None),
                    inc=p.esem["tensor"] if ti == 15 else None)
            ev = p.op("vector", lambda e, e2=e2, dh=dh, bank=bank: e.tensor_copy(
                KVP[:, e2 * 1024 + dh * 512: e2 * 1024 + dh * 512 + 512], bank.t[:, 0:512]),
                waits=tk(mm))
            bank.rd = ev
            tok_kvp.append(ev)

    zrow = g.STAT[0:1, 0:2048]
    tok_z = []
    for t in range(4):
        bank = g.PS[4 + (t % 2)]
        mm = None
        for e2 in range(2):
            mm = p.add("tensor", lambda e, e2=e2, t=t, bank=bank: e.matmul(
                bank.t[0:1, 0:512], g.ONESB[:], QP(e2, slice(t * 512, (t + 1) * 512)),
                start=(e2 == 0), stop=(e2 == 1)),
                waits=tk(tok_qfm[e2][t], bank.rd if e2 == 0 else None),
                inc=p.esem["tensor"] if e2 == 1 else None)
        ev = p.op("vector", lambda e, t=t, bank=bank: e.tensor_scalar_add(
            zrow[0:1, t * 512:(t + 1) * 512], bank.t[0:1, 0:512], 1e-8), waits=tk(mm))
        bank.rd = ev
        tok_z.append(ev)
    tok_zr = p.op("vector", lambda e: e.reciprocal(zrow, zrow), waits=tk(*tok_z))

    t_kvst = p.dma("sync", g.kv_in.rearrange("(e p) d -> p e d", p=128),
                   KVP.rearrange("p (e d) -> p e d", e=2), waits=tk(*tok_kvp))
    t_coll = p.add("gpsimd", lambda e: e.collective_compute(
        "AllReduce", ALU.add, replica_groups=GROUPS, ins=[g.kv_in[:]], outs=[g.kv_out[:]]),
        waits=tk(t_kvst), inc=p.csem, amt=1)

    HA = g.SBIG[:, 0:16384].rearrange("p (o t) -> p o t", o=32)
    HB = g.SBIG[:, 16384:32768].rearrange("p (o t) -> p o t", o=32)
    t_s1 = p.dma("sync", g.MISC[:, 0:32], dr["l0_s1"][:])
    t_s2 = p.dma("sync", g.MISC[:, 32:64], dr["l0_s2"][:])
    t_sh1 = p.dma("sync", g.MISC[:, 64:96], dr["l0_sh1"][:])
    t_sh2 = p.dma("sync", g.MISC[:, 96:128], dr["l0_sh2"][:])
    t_s3 = p.dma("sync", g.MISC[:, 128:136], dr["l0_s3"][:])

    wbtok = [None, None]
    wbtok[0] = tok_qfm[1][3]
    wbtok[1] = tok_ktm[15]
    silu_rd = [None, None]
    ffn_x_toks = []
    for t in range(4):
        ts = slice(t * 512, (t + 1) * 512)
        for og in range(16):
            b01 = og % 2
            wdst1 = g.WB[:, b01, 0:2048].rearrange("p (k o) -> p k o", k=8)
            wdst2 = g.WB[:, b01, 2048:4096].rearrange("p (k o) -> p k o", k=8)
            tw1 = p.dma("gpsimd", wdst1, dr["l0_w1"][:, og], waits=tk(wbtok[b01]))
            tw2 = p.dma("gpsimd", wdst2, dr["l0_w2"][:, og], waits=tk(wbtok[b01]))
            for oc in range(2):
                oabs = og * 2 + oc
                ba = g.PS[(2 * oc) % 4]
                bb = g.PS[(2 * oc + 1) % 4]
                m1 = m2 = None
                for c in range(FC):
                    m1 = p.add("tensor", lambda e, c=c, b01=b01, oc=oc, ts=ts, ba=ba: e.matmul(
                        ba.t[:, 0:512], g.WB[:, b01, c * 256 + oc * 128: c * 256 + oc * 128 + 128],
                        g.XN[:, c, ts], start=(c == 0), stop=(c == FC - 1)),
                        waits=tk(ln[c][t], *( (tw1, ba.rd) if c == 0 else ())),
                        inc=p.esem["tensor"] if c == FC - 1 else None)
                for c in range(FC):
                    m2 = p.add("tensor", lambda e, c=c, b01=b01, oc=oc, ts=ts, bb=bb: e.matmul(
                        bb.t[:, 0:512], g.WB[:, b01, 2048 + c * 256 + oc * 128: 2048 + c * 256 + oc * 128 + 128],
                        g.XN[:, c, ts], start=(c == 0), stop=(c == FC - 1)),
                        waits=tk(*( (tw2, bb.rd) if c == 0 else ())),
                        inc=p.esem["tensor"] if c == FC - 1 else None)
                wbtok[b01] = m2
                slot = g.SC[:, 6144 + (oabs % 2) * 512: 6144 + (oabs % 2) * 512 + 512]
                sv = p.op("scalar", lambda e, oabs=oabs, ba=ba, slot=slot: e.activation(
                    slot, ba.t[:, 0:512], AF.Silu, scale=g.MISC[:, oabs:oabs + 1]),
                    waits=tk(m1, t_s1, silu_rd[oabs % 2]))
                ba.rd = sv
                hv = p.op("vector", lambda e, oabs=oabs, t=t, bb=bb, slot=slot: e.scalar_tensor_tensor(
                    HA[:, oabs, 0:512],
                    bb.t[:, 0:512], g.MISC[:, 32 + oabs:33 + oabs], slot, ALU.mult, ALU.mult),
                    waits=tk(m2, sv, t_s2,
                             tok_kvp[-1] if (t == 0 and og == 0 and oc == 0) else None))
                bb.rd = hv
                silu_rd[oabs % 2] = hv
        for og in range(32):
            b01 = og % 2
            wdst = g.WB[:, b01, :].rearrange("p (k o) -> p k o", k=32)
            twk = p.dma("gpsimd", wdst, dr["l0_h1"][:, og], waits=tk(wbtok[b01]))
            bank = g.PS[og % 4]
            mm = None
            for k in range(32):
                mm = p.add("tensor", lambda e, k=k, b01=b01, bank=bank: e.matmul(
                    bank.t[:, 0:512], g.WB[:, b01, k * 128:(k + 1) * 128],
                    HA[:, k, 0:512], start=(k == 0), stop=(k == 31)),
                    waits=tk(*( (twk, bank.rd, hv) if k == 0 else ())),
                    inc=p.esem["tensor"] if k == 31 else None)
            wbtok[b01] = mm
            gv = p.op("scalar", lambda e, og=og, bank=bank: e.activation(
                HB[:, og, 0:512], bank.t[:, 0:512], AF.Gelu, scale=g.MISC[:, 64 + og:65 + og]),
                waits=tk(mm, t_sh1, t_kvst if og == 0 else None))
            bank.rd = gv
            if og == 31:
                hb_last = gv
        for og in range(32):
            b01 = og % 2
            wdst = g.WB[:, b01, :].rearrange("p (k o) -> p k o", k=32)
            twk = p.dma("gpsimd", wdst, dr["l0_h2"][:, og], waits=tk(wbtok[b01]))
            bank = g.PS[og % 4]
            mm = None
            for k in range(32):
                mm = p.add("tensor", lambda e, k=k, b01=b01, bank=bank: e.matmul(
                    bank.t[:, 0:512], g.WB[:, b01, k * 128:(k + 1) * 128],
                    HB[:, k, 0:512], start=(k == 0), stop=(k == 31)),
                    waits=tk(*( (twk, bank.rd, hb_last) if k == 0 else ())),
                    inc=p.esem["tensor"] if k == 31 else None)
            wbtok[b01] = mm
            gv = p.op("scalar", lambda e, og=og, bank=bank: e.activation(
                HA[:, og, 0:512], bank.t[:, 0:512], AF.Gelu, scale=g.MISC[:, 96 + og:97 + og]),
                waits=tk(mm, t_sh2))
            bank.rd = gv
            if og == 31:
                hc_last = gv
        for og in range(FC):
            b01 = og % 2
            wdst = g.WB[:, b01, :].rearrange("p (k o) -> p k o", k=32)
            twk = p.dma("gpsimd", wdst, dr["l0_w3"][:, og], waits=tk(wbtok[b01]))
            bank = g.PS[og % 4]
            mm = None
            for k in range(32):
                mm = p.add("tensor", lambda e, k=k, b01=b01, bank=bank: e.matmul(
                    bank.t[:, 0:512], g.WB[:, b01, k * 128:(k + 1) * 128],
                    HA[:, k, 0:512], start=(k == 0), stop=(k == 31)),
                    waits=tk(*( (twk, bank.rd, hc_last) if k == 0 else ())),
                    inc=p.esem["tensor"] if k == 31 else None)
            wbtok[b01] = mm
            xv = p.op("vector", lambda e, og=og, ts=ts, bank=bank: e.scalar_tensor_tensor(
                g.X[:, og, ts], bank.t[:, 0:512], g.MISC[:, 128 + og:129 + og],
                g.X[:, og, ts], ALU.mult, ALU.add),
                waits=tk(mm, t_s3))
            bank.rd = xv
            ffn_x_toks.append(xv)

    t_kvb = p.dma("gpsimd", g.SC[:, 4096:6144].rearrange("p (e d) -> p e d", e=2),
                  g.kv_out.rearrange("(e p) d -> p e d", p=128),
                  waits=tk(t_coll, *tok_kvp))

    attn_x_toks = []
    zbc_rd = [None, None]
    tmp_rd = [None, None]
    for t in range(4):
        ts = slice(t * 512, (t + 1) * 512)
        bz = g.PS[4 + (t % 2)]
        t_zb = p.op("tensor", lambda e, t=t, bz=bz: e.matmul(
            bz.t[:, 0:512], g.ONESR[:], zrow[0:1, t * 512:(t + 1) * 512],
            start=True, stop=True), waits=tk(tok_zr, bz.rd))
        zslot = g.SBIGf[:, (t % 2) * 512:(t % 2) * 512 + 512]
        t_zs = p.op("vector", lambda e, bz=bz, zslot=zslot: e.tensor_copy(zslot, bz.t[:, 0:512]),
                    waits=tk(t_zb, zbc_rd[t % 2], *(ffn_x_toks[-1:] if t == 0 else ())))
        bz.rd = t_zs
        for dch in range(FC):
            bank = g.PS[dch % 4]
            mm = None
            for e2 in range(2):
                mm = p.add("tensor", lambda e, e2=e2, dch=dch, ts=ts, bank=bank: e.matmul(
                    bank.t[:, 0:512], KVB(e2)[:, dch * 128:(dch + 1) * 128],
                    QP(e2, ts), start=(e2 == 0), stop=(e2 == 1)),
                    waits=tk(*( (t_kvb, bank.rd) if e2 == 0 else ())),
                    inc=p.esem["tensor"] if e2 == 1 else None)
            tmp = g.SBIGf[:, 1024 + (dch % 2) * 512: 1024 + (dch % 2) * 512 + 512]
            m1 = p.op("vector", lambda e, tmp=tmp, bank=bank, zslot=zslot: e.tensor_mul(
                tmp, bank.t[:, 0:512], zslot), waits=tk(mm, t_zs, tmp_rd[dch % 2]))
            bank.rd = m1
            m2 = p.op("vector", lambda e, dch=dch, ts=ts, tmp=tmp: e.tensor_add(
                g.X[:, dch, ts], g.X[:, dch, ts], tmp), waits=tk(m1))
            tmp_rd[dch % 2] = m2
            attn_x_toks.append(m2)
        zbc_rd[t % 2] = attn_x_toks[-1]

    g.x_toks = ffn_x_toks + attn_x_toks


TEMP = 8.0
MXN = 1.0 - 1e-5


def _clip_heads(g, A, a_toks, wk_out=None):
    p = g.p
    done = [[None] * 4 for _ in range(FC)]
    rsb = g.SCf[0:16, 3584:4096]
    fsb = g.SC[0:16, 6656:7168]
    for t in range(4):
        ts = slice(t * 512, (t + 1) * 512)
        nb = g.PS[4 + (t % 2)]
        mm = None
        sq_toks = []
        for c in range(FC):
            slot = g.SC[:, c * 512:(c + 1) * 512]
            sq = p.op("vector", lambda e, c=c, ts=ts, slot=slot: e.tensor_mul(
                slot, A[:, c, ts], A[:, c, ts]),
                waits=tk(a_toks[c][t], _clip_heads_guard.get(id(g), None)))
            mm = p.add("tensor", lambda e, c=c, slot=slot, nb=nb: e.matmul(
                nb.t[0:16, 0:512], g.HM[:, c, :], slot,
                start=(c == 0), stop=(c == FC - 1)),
                waits=tk(sq, g.tok_hm if c == 0 else None, nb.rd if c == 0 else None),
                inc=p.esem["tensor"] if c == FC - 1 else None)
            sq_toks.append(sq)
        _clip_heads_guard[id(g)] = mm
        if wk_out is not None:
            nk = p.op("vector", lambda e, nb=nb: e.tensor_scalar(
                rsb, nb.t[0:16, 0:512], MXN * MXN, -1.0 / TEMP, ALU.min, ALU.mult),
                waits=tk(mm))
            for q in range(4):
                pb = g.PS[(t * 4 + q) % 4]
                tt = p.op("tensor", lambda e, q=q, pb=pb: e.transpose(
                    pb.t[:, 0:16], rsb[0:16, q * 128:(q + 1) * 128], g.IDNF[0:16, 0:16]),
                    waits=tk(nk, pb.rd, g.tok_idnf))
                cp = p.op("vector", lambda e, t=t, q=q, pb=pb: e.tensor_copy(
                    g.MISC[:, 256 + (t * 4 + q) * 16: 256 + (t * 4 + q) * 16 + 16], pb.t[:, 0:16]),
                    waits=tk(tt))
                pb.rd = cp
                _nk2_toks.append(cp)
        r1 = p.op("vector", lambda e, nb=nb: e.reciprocal(rsb, nb.t[0:16, 0:512]), waits=tk(mm, *( [_nk2_toks[-1]] if wk_out is not None else [] )))
        nb.rd = r1
        r2 = p.op("scalar", lambda e: e.activation(fsb, rsb, AF.Sqrt, scale=MXN * MXN), waits=tk(r1))
        r3 = p.op("vector", lambda e: e.tensor_scalar_min(fsb, fsb, 1.0), waits=tk(r2))
        for c in range(FC):
            bb = g.PS[(c % 2)]
            bc = p.op("tensor", lambda e, c=c, bb=bb: e.matmul(
                bb.t[:, 0:512], g.SELC[:, c, :], fsb, start=True, stop=True),
                waits=tk(r3, bb.rd, g.tok_selc))
            mu = p.op("vector", lambda e, c=c, ts=ts, bb=bb: e.tensor_mul(
                A[:, c, ts], A[:, c, ts], bb.t[:, 0:512]),
                waits=tk(bc, sq_toks[c]))
            bb.rd = mu
            done[c][t] = mu
    return done


_clip_heads_guard = {}
_nk2_toks = []


def _proj_fullvec(g, A, a_toks, sq_guard_key):
    p = g.p
    done = [[None] * 4 for _ in range(FC)]
    rrow = g.SCf[0:1, 3104:3616]
    for t in range(4):
        ts = slice(t * 512, (t + 1) * 512)
        nb = g.PS[4 + (t % 2)]
        mm = None
        sq_toks = []
        for c in range(FC):
            slot = g.SC[:, c * 512:(c + 1) * 512]
            sq = p.op("vector", lambda e, c=c, ts=ts, slot=slot: e.tensor_mul(
                slot, A[:, c, ts], A[:, c, ts]),
                waits=tk(a_toks[c][t] if a_toks else None, _clip_heads_guard.get(sq_guard_key, None)))
            mm = p.add("tensor", lambda e, c=c, slot=slot, nb=nb: e.matmul(
                nb.t[0:1, 0:512], g.ONESB[:], slot,
                start=(c == 0), stop=(c == FC - 1)),
                waits=tk(sq, nb.rd if c == 0 else None),
                inc=p.esem["tensor"] if c == FC - 1 else None)
            sq_toks.append(sq)
        _clip_heads_guard[sq_guard_key] = mm
        r1 = p.op("vector", lambda e, nb=nb: e.reciprocal(rrow, nb.t[0:1, 0:512]), waits=tk(mm))
        nb.rd = r1
        r2 = p.op("scalar", lambda e: e.activation(rrow, rrow, AF.Sqrt, scale=MXN * MXN), waits=tk(r1))
        r3 = p.op("vector", lambda e: e.tensor_scalar_min(rrow, rrow, 1.0), waits=tk(r2))
        bb = g.PS[(t % 2)]
        bc = p.op("tensor", lambda e, bb=bb: e.matmul(
            bb.t[:, 0:512], g.ONESR[:], rrow, start=True, stop=True),
            waits=tk(r3, bb.rd))
        last = None
        for c in range(FC):
            mu = p.op("vector", lambda e, c=c, ts=ts, bb=bb: e.tensor_mul(
                A[:, c, ts], A[:, c, ts], bb.t[:, 0:512]),
                waits=tk(bc, sq_toks[c]))
            done[c][t] = mu
            last = mu
        bb.rd = last
    return done


def emit_l1(g, ln):
    p = g.p
    dr = g.dram
    _clip_heads_guard.clear()
    _nk2_toks.clear()
    g.tok_hm = p.dma("sync", g.HM[:], dr["hmask"][:])
    g.tok_selc = p.dma("sync", g.SELC[:], dr["selc"][:])
    tok_brow = p.dma("gpsimd", g.BROW[:], dr["l1_bv"][:])
    misc_loads = []
    mcols = {"bq": (0, 8), "bk": (8, 8), "bo": (16, 8), "g1": (24, 8), "b1": (32, 8),
             "g2": (40, 8), "b2": (48, 8), "bf1": (56, 32), "bf2": (88, 8),
             "b1m": (96, 32), "b2m": (128, 8), "wr": (136, 8)}
    mtok = {}
    for nm, (o, w) in mcols.items():
        mtok[nm] = p.dma("sync", g.MISC[:, o:o + w], dr["l1_" + nm][:])
    MC = lambda nm, i: g.MISC[:, mcols[nm][0] + i: mcols[nm][0] + i + 1]

    XL = g.SBIG[:, 0:16384].rearrange("p (c t) -> p c t", c=8)
    ln1 = emit_ln(g, src=g.XN, dst=XL, src_toks=[ln[c][t] for c in range(FC) for t in range(4)] + [mtok["g1"], mtok["b1"]],
                  gb=(g.MISC[:, 24:32], g.MISC[:, 32:40]), sqreg=g.SC,
                  tmpreg=g.SBIGf[:, 8192:9216])

    srow = g.STAT[0:1, 0:2048]
    WRB = g.SC[:, 4096:4104]
    t_wrb = p.dma("gpsimd", WRB, dr["l1_wr"][:])
    sc_toks = []
    for t in range(4):
        nb = g.PS[4 + (t % 2)]
        mm = None
        for c in range(FC):
            mm = p.add("tensor", lambda e, c=c, t=t, nb=nb: e.matmul(
                nb.t[0:1, 0:512], WRB[:, c:c + 1], g.XN[:, c, t * 512:(t + 1) * 512],
                start=(c == 0), stop=(c == FC - 1)),
                waits=tk(ln[c][t], t_wrb if c == 0 else None, nb.rd if c == 0 else None),
                inc=p.esem["tensor"] if c == FC - 1 else None)
        ev = p.op("vector", lambda e, t=t, nb=nb: e.tensor_copy(
            srow[0:1, t * 512:(t + 1) * 512], nb.t[0:1, 0:512]), waits=tk(mm))
        nb.rd = ev
        sc_toks.append(ev)
    t_sst = p.dma("sync", g.sag_in.rearrange("t one -> one t"), srow[0:1, :], waits=tk(*sc_toks))
    t_sag = p.add("gpsimd", lambda e: e.collective_compute(
        "AllGather", ALU.bypass, replica_groups=GROUPS, ins=[g.sag_in[:]], outs=[g.sag_out[:]]),
        waits=tk(t_sst), inc=p.csem, amt=1)

    KV = g.SBIG[:, 16384:32768].rearrange("p (c t) -> p c t", c=8)
    wb_rd = [None, None]

    def proj_fm(wname, bname, out, waits_extra=()):
        toks = [[None] * 4 for _ in range(FC)]
        for og in range(2):
            b01 = og % 2
            wdst = g.WB[:, b01, :].rearrange("p (k o) -> p k o", k=8)
            tw = p.dma("gpsimd", wdst, dr[wname][:, og], waits=tk(wb_rd[b01]))
            last = None
            for oc in range(4):
                c_out = og * 4 + oc
                for t in range(4):
                    bk = g.PS[(oc * 4 + t) % 4]
                    mm = None
                    for c in range(FC):
                        mm = p.add("tensor", lambda e, c=c, oc=oc, t=t, b01=b01, bk=bk: e.matmul(
                            bk.t[:, 0:512], g.WB[:, b01, c * 512 + oc * 128: c * 512 + oc * 128 + 128],
                            XL[:, c, t * 512:(t + 1) * 512], start=(c == 0), stop=(c == FC - 1)),
                            waits=tk(ln1[c][t], *( (tw, bk.rd) + tuple(waits_extra) if c == 0 else ())),
                            inc=p.esem["tensor"] if c == FC - 1 else None)
                    if bname is None:
                        ev = p.op("vector", lambda e, c_out=c_out, t=t, bk=bk: e.tensor_copy(
                            out[:, c_out, t * 512:(t + 1) * 512], bk.t[:, 0:512]), waits=tk(mm))
                    else:
                        ev = p.op("vector", lambda e, c_out=c_out, t=t, bk=bk: e.tensor_scalar_add(
                            out[:, c_out, t * 512:(t + 1) * 512], bk.t[:, 0:512], MC(bname, c_out)),
                            waits=tk(mm, mtok[bname]))
                    bk.rd = ev
                    toks[c_out][t] = ev
                    last = ev
            wb_rd[b01] = last
        return toks

    k_toks = proj_fm("l1_wk", "bk", KV)
    k_toks = _clip_heads(g, KV, k_toks, wk_out=True)
    t_kst = p.dma("sync", g.kag_in.rearrange("(c p) t -> p c t", p=128), KV[:],
                  waits=tk(*[k_toks[c][t] for c in range(FC) for t in range(4)]))
    t_kag = p.add("gpsimd", lambda e: e.collective_compute(
        "AllGather", ALU.bypass, replica_groups=GROUPS, ins=[g.kag_in[:]], outs=[g.kag_out[:]]),
        waits=tk(t_kst), inc=p.csem, amt=1)
    t_nst = p.dma("sync", g.nag_in.rearrange("(tc p) h -> p tc h", p=128),
                  g.MISC[:, 256:512].rearrange("p (tc h) -> p tc h", h=16),
                  waits=tk(*_nk2_toks))
    t_nag = p.add("gpsimd", lambda e: e.collective_compute(
        "AllGather", ALU.bypass, replica_groups=GROUPS, ins=[g.nag_in[:]], outs=[g.nag_out[:]]),
        waits=tk(t_nst), inc=p.csem, amt=1)

    v_toks = [[None] * 4 for _ in range(2)]
    VV = KV
    VT = g.SBIG[:, 16384:32768].rearrange("p (tc d) -> p tc d", tc=16)
    vlast = None
    for og in range(2):
        b01 = og % 2
        wdst = g.WB[:, b01, :].rearrange("p (k o) -> p k o", k=8)
        tw = p.dma("gpsimd", wdst, dr["l1_wv"][:, og], waits=tk(wb_rd[b01]))
        for ti in range(16):
            bk = g.PS[(ti % 4)]
            mm = None
            for c in range(FC):
                mm = p.add("tensor", lambda e, c=c, ti=ti, og=og, b01=b01, bk=bk: e.matmul(
                    bk.t[:, 0:512], XL[:, c, ti * 128:(ti + 1) * 128],
                    g.WB[:, b01, c * 512: c * 512 + 512], start=(c == 0), stop=False),
                    waits=tk(ln1[c][ti // 4], *( (tw, bk.rd, t_kst) if c == 0 else ())),
                    inc=None)
            mm = p.add("tensor", lambda e, og=og, bk=bk: e.matmul(
                bk.t[:, 0:512], g.ONESRB[:], g.BROW[0:1, og * 512:(og + 1) * 512],
                start=False, stop=True),
                waits=tk(tok_brow), inc=p.esem["tensor"])
            ev = p.op("vector", lambda e, ti=ti, og=og, bk=bk: e.tensor_copy(
                VT[:, ti, og * 512:(og + 1) * 512], bk.t[:, 0:512]), waits=tk(mm))
            bk.rd = ev
            vlast = ev
        wb_rd[b01] = vlast
    t_vst = p.dma("sync", g.vag_in.rearrange("(tc p) d -> p tc d", p=128), VT[:], waits=tk(vlast))
    t_vag = p.add("gpsimd", lambda e: e.collective_compute(
        "AllGather", ALU.bypass, replica_groups=GROUPS, ins=[g.vag_in[:]], outs=[g.vag_out[:]]),
        waits=tk(t_vst), inc=p.csem, amt=1)

    q_toks = proj_fm("l1_wq", "bq", KV, waits_extra=(t_vst,))
    q_toks = _clip_heads(g, KV, q_toks, wk_out=None)
    QV = KV

    t_sld = p.dma("sync", g.MISC[:, 144:208], g.sag_out.rearrange("(p a) one -> p (a one)", p=128),
                  waits=tk(t_sag))
    SAL = g.MISC[:, 144:208]
    LO = g.MISC[:, 208:209]; HI = g.MISC[:, 209:210]; MID = g.MISC[:, 210:211]
    CB = g.MISC[:, 211:212]; CO = g.MISC[:, 212:213]; C64 = g.MISC[:, 213:214]
    TMP64 = g.SCf[:, 2048:2112]
    ROWT = g.SCf[0:1, 3104:3296]
    i1 = p.op("vector", lambda e: e.reduce_max(CB, SAL, axis=AX.X), waits=tk(t_sld))
    d1a = p.dma("gpsimd", g.bnc[:], CB, waits=tk(i1))
    d1 = p.dma("gpsimd", ROWT[0:1, 0:128], g.bnc.rearrange("p one -> one p"), waits=tk(d1a))
    i2 = p.op("vector", lambda e: e.reduce_max(ROWT[0:1, 128:129], ROWT[0:1, 0:128], axis=AX.X), waits=tk(d1))
    b1_ = g.PS[4]
    i3 = p.op("tensor", lambda e: e.matmul(b1_.t[:, 0:1], g.ONESR[:], ROWT[0:1, 128:129],
                                           start=True, stop=True), waits=tk(i2, b1_.rd))
    i4 = p.op("vector", lambda e: e.tensor_scalar_add(HI, b1_.t[:, 0:1], 1.0), waits=tk(i3))
    b1_.rd = i4
    i5 = p.op("vector", lambda e: e.tensor_reduce(CB, SAL, axis=AX.X, op=ALU.min), waits=tk(t_sld, i1))
    d2a = p.dma("gpsimd", g.bnc[:], CB, waits=tk(i5, i2))
    d2 = p.dma("gpsimd", ROWT[0:1, 0:128], g.bnc.rearrange("p one -> one p"), waits=tk(d2a))
    i6 = p.op("vector", lambda e: e.tensor_reduce(ROWT[0:1, 128:129], ROWT[0:1, 0:128], axis=AX.X, op=ALU.min), waits=tk(d2))
    i7 = p.op("tensor", lambda e: e.matmul(b1_.t[:, 0:1], g.ONESR[:], ROWT[0:1, 128:129],
                                           start=True, stop=True), waits=tk(i6, b1_.rd))
    i8 = p.op("vector", lambda e: e.tensor_scalar_add(LO, b1_.t[:, 0:1], -1.0), waits=tk(i7))
    b1_.rd = i8
    prev = i8
    for it in range(46):
        m1 = p.op("vector", lambda e: e.tensor_add(MID, LO, HI), waits=tk(prev))
        m2 = p.op("vector", lambda e: e.tensor_scalar_mul(MID, MID, 0.5), waits=tk(m1))
        c1 = p.op("vector", lambda e: e.tensor_scalar(TMP64, SAL, MID, None, ALU.is_ge), waits=tk(m2))
        c2 = p.op("vector", lambda e: e.reduce_sum(CB, TMP64, axis=AX.X), waits=tk(c1))
        c3 = p.op("tensor", lambda e: e.matmul(b1_.t[0:1, 0:1], g.ONESF[:], CB, start=True, stop=True),
                  waits=tk(c2, b1_.rd))
        c4 = p.op("vector", lambda e: e.tensor_copy(ROWT[0:1, 129:130], b1_.t[0:1, 0:1]), waits=tk(c3))
        c5 = p.op("tensor", lambda e: e.matmul(b1_.t[:, 0:1], g.ONESR[:], ROWT[0:1, 129:130],
                                               start=True, stop=True), waits=tk(c4))
        c6 = p.op("vector", lambda e: e.tensor_copy(C64, b1_.t[:, 0:1]), waits=tk(c5))
        b1_.rd = c6
        c7 = p.op("vector", lambda e: e.tensor_scalar(CO, C64, float(S // 2), None, ALU.is_ge), waits=tk(c6))
        TMA = g.MISC[:, 214:215]
        TMB = g.MISC[:, 215:216]
        s1 = p.op("vector", lambda e: e.tensor_sub(TMA, MID, LO), waits=tk(c7))
        s2 = p.op("vector", lambda e: e.tensor_sub(TMB, HI, MID), waits=tk(c7))
        s3 = p.op("vector", lambda e: e.tensor_mul(TMA, TMA, CO), waits=tk(s1))
        s4 = p.op("vector", lambda e: e.tensor_mul(TMB, TMB, CO), waits=tk(s2))
        s5 = p.op("vector", lambda e: e.tensor_add(LO, LO, TMA), waits=tk(s3))
        s6 = p.op("vector", lambda e: e.tensor_add(HI, MID, TMB), waits=tk(s4, s2))
        prev = s6
        prev2 = s5
    THR = g.MISC[0:1, 208:209]
    SGROW = g.SCf[0:1, 0:2048]
    sg1 = p.op("scalar", lambda e: e.activation(SGROW, srow, AF.Sigmoid), waits=tk(*sc_toks, prev))
    sg2 = p.op("vector", lambda e: e.tensor_single_scalar(srow, srow, THR, ALU.is_ge), waits=tk(sg1, prev, prev2))
    sg3 = p.op("vector", lambda e: e.tensor_mul(srow, srow, SGROW), waits=tk(sg2))

    HM_ = g.SBIG[:, 0:16384].rearrange("p (o t) -> p o t", o=32)
    mod_x_toks = []
    for t in range(4):
        ts = slice(t * 512, (t + 1) * 512)
        mb = g.PS[5]
        bcm = p.op("tensor", lambda e, t=t, mb=mb: e.matmul(
            mb.t[:, 0:512], g.ONESR[:], srow[0:1, t * 512:(t + 1) * 512], start=True, stop=True),
            waits=tk(sg3, mb.rd))
        mcp = p.op("vector", lambda e, t=t, mb=mb: e.tensor_copy(g.SCf[:, (t % 2) * 512:(t % 2) * 512 + 512], mb.t[:, 0:512]),
                   waits=tk(bcm, mod_x_toks[-1] if t >= 2 else None))
        mb.rd = mcp
        MBC = g.SCf[:, (t % 2) * 512:(t % 2) * 512 + 512]
        hlast = None
        for og in range(8):
            b01 = og % 2
            wdst = g.WB[:, b01, :].rearrange("p (k o) -> p k o", k=8)
            tw = p.dma("gpsimd", wdst, dr["l1_W1"][:, og], waits=tk(wb_rd[b01]))
            for oc in range(4):
                oabs = og * 4 + oc
                bk = g.PS[oabs % 4]
                mm = None
                for c in range(FC):
                    mm = p.add("tensor", lambda e, c=c, oc=oc, b01=b01, ts=ts, bk=bk: e.matmul(
                        bk.t[:, 0:512], g.WB[:, b01, c * 512 + oc * 128: c * 512 + oc * 128 + 128],
                        g.XN[:, c, ts], start=(c == 0), stop=(c == FC - 1)),
                        waits=tk(ln[c][t], *( (tw, bk.rd) if c == 0 else ())),
                        inc=p.esem["tensor"] if c == FC - 1 else None)
                wb_rd[b01] = mm
                gv = p.op("scalar", lambda e, oabs=oabs, bk=bk: e.activation(
                    HM_[:, oabs, 0:512], bk.t[:, 0:512], AF.Gelu, bias=MC("b1m", oabs)),
                    waits=tk(mm, mtok["b1m"], *( [q_toks[FC-1][3]] if (t == 0 and og == 0 and oc == 0) else [] )))
                bk.rd = gv
                hlast = gv
        for og in range(FC):
            b01 = og % 2
            wdst = g.WB[:, b01, :].rearrange("p (k o) -> p k o", k=32)
            tw = p.dma("gpsimd", wdst, dr["l1_W2"][:, og], waits=tk(wb_rd[b01]))
            bk = g.PS[og % 4]
            mm = None
            for k in range(32):
                mm = p.add("tensor", lambda e, k=k, b01=b01, bk=bk: e.matmul(
                    bk.t[:, 0:512], g.WB[:, b01, k * 128:(k + 1) * 128],
                    HM_[:, k, 0:512], start=(k == 0), stop=(k == 31)),
                    waits=tk(*( (tw, bk.rd, hlast) if k == 0 else ())),
                    inc=p.esem["tensor"] if k == 31 else None)
            wb_rd[b01] = mm
            x1_ = p.op("vector", lambda e, og=og, bk=bk, MBC=MBC: e.scalar_tensor_tensor(
                g.SCf[:, 1024 + (og % 2) * 512: 1024 + (og % 2) * 512 + 512],
                bk.t[:, 0:512], MC("b2m", og), MBC, ALU.add, ALU.mult),
                waits=tk(mm, mtok["b2m"], mcp, mod_x_toks[-2] if len(mod_x_toks) >= 2 else None))
            bk.rd = x1_
            x2_ = p.op("vector", lambda e, og=og, ts=ts: e.tensor_add(
                g.X[:, og, ts], g.X[:, og, ts],
                g.SCf[:, 1024 + (og % 2) * 512: 1024 + (og % 2) * 512 + 512]), waits=tk(x1_))
            x3_ = p.op("vector", lambda e, og=og, ts=ts: e.tensor_add(
                g.X[:, og, ts], g.X[:, og, ts], g.XN[:, og, ts]), waits=tk(x2_))
            mod_x_toks.append(x3_)

    g.l1_state = dict(QV=QV, q_toks=q_toks, t_kag=t_kag, t_vag=t_vag, t_nag=t_nag,
                      mod_x_toks=mod_x_toks, wb_rd=wb_rd, mtok=mtok, MC=MC, XL=XL)
    _emit_l1_flash(g, ln)


def _emit_l1_flash(g, ln):
    p = g.p
    dr = g.dram
    st = g.l1_state
    QV, q_toks = st["QV"], st["q_toks"]
    MC, mtok = st["MC"], st["mtok"]
    wb_rd = st["wb_rd"]
    ATT = g.SBIG[:, 0:16384].rearrange("p (c t) -> p c t", c=8)
    KCH = lambda s: g.SC[:, s * 1024: (s + 1) * 1024].rearrange("p (c t) -> p c t", c=8)
    VCH = lambda s: g.SC[:, 2048 + s * 1024: 2048 + (s + 1) * 1024]
    ECH = lambda s: g.SC[:, 4096 + s * 512: 4096 + (s + 1) * 512]
    NKC = lambda s: g.SCf[:, 2560 + s * 16: 2560 + (s + 1) * 16]
    RBC = g.SCf[:, 2592:3104]
    kch_rd = [st["mod_x_toks"][-1], st["mod_x_toks"][-1]]
    vch_rd = [st["mod_x_toks"][-1], st["mod_x_toks"][-1]]
    ech_rd = [None, None]
    nk_rd = [None, None]
    att_toks = []
    for qt in range(4):
        qs = slice(qt * 512, (qt + 1) * 512)
        for hg in range(4):
            nbank = [g.PS[i] for i in range(4)]
            sbank = [g.PS[4], g.PS[5]]
            den_end = [None] * 4
            for kc in range(64):
                s2 = kc % 2
                r = kc // 16
                tl = (kc % 16) * 128
                tkk = p.dma("sync", KCH(s2)[:],
                            g.kag_out[r * D:(r + 1) * D, tl:tl + 128].rearrange("(c p) t -> p c t", p=128),
                            waits=tk(st["t_kag"], kch_rd[s2]))
                tkv = p.dma("sync", VCH(s2), g.vag_out[kc * 128:(kc + 1) * 128, :],
                            waits=tk(st["t_vag"], vch_rd[s2]))
                tkn = p.dma("sync", NKC(s2), g.nag_out[kc * 128:(kc + 1) * 128, :],
                            waits=tk(st["t_nag"], nk_rd[s2]))
                for h4 in range(4):
                    h = hg * 4 + h4
                    hp = 64 * (h % 2)
                    hc = h // 2
                    sb_ = sbank[h4 % 2]
                    mm = p.op("tensor", lambda e, h=h, hp=hp, hc=hc, s2=s2, sb_=sb_, qs=qs: e.matmul(
                        sb_.t[:, 0:512], KCH(s2)[hp:hp + 64, hc, :], QV[hp:hp + 64, hc, qs],
                        start=True, stop=True),
                        waits=tk(tkk, q_toks[hc][qt] if kc == 0 else None, sb_.rd))
                    ex = p.op("scalar", lambda e, h=h, s2=s2, sb_=sb_: e.activation(
                        ECH(s2), sb_.t[:, 0:512], AF.Exp, bias=NKC(s2)[:, h:h + 1], scale=2.0 / TEMP),
                        waits=tk(mm, tkn, ech_rd[s2]))
                    sb_.rd = ex
                    nb = nbank[h4]
                    hm_ = h % 16
                    mmn = p.add("tensor", lambda e, hm_=hm_, s2=s2, nb=nb, kc=kc: e.matmul(
                        nb.t[0:64, 0:512], VCH(s2)[:, 64 * hm_:64 * hm_ + 64], ECH(s2),
                        start=(kc == 0), stop=(kc == 63)),
                        waits=tk(ex, tkv, nb.rd if kc == 0 else None), inc=None)
                    mmd = p.add("tensor", lambda e, s2=s2, nb=nb, kc=kc: e.matmul(
                        nb.t[64:65, 0:512], g.ONESB[:], ECH(s2),
                        start=(kc == 0), stop=(kc == 63), tile_position=(0, 64)),
                        waits=tk(), inc=p.esem["tensor"])
                    if kc == 63:
                        den_end[h4] = mmd
                    ech_rd[s2] = mmd
                    if h4 == 3:
                        kch_rd[s2] = mm
                        vch_rd[s2] = mmd
                        nk_rd[s2] = ex
            for h4 in range(4):
                h = hg * 4 + h4
                hp = 64 * (h % 2)
                hc = h // 2
                nb = nbank[h4]
                grp_end = den_end[h4]
                DEN = g.SCf[0:1, 3232:3744]
                r1 = p.op("vector", lambda e, nb=nb, DEN=DEN: e.reciprocal(DEN[0:1, 0:512], nb.t[64:65, 0:512]),
                          waits=tk(grp_end))
                bb = sbank[h4 % 2]
                r2 = p.op("tensor", lambda e, bb=bb, DEN=DEN: e.matmul(
                    bb.t[0:64, 0:512], g.ONESR[0:1, 0:64], DEN[0:1, 0:512],
                    start=True, stop=True), waits=tk(r1, bb.rd))
                r3 = p.op("vector", lambda e, bb=bb: e.tensor_copy(RBC[0:64, :], bb.t[0:64, 0:512]), waits=tk(r2))
                bb.rd = r3
                r4 = p.op("vector", lambda e, h=h, hp=hp, hc=hc, nb=nb, qs=qs: e.tensor_mul(
                    ATT[hp:hp + 64, hc, qs], nb.t[0:64, 0:512], RBC[0:64, :]), waits=tk(r3))
                nb.rd = r4
                att_toks.append(r4)

    U = g.SBIG[:, 16384:32768].rearrange("p (c t) -> p c t", c=8)
    u_toks = [[None] * 4 for _ in range(FC)]
    for og in range(2):
        b01 = og % 2
        wdst = g.WB[:, b01, :].rearrange("p (k o) -> p k o", k=8)
        tw = p.dma("gpsimd", wdst, dr["l1_wo"][:, og], waits=tk(wb_rd[b01]))
        for oc in range(4):
            c_out = og * 4 + oc
            for t in range(4):
                bk = g.PS[(oc * 4 + t) % 4]
                mm = None
                for c in range(FC):
                    mm = p.add("tensor", lambda e, c=c, oc=oc, t=t, b01=b01, bk=bk: e.matmul(
                        bk.t[:, 0:512], g.WB[:, b01, c * 512 + oc * 128: c * 512 + oc * 128 + 128],
                        ATT[:, c, t * 512:(t + 1) * 512], start=(c == 0), stop=(c == FC - 1)),
                        waits=tk(att_toks[-1] if (c == 0 and og == 0 and oc == 0 and t == 0) else None,
                                 *( (tw, bk.rd) if c == 0 else ())),
                        inc=p.esem["tensor"] if c == FC - 1 else None)
                ev = p.op("vector", lambda e, c_out=c_out, t=t, bk=bk: e.scalar_tensor_tensor(
                    U[:, c_out, t * 512:(t + 1) * 512], bk.t[:, 0:512], MC("bo", c_out),
                    g.XN[:, c_out, t * 512:(t + 1) * 512], ALU.add, ALU.add),
                    waits=tk(mm, mtok["bo"]))
                bk.rd = ev
                u_toks[c_out][t] = ev
        wb_rd[b01] = u_toks[og * 4 + 3][3]

    u_toks = _proj_fullvec(g, U, u_toks, "l1u")

    XN2 = ATT
    ln2 = emit_ln(g, src=U, dst=XN2,
                  src_toks=[u_toks[c][t] for c in range(FC) for t in range(4)],
                  gb=(g.MISC[:, 40:48], g.MISC[:, 48:56]),
                  sqreg=g.SC, tmpreg=g.SCf[:, 2048:3072])

    u2_toks = [[None] * 4 for _ in range(FC)]
    HH = g.SC.rearrange("p (o t) -> p o t", o=32)
    hh_rd = None
    for tt in range(8):
        ts = slice(tt * 256, (tt + 1) * 256)
        t4 = tt // 2
        hlast = None
        for og in range(8):
            b01 = og % 2
            wdst = g.WB[:, b01, :].rearrange("p (k o) -> p k o", k=8)
            tw = p.dma("gpsimd", wdst, dr["l1_w1"][:, og], waits=tk(wb_rd[b01]))
            for oc in range(4):
                oabs = og * 4 + oc
                bk = g.PS[oabs % 4]
                mm = None
                for c in range(FC):
                    mm = p.add("tensor", lambda e, c=c, oc=oc, b01=b01, ts=ts, bk=bk: e.matmul(
                        bk.t[:, 0:256], g.WB[:, b01, c * 512 + oc * 128: c * 512 + oc * 128 + 128],
                        XN2[:, c, ts], start=(c == 0), stop=(c == FC - 1)),
                        waits=tk(ln2[c][t4], *( (tw, bk.rd, hh_rd) if c == 0 else ())),
                        inc=p.esem["tensor"] if c == FC - 1 else None)
                wb_rd[b01] = mm
                gv = p.op("scalar", lambda e, oabs=oabs, bk=bk: e.activation(
                    HH[:, oabs, 0:256], bk.t[:, 0:256], AF.Gelu, bias=MC("bf1", oabs)),
                    waits=tk(mm, mtok["bf1"]))
                bk.rd = gv
                hlast = gv
        for og in range(FC):
            b01 = og % 2
            wdst = g.WB[:, b01, :].rearrange("p (k o) -> p k o", k=32)
            tw = p.dma("gpsimd", wdst, dr["l1_w2"][:, og], waits=tk(wb_rd[b01]))
            bk = g.PS[og % 4]
            mm = None
            for k in range(32):
                mm = p.add("tensor", lambda e, k=k, b01=b01, bk=bk: e.matmul(
                    bk.t[:, 0:256], g.WB[:, b01, k * 128:(k + 1) * 128],
                    HH[:, k, 0:256], start=(k == 0), stop=(k == 31)),
                    waits=tk(*( (tw, bk.rd, hlast) if k == 0 else ())),
                    inc=p.esem["tensor"] if k == 31 else None)
            wb_rd[b01] = mm
            hh_rd = mm
            ev = p.op("vector", lambda e, og=og, ts=ts, bk=bk: e.scalar_tensor_tensor(
                U[:, og, ts], bk.t[:, 0:256], MC("bf2", og), U[:, og, ts], ALU.add, ALU.add),
                waits=tk(mm, mtok["bf2"]))
            bk.rd = ev
            u2_toks[og][t4] = ev

    u2_toks = _proj_fullvec(g, U, u2_toks, "l1u2")
    x_toks = list(st["mod_x_toks"])
    for t in range(4):
        ts = slice(t * 512, (t + 1) * 512)
        for c in range(FC):
            xa = p.op("vector", lambda e, c=c, ts=ts: e.tensor_add(
                g.X[:, c, ts], g.X[:, c, ts], U[:, c, ts]), waits=tk(u2_toks[c][t]))
            x_toks.append(xa)
    g.x_toks = x_toks


def _proj_generic(g, wname, src3, src_toks, out3, wb_rd, token_major=False,
                  extra_first=()):
    p = g.p
    dr = g.dram
    toks = [[None] * 4 for _ in range(FC)]
    last = None
    for og in range(2):
        b01 = og % 2
        wdst = g.WB[:, b01, :].rearrange("p (k o) -> p k o", k=8)
        tw = p.dma("gpsimd", wdst, dr[wname][:, og], waits=tk(wb_rd[b01]))
        if token_major:
            for ti in range(16):
                bk = g.PS[ti % 4]
                mm = None
                for c in range(FC):
                    mm = p.add("tensor", lambda e, c=c, ti=ti, b01=b01, bk=bk: e.matmul(
                        bk.t[:, 0:512], src3[:, c, ti * 128:(ti + 1) * 128],
                        g.WB[:, b01, c * 512: c * 512 + 512], start=(c == 0), stop=(c == FC - 1)),
                        waits=tk(src_toks[c][ti // 4], *( (tw, bk.rd) + tuple(extra_first) if c == 0 else ())),
                        inc=p.esem["tensor"] if c == FC - 1 else None)
                ev = p.op("vector", lambda e, ti=ti, og=og, bk=bk: e.tensor_copy(
                    out3[:, ti, og * 512:(og + 1) * 512], bk.t[:, 0:512]), waits=tk(mm))
                bk.rd = ev
                last = ev
            wb_rd[b01] = last
        else:
            for oc in range(4):
                c_out = og * 4 + oc
                for t in range(4):
                    bk = g.PS[(oc * 4 + t) % 4]
                    mm = None
                    for c in range(FC):
                        mm = p.add("tensor", lambda e, c=c, oc=oc, t=t, b01=b01, bk=bk: e.matmul(
                            bk.t[:, 0:512], g.WB[:, b01, c * 512 + oc * 128: c * 512 + oc * 128 + 128],
                            src3[:, c, t * 512:(t + 1) * 512], start=(c == 0), stop=(c == FC - 1)),
                            waits=tk(src_toks[c][t], *( (tw, bk.rd) + tuple(extra_first) if c == 0 else ())),
                            inc=p.esem["tensor"] if c == FC - 1 else None)
                    ev = p.op("vector", lambda e, c_out=c_out, t=t, bk=bk: e.tensor_copy(
                        out3[:, c_out, t * 512:(t + 1) * 512], bk.t[:, 0:512]), waits=tk(mm))
                    bk.rd = ev
                    toks[c_out][t] = ev
                    last = ev
            wb_rd[b01] = last
    return toks, last


def emit_l2(g, ln):
    p = g.p
    dr = g.dram
    SCL = 0.125
    QV = g.SBIG[:, 0:16384].rearrange("p (c t) -> p c t", c=8)
    KV = g.SBIG[:, 16384:32768].rearrange("p (c t) -> p c t", c=8)
    VT = g.SBIG[:, 16384:32768].rearrange("p (tc d) -> p tc d", tc=16)
    wb_rd = [None, None]

    t_bck = p.dma("sync", g.MISC[0:64, 0:1], dr["l2_bck"][:])
    t_bcv = p.dma("sync", g.MISC[0:64, 1:2], dr["l2_bcv"][:])
    t_b1 = p.dma("sync", g.MISC[:, 10:266], dr["l2_b1"][:])
    t_b2 = p.dma("sync", g.MISC[:, 266:330], dr["l2_b2"][:])

    q_toks, _ = _proj_generic(g, "l2_wq", g.XN, ln, QV, wb_rd)
    k_toks, klast = _proj_generic(g, "l2_wk", g.XN, ln, KV, wb_rd)
    t_kst = p.dma("sync", g.kag_in.rearrange("(c p) t -> p c t", p=128), KV[:],
                  waits=tk(*[k_toks[c][t] for c in range(FC) for t in range(4)]))
    t_kag = p.add("gpsimd", lambda e: e.collective_compute(
        "AllGather", ALU.bypass, replica_groups=GROUPS, ins=[g.kag_in[:]], outs=[g.kag_out[:]]),
        waits=tk(t_kst), inc=p.csem, amt=1)
    _, ktlast = _proj_generic(g, "l2_wk", g.XN, ln, VT, wb_rd, token_major=True,
                              extra_first=(t_kst,))
    t_ktst = p.dma("sync", g.ktm_in.rearrange("(tc p) d -> p tc d", p=128), VT[:], waits=tk(ktlast))
    t_ktag = p.add("gpsimd", lambda e: e.collective_compute(
        "AllGather", ALU.bypass, replica_groups=GROUPS, ins=[g.ktm_in[:]], outs=[g.ktm_out[:]]),
        waits=tk(t_ktst), inc=p.csem, amt=1)
    _, vlast = _proj_generic(g, "l2_wv", g.XN, ln, VT, wb_rd, token_major=True,
                             extra_first=(t_ktst,))
    t_vst = p.dma("sync", g.vag_in.rearrange("(tc p) d -> p tc d", p=128), VT[:], waits=tk(vlast))
    t_vag = p.add("gpsimd", lambda e: e.collective_compute(
        "AllGather", ALU.bypass, replica_groups=GROUPS, ins=[g.vag_in[:]], outs=[g.vag_out[:]]),
        waits=tk(t_vst), inc=p.csem, amt=1)
    vf_toks, vflast = _proj_generic(g, "l2_wv", g.XN, ln, KV, wb_rd, extra_first=(t_vst,))
    t_vfst = p.dma("sync", g.vfag_in.rearrange("(c p) t -> p c t", p=128), KV[:], waits=tk(vflast))
    t_vfag = p.add("gpsimd", lambda e: e.collective_compute(
        "AllGather", ALU.bypass, replica_groups=GROUPS, ins=[g.vfag_in[:]], outs=[g.vfag_out[:]]),
        waits=tk(t_vfst), inc=p.csem, amt=1)

    qs_toks = []
    for c in range(FC):
        t1 = p.op("vector", lambda e, c=c: e.reduce_sum(
            g.MISC[:, 2 + c:3 + c], QV[:, c, :], axis=AX.X),
            waits=tk(*[q_toks[c][t] for t in range(4)]))
        qs_toks.append(t1)
    t_qst = p.dma("sync", g.qs_in[:], g.MISC[:, 2:10], waits=tk(*qs_toks))
    t_qar = p.add("gpsimd", lambda e: e.collective_compute(
        "AllReduce", ALU.add, replica_groups=GROUPS, ins=[g.qs_in[:]], outs=[g.qs_out[:]]),
        waits=tk(t_qst), inc=p.csem, amt=1)

    CK = g.SBIG[:, 16384:20480].rearrange("p (hp n) -> p hp n", hp=8)
    CVF = g.SBIG[:, 20480:24576].rearrange("p (hp n) -> p hp n", hp=8)
    CVT = g.SBIG[:, 24576:28672].rearrange("p (nc d) -> p nc d", nc=4)
    WCK = g.SBIG[:, 28672:30720].rearrange("p (j o) -> p j o", j=32)
    WCV = g.SBIG[:, 30720:32768].rearrange("p (j o) -> p j o", j=32)
    t_wck = p.dma("gpsimd", WCK[:], dr["l2_wck"][:], waits=tk(t_vfst))
    t_wcv = p.dma("gpsimd", WCV[:], dr["l2_wcv"][:], waits=tk(t_vfst))
    KWS = lambda s: g.SC[:, s * 2064:(s + 1) * 2064]
    kw_rd = [None, None]
    ck_toks = []
    for src_ag, src_tok, WCB, OUT, bcol, wtok in (
            (g.kag_out, t_kag, WCK, CK, g.MISC[0:64, 0:1], t_wck),
            (g.vfag_out, t_vfag, WCV, CVF, g.MISC[0:64, 1:2], t_wcv)):
        for w in range(4):
            nblk = 128 if w < 3 else 127
            for hp in range(8):
                s = (w * 8 + hp) % 2
                tkw = p.dma("sync", KWS(s)[:, 0:T], src_ag[w * D + hp * 128: w * D + (hp + 1) * 128, :],
                            waits=tk(kw_rd[s], src_tok))
                if w < 3:
                    tkw = p.dma("sync", KWS(s)[:, T:2064],
                                src_ag[(w + 1) * D + hp * 128: (w + 1) * D + (hp + 1) * 128, 0:16],
                                waits=tk(tkw))
                mm2 = None
                for h2 in range(2):
                    hb = 64 * h2
                    bk = g.PS[(hp * 2 + h2) % 4]
                    mm = None
                    for j in range(32):
                        mm = p.add("tensor", lambda e, j=j, h2=h2, hb=hb, s=s, nblk=nblk, bk=bk, WCB=WCB: e.matmul(
                            bk.t[0:64, 0:nblk], WCB[hb:hb + 64, j, :], KWS(s)[hb:hb + 64, j: j + 16 * (nblk - 1) + 1: 16],
                            start=(j == 0), stop=(j == 31)),
                            waits=tk(tkw, *( (wtok, bk.rd) if j == 0 else ())),
                            inc=p.esem["tensor"] if j == 31 else None)
                    ev = p.op("vector", lambda e, w=w, hp=hp, h2=h2, hb=hb, nblk=nblk, bk=bk, OUT=OUT, bcol=bcol: e.tensor_scalar_add(
                        OUT[hb:hb + 64, hp, w * 128: w * 128 + nblk], bk.t[0:64, 0:nblk], bcol),
                        waits=tk(mm, t_bck, t_bcv))
                    bk.rd = ev
                    mm2 = mm
                    ck_toks.append(ev)
                kw_rd[s] = mm2
    cvt_toks = []
    for h in range(16):
        hb = 64 * (h % 2)
        hp = h // 2
        for nc4 in range(4):
            nblk = 128 if nc4 < 3 else 127
            pb = g.PT[(h * 4 + nc4) % 2]
            tt = p.op("tensor", lambda e, h=h, hb=hb, hp=hp, nc4=nc4, nblk=nblk, pb=pb: e.transpose(
                pb.t[0:nblk, 0:64], CVF[hb:hb + 64, hp, nc4 * 128: nc4 * 128 + nblk], g.IDN[hb:hb + 64, hb:hb + 64]),
                waits=tk(ck_toks[-1] if (h == 0 and nc4 == 0) else None, pb.rd, g.tok_idn))
            cp = p.op("vector", lambda e, h=h, nc4=nc4, nblk=nblk, pb=pb: e.tensor_copy(
                CVT[0:nblk, nc4, h * 64:(h + 1) * 64], pb.t[0:nblk, 0:64]), waits=tk(tt))
            pb.rd = cp
            cvt_toks.append(cp)

    BIB = g.PS[5]
    EW = g.SC[:, 2568:3592]
    bi_last = None
    first = True
    for qt in range(16):
        for h in range(16):
            hb = 64 * (h % 2)
            hp = h // 2
            sb_ = g.PS[(qt + h) % 4]
            mm = p.op("tensor", lambda e, qt=qt, h=h, hb=hb, hp=hp, sb_=sb_: e.matmul(
                sb_.t[:, 0:511], QV[hb:hb + 64, hp, qt * 128:(qt + 1) * 128], CK[hb:hb + 64, hp, 0:511],
                start=True, stop=True),
                waits=tk(ck_toks[-1] if first else None, q_toks[hp][qt // 4] if first else None, sb_.rd))
            ex = p.op("scalar", lambda e, sb_=sb_: e.activation(
                EW[:, 0:511], sb_.t[:, 0:511], AF.Exp, scale=SCL), waits=tk(mm, bi_last))
            rd_ = p.op("vector", lambda e, sb_=sb_: e.reduce_sum(
                g.MISC[:, 378:379], EW[:, 0:511], axis=AX.X), waits=tk(ex))
            rc = p.op("vector", lambda e: e.reciprocal(g.MISC[:, 379:380], g.MISC[:, 378:379]), waits=tk(rd_))
            ts_ = p.op("vector", lambda e: e.tensor_scalar_mul(EW[:, 0:511], EW[:, 0:511], g.MISC[:, 379:380]),
                       waits=tk(rc))
            sb_.rd = ts_
            bi_last = p.add("tensor", lambda e, qt=qt, h=h: e.matmul(
                BIB.t[0:1, 0:511], g.ONESB[:], EW[:, 0:511],
                start=(qt == 0 and h == 0), stop=(qt == 15 and h == 15)),
                waits=tk(ts_, BIB.rd if first else None),
                inc=p.esem["tensor"])
            first = False
    bi_ev = p.op("vector", lambda e: e.tensor_copy(g.STAT[0:1, 0:511], BIB.t[0:1, 0:511]), waits=tk(bi_last))
    BIB.rd = bi_ev
    t_bst = p.dma("sync", g.bi_in[:], g.STAT[0:1, 0:511], waits=tk(bi_ev))
    t_bar = p.add("gpsimd", lambda e: e.collective_compute(
        "AllReduce", ALU.add, replica_groups=GROUPS, ins=[g.bi_in[:]], outs=[g.bi_out[:]]),
        waits=tk(t_bst), inc=p.csem, amt=1)
    t_bld = p.dma("sync", g.STAT[0:1, 0:511], g.bi_out[:], waits=tk(t_bar))

    TOPV = g.STAT[0:1, 512:528]
    STATU = g.STAT.bitcast(mybir.dt.uint32)
    IDXU = STATU[0:1, 528:544]
    BI2 = g.STAT[0:1, 544:1055]
    t_dbi = None
    if "d_bi" in g.dbg:
        t_dbi = p.dma("sync", g.dbg["d_bi"][:], g.STAT[0:1, 0:511], waits=tk(t_bld))
    tt1 = p.op("vector", lambda e: e.max(TOPV[0:1, 0:8], g.STAT[0:1, 0:511]), waits=tk(t_bld))
    tt2 = p.op("vector", lambda e: e.match_replace(BI2[0:1, 0:511], TOPV[0:1, 0:8], g.STAT[0:1, 0:511], -1.0),
               waits=tk(tt1))
    tt3 = p.op("vector", lambda e: e.max(TOPV[0:1, 8:16], BI2[0:1, 0:511]), waits=tk(tt2))
    it1 = p.op("vector", lambda e: e.max_index(IDXU[0:1, 0:8], TOPV[0:1, 0:8], g.STAT[0:1, 0:511]), waits=tk(tt3))
    cv2 = p.op("vector", lambda e: e.max_index(IDXU[0:1, 8:16], TOPV[0:1, 8:16], BI2[0:1, 0:511]), waits=tk(it1))
    IDXF2 = g.STAT[0:1, 1056:1072]
    cf1 = p.op("vector", lambda e: e.tensor_copy(IDXF2, IDXU), waits=tk(cv2))
    if "d_idx" in g.dbg:
        p.dma("sync", g.dbg["d_idx"][:], IDXF2, waits=tk(cf1))
    bbx = g.PS[4]
    bx1 = p.op("tensor", lambda e: e.matmul(bbx.t[:, 0:16], g.ONESR[:], IDXF2, start=True, stop=True),
               waits=tk(cf1, bbx.rd))
    BLKC = g.MISC[:, 380:396]
    bx2 = p.op("vector", lambda e: e.tensor_copy(BLKC, bbx.t[:, 0:16]), waits=tk(bx1))
    bbx.rd = bx2
    t_io = p.dma("sync", g.MISC[:, 396:397], dr["iota64"][:])
    SELF_ = g.MISC[:, 397:405]
    st_toks = []
    for ci in range(8):
        for half in (0, 64):
            bidx = 2 * ci + (half // 64)
            s1_ = p.op("vector", lambda e, ci=ci, half=half, bidx=bidx: e.scalar_tensor_tensor(
                SELF_[half:half + 64, ci:ci + 1], BLKC[half:half + 64, bidx:bidx + 1], 16.0,
                g.MISC[half:half + 64, 396:397], ALU.mult, ALU.add), waits=tk(bx2, t_io))
            s2_ = p.op("vector", lambda e, ci=ci, half=half: e.tensor_scalar_min(
                SELF_[half:half + 64, ci:ci + 1], SELF_[half:half + 64, ci:ci + 1], float(S - 1)),
                waits=tk(s1_))
            st_toks.append(s2_)
    SELTI = g.MISCI[:, 405:413]
    cvt_ = p.op("vector", lambda e: e.tensor_copy(SELTI, SELF_), waits=tk(*st_toks))
    from concourse.bass_core_rust import IndirectOffsetOnAxis as _IOA
    gather_toks = []
    KTCH = g.SC[:, 0:1024]
    KSF = g.SC[:, 1024:2048].rearrange("p (c t) -> p c t", c=8)
    ktch_rd = None
    for ci in range(8):
        gk = p.add("gpsimd", lambda e, ci=ci: e.indirect_dma_start(
            KTCH, None, g.ktm_out[0:128, :], _IOA(SELTI[:, ci:ci + 1], 0)),
            waits=tk(cvt_, t_ktag, ktch_rd, kw_rd[0], kw_rd[1]), inc=p.qsem["gpsimd"], amt=16)
        trs = []
        for fc in range(8):
            pb = g.PT[fc % 2]
            tt = p.op("tensor", lambda e, fc=fc, pb=pb: e.transpose(
                pb.t[:, 0:128], KTCH.rearrange("p (c t) -> p c t", c=8)[:, fc, :], g.IDN[:]),
                waits=tk(gk, pb.rd, g.tok_idn))
            cp = p.op("vector", lambda e, fc=fc, pb=pb: e.tensor_copy(KSF[:, fc, :], pb.t[:, 0:128]),
                      waits=tk(tt))
            pb.rd = cp
            trs.append(cp)
        ktch_rd = trs[-1]
        gks = p.dma("sync", g.ksel[:, ci * 128:(ci + 1) * 128].rearrange("(c p) t -> p c t", p=128),
                    KSF[:], waits=tk(*trs))
        VTCH = g.SC[:, 2048:3072]
        gv1 = p.add("gpsimd", lambda e, ci=ci, VTCH=VTCH: e.indirect_dma_start(
            VTCH, None, g.vag_out[0:128, :], _IOA(SELTI[:, ci:ci + 1], 0)),
            waits=tk(cvt_, t_vag, gather_toks[-1] if gather_toks else None), inc=p.qsem["gpsimd"], amt=16)
        gv = p.dma("sync", g.vsel[ci * 128:(ci + 1) * 128, :], VTCH, waits=tk(gv1))
        gather_toks += [gks, gv]

    t_qld = p.dma("sync", g.MISC[:, 2:10], g.qs_out[:], waits=tk(t_qar))
    t_wgb = p.dma("gpsimd", g.SC[:, 3592:3595], dr["l2_wgb"][:])
    t_bgr = p.dma("sync", g.STAT[0:1, 1100:1148], dr["l2_bgr"][:])
    QSB = g.SC[:, 3596:3604]
    cq = p.op("vector", lambda e: e.tensor_copy(QSB, g.MISC[:, 2:10]), waits=tk(t_qld))
    GROW = g.STAT[0:1, 1148:1196]
    g_toks = []
    for h in range(16):
        hb = 64 * (h % 2)
        bb = g.PS[4 + (h % 2)]
        m1 = p.op("tensor", lambda e, h=h, hb=hb, bb=bb: e.matmul(
            bb.t[0:1, 0:3], QSB[hb:hb + 64, h // 2: h // 2 + 1], g.SC[hb:hb + 64, 3592:3595],
            start=True, stop=True), waits=tk(cq, t_wgb, bb.rd))
        c1 = p.op("vector", lambda e, h=h, bb=bb: e.tensor_copy(GROW[0:1, 3 * h:3 * h + 3], bb.t[0:1, 0:3]),
                  waits=tk(m1))
        bb.rd = c1
        g_toks.append(c1)
    gs1 = p.op("vector", lambda e: e.tensor_scalar_mul(GROW, GROW, 1.0 / S), waits=tk(*g_toks))
    gs2 = p.op("vector", lambda e: e.tensor_add(GROW, GROW, g.STAT[0:1, 1100:1148]), waits=tk(gs1, t_bgr))
    gs3 = p.op("scalar", lambda e: e.activation(GROW, GROW, AF.Sigmoid), waits=tk(gs2))
    GC = g.MISC[0:64, 412:460]
    gc_toks = []
    for j in range(48):
        bb = g.PS[4 + (j % 2)]
        m1 = p.op("tensor", lambda e, j=j, bb=bb: e.matmul(
            bb.t[0:64, 0:1], g.ONESR[0:1, 0:64], GROW[0:1, j:j + 1], start=True, stop=True),
            waits=tk(gs3, bb.rd))
        c1 = p.op("vector", lambda e, j=j, bb=bb: e.tensor_copy(GC[:, j:j + 1], bb.t[0:64, 0:1]), waits=tk(m1))
        bb.rd = c1
        gc_toks.append(c1)
    gate_done = gc_toks[-1]

    ATT_A = g.SBIG[:, 20480:24576].rearrange("p (c t) -> p c t", c=4)
    ATT_B = g.SBIG[:, 28672:32768].rearrange("p (c t) -> p c t", c=4)
    def att_sl(c, qs_):
        return ATT_A[:, c, qs_] if c < 4 else ATT_B[:, c - 4, qs_]
    KCH = lambda s: g.SC[:, s * 1024: (s + 1) * 1024].rearrange("p (c t) -> p c t", c=8)
    VCH = lambda s: g.SC[:, 2048 + s * 1024: 2048 + (s + 1) * 1024]
    ECH = lambda s: g.SC[:, 4096 + s * 512: 4096 + (s + 1) * 512]
    RBC = g.SCf[:, 2592:3104]
    DENR = g.STAT[0:1, 0:512]
    TS_ = g.SCf[0:64, 3104:3616]
    kch_rd = [cvt_toks[-1], cvt_toks[-1]]
    vch_rd = [None, None]
    ech_rd = [None, None]
    x_toks = []
    for qh in range(2):
        att_toks = []
        for qt2 in range(2):
            q0 = qh * 1024 + qt2 * 512
            for hg in range(4):
                nbank = [g.PS[i] for i in range(4)]
                sbank = [g.PS[4], g.PS[5]]
                for src_i, nch in ((0, 4), (1, 8), (2, 4)):
                    den_end = [None] * 4
                    for kc in range(nch):
                        s2 = kc % 2
                        nblk = 128
                        if src_i == 0:
                            tkk = tkv = None
                            if kc == 3:
                                nblk = 127
                        elif src_i == 1:
                            tkk = p.dma("sync", KCH(s2)[:], g.ksel[:, kc * 128:(kc + 1) * 128].rearrange(
                                "(c p) t -> p c t", p=128), waits=tk(*gather_toks, kch_rd[s2]))
                            tkv = p.dma("sync", VCH(s2), g.vsel[kc * 128:(kc + 1) * 128, :],
                                        waits=tk(gather_toks[-1], vch_rd[s2]))
                        else:
                            w0 = T - 512 + kc * 128
                            tkk = p.dma("sync", KCH(s2)[:], g.kag_out[3 * D:4 * D, w0:w0 + 128].rearrange(
                                "(c p) t -> p c t", p=128), waits=tk(t_kag, kch_rd[s2]))
                            tkv = p.dma("sync", VCH(s2), g.vag_out[S - 512 + kc * 128: S - 512 + (kc + 1) * 128, :],
                                        waits=tk(t_vag, vch_rd[s2]))
                        for h4 in range(4):
                            h = hg * 4 + h4
                            hb = 64 * (h % 2)
                            hp = h // 2
                            sb_ = sbank[h4 % 2]
                            if src_i == 0:
                                mm = p.op("tensor", lambda e, hb=hb, hp=hp, kc=kc, nblk=nblk, sb_=sb_, q0=q0: e.matmul(
                                    sb_.t[0:nblk, 0:512], CK[hb:hb + 64, hp, kc * 128: kc * 128 + nblk],
                                    QV[hb:hb + 64, hp, q0:q0 + 512], start=True, stop=True),
                                    waits=tk(sb_.rd, *( (gate_done, t_dbi) if (qh == 0 and qt2 == 0 and hg == 0 and kc == 0 and h4 == 0) else ())))
                            else:
                                mm = p.op("tensor", lambda e, hb=hb, hp=hp, s2=s2, sb_=sb_, q0=q0: e.matmul(
                                    sb_.t[:, 0:512], KCH(s2)[hb:hb + 64, hp, :],
                                    QV[hb:hb + 64, hp, q0:q0 + 512], start=True, stop=True),
                                    waits=tk(tkk, sb_.rd))
                            ex = p.op("scalar", lambda e, s2=s2, sb_=sb_, nblk=nblk: e.activation(
                                ECH(s2)[0:nblk, :], sb_.t[0:nblk, 0:512], AF.Exp, scale=SCL),
                                waits=tk(mm, ech_rd[s2]))
                            sb_.rd = ex
                            nb = nbank[h4]
                            if src_i == 0:
                                lhsV = (lambda e, h=h, kc=kc, nblk=nblk: CVT[0:nblk, kc, 64 * h:64 * h + 64])
                            else:
                                lhsV = (lambda e, h=h, s2=s2: VCH(s2)[:, 64 * h:64 * h + 64])
                            mmn = p.add("tensor", lambda e, lhsV=lhsV, s2=s2, nb=nb, kc=kc, nblk=nblk, nch=nch: e.matmul(
                                nb.t[0:64, 0:512], lhsV(e), ECH(s2)[0:nblk, :],
                                start=(kc == 0), stop=(kc == nch - 1)),
                                waits=tk(ex, tkv, nb.rd if kc == 0 else None), inc=None)
                            mmd = p.add("tensor", lambda e, s2=s2, nb=nb, kc=kc, nblk=nblk, nch=nch: e.matmul(
                                nb.t[64:65, 0:512], g.ONESB[0:nblk, :], ECH(s2)[0:nblk, :],
                                start=(kc == 0), stop=(kc == nch - 1), tile_position=(0, 64)),
                                waits=tk(), inc=p.esem["tensor"])
                            if kc == nch - 1:
                                den_end[h4] = mmd
                            ech_rd[s2] = mmd
                            if h4 == 3:
                                kch_rd[s2] = mm
                                vch_rd[s2] = mmd
                    for h4 in range(4):
                        h = hg * 4 + h4
                        hb = 64 * (h % 2)
                        hc = h // 2
                        nb = nbank[h4]
                        r1 = p.op("vector", lambda e, nb=nb: e.reciprocal(DENR[0:1, 0:512], nb.t[64:65, 0:512]),
                                  waits=tk(den_end[h4]))
                        bb = sbank[h4 % 2]
                        r2 = p.op("tensor", lambda e, bb=bb: e.matmul(
                            bb.t[0:64, 0:512], g.ONESR[0:1, 0:64], DENR[0:1, 0:512],
                            start=True, stop=True), waits=tk(r1, bb.rd))
                        r3 = p.op("vector", lambda e, bb=bb: e.tensor_copy(RBC[0:64, :], bb.t[0:64, 0:512]), waits=tk(r2))
                        bb.rd = r3
                        qsl = slice(qt2 * 512, qt2 * 512 + 512)
                        TSH = g.SCf[hb:hb + 64, 3104:3616]
                        r4 = p.op("vector", lambda e, h=h, src_i=src_i, nb=nb, TSH=TSH: e.scalar_tensor_tensor(
                            TSH, nb.t[0:64, 0:512], GC[0:64, 3 * h + src_i: 3 * h + src_i + 1], RBC[0:64, :],
                            ALU.mult, ALU.mult), waits=tk(r3))
                        nb.rd = r4
                        if src_i == 0:
                            r5 = p.op("vector", lambda e, hb=hb, hc=hc, qsl=qsl, TSH=TSH: e.tensor_copy(
                                att_sl(hc, qsl)[hb:hb + 64, :], TSH), waits=tk(r4))
                        else:
                            r5 = p.op("vector", lambda e, hb=hb, hc=hc, qsl=qsl, TSH=TSH: e.tensor_add(
                                att_sl(hc, qsl)[hb:hb + 64, :], att_sl(hc, qsl)[hb:hb + 64, :], TSH), waits=tk(r4))
                        att_toks.append(r5)
        for og in range(2):
            b01 = og % 2
            wdst = g.WB[:, b01, :].rearrange("p (k o) -> p k o", k=8)
            tw = p.dma("gpsimd", wdst, dr["l2_wo"][:, og], waits=tk(wb_rd[b01]))
            for oc in range(4):
                c_out = og * 4 + oc
                for t2 in range(2):
                    qsl = slice(t2 * 512, t2 * 512 + 512)
                    bk = g.PS[(oc * 2 + t2) % 4]
                    mm = None
                    for c in range(FC):
                        mm = p.add("tensor", lambda e, c=c, oc=oc, b01=b01, qsl=qsl, bk=bk: e.matmul(
                            bk.t[:, 0:512], g.WB[:, b01, c * 512 + oc * 128: c * 512 + oc * 128 + 128],
                            att_sl(c, qsl), start=(c == 0), stop=(c == FC - 1)),
                            waits=tk(att_toks[-1] if (c == 0 and og == 0 and oc == 0 and t2 == 0) else None,
                                     *( (tw, bk.rd) if c == 0 else ())),
                            inc=p.esem["tensor"] if c == FC - 1 else None)
                    xa = p.op("vector", lambda e, c_out=c_out, qh=qh, qsl=qsl, bk=bk: e.tensor_add(
                        g.X[:, c_out, qh * 1024 + qsl.start: qh * 1024 + qsl.stop],
                        g.X[:, c_out, qh * 1024 + qsl.start: qh * 1024 + qsl.stop], bk.t[:, 0:512]),
                        waits=tk(mm))
                    bk.rd = xa
                    x_toks.append(xa)
            wb_rd[b01] = x_toks[-1]

    WRB2 = g.SC[:, 0:64].rearrange("p (k o) -> p k o", k=8)
    t_wr2 = p.dma("gpsimd", WRB2[:], dr["l2_wr"][:, 0], waits=tk(x_toks[-1] if x_toks else None, kch_rd[0], kch_rd[1]))
    t_br2 = p.dma("gpsimd", g.BROW[0:1, 0:8], dr["l2_br"][:], waits=tk())
    LG = g.SCf[:, 512:640].rearrange("p (tc e) -> p tc e", tc=16)
    lg_toks = []
    for ti in range(16):
        bk = g.PS[ti % 4]
        mm = None
        for c in range(FC):
            mm = p.add("tensor", lambda e, c=c, ti=ti, bk=bk: e.matmul(
                bk.t[0:128, 0:8], g.XN[:, c, ti * 128:(ti + 1) * 128], WRB2[:, c, :],
                start=(c == 0), stop=False),
                waits=tk(t_wr2 if c == 0 else None, bk.rd if c == 0 else None), inc=None)
        mm = p.add("tensor", lambda e, bk=bk: e.matmul(
            bk.t[0:128, 0:8], g.ONESRB[:], g.BROW[0:1, 0:8], start=False, stop=True),
            waits=tk(t_br2), inc=p.esem["tensor"])
        ev = p.op("vector", lambda e, ti=ti, bk=bk: e.tensor_copy(LG[:, ti, :], bk.t[0:128, 0:8]), waits=tk(mm))
        bk.rd = ev
        lg_toks.append(ev)
    WEX = g.SC[:, 1024:1152].rearrange("p (tc e) -> p tc e", tc=16)
    TC1 = g.MISC[:, 378:379]; TC2 = g.MISC[:, 379:380]
    EQ1 = g.SCf[:, 640:648]; EQ2 = g.SCf[:, 648:656]
    EXR = g.MISC[:, 396:397]; WA = g.MISC[:, 397:398]; WBC2 = g.MISC[:, 398:399]
    wex_toks = []
    for ti in range(16):
        m1 = p.op("vector", lambda e, ti=ti: e.reduce_max(TC1, LG[:, ti, :], axis=AX.X), waits=tk(lg_toks[ti], *wex_toks[-1:]))
        e1 = p.op("vector", lambda e, ti=ti: e.tensor_scalar(EQ1, LG[:, ti, :], TC1, None, ALU.is_ge), waits=tk(m1))
        s1_ = p.op("vector", lambda e, ti=ti: e.scalar_tensor_tensor(EQ2, EQ1, -1e30, LG[:, ti, :], ALU.mult, ALU.add), waits=tk(e1))
        m2 = p.op("vector", lambda e: e.reduce_max(TC2, EQ2, axis=AX.X), waits=tk(s1_))
        e2 = p.op("vector", lambda e, ti=ti: e.tensor_scalar(EQ2, LG[:, ti, :], TC2, None, ALU.is_ge), waits=tk(m2))
        d1 = p.op("vector", lambda e: e.tensor_sub(EXR, TC2, TC1), waits=tk(m2, e2))
        sg = p.op("scalar", lambda e: e.activation(WA, EXR, AF.Sigmoid), waits=tk(d1))
        w1_ = p.op("vector", lambda e: e.tensor_scalar(WBC2, WA, -1.0, 1.0, ALU.mult, ALU.add), waits=tk(sg))
        c1 = p.op("vector", lambda e, ti=ti: e.tensor_scalar_mul(EQ1, EQ1, WBC2), waits=tk(w1_, e1))
        c2 = p.op("vector", lambda e, ti=ti: e.tensor_scalar_mul(EQ2, EQ2, WA), waits=tk(sg, e2, c1))
        c3 = p.op("vector", lambda e, ti=ti: e.tensor_add(WEX[:, ti, :], EQ1, EQ2), waits=tk(c2))
        wex_toks.append(c3)
    wf_toks = []
    WFM = g.SCf[0:8, 656:2704]
    for ti in range(16):
        pb = g.PS[4 + ti % 2]
        tt = p.op("tensor", lambda e, ti=ti, pb=pb: e.matmul(
            pb.t[0:8, 0:128], WEX[:, ti, :], g.IDN[:], start=True, stop=True),
            waits=tk(wex_toks[ti], pb.rd, g.tok_idn))
        cp = p.op("vector", lambda e, ti=ti, pb=pb: e.tensor_copy(
            WFM[0:8, ti * 128:(ti + 1) * 128], pb.t[0:8, 0:128]), waits=tk(tt))
        pb.rd = cp
        wf_toks.append(cp)
    t_wfd = p.dma("sync", g.wrow_d[:], WFM, waits=tk(*wf_toks))

    HMO = g.SBIG[:, 0:16384].rearrange("p (o t) -> p o t", o=32)
    moe_x_toks = []
    wrow_rd = None
    for e_ in range(8):
        t_wr = p.dma("sync", g.STAT[0:1, 0:2048], g.wrow_d[e_:e_ + 1, :], waits=tk(t_wfd, wrow_rd))
        for t in range(4):
            ts = slice(t * 512, (t + 1) * 512)
            mb = g.PS[5]
            bcm = p.op("tensor", lambda e, t=t, mb=mb: e.matmul(
                mb.t[:, 0:512], g.ONESR[:], g.STAT[0:1, t * 512:(t + 1) * 512], start=True, stop=True),
                waits=tk(t_wr, mb.rd))
            WBC = g.SCf[:, 2704 + (t % 2) * 512: 2704 + (t % 2) * 512 + 512]
            mcp = p.op("vector", lambda e, mb=mb, WBC=WBC: e.tensor_copy(WBC, mb.t[:, 0:512]),
                       waits=tk(bcm, moe_x_toks[-15] if len(moe_x_toks) >= 15 else None))
            mb.rd = mcp
            wrow_rd = mcp
            hlast = None
            for og in range(8):
                b01 = og % 2
                wdst = g.WB[:, b01, :].rearrange("p (k o) -> p k o", k=8)
                tw = p.dma("gpsimd", wdst, dr["l2_W1"][e_][:, og], waits=tk(wb_rd[b01]))
                for oc in range(4):
                    oabs = og * 4 + oc
                    bk = g.PS[oabs % 4]
                    mm = None
                    for c in range(FC):
                        mm = p.add("tensor", lambda e, c=c, oc=oc, b01=b01, ts=ts, bk=bk: e.matmul(
                            bk.t[:, 0:512], g.WB[:, b01, c * 512 + oc * 128: c * 512 + oc * 128 + 128],
                            g.XN[:, c, ts], start=(c == 0), stop=(c == FC - 1)),
                            waits=tk(ln[c][t], *( (tw, bk.rd) if c == 0 else ())),
                            inc=p.esem["tensor"] if c == FC - 1 else None)
                    wb_rd[b01] = mm
                    gv = p.op("scalar", lambda e, e_=e_, oabs=oabs, bk=bk: e.activation(
                        HMO[:, oabs, 0:512], bk.t[:, 0:512], AF.Gelu, bias=g.MISC[:, 10 + e_ * 32 + oabs: 11 + e_ * 32 + oabs]),
                        waits=tk(mm, t_b1))
                    bk.rd = gv
                    hlast = gv
            for og in range(FC):
                b01 = og % 2
                wdst = g.WB[:, b01, :].rearrange("p (k o) -> p k o", k=32)
                tw = p.dma("gpsimd", wdst, dr["l2_W2"][e_][:, og], waits=tk(wb_rd[b01]))
                bk = g.PS[og % 4]
                mm = None
                for k in range(32):
                    mm = p.add("tensor", lambda e, k=k, b01=b01, bk=bk: e.matmul(
                        bk.t[:, 0:512], g.WB[:, b01, k * 128:(k + 1) * 128],
                        HMO[:, k, 0:512], start=(k == 0), stop=(k == 31)),
                        waits=tk(*( (tw, bk.rd, hlast) if k == 0 else ())),
                        inc=p.esem["tensor"] if k == 31 else None)
                wb_rd[b01] = mm
                x1_ = p.op("vector", lambda e, e_=e_, og=og, bk=bk, WBC=WBC: e.scalar_tensor_tensor(
                    g.SCf[:, 656 + (og % 2) * 512: 656 + (og % 2) * 512 + 512],
                    bk.t[:, 0:512], g.MISC[:, 266 + e_ * 8 + og: 267 + e_ * 8 + og], WBC, ALU.add, ALU.mult),
                    waits=tk(mm, t_b2, mcp, moe_x_toks[-2] if len(moe_x_toks) >= 2 else None))
                bk.rd = x1_
                x2_ = p.op("vector", lambda e, og=og, ts=ts: e.tensor_add(
                    g.X[:, og, ts], g.X[:, og, ts],
                    g.SCf[:, 656 + (og % 2) * 512: 656 + (og % 2) * 512 + 512]), waits=tk(x1_))
                moe_x_toks.append(x2_)
    g.x_toks = x_toks + moe_x_toks

def emit_l3(g, ln):
    p = g.p
    dr = g.dram
    EPS = 1e-5
    ZX = g.SBIG[:, 0:17408].rearrange("p (c t) -> p c t", c=34)
    XBCC = g.SBIG[:, 17408:26112].rearrange("p (c t) -> p c t", c=17)
    XT = g.SBIG[:, 26112:28160].rearrange("p (c f) -> p c f", c=16)
    YV = g.SBIG[:, 8192:16384].rearrange("p (c t) -> p c t", c=16)
    SSTB = g.SBIG[:, 28160:30208].rearrange("p (h n) -> p h n", h=32)
    CSAVE = g.SBIG[:, 30208:32256].rearrange("p (c t) -> p c t", c=16)
    sst_sl = lambda h: SSTB[0:64, h, :]
    TAIL = g.SC[:, 0:51].rearrange("p (c t) -> p c t", c=17)
    CALN = g.SC[:, 64:576]
    BTM = g.SC[:, 576:640]
    SMT = lambda s: g.SC[:, 640 + s * 128: 640 + (s + 1) * 128]
    GSB = g.SC[:, 896:1024]
    CESL = lambda s: g.SC[:, 1024 + s * 128: 1024 + (s + 1) * 128]
    BESL = g.SC[:, 1280:1344]
    AIH = g.SC[:, 1344:1408]
    DTT = g.SCf[:, 704:736]
    CAT = g.SCf[:, 736:768]
    EBT = g.SCf[:, 768:800]
    ACOL = g.SCf[:, 800:832]
    CA = [g.SCf[0:32, 832:960], g.SCf[0:32, 960:1088]]
    RUNC = g.SCf[0:32, 1088:1089]
    CAEND = g.SCf[0:1, 1089:1121]
    DTQ = g.SCf[0:32, 1152:1664]
    LDA = g.SCf[0:32, 1664:2176]
    CACC = g.SCf[:, 2176:2688]
    NROW = lambda i: g.SCf[0:1, 2688 + i * 128: 2688 + (i + 1) * 128]
    EROW = g.SCf[0:1, 2944:3072]

    t_convw = p.dma("sync", g.MISC[:, 51:119], dr["l3_convw"].rearrange("p c f -> p (c f)"))
    t_convb = p.dma("sync", g.MISC[:, 34:51], dr["l3_convb"][:])
    t_dtb = p.dma("sync", g.MISC[0:32, 0:1], dr["l3_dtb"][:])
    t_alog = p.dma("sync", g.MISC[0:32, 1:2], dr["l3_alog"][:])
    t_dcol = p.dma("sync", g.MISC[:, 2:18], dr["l3_dcol"][:])
    t_nw = p.dma("sync", g.MISC[:, 18:34], dr["l3_nw"][:])
    t_seg = p.dma("sync", g.MISC[:, 119:122], dr["l3_seg"][:])
    t_hm = p.dma("sync", g.MISC[:, 122:126], dr["l3_hmask"][:])
    t_trin = p.dma("sync", g.TRINEG[:], dr["trineg"][:])
    t_triu = t_trin
    SMTF_ = lambda s: g.SCf[:, 3232 + s * 128: 3232 + (s + 1) * 128]
    a1 = p.op("scalar", lambda e: e.activation(g.MISC[0:32, 1:2], g.MISC[0:32, 1:2], AF.Exp), waits=tk(t_alog))
    t_negA = p.op("vector", lambda e: e.tensor_scalar_mul(g.MISC[0:32, 1:2], g.MISC[0:32, 1:2], -1.0), waits=tk(a1))

    t_hst = p.dma("sync", g.hag_in.rearrange("(c p) t -> p c t", p=128), g.XN[:, :, T - 3:T],
                  waits=tk(*[ln[c][3] for c in range(FC)]))
    t_hag = p.add("gpsimd", lambda e: e.collective_compute(
        "AllGather", ALU.bypass, replica_groups=GROUPS, ins=[g.hag_in[:]], outs=[g.hag_out[:]]),
        waits=tk(t_hst), inc=p.csem, amt=1)
    HALL = g.SCf[:, 3072:3120].bitcast(BF16)
    t_hld = p.dma("sync", HALL.rearrange("p (r c t) -> p r c t", r=4, c=8),
                  g.hag_out.rearrange("(r c p) t -> p r c t", p=128, r=4), waits=tk(t_hag))
    HSUM = g.SCf[:, 3120:3132].bitcast(BF16)
    hz = p.op("vector", lambda e: e.memset(HSUM[:, 0:24], 0.0), waits=tk())
    hs_toks = []
    for r in range(4):
        hv = p.op("vector", lambda e, r=r: e.scalar_tensor_tensor(
            HSUM[:, 0:24], HALL[:, r * 24:(r + 1) * 24], g.MISC[:, 122 + r:123 + r],
            HSUM[:, 0:24], ALU.mult, ALU.add), waits=tk(t_hld, hz, t_hm))
        hs_toks.append(hv)

    wb_rd = [None, None]
    tail_toks = [None] * 17
    sst_tok = [p.op("vector", lambda e: e.memset(SSTB[0:64, :, :], 0.0), waits=tk())] * 32
    runc_tok = p.op("vector", lambda e: e.memset(RUNC[:], 0.0), waits=tk())
    ca_rd = None
    ca_bd = None
    row_rd = None
    erow_rd = None
    sm_rd = [None, None]
    smtf_rd = [None, None]
    nrow_rd = [None, None]
    ce_rd = [None, None]
    be_rd = None
    ai_rd = None
    ty = tz = None
    t_acore = None
    conv_toks = [None] * 17

    for q in range(4):
        ts = slice(q * 512, (q + 1) * 512)
        zx_toks = [None] * 34
        first_ev = True
        for og in range(17):
            b01 = og % 2
            wdst = g.WB[:, b01, 0:2048].rearrange("p (k o) -> p k o", k=8)
            tw = p.dma("gpsimd", wdst, dr["l3_win"][:, og], waits=tk(wb_rd[b01]))
            last_mm = None
            for oc in range(2):
                c_out = og * 2 + oc
                bk = g.PS[(og * 2 + oc) % 4]
                mm = None
                for c in range(FC):
                    mm = p.add("tensor", lambda e, c=c, oc=oc, b01=b01, ts=ts, bk=bk: e.matmul(
                        bk.t[:, 0:512], g.WB[:, b01, c * 256 + oc * 128: c * 256 + oc * 128 + 128],
                        g.XN[:, c, ts], start=(c == 0), stop=(c == FC - 1)),
                        waits=tk(ln[c][q], *( (tw, bk.rd) if c == 0 else ())),
                        inc=p.esem["tensor"] if c == FC - 1 else None)
                ev = p.op("vector", lambda e, c_out=c_out, bk=bk: e.tensor_copy(
                    ZX[:, c_out, 0:512], bk.t[:, 0:512]),
                    waits=tk(mm, *( (ty, tz) if (first_ev and q > 0) else ())))
                first_ev = False
                bk.rd = ev
                zx_toks[c_out] = ev
                last_mm = mm
            if q == 0 and 8 <= og <= 16:
                for oc in range(2):
                    c_out = og * 2 + oc
                    if not (16 <= c_out <= 32):
                        continue
                    ch = c_out - 16
                    bk = g.PS[4 + (ch % 2)]
                    mm = None
                    for c in range(FC):
                        mm = p.add("tensor", lambda e, c=c, oc=oc, b01=b01, bk=bk: e.matmul(
                            bk.t[:, 0:3], g.WB[:, b01, c * 256 + oc * 128: c * 256 + oc * 128 + 128],
                            HSUM[:, 0:24].rearrange("p (c t) -> p c t", c=8)[:, c, :],
                            start=(c == 0), stop=(c == FC - 1)),
                            waits=tk(*( (hs_toks[-1], bk.rd) if c == 0 else ())),
                            inc=p.esem["tensor"] if c == FC - 1 else None)
                    tv = p.op("vector", lambda e, ch=ch, bk=bk: e.tensor_copy(
                        TAIL[:, ch, :], bk.t[:, 0:3]), waits=tk(mm))
                    bk.rd = tv
                    tail_toks[ch] = tv
            wb_rd[b01] = last_mm
        d1 = p.op("vector", lambda e: e.tensor_scalar_add(DTQ, ZX[0:32, 33, 0:512], g.MISC[0:32, 0:1]),
                  waits=tk(zx_toks[33], t_dtb))
        d2 = p.op("scalar", lambda e: e.activation(LDA, DTQ, AF.Abs), waits=tk(d1))
        d3 = p.op("scalar", lambda e: e.activation(LDA, LDA, AF.Exp, scale=-1.0), waits=tk(d2))
        d4 = p.op("scalar", lambda e: e.activation(LDA, LDA, AF.Ln, bias=1.0), waits=tk(d3))
        d5 = p.op("vector", lambda e: e.tensor_scalar_max(DTQ, DTQ, 0.0), waits=tk(d2))
        d6 = p.op("vector", lambda e: e.tensor_add(DTQ, DTQ, LDA), waits=tk(d4, d5))
        d7 = p.op("vector", lambda e: e.tensor_scalar_mul(LDA, DTQ, g.MISC[0:32, 1:2]), waits=tk(d6, t_negA))
        conv_prev_q = conv_toks[16] if q else None
        conv_toks = [None] * 17
        for ch in range(17):
            wc = lambda d_, ch=ch: g.MISC[:, 51 + ch * 4 + d_: 52 + ch * 4 + d_]
            raw = ZX[:, 16 + ch, 0:512]
            cc = p.op("vector", lambda e, raw=raw, wc=wc: e.tensor_scalar_mul(CACC, raw, wc(3)),
                      waits=tk(zx_toks[16 + ch], t_convw, conv_toks[ch - 1] if ch else (conv_prev_q if q else None)))
            for s in range(1, 4):
                cc = p.op("vector", lambda e, s=s, raw=raw, wc=wc: e.scalar_tensor_tensor(
                    CACC[:, s:512], raw[:, 0:512 - s], wc(3 - s), CACC[:, s:512], ALU.mult, ALU.add),
                    waits=tk(cc))
                cc = p.op("vector", lambda e, s=s, ch=ch, wc=wc: e.scalar_tensor_tensor(
                    CACC[:, 0:s], TAIL[:, ch, 3 - s:3], wc(3 - s), CACC[:, 0:s], ALU.mult, ALU.add),
                    waits=tk(cc, tail_toks[ch]))
            tu = p.op("vector", lambda e, ch=ch, raw=raw: e.tensor_copy(TAIL[:, ch, :], raw[:, 509:512]),
                      waits=tk(cc))
            tail_toks[ch] = tu
            cs_ = p.op("scalar", lambda e, ch=ch: e.activation(
                XBCC[:, ch, 0:512], CACC, AF.Silu, bias=g.MISC[:, 34 + ch:35 + ch]),
                waits=tk(cc, t_convb))
            conv_toks[ch] = cs_
        t_cal = p.dma("sync", CALN[0:64, :], XBCC[64:128, 16, 0:512], waits=tk(conv_toks[16], ce_rd[0], ce_rd[1]))
        for c4 in range(4):
            p.op("vector", lambda e, q=q, c4=c4: e.tensor_copy(
                CSAVE[0:64, q * 4 + c4, :], CALN.rearrange("p (a t) -> p a t", a=4)[0:64, c4, :]),
                waits=tk(t_cal))
        for c4 in range(4):
            gidx = q * 4 + c4
            cs_sl = slice(c4 * 128, (c4 + 1) * 128)
            cur = 0
            lt = p.op("vector", lambda e, c4=c4: e.tensor_copy(CA[0], LDA[:, c4 * 128:(c4 + 1) * 128]),
                      waits=tk(d7, ca_rd, ca_bd))
            for s in (1, 2, 4, 8, 16, 32, 64):
                nxt = 1 - cur
                l1 = p.op("vector", lambda e, s=s, cur=cur, nxt=nxt: e.tensor_copy(
                    CA[nxt][:, 0:s], CA[cur][:, 0:s]), waits=tk(lt))
                lt = p.op("vector", lambda e, s=s, cur=cur, nxt=nxt: e.tensor_add(
                    CA[nxt][:, s:128], CA[cur][:, s:128], CA[cur][:, 0:128 - s]), waits=tk(l1))
                cur = nxt
            lt = p.op("vector", lambda e, cur=cur: e.tensor_scalar_add(CA[cur], CA[cur], RUNC),
                      waits=tk(lt, runc_tok))
            runc_tok = p.op("vector", lambda e, cur=cur: e.tensor_copy(RUNC, CA[cur][:, 127:128]), waits=tk(lt))
            bd = p.dma("sync", g.cag[:, gidx * 128:(gidx + 1) * 128], CA[cur], waits=tk(lt))
            ca_bd = bd
            pb = g.PS[4]
            ct1 = p.op("tensor", lambda e, cur=cur, pb=pb: e.transpose(
                pb.t[:, 0:32], CA[cur], g.IDNF[0:32, 0:32]), waits=tk(lt, pb.rd, g.tok_idnf))
            ct2 = p.op("vector", lambda e, pb=pb: e.tensor_copy(CAT, pb.t[:, 0:32]), waits=tk(ct1))
            pb.rd = ct2
            ca_rd = ct1
            pb2 = g.PS[5]
            dt1 = p.op("tensor", lambda e, c4=c4, pb2=pb2: e.transpose(
                pb2.t[:, 0:32], DTQ[:, c4 * 128:(c4 + 1) * 128], g.IDNF[0:32, 0:32]),
                waits=tk(d6, pb2.rd, g.tok_idnf))
            dt2 = p.op("vector", lambda e, pb2=pb2: e.tensor_copy(DTT, pb2.t[:, 0:32]), waits=tk(dt1))
            pb2.rd = dt2
            gb = g.PS[4]
            g1 = p.op("tensor", lambda e, c4=c4, gb=gb, cs_sl=cs_sl: e.matmul(
                gb.t[:, 0:128], XBCC[0:64, 16, cs_sl], CALN[0:64, cs_sl], start=True, stop=True),
                waits=tk(t_cal, conv_toks[16], gb.rd))
            g2 = p.op("vector", lambda e, gb=gb: e.tensor_copy(GSB, gb.t[:, 0:128]), waits=tk(g1))
            gb.rd = g2
            bb_ = g.PT[0]
            b1 = p.op("tensor", lambda e, c4=c4, bb_=bb_, cs_sl=cs_sl: e.transpose(
                bb_.t[:, 0:64], XBCC[0:64, 16, cs_sl], g.IDN[0:64, 0:64]),
                waits=tk(conv_toks[16], bb_.rd, g.tok_idn, be_rd))
            b2 = p.op("vector", lambda e, bb_=bb_: e.tensor_copy(BTM, bb_.t[:, 0:64]), waits=tk(b1))
            bb_.rd = b2
            xt_toks = []
            for zc in range(16):
                pbx = g.PT[zc % 2]
                x1 = p.op("tensor", lambda e, zc=zc, cs_sl=cs_sl, pbx=pbx: e.transpose(
                    pbx.t[:, 0:128], XBCC[:, zc, cs_sl], g.IDN[:]),
                    waits=tk(conv_toks[zc], pbx.rd, g.tok_idn))
                x2 = p.op("vector", lambda e, zc=zc, pbx=pbx: e.tensor_copy(
                    XT[:, zc, :], pbx.t[:, 0:128]), waits=tk(x1))
                pbx.rd = x2
                xt_toks.append(x2)
            for half in range(2):
                bl = p.dma("sync", g.STAT[0:1, 0:2048].rearrange("o (h t) -> o h t", h=16),
                           g.cag[half * 16:(half + 1) * 16, gidx * 128:(gidx + 1) * 128]
                           .rearrange("h t -> () h t" if False else "h t -> h t")[None, :, :] if False else
                           g.cag[half * 16:(half + 1) * 16, gidx * 128:(gidx + 1) * 128].rearrange("h t -> o h t", o=1) if False else
                           g.cag[half * 16:(half + 1) * 16, gidx * 128:(gidx + 1) * 128],
                           waits=tk(bd, row_rd))
                hsl = slice(half * 16, half * 16 + 16)
                ce1 = p.op("vector", lambda e, hsl=hsl: e.tensor_copy(
                    CAEND[0:1, hsl], g.STAT[0:1, 127:2048:128]), waits=tk(bl))
                pb3 = g.PS[4]
                eb1 = p.op("tensor", lambda e, hsl=hsl, pb3=pb3: e.matmul(
                    pb3.t[:, 0:16], g.ONESR[:], CAEND[0:1, hsl], start=True, stop=True),
                    waits=tk(ce1, pb3.rd))
                eb2 = p.op("vector", lambda e, hsl=hsl, pb3=pb3: e.tensor_sub(
                    EBT[:, hsl], pb3.t[:, 0:16], CAT[:, hsl]), waits=tk(eb1, ct2))
                pb3.rd = eb2
                eb3 = p.op("scalar", lambda e, hsl=hsl: e.activation(EBT[:, hsl], EBT[:, hsl], AF.Exp),
                           waits=tk(eb2))
                eb4 = p.op("vector", lambda e, hsl=hsl: e.tensor_mul(EBT[:, hsl], EBT[:, hsl], DTT[:, hsl]),
                           waits=tk(eb3, dt2))
                av1 = p.op("scalar", lambda e, hsl=hsl: e.activation(CAEND[0:1, hsl], CAEND[0:1, hsl], AF.Exp),
                           waits=tk(eb2))
                pb4 = g.PS[5]
                av2 = p.op("tensor", lambda e, hsl=hsl, pb4=pb4: e.matmul(
                    pb4.t[:, 0:16], g.ONESR[:], CAEND[0:1, hsl], start=True, stop=True),
                    waits=tk(av1, pb4.rd))
                av3 = p.op("vector", lambda e, hsl=hsl, pb4=pb4: e.tensor_copy(
                    ACOL[:, hsl], pb4.t[:, 0:16]), waits=tk(av2))
                pb4.rd = av3
                if gidx == 15 and half == 1:
                    t_acore = p.dma("sync", g.a_in[:], CAEND, waits=tk(av1, ce1))
                for hh in range(16):
                    h = half * 16 + hh
                    hb = 64 * (h % 2)
                    hc = h // 2
                    zc = hc
                    carow = g.STAT[0:1, hh * 128:(hh + 1) * 128]
                    nr = NROW(h % 2)
                    n1 = p.op("vector", lambda e, nr=nr, carow=carow: e.tensor_scalar_mul(nr, carow, -1.0),
                              waits=tk(bl, nrow_rd[h % 2]))
                    db = g.PS[h % 2]
                    m1 = p.op("tensor", lambda e, db=db, carow=carow: e.matmul(
                        db.t[:, 0:128], g.ONESR[:], carow, start=True, stop=False),
                        waits=tk(bl, db.rd))
                    m2 = p.add("tensor", lambda e, db=db, nr=nr: e.matmul(
                        db.t[:, 0:128], nr, g.ONESR[:], start=False, stop=True),
                        waits=tk(n1), inc=p.esem["tensor"])
                    nrow_rd[h % 2] = m2
                    sm = SMT(h % 2)
                    smf = SMTF_(h % 2)
                    s0 = p.op("vector", lambda e, db=db, smf=smf: e.tensor_add(smf, db.t[:, 0:128], g.TRINEG[:]),
                              waits=tk(m2, t_trin, smtf_rd[h % 2]))
                    db.rd = s0
                    s1 = p.op("scalar", lambda e, sm=sm, smf=smf: e.activation(sm, smf, AF.Exp),
                              waits=tk(s0, sm_rd[h % 2]))
                    smtf_rd[h % 2] = s1
                    s3 = p.op("vector", lambda e, sm=sm: e.tensor_mul(sm, sm, GSB), waits=tk(s1, g2))
                    s4 = p.op("vector", lambda e, sm=sm, h=h: e.tensor_scalar_mul(sm, sm, DTT[:, h:h + 1]),
                              waits=tk(s3, dt2))
                    er1 = p.op("scalar", lambda e, carow=carow: e.activation(EROW, carow, AF.Exp),
                               waits=tk(bl, erow_rd))
                    eb_ = g.PS[4 + (h % 2)]
                    er2 = p.op("tensor", lambda e, eb_=eb_: e.matmul(
                        eb_.t[0:64, 0:128], g.ONESR[0:1, 0:64], EROW, start=True, stop=True),
                        waits=tk(er1, eb_.rd))
                    erow_rd = er2
                    ce_ = CESL(h % 2)
                    er3 = p.op("vector", lambda e, eb_=eb_, ce_=ce_, cs_sl=cs_sl: e.tensor_mul(
                        ce_[0:64, :], CALN[0:64, cs_sl], eb_.t[0:64, 0:128]),
                        waits=tk(er2, t_cal, ce_rd[h % 2]))
                    eb_.rd = er3
                    py = g.PS[2 + (h % 2)]
                    y1 = p.op("tensor", lambda e, py=py, zc=zc, hb=hb, sm=sm: e.matmul(
                        py.t[0:64, 0:128], XT[:, zc, hb:hb + 64], sm, start=True, stop=False),
                        waits=tk(s4, xt_toks[zc], py.rd))
                    sm_rd[h % 2] = y1
                    y2 = p.add("tensor", lambda e, py=py, h=h, ce_=ce_: e.matmul(
                        py.t[0:64, 0:128], sst_sl(h), ce_[0:64, :], start=False, stop=True),
                        waits=tk(er3, sst_tok[h]), inc=p.esem["tensor"])
                    ce_rd[h % 2] = y2
                    yv = p.op("vector", lambda e, py=py, hb=hb, hc=hc, cs_sl=cs_sl: e.scalar_tensor_tensor(
                        YV[hb:hb + 64, hc, cs_sl], XBCC[hb:hb + 64, hc, cs_sl],
                        g.MISC[hb:hb + 64, 2 + hc:3 + hc], py.t[0:64, 0:128], ALU.mult, ALU.add),
                        waits=tk(y2, t_dcol, conv_toks[hc]))
                    py.rd = yv
                    ai1 = p.op("vector", lambda e, h=h: e.tensor_scalar_mul(
                        AIH[0:64, :], g.IDN[0:64, 0:64], ACOL[0:64, h:h + 1]), waits=tk(av3, ai_rd))
                    be1 = p.op("vector", lambda e, h=h: e.tensor_scalar_mul(BESL, BTM, EBT[:, h:h + 1]),
                               waits=tk(b2, eb4, be_rd))
                    st1 = p.op("tensor", lambda e, db=db, zc=zc, hb=hb: e.matmul(
                        db.t[0:64, 0:64], BESL, XT[:, zc, hb:hb + 64], start=True, stop=False),
                        waits=tk(be1, xt_toks[zc], s1))
                    be_rd = st1
                    st2 = p.add("tensor", lambda e, db=db, h=h: e.matmul(
                        db.t[0:64, 0:64], AIH[0:64, :], sst_sl(h), start=False, stop=True),
                        waits=tk(ai1, sst_tok[h]), inc=p.esem["tensor"])
                    ai_rd = st2
                    sv = p.op("vector", lambda e, db=db, h=h: e.tensor_copy(sst_sl(h), db.t[0:64, 0:64]),
                              waits=tk(st2))
                    db.rd = sv
                    sst_tok[h] = sv
                row_rd = er2
        sz_toks = []
        last_yv_tok = sst_tok[31]
        for zc in range(16):
            z1 = p.op("scalar", lambda e, zc=zc: e.activation(
                ZX[:, zc, 0:512], ZX[:, zc, 0:512], AF.Silu), waits=tk(zx_toks[zc]))
            w1 = p.op("vector", lambda e, zc=zc: e.tensor_mul(
                YV[:, zc, 0:512], YV[:, zc, 0:512], ZX[:, zc, 0:512]), waits=tk(z1, last_yv_tok))
            sz_toks.append(w1)
        ty = p.dma("sync", g.ysc[:, ts].rearrange("(c p) t -> p c t", p=128), YV[:, :, 0:512],
                   waits=tk(*sz_toks))
        tz = p.dma("sync", g.zsc[:, ts].rearrange("(c p) t -> p c t", p=128), ZX[:, 0:16, 0:512],
                   waits=tk(*sz_toks))

    STAGE = g.SBIGf[0:64, 6144:8192]
    t_sst_c = p.op("vector", lambda e: e.tensor_copy(
        STAGE.rearrange("p (h n) -> p h n", h=32), SSTB[0:64, :, :]), waits=tk(*sst_tok, ty, tz))
    t_sstst = p.dma("sync", g.sst_in[:], STAGE, waits=tk(t_sst_c))
    t_sag2 = p.add("gpsimd", lambda e: e.collective_compute(
        "AllGather", ALU.bypass, replica_groups=GROUPS, ins=[g.sst_in[:]], outs=[g.sst_out[:]]),
        waits=tk(t_sstst), inc=p.csem, amt=1)
    t_aag = p.add("gpsimd", lambda e: e.collective_compute(
        "AllGather", ALU.bypass, replica_groups=GROUPS, ins=[g.a_in[:]], outs=[g.a_out[:]]),
        waits=tk(t_acore), inc=p.csem, amt=1)

    SALL = g.SBIGf[0:64, 0:8192].rearrange("p (r h n) -> p r h n", r=4, h=32)
    t_sld2 = p.dma("sync", g.SBIGf[0:64, 0:8192].rearrange("p (r x) -> p r x", r=4),
                   g.sst_out.rearrange("(r p) x -> p r x", p=64), waits=tk(t_sag2, ty, tz))
    AROW4 = g.SCf[0:1, 3072:3200]
    t_ald = p.dma("sync", AROW4,
                  g.a_out.rearrange("r h -> (r h)").rearrange("(o x) -> o x", o=1), waits=tk(t_aag))
    TCUR = g.SBIGf[0:64, 8192:10240].rearrange("p (h n) -> p h n", h=32)
    SIN = g.SBIGf[0:64, 10240:12288].rearrange("p (h n) -> p h n", h=32)
    SINB = g.SBIG[0:64, 24576:26624].rearrange("p (h n) -> p h n", h=32)
    ACM = g.SCf[:, 3200:3232]
    tz1 = p.op("vector", lambda e: e.memset(TCUR[:, :, :], 0.0), waits=tk(t_sld2))
    tz2 = p.op("vector", lambda e: e.memset(SIN[:, :, :], 0.0), waits=tk(t_sld2, t_sst_c))
    tprev = tz1
    sprev = tz2
    for r in range(3):
        pb5 = g.PS[4]
        am1 = p.op("tensor", lambda e, r=r, pb5=pb5: e.matmul(
            pb5.t[:, 0:32], g.ONESR[:], AROW4[0:1, r * 32:(r + 1) * 32], start=True, stop=True),
            waits=tk(t_ald, pb5.rd))
        am2 = p.op("vector", lambda e, pb5=pb5: e.tensor_copy(ACM, pb5.t[:, 0:32]), waits=tk(am1))
        pb5.rd = am2
        for h in range(32):
            t1 = p.op("vector", lambda e, r=r, h=h: e.scalar_tensor_tensor(
                TCUR[:, h, :], TCUR[:, h, :], ACM[0:64, h:h + 1], SALL[:, r, h, :], ALU.mult, ALU.add),
                waits=tk(am2, tprev))
            t2 = p.op("vector", lambda e, r=r, h=h: e.scalar_tensor_tensor(
                SIN[:, h, :], TCUR[:, h, :], g.MISC[0:64, 119 + r:120 + r], SIN[:, h, :], ALU.mult, ALU.add),
                waits=tk(t1, t_seg, sprev))
            tprev = t1
            sprev = t2
    sinb = p.op("vector", lambda e: e.tensor_copy(SINB[:, :, :], SIN[:, :, :]), waits=tk(sprev))

    WQ = g.SBIG[:, 16384:24576].rearrange("p (c t) -> p c t", c=16)
    SZQ = g.SBIG[:, 24576:32768 - 2560].rearrange("p (c t) -> p c t", c=16) if False else None
    SZQ = g.SBIG[:, 0:8192].rearrange("p (c t) -> p c t", c=16)
    x_toks = []
    wq_rd = None
    sq_rd = None
    r4_prev = None
    for q in range(4):
        ts = slice(q * 512, (q + 1) * 512)
        t_wld = p.dma("sync", WQ[:, :, 0:512], g.ysc[:, ts].rearrange("(c p) t -> p c t", p=128),
                      waits=tk(wq_rd, sinb))
        t_szld = p.dma("sync", SZQ[:, :, 0:512], g.zsc[:, ts].rearrange("(c p) t -> p c t", p=128),
                       waits=tk(sq_rd, sinb, tz))
        corr_toks = []
        for half in range(2):
            for hh in range(16):
                if getattr(emit_l3, "NO_CORR", False):
                    continue
                h = half * 16 + hh
                hb = 64 * (h % 2)
                hc = h // 2
                erw = p.dma("sync", g.STAT[0:1, 0:512],
                            g.cag[h:h + 1, q * 512:(q + 1) * 512], waits=tk(row_rd))
                ex1 = p.op("scalar", lambda e: e.activation(g.STAT[0:1, 0:512], g.STAT[0:1, 0:512], AF.Exp),
                           waits=tk(erw))
                row_rd = ex1
                for c4 in range(4):
                    gidx = q * 4 + c4
                    cs_sl = slice(c4 * 128, (c4 + 1) * 128)
                    eb_ = g.PS[4 + (c4 % 2)]
                    er2 = p.op("tensor", lambda e, c4=c4, eb_=eb_: e.matmul(
                        eb_.t[0:64, 0:128], g.ONESR[0:1, 0:64], g.STAT[0:1, c4 * 128:(c4 + 1) * 128],
                        start=True, stop=True), waits=tk(ex1, eb_.rd))
                    row_rd = er2
                    ce_ = CESL(c4 % 2)
                    er3 = p.op("vector", lambda e, gidx=gidx, eb_=eb_, ce_=ce_: e.tensor_mul(
                        ce_[0:64, :], CSAVE[0:64, gidx, :], eb_.t[0:64, 0:128]),
                        waits=tk(er2, ce_rd[c4 % 2]))
                    eb_.rd = er3
                    pc = g.PS[c4 % 2]
                    yc = p.op("tensor", lambda e, h=h, pc=pc, ce_=ce_: e.matmul(
                        pc.t[0:64, 0:128], SINB[0:64, h, :], ce_[0:64, :],
                        start=True, stop=True), waits=tk(er3, sinb, pc.rd))
                    ce_rd[c4 % 2] = yc
                    TMPH = g.SC[hb:hb + 64, 1408 + (c4 % 2) * 128: 1408 + (c4 % 2) * 128 + 128]
                    c1 = p.op("vector", lambda e, hb=hb, hc=hc, cs_sl=cs_sl, pc=pc, TMPH=TMPH: e.tensor_mul(
                        TMPH, pc.t[0:64, 0:128], SZQ[hb:hb + 64, hc, cs_sl]),
                        waits=tk(yc, t_szld))
                    pc.rd = c1
                    c2 = p.op("vector", lambda e, hb=hb, hc=hc, cs_sl=cs_sl, TMPH=TMPH: e.tensor_add(
                        WQ[hb:hb + 64, hc, cs_sl], WQ[hb:hb + 64, hc, cs_sl], TMPH),
                        waits=tk(c1, t_wld))
                    corr_toks.append(c2)
        if q == 0 and "d_wq0" in g.dbg:
            dd1 = p.dma("sync", g.dbg["d_wq0"][:], g.SBIG[:, 16384:24576],
                        waits=tk(corr_toks[-1] if corr_toks else t_wld))
            p.wait("sync", [dd1])
        nb = g.PS[4]
        mm = None
        sqt_toks = []
        for chh in range(16):
            slot = g.SC[:, 1664 + (chh % 4) * 512: 1664 + (chh % 4) * 512 + 512]
            sq = p.op("vector", lambda e, chh=chh, slot=slot: e.tensor_mul(
                slot, WQ[:, chh, 0:512], WQ[:, chh, 0:512]),
                waits=tk(corr_toks[-1] if corr_toks else t_wld, sq_rd if chh < 4 else sqt_toks[chh - 4]))
            mm = p.add("tensor", lambda e, chh=chh, slot=slot, nb=nb: e.matmul(
                nb.t[0:1, 0:512], g.ONESB[:], slot, start=(chh == 0), stop=(chh == 15)),
                waits=tk(sq, nb.rd if chh == 0 else None),
                inc=p.esem["tensor"])
            sqt_toks.append(mm)
        r1 = p.op("vector", lambda e, nb=nb: e.tensor_scalar(
            g.SCf[0:1, 2944:3456][0:1, 0:512], nb.t[0:1, 0:512], 1.0 / (2 * D), EPS, ALU.mult, ALU.add),
            waits=tk(mm, r4_prev))
        nb.rd = r1
        r2 = p.op("vector", lambda e: e.reciprocal(g.SCf[0:1, 2944:3456][0:1, 0:512],
                                                   g.SCf[0:1, 2944:3456][0:1, 0:512]), waits=tk(r1))
        r3 = p.op("scalar", lambda e: e.activation(g.SCf[0:1, 2944:3456][0:1, 0:512],
                                                   g.SCf[0:1, 2944:3456][0:1, 0:512], AF.Sqrt), waits=tk(r2))
        nb2 = g.PS[5]
        r4 = p.op("tensor", lambda e, nb2=nb2: e.matmul(
            nb2.t[:, 0:512], g.ONESR[:], g.SCf[0:1, 2944:3456][0:1, 0:512], start=True, stop=True),
            waits=tk(r3, nb2.rd))
        r4_prev = r4
        appl_toks = []
        for chh in range(16):
            ap1 = p.op("vector", lambda e, chh=chh, nb2=nb2: e.scalar_tensor_tensor(
                WQ[:, chh, 0:512], WQ[:, chh, 0:512], g.MISC[:, 18 + chh:19 + chh], nb2.t[:, 0:512],
                ALU.mult, ALU.mult), waits=tk(r4, t_nw, corr_toks[-1] if corr_toks else t_wld))
            appl_toks.append(ap1)
        nb2.rd = appl_toks[-1]
        sq_rd = appl_toks[-1]

        for og in range(4):
            b01 = og % 2
            wdst = g.WB[:, b01, :].rearrange("p (k o) -> p k o", k=16)
            tww = p.dma("gpsimd", wdst, dr["l3_wout"][:, og], waits=tk(wb_rd[b01]))
            for oc in range(2):
                c_out = og * 2 + oc
                bk = g.PS[(og * 2 + oc) % 4]
                mm2 = None
                for k in range(16):
                    mm2 = p.add("tensor", lambda e, k=k, oc=oc, b01=b01, bk=bk: e.matmul(
                        bk.t[:, 0:512], g.WB[:, b01, k * 256 + oc * 128: k * 256 + oc * 128 + 128],
                        WQ[:, k, 0:512], start=(k == 0), stop=(k == 15)),
                        waits=tk(appl_toks[k], *( (tww, bk.rd) if k == 0 else ())),
                        inc=p.esem["tensor"] if k == 15 else None)
                xa = p.op("vector", lambda e, c_out=c_out, ts=ts, bk=bk: e.tensor_add(
                    g.X[:, c_out, ts], g.X[:, c_out, ts], bk.t[:, 0:512]), waits=tk(mm2))
                bk.rd = xa
                x_toks.append(xa)
            wb_rd[b01] = x_toks[-1]
        wq_rd = x_toks[-1]

    if getattr(emit_l3, "NO_PEER", False):
        g.x_toks = x_toks
        return
    VALS = g.SBIG[:, 0:16384].rearrange("p (e d) -> p e d", e=16)
    SELB = g.SBIG[:, 16384:16512].rearrange("p (c e) -> p c e", c=8)
    last_x = x_toks[-1]
    t_vals = p.dma("gpsimd", VALS[:], dr["l3_vals"].rearrange("e p d -> p e d"), waits=tk(last_x))
    t_selt = p.dma("gpsimd", SELB[:], dr["l3_selt"][:], waits=tk(last_x))
    SCW = g.SC.rearrange("p (e t) -> p e t", e=16)
    peer_tile_done = None
    selrow_rd = last_x
    scw_rd = [None] * 16
    for t in range(4):
        ts = slice(t * 512, (t + 1) * 512)
        for e_ in range(16):
            kb01 = e_ % 2
            wdst = g.WB[:, kb01, 0:1024].rearrange("p (k o) -> p k o", k=8)
            twk = p.dma("gpsimd", wdst, dr["l3_keys"][e_][:, 0], waits=tk(wb_rd[kb01]))
            bk = g.PS[e_ % 2]
            mm = None
            for c in range(FC):
                mm = p.add("tensor", lambda e, c=c, kb01=kb01, ts=ts, bk=bk: e.matmul(
                    bk.t[:, 0:512], g.WB[:, kb01, c * 128:(c + 1) * 128],
                    g.XN[:, c, ts], start=(c == 0), stop=(c == FC - 1)),
                    waits=tk(ln[c][t], *( (twk, bk.rd) if c == 0 else ())),
                    inc=p.esem["tensor"] if c == FC - 1 else None)
            wb_rd[kb01] = mm
            nbs = g.PS[4 + (e_ % 2)]
            mms = None
            for c in range(FC):
                mms = p.add("tensor", lambda e, c=c, e_=e_, ts=ts, nbs=nbs: e.matmul(
                    nbs.t[0:1, 0:512], SELB[:, c, e_:e_ + 1], g.XN[:, c, ts],
                    start=(c == 0), stop=(c == FC - 1)),
                    waits=tk(*( (t_selt, nbs.rd) if c == 0 else ())),
                    inc=p.esem["tensor"] if c == FC - 1 else None)
            srow2 = g.STAT[0:1, 0:512]
            sg = p.op("scalar", lambda e, nbs=nbs, srow2=srow2: e.activation(
                srow2, nbs.t[0:1, 0:512], AF.Sigmoid), waits=tk(mms, selrow_rd))
            nbs.rd = sg
            bcb = g.PS[2 + (e_ % 2)]
            sb1 = p.op("tensor", lambda e, bcb=bcb, srow2=srow2: e.matmul(
                bcb.t[:, 0:512], g.ONESR[:], srow2, start=True, stop=True),
                waits=tk(sg, bcb.rd))
            selrow_rd = sb1
            sc1 = p.op("vector", lambda e, e_=e_, bk=bk: e.tensor_scalar_max(
                SCW[:, e_, 0:512], bk.t[:, 0:512], 0.0), waits=tk(mm, scw_rd[e_], peer_tile_done))
            bk.rd = sc1
            if getattr(emit_l3, "NO_SEL", False):
                sc2 = p.op("vector", lambda e, e_=e_, bcb=bcb: e.tensor_copy(
                    SCW[:, e_, 0:512], SCW[:, e_, 0:512]), waits=tk(sc1, sb1))
            else:
                sc2 = p.op("vector", lambda e, e_=e_, bcb=bcb: e.tensor_mul(
                    SCW[:, e_, 0:512], SCW[:, e_, 0:512], bcb.t[:, 0:512]), waits=tk(sc1, sb1))
            bcb.rd = sc2
            scw_rd[e_] = sc2
        for dch in range(FC):
            bk = g.PS[dch % 2]
            mm = None
            for e_ in range(16):
                mm = p.add("tensor", lambda e, e_=e_, dch=dch, bk=bk: e.matmul(
                    bk.t[:, 0:512], VALS[:, e_, dch * 128:(dch + 1) * 128], SCW[:, e_, 0:512],
                    start=(e_ == 0), stop=(e_ == 15)),
                    waits=tk(scw_rd[e_], bk.rd if e_ == 0 else None,
                             t_vals if (dch == 0 and t == 0 and e_ == 0) else None),
                    inc=p.esem["tensor"] if e_ == 15 else None)
            xa = p.op("vector", lambda e, dch=dch, ts=ts, bk=bk: e.tensor_add(
                g.X[:, dch, ts], g.X[:, dch, ts], bk.t[:, 0:512]), waits=tk(mm))
            bk.rd = xa
            x_toks.append(xa)
            if dch == FC - 1:
                peer_tile_done = mm
    g.x_toks = x_toks



def tile_w(w, og):
    K, O = w.shape
    return np.ascontiguousarray(
        w.reshape(K // 128, 128, O // og, og).transpose(1, 2, 0, 3))


def col128(v):
    return np.ascontiguousarray(v.reshape(-1, 128).T)


def prep_weights(inputs, layers):
    d = {}
    f32 = lambda a: np.ascontiguousarray(a, dtype=np.float32)
    if 0 in layers:
        d["l0_qp"] = tile_w(f32(inputs["L0_attn_q_proj"]), 256)
        d["l0_kp"] = tile_w(f32(inputs["L0_attn_k_proj"]), 256)
        d["l0_w1"] = tile_w(f32(inputs["L0_ffn_w1_w"]).T, 256)
        d["l0_w2"] = tile_w(f32(inputs["L0_ffn_w2_w"]).T, 256)
        d["l0_h1"] = tile_w(f32(inputs["L0_ffn_h1_w"]).T, 128)
        d["l0_h2"] = tile_w(f32(inputs["L0_ffn_h2_w"]).T, 128)
        d["l0_w3"] = tile_w(f32(inputs["L0_ffn_w3_w"]).T, 128)
        d["l0_s1"] = col128(f32(inputs["L0_ffn_w1_s"]))
        d["l0_s2"] = col128(f32(inputs["L0_ffn_w2_s"]))
        d["l0_sh1"] = col128(f32(inputs["L0_ffn_h1_s"]))
        d["l0_sh2"] = col128(f32(inputs["L0_ffn_h2_s"]))
        d["l0_s3"] = col128(f32(inputs["L0_ffn_w3_s"]))
    if 2 in layers:
        d["l2_wq"] = tile_w(f32(inputs["L2_attn_Wq"]), 512)
        d["l2_wk"] = tile_w(f32(inputs["L2_attn_Wk"]), 512)
        d["l2_wv"] = tile_w(f32(inputs["L2_attn_Wv"]), 512)
        d["l2_wo"] = tile_w(f32(inputs["L2_attn_Wo"]), 512)
        wck = f32(inputs["L2_attn_Wck"]).reshape(32, 64, 64).transpose(1, 0, 2)
        d["l2_wck"] = np.ascontiguousarray(np.concatenate([wck, wck], axis=0))
        wcv = f32(inputs["L2_attn_Wcv"]).reshape(32, 64, 64).transpose(1, 0, 2)
        d["l2_wcv"] = np.ascontiguousarray(np.concatenate([wcv, wcv], axis=0))
        d["l2_bck"] = f32(inputs["L2_attn_bck"]).reshape(64, 1)
        d["l2_bcv"] = f32(inputs["L2_attn_bcv"]).reshape(64, 1)
        wg = f32(inputs["L2_attn_Wg"])
        d["l2_wgb"] = np.ascontiguousarray(np.concatenate([wg, wg], axis=0))
        d["l2_bgr"] = np.ascontiguousarray(np.tile(f32(inputs["L2_attn_bg"]), 16).reshape(1, 48))
        d["l2_wr"] = tile_w(f32(inputs["L2_ffn_Wr"]), 8)
        d["l2_br"] = f32(inputs["L2_ffn_br"]).reshape(1, 8)
        d["l2_W1"] = np.ascontiguousarray(np.stack([tile_w(f32(inputs["L2_ffn_W1"][e]), 512) for e in range(8)]))
        d["l2_b1"] = np.ascontiguousarray(np.concatenate([col128(f32(inputs["L2_ffn_b1"][e])) for e in range(8)], axis=1))
        d["l2_W2"] = np.ascontiguousarray(np.stack([tile_w(f32(inputs["L2_ffn_W2"][e]), 128) for e in range(8)]))
        d["l2_b2"] = np.ascontiguousarray(np.concatenate([col128(f32(inputs["L2_ffn_b2"][e])) for e in range(8)], axis=1))
        d["iota64"] = np.ascontiguousarray((np.arange(128) % 64).astype(np.float32).reshape(128, 1))
    if 3 in layers:
        win = np.zeros((1024, 4352), np.float32)
        win[:, 0:4256] = f32(inputs["L3_attn_W_in"])
        d["l3_win"] = tile_w(win, 256)
        d["l3_wout"] = tile_w(f32(inputs["L3_attn_W_out"]), 256)
        cw = f32(inputs["L3_attn_conv_w"]).reshape(17, 128, 4).transpose(1, 0, 2)
        d["l3_convw"] = np.ascontiguousarray(cw)
        d["l3_convb"] = col128(f32(inputs["L3_attn_conv_b"]))
        d["l3_dtb"] = f32(inputs["L3_attn_dt_bias"]).reshape(32, 1)
        d["l3_alog"] = f32(inputs["L3_attn_A_log"]).reshape(32, 1)
        dd = f32(inputs["L3_attn_D"])
        d["l3_dcol"] = np.ascontiguousarray(
            dd[2 * np.arange(16)[None, :] + (np.arange(128)[:, None] >= 64)].astype(np.float32))
        d["l3_nw"] = col128(f32(inputs["L3_attn_norm_weight"]))
        d["l3_keys"] = np.ascontiguousarray(np.stack([tile_w(f32(inputs["L3_ffn_keys"][e]), 128) for e in range(16)]))
        d["l3_vals"] = np.ascontiguousarray(f32(inputs["L3_ffn_values"]))
        d["l3_selt"] = tile_w(np.ascontiguousarray(f32(inputs["L3_ffn_expert_sel"]).T), 16)[:, 0]
        tri = np.zeros((128, 128), dtype=ml_dtypes.bfloat16)
        tri[np.tril_indices(128)] = 1
        d["trineg"] = np.ascontiguousarray(np.where(np.triu(np.ones((128, 128), np.bool_)), 0.0, -1e30).astype(np.float32))
    if 1 in layers:
        d["l1_wq"] = tile_w(f32(inputs["L1_attn_wq"]).T, 512)
        d["l1_wk"] = tile_w(f32(inputs["L1_attn_wk"]).T, 512)
        d["l1_wv"] = tile_w(f32(inputs["L1_attn_wv"]).T, 512)
        d["l1_wo"] = tile_w(f32(inputs["L1_attn_wo"]).T, 512)
        d["l1_bq"] = col128(f32(inputs["L1_attn_bq"]))
        d["l1_bk"] = col128(f32(inputs["L1_attn_bk"]))
        d["l1_bv"] = np.ascontiguousarray(f32(inputs["L1_attn_bv"]).reshape(1, 1024))
        d["l1_bo"] = col128(f32(inputs["L1_attn_bo"]))
        d["l1_g1"] = col128(f32(inputs["L1_attn_g1"]))
        d["l1_b1"] = col128(f32(inputs["L1_attn_b1"]))
        d["l1_g2"] = col128(f32(inputs["L1_attn_g2"]))
        d["l1_b2"] = col128(f32(inputs["L1_attn_b2"]))
        d["l1_w1"] = tile_w(f32(inputs["L1_attn_w1"]).T, 512)
        d["l1_bf1"] = col128(f32(inputs["L1_attn_bf1"]))
        d["l1_w2"] = tile_w(f32(inputs["L1_attn_w2"]).T, 128)
        d["l1_bf2"] = col128(f32(inputs["L1_attn_bf2"]))
        d["l1_wr"] = col128(f32(inputs["L1_ffn_w_router"]))
        d["l1_W1"] = tile_w(f32(inputs["L1_ffn_W1"]), 512)
        d["l1_b1m"] = col128(f32(inputs["L1_ffn_b1"]))
        d["l1_W2"] = tile_w(f32(inputs["L1_ffn_W2"]), 128)
        d["l1_b2m"] = col128(f32(inputs["L1_ffn_b2"]))
        hm = np.zeros((128, 8, 16), dtype=ml_dtypes.bfloat16)
        for p_ in range(128):
            for c_ in range(8):
                hm[p_, c_, 2 * c_ + p_ // 64] = 1
        d["hmask"] = hm
        sc_ = np.zeros((16, 8, 128), dtype=ml_dtypes.bfloat16)
        for h_ in range(16):
            for p_ in range(128):
                if h_ % 2 == p_ // 64:
                    sc_[h_, h_ // 2, p_] = 1
        d["selc"] = sc_
    d["identity"] = np.eye(128, dtype=ml_dtypes.bfloat16)
    d["identityf"] = np.eye(64, dtype=np.float32)
    return d


def build_nc(layers):
    Prog.STRICT_T = False
    g = make_ctx(layers)
    emit_prelude(g)
    for i, L in enumerate(sorted(layers)):
        if i > 0:
            g.p.bar = {e: tk(*g.x_toks) for e in ENGS}
        Prog.STRICT_T = (L == 3)
        ln = emit_ln(g)
        if L == 0:
            emit_l0(g, ln)
        elif L == 1:
            emit_l1(g, ln)
        elif L == 2:
            emit_l2(g, ln)
        elif L == 3:
            emit_l3(g, ln)
        else:
            raise NotImplementedError(f"layer {L}")
    p = g.p
    if "d_stat" in g.dbg:
        w = tk(*g.x_toks)
        s1 = p.dma("sync", g.dbg["d_stat"][:], g.STAT[:], waits=w)
        s2 = p.dma("sync", g.dbg["d_misc"][:], g.MISC[:], waits=w)
        s3 = p.dma("sync", g.dbg["d_sbig"][:], g.SBIGf[:, 8192:16384], waits=w)
        s4 = p.dma("sync", g.dbg["d_sc"][:], g.SCf[:], waits=w)
        p.wait("sync", [s1, s2, s3, s4])
    if "d_ysc" in g.dbg:
        w = tk(*g.x_toks)
        s1 = p.dma("sync", g.dbg["d_ysc"][:], g.ysc[:], waits=w)
        s2 = p.dma("sync", g.dbg["d_zsc"][:], g.zsc[:], waits=w)
        s3 = p.dma("sync", g.dbg["d_cag"][:], g.cag[:], waits=w)
        s4 = p.dma("sync", g.dbg["d_sst"][:], g.sst_out[:], waits=w)
        p.wait("sync", [s1, s2, s3, s4])
    if "d_xn" in g.dbg:
        w = tk(*g.x_toks)
        st1 = p.dma("sync", g.dbg["d_xn"][:], g.XN[:], waits=w)
        st2 = p.dma("sync", g.dbg["d_qp"][:], g.SC[:, 0:4096], waits=w)
        st3 = p.dma("sync", g.dbg["d_kvb"][:], g.SC[:, 4096:6144], waits=w)
        st4 = p.dma("sync", g.dbg["d_z"][:], g.STAT[0:1, 0:2048], waits=w)
        p.wait("sync", [st1, st2, st3, st4])
    emit_store_x(g)
    g.p.emit()
    return g


def run_layers(x_full, inputs, layers):
    g = build_nc(layers)
    wd = prep_weights(inputs, layers)
    in_maps = []
    for c in range(NCORE):
        b, q = c // 4, c % 4
        m = dict(wd)
        m["x_in"] = np.ascontiguousarray(x_full[b, :, q * T:(q + 1) * T])
        if 3 in layers:
            seg = np.zeros((128, 3), np.float32)
            if q >= 1:
                seg[:, q - 1] = 1.0
            m["l3_seg"] = seg
            hm = np.zeros((128, 4), np.float32)
            if q >= 1:
                hm[:, q - 1] = 1.0
            m["l3_hmask"] = hm
        in_maps.append(m)
    res = run_bass_kernel_spmd(g.nc, in_maps, core_ids=list(range(NCORE)))
    out = np.empty_like(x_full)
    for c in range(NCORE):
        b, q = c // 4, c % 4
        out[b, :, q * T:(q + 1) * T] = res.results[c]["y_out"]
    if g.dbg:
        return out, res
    return out


def kernel(**inputs):
    x = np.ascontiguousarray(
        np.transpose(np.asarray(inputs["L0_attn_x"], dtype=np.float32), (0, 2, 1)))
    x = run_layers(x, inputs, [0, 1, 2, 3])
    return np.ascontiguousarray(np.transpose(x, (0, 2, 1)))


# version 05 (final)
# speedup vs baseline: 1.5595x; 1.0360x over previous
import numpy as np
import ml_dtypes
import concourse.bass as bass
import concourse.mybir as mybir
from concourse.bass_utils import run_bass_kernel_spmd

F32 = mybir.dt.float32
BF16 = mybir.dt.bfloat16
AF = mybir.ActivationFunctionType
ALU = mybir.AluOpType
AX = mybir.AxisListType

NCORE = 8
B = 2
S = 8192
D = 1024
T = 2048
FC = D // 128
GROUPS = [[0, 1, 2, 3], [4, 5, 6, 7]]
ENGS = ("tensor", "vector", "scalar", "gpsimd", "sync")


def tk(*toks):
    return tuple(t for t in toks if t is not None)


class Sem:
    __slots__ = ("name", "count", "handle")

    def __init__(self, name):
        self.name = name
        self.count = 0
        self.handle = None


class Prog:
    STRICT_MODE = True

    def __init__(self, nc):
        self.nc = nc
        self.ops = {e: [] for e in ENGS}
        self.sems = []
        self.esem = {e: self.sem("s_" + e) for e in ENGS}
        self.qsem = {e: self.sem("q_" + e) for e in ENGS}
        self.csem = self.sem("s_coll")
        self.ser = {e: self.sem("ser_" + e) for e in Prog.STRICT + ("tensor",)}
        self.last_tok = {}
        self.bar = {}

    def sem(self, name):
        s = Sem(name)
        self.sems.append(s)
        return s

    def add(self, eng, fn, waits=(), inc=None, amt=1, is_dma=False):
        waits = tuple(waits) + self.bar.get(eng, ())
        strict = Prog.STRICT_MODE and fn is not None and (
            eng in Prog.STRICT or (Prog.STRICT_T and eng == "tensor"))
        if strict:
            waits = tuple(waits) + (self.last_tok.get(eng),)
            if inc is None:
                inc = self.ser[eng]
                amt = 16 if (is_dma or amt >= 16) else 1
        waits = tuple(w for w in waits if w is not None)
        tok = None
        if inc is not None:
            inc.count += amt
            tok = (inc, inc.count)
        if strict:
            self.last_tok[eng] = tok
        self.ops[eng].append((fn, waits, inc, amt, is_dma or amt >= 16))
        return tok

    def op(self, eng, fn, waits=()):
        return self.add(eng, fn, waits, inc=self.esem[eng], amt=1)

    def dma(self, queue, out, in_, waits=(), **kw):
        return self.add(queue, lambda e: e.dma_start(out=out, in_=in_, **kw),
                        waits, inc=self.qsem[queue], amt=16, is_dma=True)

    def wait(self, eng, waits):
        self.add(eng, None, waits)

    STRICT = ("vector", "scalar", "gpsimd")
    STRICT_T = False

    def emit(self):
        nc = self.nc
        for s in self.sems:
            s.handle = nc.alloc_semaphore(s.name)
        with nc.Block() as block:
            for e in ENGS:
                ops = self.ops[e]
                if not ops:
                    continue

                def body(eng, ops=ops):
                    hw = {}
                    for fn, waits, inc, amt, isd in ops:
                        for (ws, wv) in waits:
                            if hw.get(id(ws), 0) < wv:
                                eng.wait_ge(ws.handle, wv)
                                hw[id(ws)] = wv
                        if fn is None:
                            continue
                        inst = fn(eng)
                        if inc is not None:
                            inst.then_inc(inc.handle, amt)
                getattr(block, e)(body)


class Bank:
    __slots__ = ("t", "rd")

    def __init__(self, t):
        self.t = t
        self.rd = None


class Ctx:
    pass


def make_ctx(layers):
    nc = bass.Bass("TRN2", target_bir_lowering=False, debug=False, num_devices=NCORE)
    g = Ctx()
    g.nc = nc
    g.layers = layers
    g.dram = {}

    def din(name, shape, dtype=F32):
        g.dram[name] = nc.dram_tensor(name, list(shape), dtype, kind="ExternalInput").ap()
        return g.dram[name]

    g.din = din
    din("x_in", [D, T])
    din("identityf", [64, 64], F32)
    g.y_out = nc.dram_tensor("y_out", [D, T], F32, kind="ExternalOutput").ap()
    din("identity", [128, 128], BF16)

    if 0 in layers:
        din("l0_qp", [128, 1, 8, 256])
        din("l0_kp", [128, 1, 8, 256])
        din("l0_w1", [128, 16, 8, 256])
        din("l0_w2", [128, 16, 8, 256])
        din("l0_h1", [128, 32, 32, 128])
        din("l0_h2", [128, 32, 32, 128])
        din("l0_w3", [128, 8, 32, 128])
        din("l0_s1", [128, 32])
        din("l0_s2", [128, 32])
        din("l0_sh1", [128, 32])
        din("l0_sh2", [128, 32])
        din("l0_s3", [128, 8])
        g.kv_in = nc.dram_tensor("l0_kv_in", [256, D], F32, kind="Internal").ap()
        g.kv_out = nc.dram_tensor("l0_kv_out", [256, D], F32, kind="Internal").ap()

    if 1 in layers:
        din("l1_wq", [128, 2, 8, 512])
        din("l1_wk", [128, 2, 8, 512])
        din("l1_wv", [128, 2, 8, 512])
        din("l1_wo", [128, 2, 8, 512])
        din("l1_bq", [128, 8])
        din("l1_bk", [128, 8])
        din("l1_bv", [1, 1024])
        din("l1_bo", [128, 8])
        din("l1_g1", [128, 8])
        din("l1_b1", [128, 8])
        din("l1_g2", [128, 8])
        din("l1_b2", [128, 8])
        din("l1_w1", [128, 8, 8, 512])
        din("l1_bf1", [128, 32])
        din("l1_w2", [128, 8, 32, 128])
        din("l1_bf2", [128, 8])
        din("l1_wr", [128, 8])
        din("l1_W1", [128, 8, 8, 512])
        din("l1_b1m", [128, 32])
        din("l1_W2", [128, 8, 32, 128])
        din("l1_b2m", [128, 8])
        din("hmask", [128, 8, 16], BF16)
        din("selc", [16, 8, 128], BF16)
        g.nag_in = nc.dram_tensor("l1_nag_in", [T, 16], F32, kind="Internal").ap()
        g.nag_out = nc.dram_tensor("l1_nag_out", [S, 16], F32, kind="Internal").ap()
        g.sag_in = nc.dram_tensor("l1_sag_in", [T, 1], F32, kind="Internal").ap()
        g.sag_out = nc.dram_tensor("l1_sag_out", [S, 1], F32, kind="Internal").ap()
        g.bnc = nc.dram_tensor("l1_bnc", [128, 1], F32, kind="Internal").ap()

    if 2 in layers:
        din("l2_wq", [128, 2, 8, 512])
        din("l2_wk", [128, 2, 8, 512])
        din("l2_wv", [128, 2, 8, 512])
        din("l2_wo", [128, 2, 8, 512])
        din("l2_wck", [128, 32, 64])
        din("l2_wcv", [128, 32, 64])
        din("l2_bck", [64, 1])
        din("l2_bcv", [64, 1])
        din("l2_wgb", [128, 3])
        din("l2_bgr", [1, 48])
        din("l2_wr", [128, 1, 8, 8])
        din("l2_br", [1, 8])
        din("l2_W1", [8, 128, 8, 8, 512])
        din("l2_b1", [128, 256])
        din("l2_W2", [8, 128, 8, 32, 128])
        din("l2_b2", [128, 64])
        g.vfag_in = nc.dram_tensor("l2_vfag_in", [D, T], BF16, kind="Internal").ap()
        g.vfag_out = nc.dram_tensor("l2_vfag_out", [4 * D, T], BF16, kind="Internal").ap()
        g.ktm_in = nc.dram_tensor("l2_ktm_in", [T, D], BF16, kind="Internal").ap()
        g.ktm_out = nc.dram_tensor("l2_ktm_out", [S, D], BF16, kind="Internal").ap()
        g.ksel = nc.dram_tensor("l2_ksel", [D, 1024], BF16, kind="Internal").ap()
        g.vsel = nc.dram_tensor("l2_vsel", [1024, D], BF16, kind="Internal").ap()
        din("iota64", [128, 1])
        g.bi_in = nc.dram_tensor("l2_bi_in", [1, 511], F32, kind="Internal").ap()
        g.bi_out = nc.dram_tensor("l2_bi_out", [1, 511], F32, kind="Internal").ap()
        g.qs_in = nc.dram_tensor("l2_qs_in", [128, 8], F32, kind="Internal").ap()
        g.qs_out = nc.dram_tensor("l2_qs_out", [128, 8], F32, kind="Internal").ap()
        g.wrow_d = nc.dram_tensor("l2_wrow_d", [8, T], F32, kind="Internal").ap()
    if 3 in layers:
        din("l3_win", [128, 17, 8, 256])
        din("l3_wout", [128, 4, 16, 256])
        din("l3_convw", [128, 17, 4])
        din("l3_convb", [128, 17])
        din("l3_dtb", [32, 1])
        din("l3_alog", [32, 1])
        din("l3_dcol", [128, 16])
        din("l3_nw", [128, 16])
        din("l3_keys", [16, 128, 1, 8, 128])
        din("l3_vals", [16, 128, 1024])
        din("l3_selt", [128, 8, 16])
        din("l3_seg", [128, 3])
        din("l3_hmask", [128, 4])
        din("trineg", [128, 128], F32)
        g.ysc = nc.dram_tensor("l3_ysc", [2 * D, T], BF16, kind="Internal").ap()
        g.zsc = nc.dram_tensor("l3_zsc", [2 * D, T], BF16, kind="Internal").ap()
        g.cag = nc.dram_tensor("l3_cag", [32, T], F32, kind="Internal").ap()
        g.hag_in = nc.dram_tensor("l3_hag_in", [D, 3], BF16, kind="Internal").ap()
        g.hag_out = nc.dram_tensor("l3_hag_out", [4 * D, 3], BF16, kind="Internal").ap()
        g.sst_in = nc.dram_tensor("l3_sst_in", [64, 2048], F32, kind="Internal").ap()
        g.sst_out = nc.dram_tensor("l3_sst_out", [256, 2048], F32, kind="Internal").ap()
        g.a_in = nc.dram_tensor("l3_a_in", [1, 32], F32, kind="Internal").ap()
        g.a_out = nc.dram_tensor("l3_a_out", [4, 32], F32, kind="Internal").ap()
    if (1 in layers) or (2 in layers):
        g.kag_in = nc.dram_tensor("ag_kag_in", [D, T], BF16, kind="Internal").ap()
        g.kag_out = nc.dram_tensor("ag_kag_out", [4 * D, T], BF16, kind="Internal").ap()
        g.vag_in = nc.dram_tensor("ag_vag_in", [T, D], BF16, kind="Internal").ap()
        g.vag_out = nc.dram_tensor("ag_vag_out", [S, D], BF16, kind="Internal").ap()

    sb = nc.alloc_sbuf_tensor
    g.X = sb("X", [128, FC, T], F32)
    g.XN = sb("XN", [128, FC, T], BF16)
    g.SBIGf = sb("SBIG", [128, 16384], F32)
    g.SBIG = g.SBIGf.bitcast(BF16)
    g.SC = sb("SC", [128, 8192], BF16)
    g.SCf = g.SC.bitcast(F32)
    g.WB = sb("WB", [128, 2, 4096], BF16)
    g.MISC = sb("MISC", [128, 520], F32)
    g.MISCI = g.MISC.bitcast(mybir.dt.int32)
    g.STAT = sb("STAT", [1, 2048], F32)
    g.IDN = sb("IDN", [128, 128], BF16)
    g.IDNF = sb("IDNF", [64, 64], F32)
    if 3 in layers:
        g.TRINEG = g.SBIGf[:, 16256:16384]
    g.ONESB = sb("ONESB", [128, 1], BF16)
    g.ONESF = sb("ONESF", [128, 1], F32)
    g.ONESR = sb("ONESR", [1, 128], F32)
    g.ONESRB = sb("ONESRB", [1, 128], BF16)
    if 1 in layers:
        g.HM = sb("HM", [128, 8, 16], BF16)
        g.SELC = sb("SELC", [16, 8, 128], BF16)
    if (1 in layers) or (2 in layers):
        g.BROW = sb("BROW", [1, 1024], BF16)

    ps = nc.alloc_psum_tensor
    g.PS = [Bank(ps(f"ps{i}", [128, 512], F32)) for i in range(6)]
    g.PT = [Bank(ps("pst0", [128, 256], BF16)), Bank(ps("pst1", [128, 256], BF16))]

    g.dbg = {}
    if getattr(make_ctx, "DBG", False):
        for nm, shape, dt in [("d_xn", [128, FC, T], BF16), ("d_qp", [128, 4096], BF16),
                              ("d_kvb", [128, 2048], BF16), ("d_z", [1, 2048], F32)]:
            g.dbg[nm] = nc.dram_tensor(nm, shape, dt, kind="ExternalOutput").ap()
    if getattr(make_ctx, "DBG3", False):
        for nm, shape, dt in [("d_ysc", [2 * D, T], BF16), ("d_zsc", [2 * D, T], BF16),
                              ("d_cag", [32, T], F32), ("d_sst", [256, 2048], F32),
                              ("d_wq0", [128, 8192], BF16), ("d_nr0", [1, 512], F32)]:
            g.dbg[nm] = nc.dram_tensor(nm, shape, dt, kind="ExternalOutput").ap()
    if getattr(make_ctx, "DBG2", False):
        for nm, shape, dt in [("d_stat", [1, 2048], F32), ("d_misc", [128, 520], F32),
                              ("d_sbig", [128, 8192], F32), ("d_sc", [128, 4096], F32),
                              ("d_bi", [1, 511], F32), ("d_idx", [1, 16], F32)]:
            g.dbg[nm] = nc.dram_tensor(nm, shape, dt, kind="ExternalOutput").ap()

    g.p = Prog(nc)
    return g


def emit_prelude(g):
    p = g.p
    g.tok_idn = p.dma("sync", g.IDN[:], g.dram["identity"][:])
    g.tok_idnf = p.dma("sync", g.IDNF[:], g.dram["identityf"][:])
    p.op("vector", lambda e: e.memset(g.ONESB[:], 1.0))
    p.op("vector", lambda e: e.memset(g.ONESF[:], 1.0))
    p.op("vector", lambda e: e.memset(g.ONESR[:], 1.0))
    g.tok_ones = p.op("vector", lambda e: e.memset(g.ONESRB[:], 1.0))
    x = g.dram["x_in"].rearrange("(c p) t -> p c t", p=128)
    t = p.dma("sync", g.X[:], x)
    g.x_toks = [t]


def emit_store_x(g):
    p = g.p
    y = g.y_out.rearrange("(c p) t -> p c t", p=128)
    st = p.dma("sync", y, g.X[:], waits=tk(*g.x_toks))
    p.wait("sync", [st])


def emit_ln(g, src=None, dst=None, src_toks=None, gb=None, sqreg=None, tmpreg=None):
    p = g.p
    if src is None:
        src = g.X
    if dst is None:
        dst = g.XN
    if src_toks is None:
        src_toks = g.x_toks
    if sqreg is None:
        sqreg = g.SC
    if tmpreg is None:
        tmpreg = g.SBIGf
    is_f32 = (src.dtype == F32)
    ones = g.ONESF if is_f32 else g.ONESB
    ln_done = [[None] * 4 for _ in range(FC)]
    sq_guard = None
    bc_banks = [g.PS[4], g.PS[5]]
    for t in range(4):
        ts = slice(t * 512, (t + 1) * 512)
        s0 = g.PS[(2 * t) % 4]
        s1 = g.PS[(2 * t + 1) % 4]
        mm0 = mm1 = None
        for c in range(FC):
            slot = sqreg[:, c * 512:(c + 1) * 512]
            sqt = p.op("vector", lambda e, c=c, ts=ts, slot=slot: e.tensor_mul(
                slot, src[:, c, ts], src[:, c, ts]),
                waits=tk(*src_toks, sq_guard))
            mm0 = p.add("tensor", lambda e, c=c, ts=ts, s0=s0: e.matmul(
                s0.t[0:1, 0:512], ones[:], src[:, c, ts],
                start=(c == 0), stop=(c == FC - 1)),
                waits=tk(*(list(src_toks) + [g.tok_ones, s0.rd] if c == 0 else ())),
                inc=p.esem["tensor"] if c == FC - 1 else None)
            mm1 = p.add("tensor", lambda e, c=c, s1=s1, slot=slot: e.matmul(
                s1.t[0:1, 0:512], g.ONESB[:], slot,
                start=(c == 0), stop=(c == FC - 1)),
                waits=tk(sqt, s1.rd if c == 0 else None),
                inc=p.esem["tensor"] if c == FC - 1 else None)
        sq_guard = mm1
        mrow = g.STAT[0:1, 0:512]
        vrow = g.STAT[0:1, 512:1024]
        trow = g.STAT[0:1, 1024:1536]
        rrow = g.STAT[0:1, 1536:2048]
        t_m = p.op("vector", lambda e, s0=s0: e.tensor_scalar_mul(mrow, s0.t[0:1, 0:512], 1.0 / D), waits=tk(mm0))
        s0.rd = t_m
        t_v = p.op("vector", lambda e, s1=s1: e.tensor_scalar_mul(vrow, s1.t[0:1, 0:512], 1.0 / D), waits=tk(mm1))
        s1.rd = t_v
        t_m2 = p.op("vector", lambda e: e.tensor_mul(trow, mrow, mrow), waits=tk(t_m))
        t_v2 = p.op("vector", lambda e: e.tensor_sub(vrow, vrow, trow), waits=tk(t_v, t_m2))
        t_v3 = p.op("vector", lambda e: e.tensor_scalar_add(vrow, vrow, 1e-5), waits=tk(t_v2))
        t_v4 = p.op("vector", lambda e: e.reciprocal(vrow, vrow), waits=tk(t_v3))
        t_r = p.op("scalar", lambda e: e.activation(rrow, vrow, AF.Sqrt), waits=tk(t_v4))
        bm = bc_banks[0]
        br = bc_banks[1]
        t_bm = p.op("tensor", lambda e, bm=bm: e.matmul(
            bm.t[:, 0:512], g.ONESR[:], mrow, start=True, stop=True), waits=tk(t_m, bm.rd, g.tok_ones))
        t_br = p.op("tensor", lambda e, br=br: e.matmul(
            br.t[:, 0:512], g.ONESR[:], rrow, start=True, stop=True), waits=tk(t_r, br.rd))
        last_a = []
        for c in range(FC):
            tmp = tmpreg[:, (c % 2) * 512:(c % 2) * 512 + 512]
            a1 = p.op("vector", lambda e, c=c, ts=ts, tmp=tmp, bm=bm: e.tensor_sub(
                tmp, src[:, c, ts], bm.t[:, 0:512]), waits=tk(t_bm, ln_done[c - 2][t] if c >= 2 else None))
            if gb is None:
                a2 = p.op("vector", lambda e, c=c, ts=ts, tmp=tmp, br=br: e.tensor_mul(
                    dst[:, c, ts], tmp, br.t[:, 0:512]), waits=tk(a1, t_br))
            else:
                gcol, bcol = gb
                a2 = p.op("vector", lambda e, c=c, ts=ts, tmp=tmp, br=br, gcol=gcol: e.scalar_tensor_tensor(
                    dst[:, c, ts], tmp, gcol[:, c:c + 1], br.t[:, 0:512], ALU.mult, ALU.mult),
                    waits=tk(a1, t_br))
                a2 = p.op("vector", lambda e, c=c, ts=ts, bcol=bcol: e.tensor_scalar_add(
                    dst[:, c, ts], dst[:, c, ts], bcol[:, c:c + 1]), waits=tk(a2))
            ln_done[c][t] = a2
            last_a.append(a2)
        bm.rd = last_a[-1]
        br.rd = last_a[-1]
    return ln_done


def emit_l0(g, ln):
    p = g.p
    dr = g.dram
    QP = lambda e, ts: g.SC[:, e * 2048 + ts.start: e * 2048 + ts.stop]
    KTM = lambda ti: g.SC[:, 4096 + ti * 256: 4096 + (ti + 1) * 256]
    XNT = g.SBIG[:, 0:16384]
    KVP = g.SBIGf[:, 8192:10240]
    KVB = lambda e: g.SC[:, 4096 + e * 1024: 4096 + (e + 1) * 1024]

    wq = g.WB[:, 0, 0:2048].rearrange("p (k o) -> p k o", k=8)
    wk = g.WB[:, 1, 0:2048].rearrange("p (k o) -> p k o", k=8)
    t_wq = p.dma("gpsimd", wq, dr["l0_qp"][:, 0], waits=tk())
    t_wk = p.dma("gpsimd", wk, dr["l0_kp"][:, 0], waits=tk())

    tok_qfm = [[None] * 4 for _ in range(2)]
    for e2 in range(2):
        for t in range(4):
            ts = slice(t * 512, (t + 1) * 512)
            bank = g.PS[(e2 * 4 + t) % 4]
            mm = None
            for c in range(FC):
                mm = p.add("tensor", lambda e, c=c, e2=e2, ts=ts, bank=bank: e.matmul(
                    bank.t[:, 0:512], g.WB[:, 0, c * 256 + e2 * 128: c * 256 + e2 * 128 + 128],
                    g.XN[:, c, ts], start=(c == 0), stop=(c == FC - 1)),
                    waits=tk(ln[c][t], *( (t_wq, bank.rd) if c == 0 else ())),
                    inc=p.esem["tensor"] if c == FC - 1 else None)
            ev = p.op("scalar", lambda e, e2=e2, ts=ts, bank=bank: e.activation(
                QP(e2, ts), bank.t[:, 0:512], AF.Exp), waits=tk(mm))
            bank.rd = ev
            tok_qfm[e2][t] = ev

    tok_ktm = [None] * 16
    for ti in range(16):
        bank = g.PS[4 + (ti % 2)]
        mm = None
        for c in range(FC):
            mm = p.add("tensor", lambda e, c=c, ti=ti, bank=bank: e.matmul(
                bank.t[:, 0:256], g.XN[:, c, ti * 128:(ti + 1) * 128],
                g.WB[:, 1, c * 256: c * 256 + 256], start=(c == 0), stop=(c == FC - 1)),
                waits=tk(ln[c][ti // 4], *( (t_wk, bank.rd) if c == 0 else ())),
                inc=p.esem["tensor"] if c == FC - 1 else None)
        ev = p.op("scalar", lambda e, ti=ti, bank=bank: e.activation(
            KTM(ti), bank.t[:, 0:256], AF.Exp), waits=tk(mm))
        bank.rd = ev
        tok_ktm[ti] = ev

    tok_xnt = [None] * 16
    for ti in range(16):
        last = None
        for c in range(FC):
            pb = g.PT[(ti * FC + c) % 2]
            tt = p.op("tensor", lambda e, c=c, ti=ti, pb=pb: e.transpose(
                pb.t[:, 0:128], g.XN[:, c, ti * 128:(ti + 1) * 128], g.IDN[:]),
                waits=tk(ln[c][ti // 4], g.tok_idn, pb.rd))
            cp = p.op("vector", lambda e, c=c, ti=ti, pb=pb: e.tensor_copy(
                XNT[:, ti * 1024 + c * 128: ti * 1024 + (c + 1) * 128], pb.t[:, 0:128]),
                waits=tk(tt))
            pb.rd = cp
            last = cp
        tok_xnt[ti] = last

    tok_kvp = []
    for e2 in range(2):
        for dh in range(2):
            bank = g.PS[(e2 * 2 + dh) % 4]
            mm = None
            for ti in range(16):
                mm = p.add("tensor", lambda e, e2=e2, dh=dh, ti=ti, bank=bank: e.matmul(
                    bank.t[:, 0:512], KTM(ti)[:, e2 * 128:(e2 + 1) * 128],
                    XNT[:, ti * 1024 + dh * 512: ti * 1024 + dh * 512 + 512],
                    start=(ti == 0), stop=(ti == 15)),
                    waits=tk(tok_ktm[ti], tok_xnt[ti], bank.rd if ti == 0 else None),
                    inc=p.esem["tensor"] if ti == 15 else None)
            ev = p.op("vector", lambda e, e2=e2, dh=dh, bank=bank: e.tensor_copy(
                KVP[:, e2 * 1024 + dh * 512: e2 * 1024 + dh * 512 + 512], bank.t[:, 0:512]),
                waits=tk(mm))
            bank.rd = ev
            tok_kvp.append(ev)

    zrow = g.STAT[0:1, 0:2048]
    tok_z = []
    for t in range(4):
        bank = g.PS[4 + (t % 2)]
        mm = None
        for e2 in range(2):
            mm = p.add("tensor", lambda e, e2=e2, t=t, bank=bank: e.matmul(
                bank.t[0:1, 0:512], g.ONESB[:], QP(e2, slice(t * 512, (t + 1) * 512)),
                start=(e2 == 0), stop=(e2 == 1)),
                waits=tk(tok_qfm[e2][t], bank.rd if e2 == 0 else None),
                inc=p.esem["tensor"] if e2 == 1 else None)
        ev = p.op("vector", lambda e, t=t, bank=bank: e.tensor_scalar_add(
            zrow[0:1, t * 512:(t + 1) * 512], bank.t[0:1, 0:512], 1e-8), waits=tk(mm))
        bank.rd = ev
        tok_z.append(ev)
    tok_zr = p.op("vector", lambda e: e.reciprocal(zrow, zrow), waits=tk(*tok_z))

    t_kvst = p.dma("sync", g.kv_in.rearrange("(e p) d -> p e d", p=128),
                   KVP.rearrange("p (e d) -> p e d", e=2), waits=tk(*tok_kvp))
    t_coll = p.add("gpsimd", lambda e: e.collective_compute(
        "AllReduce", ALU.add, replica_groups=GROUPS, ins=[g.kv_in[:]], outs=[g.kv_out[:]]),
        waits=tk(t_kvst), inc=p.csem, amt=1)

    HA = g.SBIG[:, 0:16384].rearrange("p (o t) -> p o t", o=32)
    HB = g.SBIG[:, 16384:32768].rearrange("p (o t) -> p o t", o=32)
    t_s1 = p.dma("sync", g.MISC[:, 0:32], dr["l0_s1"][:])
    t_s2 = p.dma("sync", g.MISC[:, 32:64], dr["l0_s2"][:])
    t_sh1 = p.dma("sync", g.MISC[:, 64:96], dr["l0_sh1"][:])
    t_sh2 = p.dma("sync", g.MISC[:, 96:128], dr["l0_sh2"][:])
    t_s3 = p.dma("sync", g.MISC[:, 128:136], dr["l0_s3"][:])

    wbtok = [None, None]
    wbtok[0] = tok_qfm[1][3]
    wbtok[1] = tok_ktm[15]
    silu_rd = [None, None]
    ffn_x_toks = []
    for t in range(4):
        ts = slice(t * 512, (t + 1) * 512)
        for og in range(16):
            b01 = og % 2
            wdst1 = g.WB[:, b01, 0:2048].rearrange("p (k o) -> p k o", k=8)
            wdst2 = g.WB[:, b01, 2048:4096].rearrange("p (k o) -> p k o", k=8)
            tw1 = p.dma("gpsimd", wdst1, dr["l0_w1"][:, og], waits=tk(wbtok[b01]))
            tw2 = p.dma("gpsimd", wdst2, dr["l0_w2"][:, og], waits=tk(wbtok[b01]))
            for oc in range(2):
                oabs = og * 2 + oc
                ba = g.PS[(2 * oc) % 4]
                bb = g.PS[(2 * oc + 1) % 4]
                m1 = m2 = None
                for c in range(FC):
                    m1 = p.add("tensor", lambda e, c=c, b01=b01, oc=oc, ts=ts, ba=ba: e.matmul(
                        ba.t[:, 0:512], g.WB[:, b01, c * 256 + oc * 128: c * 256 + oc * 128 + 128],
                        g.XN[:, c, ts], start=(c == 0), stop=(c == FC - 1)),
                        waits=tk(ln[c][t], *( (tw1, ba.rd) if c == 0 else ())),
                        inc=p.esem["tensor"] if c == FC - 1 else None)
                for c in range(FC):
                    m2 = p.add("tensor", lambda e, c=c, b01=b01, oc=oc, ts=ts, bb=bb: e.matmul(
                        bb.t[:, 0:512], g.WB[:, b01, 2048 + c * 256 + oc * 128: 2048 + c * 256 + oc * 128 + 128],
                        g.XN[:, c, ts], start=(c == 0), stop=(c == FC - 1)),
                        waits=tk(*( (tw2, bb.rd) if c == 0 else ())),
                        inc=p.esem["tensor"] if c == FC - 1 else None)
                wbtok[b01] = m2
                slot = g.SC[:, 6144 + (oabs % 2) * 512: 6144 + (oabs % 2) * 512 + 512]
                sv = p.op("scalar", lambda e, oabs=oabs, ba=ba, slot=slot: e.activation(
                    slot, ba.t[:, 0:512], AF.Silu, scale=g.MISC[:, oabs:oabs + 1]),
                    waits=tk(m1, t_s1, silu_rd[oabs % 2]))
                ba.rd = sv
                hv = p.op("vector", lambda e, oabs=oabs, t=t, bb=bb, slot=slot: e.scalar_tensor_tensor(
                    HA[:, oabs, 0:512],
                    bb.t[:, 0:512], g.MISC[:, 32 + oabs:33 + oabs], slot, ALU.mult, ALU.mult),
                    waits=tk(m2, sv, t_s2,
                             tok_kvp[-1] if (t == 0 and og == 0 and oc == 0) else None))
                bb.rd = hv
                silu_rd[oabs % 2] = hv
        for og in range(32):
            b01 = og % 2
            wdst = g.WB[:, b01, :].rearrange("p (k o) -> p k o", k=32)
            twk = p.dma("gpsimd", wdst, dr["l0_h1"][:, og], waits=tk(wbtok[b01]))
            bank = g.PS[og % 4]
            mm = None
            for k in range(32):
                mm = p.add("tensor", lambda e, k=k, b01=b01, bank=bank: e.matmul(
                    bank.t[:, 0:512], g.WB[:, b01, k * 128:(k + 1) * 128],
                    HA[:, k, 0:512], start=(k == 0), stop=(k == 31)),
                    waits=tk(*( (twk, bank.rd, hv) if k == 0 else ())),
                    inc=p.esem["tensor"] if k == 31 else None)
            wbtok[b01] = mm
            gv = p.op("scalar", lambda e, og=og, bank=bank: e.activation(
                HB[:, og, 0:512], bank.t[:, 0:512], AF.Gelu, scale=g.MISC[:, 64 + og:65 + og]),
                waits=tk(mm, t_sh1, t_kvst if og == 0 else None))
            bank.rd = gv
            if og == 31:
                hb_last = gv
        for og in range(32):
            b01 = og % 2
            wdst = g.WB[:, b01, :].rearrange("p (k o) -> p k o", k=32)
            twk = p.dma("gpsimd", wdst, dr["l0_h2"][:, og], waits=tk(wbtok[b01]))
            bank = g.PS[og % 4]
            mm = None
            for k in range(32):
                mm = p.add("tensor", lambda e, k=k, b01=b01, bank=bank: e.matmul(
                    bank.t[:, 0:512], g.WB[:, b01, k * 128:(k + 1) * 128],
                    HB[:, k, 0:512], start=(k == 0), stop=(k == 31)),
                    waits=tk(*( (twk, bank.rd, hb_last) if k == 0 else ())),
                    inc=p.esem["tensor"] if k == 31 else None)
            wbtok[b01] = mm
            gv = p.op("scalar", lambda e, og=og, bank=bank: e.activation(
                HA[:, og, 0:512], bank.t[:, 0:512], AF.Gelu, scale=g.MISC[:, 96 + og:97 + og]),
                waits=tk(mm, t_sh2))
            bank.rd = gv
            if og == 31:
                hc_last = gv
        for og in range(FC):
            b01 = og % 2
            wdst = g.WB[:, b01, :].rearrange("p (k o) -> p k o", k=32)
            twk = p.dma("gpsimd", wdst, dr["l0_w3"][:, og], waits=tk(wbtok[b01]))
            bank = g.PS[og % 4]
            mm = None
            for k in range(32):
                mm = p.add("tensor", lambda e, k=k, b01=b01, bank=bank: e.matmul(
                    bank.t[:, 0:512], g.WB[:, b01, k * 128:(k + 1) * 128],
                    HA[:, k, 0:512], start=(k == 0), stop=(k == 31)),
                    waits=tk(*( (twk, bank.rd, hc_last) if k == 0 else ())),
                    inc=p.esem["tensor"] if k == 31 else None)
            wbtok[b01] = mm
            xv = p.op("vector", lambda e, og=og, ts=ts, bank=bank: e.scalar_tensor_tensor(
                g.X[:, og, ts], bank.t[:, 0:512], g.MISC[:, 128 + og:129 + og],
                g.X[:, og, ts], ALU.mult, ALU.add),
                waits=tk(mm, t_s3))
            bank.rd = xv
            ffn_x_toks.append(xv)

    t_kvb = p.dma("gpsimd", g.SC[:, 4096:6144].rearrange("p (e d) -> p e d", e=2),
                  g.kv_out.rearrange("(e p) d -> p e d", p=128),
                  waits=tk(t_coll, *tok_kvp))

    attn_x_toks = []
    zbc_rd = [None, None]
    tmp_rd = [None, None]
    for t in range(4):
        ts = slice(t * 512, (t + 1) * 512)
        bz = g.PS[4 + (t % 2)]
        t_zb = p.op("tensor", lambda e, t=t, bz=bz: e.matmul(
            bz.t[:, 0:512], g.ONESR[:], zrow[0:1, t * 512:(t + 1) * 512],
            start=True, stop=True), waits=tk(tok_zr, bz.rd))
        zslot = g.SBIGf[:, (t % 2) * 512:(t % 2) * 512 + 512]
        t_zs = p.op("vector", lambda e, bz=bz, zslot=zslot: e.tensor_copy(zslot, bz.t[:, 0:512]),
                    waits=tk(t_zb, zbc_rd[t % 2], *(ffn_x_toks[-1:] if t == 0 else ())))
        bz.rd = t_zs
        for dch in range(FC):
            bank = g.PS[dch % 4]
            mm = None
            for e2 in range(2):
                mm = p.add("tensor", lambda e, e2=e2, dch=dch, ts=ts, bank=bank: e.matmul(
                    bank.t[:, 0:512], KVB(e2)[:, dch * 128:(dch + 1) * 128],
                    QP(e2, ts), start=(e2 == 0), stop=(e2 == 1)),
                    waits=tk(*( (t_kvb, bank.rd) if e2 == 0 else ())),
                    inc=p.esem["tensor"] if e2 == 1 else None)
            tmp = g.SBIGf[:, 1024 + (dch % 2) * 512: 1024 + (dch % 2) * 512 + 512]
            m1 = p.op("vector", lambda e, tmp=tmp, bank=bank, zslot=zslot: e.tensor_mul(
                tmp, bank.t[:, 0:512], zslot), waits=tk(mm, t_zs, tmp_rd[dch % 2]))
            bank.rd = m1
            m2 = p.op("vector", lambda e, dch=dch, ts=ts, tmp=tmp: e.tensor_add(
                g.X[:, dch, ts], g.X[:, dch, ts], tmp), waits=tk(m1))
            tmp_rd[dch % 2] = m2
            attn_x_toks.append(m2)
        zbc_rd[t % 2] = attn_x_toks[-1]

    g.x_toks = ffn_x_toks + attn_x_toks


TEMP = 8.0
MXN = 1.0 - 1e-5


def _clip_heads(g, A, a_toks, wk_out=None):
    p = g.p
    done = [[None] * 4 for _ in range(FC)]
    rsb = g.SCf[0:16, 3584:4096]
    fsb = g.SC[0:16, 6656:7168]
    for t in range(4):
        ts = slice(t * 512, (t + 1) * 512)
        nb = g.PS[4 + (t % 2)]
        mm = None
        sq_toks = []
        for c in range(FC):
            slot = g.SC[:, c * 512:(c + 1) * 512]
            sq = p.op("vector", lambda e, c=c, ts=ts, slot=slot: e.tensor_mul(
                slot, A[:, c, ts], A[:, c, ts]),
                waits=tk(a_toks[c][t], _clip_heads_guard.get(id(g), None)))
            mm = p.add("tensor", lambda e, c=c, slot=slot, nb=nb: e.matmul(
                nb.t[0:16, 0:512], g.HM[:, c, :], slot,
                start=(c == 0), stop=(c == FC - 1)),
                waits=tk(sq, g.tok_hm if c == 0 else None, nb.rd if c == 0 else None),
                inc=p.esem["tensor"] if c == FC - 1 else None)
            sq_toks.append(sq)
        _clip_heads_guard[id(g)] = mm
        if wk_out is not None:
            nk = p.op("vector", lambda e, nb=nb: e.tensor_scalar(
                rsb, nb.t[0:16, 0:512], MXN * MXN, -1.0 / TEMP, ALU.min, ALU.mult),
                waits=tk(mm))
            for q in range(4):
                pb = g.PS[(t * 4 + q) % 4]
                tt = p.op("tensor", lambda e, q=q, pb=pb: e.transpose(
                    pb.t[:, 0:16], rsb[0:16, q * 128:(q + 1) * 128], g.IDNF[0:16, 0:16]),
                    waits=tk(nk, pb.rd, g.tok_idnf))
                cp = p.op("vector", lambda e, t=t, q=q, pb=pb: e.tensor_copy(
                    g.MISC[:, 256 + (t * 4 + q) * 16: 256 + (t * 4 + q) * 16 + 16], pb.t[:, 0:16]),
                    waits=tk(tt))
                pb.rd = cp
                _nk2_toks.append(cp)
        r1 = p.op("vector", lambda e, nb=nb: e.reciprocal(rsb, nb.t[0:16, 0:512]), waits=tk(mm, *( [_nk2_toks[-1]] if wk_out is not None else [] )))
        nb.rd = r1
        r2 = p.op("scalar", lambda e: e.activation(fsb, rsb, AF.Sqrt, scale=MXN * MXN), waits=tk(r1))
        r3 = p.op("vector", lambda e: e.tensor_scalar_min(fsb, fsb, 1.0), waits=tk(r2))
        for c in range(FC):
            bb = g.PS[(c % 2)]
            bc = p.op("tensor", lambda e, c=c, bb=bb: e.matmul(
                bb.t[:, 0:512], g.SELC[:, c, :], fsb, start=True, stop=True),
                waits=tk(r3, bb.rd, g.tok_selc))
            mu = p.op("vector", lambda e, c=c, ts=ts, bb=bb: e.tensor_mul(
                A[:, c, ts], A[:, c, ts], bb.t[:, 0:512]),
                waits=tk(bc, sq_toks[c]))
            bb.rd = mu
            done[c][t] = mu
    return done


_clip_heads_guard = {}
_nk2_toks = []


def _proj_fullvec(g, A, a_toks, sq_guard_key):
    p = g.p
    done = [[None] * 4 for _ in range(FC)]
    rrow = g.SCf[0:1, 3104:3616]
    for t in range(4):
        ts = slice(t * 512, (t + 1) * 512)
        nb = g.PS[4 + (t % 2)]
        mm = None
        sq_toks = []
        for c in range(FC):
            slot = g.SC[:, c * 512:(c + 1) * 512]
            sq = p.op("vector", lambda e, c=c, ts=ts, slot=slot: e.tensor_mul(
                slot, A[:, c, ts], A[:, c, ts]),
                waits=tk(a_toks[c][t] if a_toks else None, _clip_heads_guard.get(sq_guard_key, None)))
            mm = p.add("tensor", lambda e, c=c, slot=slot, nb=nb: e.matmul(
                nb.t[0:1, 0:512], g.ONESB[:], slot,
                start=(c == 0), stop=(c == FC - 1)),
                waits=tk(sq, nb.rd if c == 0 else None),
                inc=p.esem["tensor"] if c == FC - 1 else None)
            sq_toks.append(sq)
        _clip_heads_guard[sq_guard_key] = mm
        r1 = p.op("vector", lambda e, nb=nb: e.reciprocal(rrow, nb.t[0:1, 0:512]), waits=tk(mm))
        nb.rd = r1
        r2 = p.op("scalar", lambda e: e.activation(rrow, rrow, AF.Sqrt, scale=MXN * MXN), waits=tk(r1))
        r3 = p.op("vector", lambda e: e.tensor_scalar_min(rrow, rrow, 1.0), waits=tk(r2))
        bb = g.PS[(t % 2)]
        bc = p.op("tensor", lambda e, bb=bb: e.matmul(
            bb.t[:, 0:512], g.ONESR[:], rrow, start=True, stop=True),
            waits=tk(r3, bb.rd))
        last = None
        for c in range(FC):
            mu = p.op("vector", lambda e, c=c, ts=ts, bb=bb: e.tensor_mul(
                A[:, c, ts], A[:, c, ts], bb.t[:, 0:512]),
                waits=tk(bc, sq_toks[c]))
            done[c][t] = mu
            last = mu
        bb.rd = last
    return done


def emit_l1(g, ln):
    p = g.p
    dr = g.dram
    _clip_heads_guard.clear()
    _nk2_toks.clear()
    g.tok_hm = p.dma("sync", g.HM[:], dr["hmask"][:])
    g.tok_selc = p.dma("sync", g.SELC[:], dr["selc"][:])
    tok_brow = p.dma("gpsimd", g.BROW[:], dr["l1_bv"][:])
    misc_loads = []
    mcols = {"bq": (0, 8), "bk": (8, 8), "bo": (16, 8), "g1": (24, 8), "b1": (32, 8),
             "g2": (40, 8), "b2": (48, 8), "bf1": (56, 32), "bf2": (88, 8),
             "b1m": (96, 32), "b2m": (128, 8), "wr": (136, 8)}
    mtok = {}
    for nm, (o, w) in mcols.items():
        mtok[nm] = p.dma("sync", g.MISC[:, o:o + w], dr["l1_" + nm][:])
    MC = lambda nm, i: g.MISC[:, mcols[nm][0] + i: mcols[nm][0] + i + 1]

    XL = g.SBIG[:, 0:16384].rearrange("p (c t) -> p c t", c=8)
    ln1 = emit_ln(g, src=g.XN, dst=XL, src_toks=[ln[c][t] for c in range(FC) for t in range(4)] + [mtok["g1"], mtok["b1"]],
                  gb=(g.MISC[:, 24:32], g.MISC[:, 32:40]), sqreg=g.SC,
                  tmpreg=g.SBIGf[:, 8192:9216])

    srow = g.STAT[0:1, 0:2048]
    WRB = g.SC[:, 4096:4104]
    t_wrb = p.dma("gpsimd", WRB, dr["l1_wr"][:])
    sc_toks = []
    for t in range(4):
        nb = g.PS[4 + (t % 2)]
        mm = None
        for c in range(FC):
            mm = p.add("tensor", lambda e, c=c, t=t, nb=nb: e.matmul(
                nb.t[0:1, 0:512], WRB[:, c:c + 1], g.XN[:, c, t * 512:(t + 1) * 512],
                start=(c == 0), stop=(c == FC - 1)),
                waits=tk(ln[c][t], t_wrb if c == 0 else None, nb.rd if c == 0 else None),
                inc=p.esem["tensor"] if c == FC - 1 else None)
        ev = p.op("vector", lambda e, t=t, nb=nb: e.tensor_copy(
            srow[0:1, t * 512:(t + 1) * 512], nb.t[0:1, 0:512]), waits=tk(mm))
        nb.rd = ev
        sc_toks.append(ev)
    t_sst = p.dma("sync", g.sag_in.rearrange("t one -> one t"), srow[0:1, :], waits=tk(*sc_toks))
    t_sag = p.add("gpsimd", lambda e: e.collective_compute(
        "AllGather", ALU.bypass, replica_groups=GROUPS, ins=[g.sag_in[:]], outs=[g.sag_out[:]]),
        waits=tk(t_sst), inc=p.csem, amt=1)

    KV = g.SBIG[:, 16384:32768].rearrange("p (c t) -> p c t", c=8)
    wb_rd = [None, None]

    def proj_fm(wname, bname, out, waits_extra=()):
        toks = [[None] * 4 for _ in range(FC)]
        for og in range(2):
            b01 = og % 2
            wdst = g.WB[:, b01, :].rearrange("p (k o) -> p k o", k=8)
            tw = p.dma("gpsimd", wdst, dr[wname][:, og], waits=tk(wb_rd[b01]))
            last = None
            for oc in range(4):
                c_out = og * 4 + oc
                for t in range(4):
                    bk = g.PS[(oc * 4 + t) % 4]
                    mm = None
                    for c in range(FC):
                        mm = p.add("tensor", lambda e, c=c, oc=oc, t=t, b01=b01, bk=bk: e.matmul(
                            bk.t[:, 0:512], g.WB[:, b01, c * 512 + oc * 128: c * 512 + oc * 128 + 128],
                            XL[:, c, t * 512:(t + 1) * 512], start=(c == 0), stop=(c == FC - 1)),
                            waits=tk(ln1[c][t], *( (tw, bk.rd) + tuple(waits_extra) if c == 0 else ())),
                            inc=p.esem["tensor"] if c == FC - 1 else None)
                    if bname is None:
                        ev = p.op("vector", lambda e, c_out=c_out, t=t, bk=bk: e.tensor_copy(
                            out[:, c_out, t * 512:(t + 1) * 512], bk.t[:, 0:512]), waits=tk(mm))
                    else:
                        ev = p.op("vector", lambda e, c_out=c_out, t=t, bk=bk: e.tensor_scalar_add(
                            out[:, c_out, t * 512:(t + 1) * 512], bk.t[:, 0:512], MC(bname, c_out)),
                            waits=tk(mm, mtok[bname]))
                    bk.rd = ev
                    toks[c_out][t] = ev
                    last = ev
            wb_rd[b01] = last
        return toks

    k_toks = proj_fm("l1_wk", "bk", KV)
    k_toks = _clip_heads(g, KV, k_toks, wk_out=True)
    t_kst = p.dma("sync", g.kag_in.rearrange("(c p) t -> p c t", p=128), KV[:],
                  waits=tk(*[k_toks[c][t] for c in range(FC) for t in range(4)]))
    t_kag = p.add("gpsimd", lambda e: e.collective_compute(
        "AllGather", ALU.bypass, replica_groups=GROUPS, ins=[g.kag_in[:]], outs=[g.kag_out[:]]),
        waits=tk(t_kst), inc=p.csem, amt=1)
    t_nst = p.dma("sync", g.nag_in.rearrange("(tc p) h -> p tc h", p=128),
                  g.MISC[:, 256:512].rearrange("p (tc h) -> p tc h", h=16),
                  waits=tk(*_nk2_toks))
    t_nag = p.add("gpsimd", lambda e: e.collective_compute(
        "AllGather", ALU.bypass, replica_groups=GROUPS, ins=[g.nag_in[:]], outs=[g.nag_out[:]]),
        waits=tk(t_nst), inc=p.csem, amt=1)

    v_toks = [[None] * 4 for _ in range(2)]
    VV = KV
    VT = g.SBIG[:, 16384:32768].rearrange("p (tc d) -> p tc d", tc=16)
    vlast = None
    for og in range(2):
        b01 = og % 2
        wdst = g.WB[:, b01, :].rearrange("p (k o) -> p k o", k=8)
        tw = p.dma("gpsimd", wdst, dr["l1_wv"][:, og], waits=tk(wb_rd[b01]))
        for ti in range(16):
            bk = g.PS[(ti % 4)]
            mm = None
            for c in range(FC):
                mm = p.add("tensor", lambda e, c=c, ti=ti, og=og, b01=b01, bk=bk: e.matmul(
                    bk.t[:, 0:512], XL[:, c, ti * 128:(ti + 1) * 128],
                    g.WB[:, b01, c * 512: c * 512 + 512], start=(c == 0), stop=False),
                    waits=tk(ln1[c][ti // 4], *( (tw, bk.rd, t_kst) if c == 0 else ())),
                    inc=None)
            mm = p.add("tensor", lambda e, og=og, bk=bk: e.matmul(
                bk.t[:, 0:512], g.ONESRB[:], g.BROW[0:1, og * 512:(og + 1) * 512],
                start=False, stop=True),
                waits=tk(tok_brow), inc=p.esem["tensor"])
            ev = p.op("vector", lambda e, ti=ti, og=og, bk=bk: e.tensor_copy(
                VT[:, ti, og * 512:(og + 1) * 512], bk.t[:, 0:512]), waits=tk(mm))
            bk.rd = ev
            vlast = ev
        wb_rd[b01] = vlast
    t_vst = p.dma("sync", g.vag_in.rearrange("(tc p) d -> p tc d", p=128), VT[:], waits=tk(vlast))
    t_vag = p.add("gpsimd", lambda e: e.collective_compute(
        "AllGather", ALU.bypass, replica_groups=GROUPS, ins=[g.vag_in[:]], outs=[g.vag_out[:]]),
        waits=tk(t_vst), inc=p.csem, amt=1)

    q_toks = proj_fm("l1_wq", "bq", KV, waits_extra=(t_vst,))
    q_toks = _clip_heads(g, KV, q_toks, wk_out=None)
    QV = KV

    t_sld = p.dma("sync", g.MISC[:, 144:208], g.sag_out.rearrange("(p a) one -> p (a one)", p=128),
                  waits=tk(t_sag))
    SAL = g.MISC[:, 144:208]
    LO = g.MISC[:, 208:209]; HI = g.MISC[:, 209:210]; MID = g.MISC[:, 210:211]
    CB = g.MISC[:, 211:212]; CO = g.MISC[:, 212:213]; C64 = g.MISC[:, 213:214]
    TMP64 = g.SCf[:, 2048:2112]
    ROWT = g.SCf[0:1, 3104:3296]
    i1 = p.op("vector", lambda e: e.reduce_max(CB, SAL, axis=AX.X), waits=tk(t_sld))
    d1a = p.dma("gpsimd", g.bnc[:], CB, waits=tk(i1))
    d1 = p.dma("gpsimd", ROWT[0:1, 0:128], g.bnc.rearrange("p one -> one p"), waits=tk(d1a))
    i2 = p.op("vector", lambda e: e.reduce_max(ROWT[0:1, 128:129], ROWT[0:1, 0:128], axis=AX.X), waits=tk(d1))
    b1_ = g.PS[4]
    i3 = p.op("tensor", lambda e: e.matmul(b1_.t[:, 0:1], g.ONESR[:], ROWT[0:1, 128:129],
                                           start=True, stop=True), waits=tk(i2, b1_.rd))
    i4 = p.op("vector", lambda e: e.tensor_scalar_add(HI, b1_.t[:, 0:1], 1.0), waits=tk(i3))
    b1_.rd = i4
    i5 = p.op("vector", lambda e: e.tensor_reduce(CB, SAL, axis=AX.X, op=ALU.min), waits=tk(t_sld, i1))
    d2a = p.dma("gpsimd", g.bnc[:], CB, waits=tk(i5, i2))
    d2 = p.dma("gpsimd", ROWT[0:1, 0:128], g.bnc.rearrange("p one -> one p"), waits=tk(d2a))
    i6 = p.op("vector", lambda e: e.tensor_reduce(ROWT[0:1, 128:129], ROWT[0:1, 0:128], axis=AX.X, op=ALU.min), waits=tk(d2))
    i7 = p.op("tensor", lambda e: e.matmul(b1_.t[:, 0:1], g.ONESR[:], ROWT[0:1, 128:129],
                                           start=True, stop=True), waits=tk(i6, b1_.rd))
    i8 = p.op("vector", lambda e: e.tensor_scalar_add(LO, b1_.t[:, 0:1], -1.0), waits=tk(i7))
    b1_.rd = i8
    prev = i8
    for it in range(46):
        m1 = p.op("vector", lambda e: e.tensor_add(MID, LO, HI), waits=tk(prev))
        m2 = p.op("vector", lambda e: e.tensor_scalar_mul(MID, MID, 0.5), waits=tk(m1))
        c1 = p.op("vector", lambda e: e.tensor_scalar(TMP64, SAL, MID, None, ALU.is_ge), waits=tk(m2))
        c2 = p.op("vector", lambda e: e.reduce_sum(CB, TMP64, axis=AX.X), waits=tk(c1))
        c3 = p.op("tensor", lambda e: e.matmul(b1_.t[0:1, 0:1], g.ONESF[:], CB, start=True, stop=True),
                  waits=tk(c2, b1_.rd))
        c4 = p.op("vector", lambda e: e.tensor_copy(ROWT[0:1, 129:130], b1_.t[0:1, 0:1]), waits=tk(c3))
        c5 = p.op("tensor", lambda e: e.matmul(b1_.t[:, 0:1], g.ONESR[:], ROWT[0:1, 129:130],
                                               start=True, stop=True), waits=tk(c4))
        c6 = p.op("vector", lambda e: e.tensor_copy(C64, b1_.t[:, 0:1]), waits=tk(c5))
        b1_.rd = c6
        c7 = p.op("vector", lambda e: e.tensor_scalar(CO, C64, float(S // 2), None, ALU.is_ge), waits=tk(c6))
        TMA = g.MISC[:, 214:215]
        TMB = g.MISC[:, 215:216]
        s1 = p.op("vector", lambda e: e.tensor_sub(TMA, MID, LO), waits=tk(c7))
        s2 = p.op("vector", lambda e: e.tensor_sub(TMB, HI, MID), waits=tk(c7))
        s3 = p.op("vector", lambda e: e.tensor_mul(TMA, TMA, CO), waits=tk(s1))
        s4 = p.op("vector", lambda e: e.tensor_mul(TMB, TMB, CO), waits=tk(s2))
        s5 = p.op("vector", lambda e: e.tensor_add(LO, LO, TMA), waits=tk(s3))
        s6 = p.op("vector", lambda e: e.tensor_add(HI, MID, TMB), waits=tk(s4, s2))
        prev = s6
        prev2 = s5
    THR = g.MISC[0:1, 208:209]
    SGROW = g.SCf[0:1, 0:2048]
    sg1 = p.op("scalar", lambda e: e.activation(SGROW, srow, AF.Sigmoid), waits=tk(*sc_toks, prev))
    sg2 = p.op("vector", lambda e: e.tensor_single_scalar(srow, srow, THR, ALU.is_ge), waits=tk(sg1, prev, prev2))
    sg3 = p.op("vector", lambda e: e.tensor_mul(srow, srow, SGROW), waits=tk(sg2))

    HM_ = g.SBIG[:, 0:16384].rearrange("p (o t) -> p o t", o=32)
    mod_x_toks = []
    for t in range(4):
        ts = slice(t * 512, (t + 1) * 512)
        mb = g.PS[5]
        bcm = p.op("tensor", lambda e, t=t, mb=mb: e.matmul(
            mb.t[:, 0:512], g.ONESR[:], srow[0:1, t * 512:(t + 1) * 512], start=True, stop=True),
            waits=tk(sg3, mb.rd))
        mcp = p.op("vector", lambda e, t=t, mb=mb: e.tensor_copy(g.SCf[:, (t % 2) * 512:(t % 2) * 512 + 512], mb.t[:, 0:512]),
                   waits=tk(bcm, mod_x_toks[-1] if t >= 2 else None))
        mb.rd = mcp
        MBC = g.SCf[:, (t % 2) * 512:(t % 2) * 512 + 512]
        hlast = None
        for og in range(8):
            b01 = og % 2
            wdst = g.WB[:, b01, :].rearrange("p (k o) -> p k o", k=8)
            tw = p.dma("gpsimd", wdst, dr["l1_W1"][:, og], waits=tk(wb_rd[b01]))
            for oc in range(4):
                oabs = og * 4 + oc
                bk = g.PS[oabs % 4]
                mm = None
                for c in range(FC):
                    mm = p.add("tensor", lambda e, c=c, oc=oc, b01=b01, ts=ts, bk=bk: e.matmul(
                        bk.t[:, 0:512], g.WB[:, b01, c * 512 + oc * 128: c * 512 + oc * 128 + 128],
                        g.XN[:, c, ts], start=(c == 0), stop=(c == FC - 1)),
                        waits=tk(ln[c][t], *( (tw, bk.rd) if c == 0 else ())),
                        inc=p.esem["tensor"] if c == FC - 1 else None)
                wb_rd[b01] = mm
                gv = p.op("scalar", lambda e, oabs=oabs, bk=bk: e.activation(
                    HM_[:, oabs, 0:512], bk.t[:, 0:512], AF.Gelu, bias=MC("b1m", oabs)),
                    waits=tk(mm, mtok["b1m"], *( [q_toks[FC-1][3]] if (t == 0 and og == 0 and oc == 0) else [] )))
                bk.rd = gv
                hlast = gv
        for og in range(FC):
            b01 = og % 2
            wdst = g.WB[:, b01, :].rearrange("p (k o) -> p k o", k=32)
            tw = p.dma("gpsimd", wdst, dr["l1_W2"][:, og], waits=tk(wb_rd[b01]))
            bk = g.PS[og % 4]
            mm = None
            for k in range(32):
                mm = p.add("tensor", lambda e, k=k, b01=b01, bk=bk: e.matmul(
                    bk.t[:, 0:512], g.WB[:, b01, k * 128:(k + 1) * 128],
                    HM_[:, k, 0:512], start=(k == 0), stop=(k == 31)),
                    waits=tk(*( (tw, bk.rd, hlast) if k == 0 else ())),
                    inc=p.esem["tensor"] if k == 31 else None)
            wb_rd[b01] = mm
            x1_ = p.op("vector", lambda e, og=og, bk=bk, MBC=MBC: e.scalar_tensor_tensor(
                g.SCf[:, 1024 + (og % 2) * 512: 1024 + (og % 2) * 512 + 512],
                bk.t[:, 0:512], MC("b2m", og), MBC, ALU.add, ALU.mult),
                waits=tk(mm, mtok["b2m"], mcp, mod_x_toks[-2] if len(mod_x_toks) >= 2 else None))
            bk.rd = x1_
            x2_ = p.op("vector", lambda e, og=og, ts=ts: e.tensor_add(
                g.X[:, og, ts], g.X[:, og, ts],
                g.SCf[:, 1024 + (og % 2) * 512: 1024 + (og % 2) * 512 + 512]), waits=tk(x1_))
            x3_ = p.op("vector", lambda e, og=og, ts=ts: e.tensor_add(
                g.X[:, og, ts], g.X[:, og, ts], g.XN[:, og, ts]), waits=tk(x2_))
            mod_x_toks.append(x3_)

    g.l1_state = dict(QV=QV, q_toks=q_toks, t_kag=t_kag, t_vag=t_vag, t_nag=t_nag,
                      mod_x_toks=mod_x_toks, wb_rd=wb_rd, mtok=mtok, MC=MC, XL=XL)
    _emit_l1_flash(g, ln)


def _emit_l1_flash(g, ln):
    p = g.p
    dr = g.dram
    st = g.l1_state
    QV, q_toks = st["QV"], st["q_toks"]
    MC, mtok = st["MC"], st["mtok"]
    wb_rd = st["wb_rd"]
    ATT = g.SBIG[:, 0:16384].rearrange("p (c t) -> p c t", c=8)
    KCH = lambda s: g.SC[:, s * 1024: (s + 1) * 1024].rearrange("p (c t) -> p c t", c=8)
    VCH = lambda s: g.SC[:, 2048 + s * 1024: 2048 + (s + 1) * 1024]
    ECH = lambda s: g.SC[:, 4096 + s * 512: 4096 + (s + 1) * 512]
    NKC = lambda s: g.SCf[:, 2560 + s * 16: 2560 + (s + 1) * 16]
    RBC = g.SCf[:, 2592:3104]
    kch_rd = [st["mod_x_toks"][-1], st["mod_x_toks"][-1]]
    vch_rd = [st["mod_x_toks"][-1], st["mod_x_toks"][-1]]
    ech_rd = [None, None]
    nk_rd = [None, None]
    att_toks = []
    for qt in range(4):
        qs = slice(qt * 512, (qt + 1) * 512)
        for hg in range(4):
            nbank = [g.PS[i] for i in range(4)]
            sbank = [g.PS[4], g.PS[5]]
            den_end = [None] * 4
            for kc in range(64):
                s2 = kc % 2
                r = kc // 16
                tl = (kc % 16) * 128
                tkk = p.dma("sync", KCH(s2)[:],
                            g.kag_out[r * D:(r + 1) * D, tl:tl + 128].rearrange("(c p) t -> p c t", p=128),
                            waits=tk(st["t_kag"], kch_rd[s2]))
                tkv = p.dma("sync", VCH(s2), g.vag_out[kc * 128:(kc + 1) * 128, :],
                            waits=tk(st["t_vag"], vch_rd[s2]))
                tkn = p.dma("sync", NKC(s2), g.nag_out[kc * 128:(kc + 1) * 128, :],
                            waits=tk(st["t_nag"], nk_rd[s2]))
                for h4 in range(4):
                    h = hg * 4 + h4
                    hp = 64 * (h % 2)
                    hc = h // 2
                    sb_ = sbank[h4 % 2]
                    mm = p.op("tensor", lambda e, h=h, hp=hp, hc=hc, s2=s2, sb_=sb_, qs=qs: e.matmul(
                        sb_.t[:, 0:512], KCH(s2)[hp:hp + 64, hc, :], QV[hp:hp + 64, hc, qs],
                        start=True, stop=True),
                        waits=tk(tkk, q_toks[hc][qt] if kc == 0 else None, sb_.rd))
                    ex = p.op("scalar", lambda e, h=h, s2=s2, sb_=sb_: e.activation(
                        ECH(s2), sb_.t[:, 0:512], AF.Exp, bias=NKC(s2)[:, h:h + 1], scale=2.0 / TEMP),
                        waits=tk(mm, tkn, ech_rd[s2]))
                    sb_.rd = ex
                    nb = nbank[h4]
                    hm_ = h % 16
                    mmn = p.add("tensor", lambda e, hm_=hm_, s2=s2, nb=nb, kc=kc: e.matmul(
                        nb.t[0:64, 0:512], VCH(s2)[:, 64 * hm_:64 * hm_ + 64], ECH(s2),
                        start=(kc == 0), stop=(kc == 63)),
                        waits=tk(ex, tkv, nb.rd if kc == 0 else None), inc=None)
                    mmd = p.add("tensor", lambda e, s2=s2, nb=nb, kc=kc: e.matmul(
                        nb.t[64:65, 0:512], g.ONESB[:], ECH(s2),
                        start=(kc == 0), stop=(kc == 63), tile_position=(0, 64)),
                        waits=tk(), inc=p.esem["tensor"])
                    if kc == 63:
                        den_end[h4] = mmd
                    ech_rd[s2] = mmd
                    if h4 == 3:
                        kch_rd[s2] = mm
                        vch_rd[s2] = mmd
                        nk_rd[s2] = ex
            for h4 in range(4):
                h = hg * 4 + h4
                hp = 64 * (h % 2)
                hc = h // 2
                nb = nbank[h4]
                grp_end = den_end[h4]
                DEN = g.SCf[0:1, 3232:3744]
                r1 = p.op("vector", lambda e, nb=nb, DEN=DEN: e.reciprocal(DEN[0:1, 0:512], nb.t[64:65, 0:512]),
                          waits=tk(grp_end))
                bb = sbank[h4 % 2]
                r2 = p.op("tensor", lambda e, bb=bb, DEN=DEN: e.matmul(
                    bb.t[0:64, 0:512], g.ONESR[0:1, 0:64], DEN[0:1, 0:512],
                    start=True, stop=True), waits=tk(r1, bb.rd))
                r3 = p.op("vector", lambda e, bb=bb: e.tensor_copy(RBC[0:64, :], bb.t[0:64, 0:512]), waits=tk(r2))
                bb.rd = r3
                r4 = p.op("vector", lambda e, h=h, hp=hp, hc=hc, nb=nb, qs=qs: e.tensor_mul(
                    ATT[hp:hp + 64, hc, qs], nb.t[0:64, 0:512], RBC[0:64, :]), waits=tk(r3))
                nb.rd = r4
                att_toks.append(r4)

    U = g.SBIG[:, 16384:32768].rearrange("p (c t) -> p c t", c=8)
    u_toks = [[None] * 4 for _ in range(FC)]
    for og in range(2):
        b01 = og % 2
        wdst = g.WB[:, b01, :].rearrange("p (k o) -> p k o", k=8)
        tw = p.dma("gpsimd", wdst, dr["l1_wo"][:, og], waits=tk(wb_rd[b01]))
        for oc in range(4):
            c_out = og * 4 + oc
            for t in range(4):
                bk = g.PS[(oc * 4 + t) % 4]
                mm = None
                for c in range(FC):
                    mm = p.add("tensor", lambda e, c=c, oc=oc, t=t, b01=b01, bk=bk: e.matmul(
                        bk.t[:, 0:512], g.WB[:, b01, c * 512 + oc * 128: c * 512 + oc * 128 + 128],
                        ATT[:, c, t * 512:(t + 1) * 512], start=(c == 0), stop=(c == FC - 1)),
                        waits=tk(att_toks[-1] if (c == 0 and og == 0 and oc == 0 and t == 0) else None,
                                 *( (tw, bk.rd) if c == 0 else ())),
                        inc=p.esem["tensor"] if c == FC - 1 else None)
                ev = p.op("vector", lambda e, c_out=c_out, t=t, bk=bk: e.scalar_tensor_tensor(
                    U[:, c_out, t * 512:(t + 1) * 512], bk.t[:, 0:512], MC("bo", c_out),
                    g.XN[:, c_out, t * 512:(t + 1) * 512], ALU.add, ALU.add),
                    waits=tk(mm, mtok["bo"]))
                bk.rd = ev
                u_toks[c_out][t] = ev
        wb_rd[b01] = u_toks[og * 4 + 3][3]

    u_toks = _proj_fullvec(g, U, u_toks, "l1u")

    XN2 = ATT
    ln2 = emit_ln(g, src=U, dst=XN2,
                  src_toks=[u_toks[c][t] for c in range(FC) for t in range(4)],
                  gb=(g.MISC[:, 40:48], g.MISC[:, 48:56]),
                  sqreg=g.SC, tmpreg=g.SCf[:, 2048:3072])

    u2_toks = [[None] * 4 for _ in range(FC)]
    HH = g.SC.rearrange("p (o t) -> p o t", o=32)
    hh_rd = None
    for tt in range(8):
        ts = slice(tt * 256, (tt + 1) * 256)
        t4 = tt // 2
        hlast = None
        for og in range(8):
            b01 = og % 2
            wdst = g.WB[:, b01, :].rearrange("p (k o) -> p k o", k=8)
            tw = p.dma("gpsimd", wdst, dr["l1_w1"][:, og], waits=tk(wb_rd[b01]))
            for oc in range(4):
                oabs = og * 4 + oc
                bk = g.PS[oabs % 4]
                mm = None
                for c in range(FC):
                    mm = p.add("tensor", lambda e, c=c, oc=oc, b01=b01, ts=ts, bk=bk: e.matmul(
                        bk.t[:, 0:256], g.WB[:, b01, c * 512 + oc * 128: c * 512 + oc * 128 + 128],
                        XN2[:, c, ts], start=(c == 0), stop=(c == FC - 1)),
                        waits=tk(ln2[c][t4], *( (tw, bk.rd, hh_rd) if c == 0 else ())),
                        inc=p.esem["tensor"] if c == FC - 1 else None)
                wb_rd[b01] = mm
                gv = p.op("scalar", lambda e, oabs=oabs, bk=bk: e.activation(
                    HH[:, oabs, 0:256], bk.t[:, 0:256], AF.Gelu, bias=MC("bf1", oabs)),
                    waits=tk(mm, mtok["bf1"]))
                bk.rd = gv
                hlast = gv
        for og in range(FC):
            b01 = og % 2
            wdst = g.WB[:, b01, :].rearrange("p (k o) -> p k o", k=32)
            tw = p.dma("gpsimd", wdst, dr["l1_w2"][:, og], waits=tk(wb_rd[b01]))
            bk = g.PS[og % 4]
            mm = None
            for k in range(32):
                mm = p.add("tensor", lambda e, k=k, b01=b01, bk=bk: e.matmul(
                    bk.t[:, 0:256], g.WB[:, b01, k * 128:(k + 1) * 128],
                    HH[:, k, 0:256], start=(k == 0), stop=(k == 31)),
                    waits=tk(*( (tw, bk.rd, hlast) if k == 0 else ())),
                    inc=p.esem["tensor"] if k == 31 else None)
            wb_rd[b01] = mm
            hh_rd = mm
            ev = p.op("vector", lambda e, og=og, ts=ts, bk=bk: e.scalar_tensor_tensor(
                U[:, og, ts], bk.t[:, 0:256], MC("bf2", og), U[:, og, ts], ALU.add, ALU.add),
                waits=tk(mm, mtok["bf2"]))
            bk.rd = ev
            u2_toks[og][t4] = ev

    u2_toks = _proj_fullvec(g, U, u2_toks, "l1u2")
    x_toks = list(st["mod_x_toks"])
    for t in range(4):
        ts = slice(t * 512, (t + 1) * 512)
        for c in range(FC):
            xa = p.op("vector", lambda e, c=c, ts=ts: e.tensor_add(
                g.X[:, c, ts], g.X[:, c, ts], U[:, c, ts]), waits=tk(u2_toks[c][t]))
            x_toks.append(xa)
    g.x_toks = x_toks


def _proj_generic(g, wname, src3, src_toks, out3, wb_rd, token_major=False,
                  extra_first=()):
    p = g.p
    dr = g.dram
    toks = [[None] * 4 for _ in range(FC)]
    last = None
    for og in range(2):
        b01 = og % 2
        wdst = g.WB[:, b01, :].rearrange("p (k o) -> p k o", k=8)
        tw = p.dma("gpsimd", wdst, dr[wname][:, og], waits=tk(wb_rd[b01]))
        if token_major:
            for ti in range(16):
                bk = g.PS[ti % 4]
                mm = None
                for c in range(FC):
                    mm = p.add("tensor", lambda e, c=c, ti=ti, b01=b01, bk=bk: e.matmul(
                        bk.t[:, 0:512], src3[:, c, ti * 128:(ti + 1) * 128],
                        g.WB[:, b01, c * 512: c * 512 + 512], start=(c == 0), stop=(c == FC - 1)),
                        waits=tk(src_toks[c][ti // 4], *( (tw, bk.rd) + tuple(extra_first) if c == 0 else ())),
                        inc=p.esem["tensor"] if c == FC - 1 else None)
                ev = p.op("vector", lambda e, ti=ti, og=og, bk=bk: e.tensor_copy(
                    out3[:, ti, og * 512:(og + 1) * 512], bk.t[:, 0:512]), waits=tk(mm))
                bk.rd = ev
                last = ev
            wb_rd[b01] = last
        else:
            for oc in range(4):
                c_out = og * 4 + oc
                for t in range(4):
                    bk = g.PS[(oc * 4 + t) % 4]
                    mm = None
                    for c in range(FC):
                        mm = p.add("tensor", lambda e, c=c, oc=oc, t=t, b01=b01, bk=bk: e.matmul(
                            bk.t[:, 0:512], g.WB[:, b01, c * 512 + oc * 128: c * 512 + oc * 128 + 128],
                            src3[:, c, t * 512:(t + 1) * 512], start=(c == 0), stop=(c == FC - 1)),
                            waits=tk(src_toks[c][t], *( (tw, bk.rd) + tuple(extra_first) if c == 0 else ())),
                            inc=p.esem["tensor"] if c == FC - 1 else None)
                    ev = p.op("vector", lambda e, c_out=c_out, t=t, bk=bk: e.tensor_copy(
                        out3[:, c_out, t * 512:(t + 1) * 512], bk.t[:, 0:512]), waits=tk(mm))
                    bk.rd = ev
                    toks[c_out][t] = ev
                    last = ev
            wb_rd[b01] = last
    return toks, last


def emit_l2(g, ln):
    p = g.p
    dr = g.dram
    SCL = 0.125
    QV = g.SBIG[:, 0:16384].rearrange("p (c t) -> p c t", c=8)
    KV = g.SBIG[:, 16384:32768].rearrange("p (c t) -> p c t", c=8)
    VT = g.SBIG[:, 16384:32768].rearrange("p (tc d) -> p tc d", tc=16)
    wb_rd = [None, None]

    t_bck = p.dma("sync", g.MISC[0:64, 0:1], dr["l2_bck"][:])
    t_bcv = p.dma("sync", g.MISC[0:64, 1:2], dr["l2_bcv"][:])
    t_b1 = p.dma("sync", g.MISC[:, 10:266], dr["l2_b1"][:])
    t_b2 = p.dma("sync", g.MISC[:, 266:330], dr["l2_b2"][:])

    q_toks, _ = _proj_generic(g, "l2_wq", g.XN, ln, QV, wb_rd)
    k_toks, klast = _proj_generic(g, "l2_wk", g.XN, ln, KV, wb_rd)
    t_kst = p.dma("sync", g.kag_in.rearrange("(c p) t -> p c t", p=128), KV[:],
                  waits=tk(*[k_toks[c][t] for c in range(FC) for t in range(4)]))
    t_kag = p.add("gpsimd", lambda e: e.collective_compute(
        "AllGather", ALU.bypass, replica_groups=GROUPS, ins=[g.kag_in[:]], outs=[g.kag_out[:]]),
        waits=tk(t_kst), inc=p.csem, amt=1)
    _, ktlast = _proj_generic(g, "l2_wk", g.XN, ln, VT, wb_rd, token_major=True,
                              extra_first=(t_kst,))
    t_ktst = p.dma("sync", g.ktm_in.rearrange("(tc p) d -> p tc d", p=128), VT[:], waits=tk(ktlast))
    t_ktag = p.add("gpsimd", lambda e: e.collective_compute(
        "AllGather", ALU.bypass, replica_groups=GROUPS, ins=[g.ktm_in[:]], outs=[g.ktm_out[:]]),
        waits=tk(t_ktst), inc=p.csem, amt=1)
    _, vlast = _proj_generic(g, "l2_wv", g.XN, ln, VT, wb_rd, token_major=True,
                             extra_first=(t_ktst,))
    t_vst = p.dma("sync", g.vag_in.rearrange("(tc p) d -> p tc d", p=128), VT[:], waits=tk(vlast))
    t_vag = p.add("gpsimd", lambda e: e.collective_compute(
        "AllGather", ALU.bypass, replica_groups=GROUPS, ins=[g.vag_in[:]], outs=[g.vag_out[:]]),
        waits=tk(t_vst), inc=p.csem, amt=1)
    vf_toks, vflast = _proj_generic(g, "l2_wv", g.XN, ln, KV, wb_rd, extra_first=(t_vst,))
    t_vfst = p.dma("sync", g.vfag_in.rearrange("(c p) t -> p c t", p=128), KV[:], waits=tk(vflast))
    t_vfag = p.add("gpsimd", lambda e: e.collective_compute(
        "AllGather", ALU.bypass, replica_groups=GROUPS, ins=[g.vfag_in[:]], outs=[g.vfag_out[:]]),
        waits=tk(t_vfst), inc=p.csem, amt=1)

    qs_toks = []
    for c in range(FC):
        t1 = p.op("vector", lambda e, c=c: e.reduce_sum(
            g.MISC[:, 2 + c:3 + c], QV[:, c, :], axis=AX.X),
            waits=tk(*[q_toks[c][t] for t in range(4)]))
        qs_toks.append(t1)
    t_qst = p.dma("sync", g.qs_in[:], g.MISC[:, 2:10], waits=tk(*qs_toks))
    t_qar = p.add("gpsimd", lambda e: e.collective_compute(
        "AllReduce", ALU.add, replica_groups=GROUPS, ins=[g.qs_in[:]], outs=[g.qs_out[:]]),
        waits=tk(t_qst), inc=p.csem, amt=1)

    CK = g.SBIG[:, 16384:20480].rearrange("p (hp n) -> p hp n", hp=8)
    CVF = g.SBIG[:, 20480:24576].rearrange("p (hp n) -> p hp n", hp=8)
    CVT = g.SBIG[:, 24576:28672].rearrange("p (nc d) -> p nc d", nc=4)
    WCK = g.SBIG[:, 28672:30720].rearrange("p (j o) -> p j o", j=32)
    WCV = g.SBIG[:, 30720:32768].rearrange("p (j o) -> p j o", j=32)
    t_wck = p.dma("gpsimd", WCK[:], dr["l2_wck"][:], waits=tk(t_vfst))
    t_wcv = p.dma("gpsimd", WCV[:], dr["l2_wcv"][:], waits=tk(t_vfst))
    KWS = lambda s: g.SC[:, s * 2064:(s + 1) * 2064]
    kw_rd = [None, None]
    ck_toks = []
    for src_ag, src_tok, WCB, OUT, bcol, wtok in (
            (g.kag_out, t_kag, WCK, CK, g.MISC[0:64, 0:1], t_wck),
            (g.vfag_out, t_vfag, WCV, CVF, g.MISC[0:64, 1:2], t_wcv)):
        for w in range(4):
            nblk = 128 if w < 3 else 127
            for hp in range(8):
                s = (w * 8 + hp) % 2
                tkw = p.dma("sync", KWS(s)[:, 0:T], src_ag[w * D + hp * 128: w * D + (hp + 1) * 128, :],
                            waits=tk(kw_rd[s], src_tok))
                if w < 3:
                    tkw = p.dma("sync", KWS(s)[:, T:2064],
                                src_ag[(w + 1) * D + hp * 128: (w + 1) * D + (hp + 1) * 128, 0:16],
                                waits=tk(tkw))
                mm2 = None
                for h2 in range(2):
                    hb = 64 * h2
                    bk = g.PS[(hp * 2 + h2) % 4]
                    mm = None
                    for j in range(32):
                        mm = p.add("tensor", lambda e, j=j, h2=h2, hb=hb, s=s, nblk=nblk, bk=bk, WCB=WCB: e.matmul(
                            bk.t[0:64, 0:nblk], WCB[hb:hb + 64, j, :], KWS(s)[hb:hb + 64, j: j + 16 * (nblk - 1) + 1: 16],
                            start=(j == 0), stop=(j == 31)),
                            waits=tk(tkw, *( (wtok, bk.rd) if j == 0 else ())),
                            inc=p.esem["tensor"] if j == 31 else None)
                    ev = p.op("vector", lambda e, w=w, hp=hp, h2=h2, hb=hb, nblk=nblk, bk=bk, OUT=OUT, bcol=bcol: e.tensor_scalar_add(
                        OUT[hb:hb + 64, hp, w * 128: w * 128 + nblk], bk.t[0:64, 0:nblk], bcol),
                        waits=tk(mm, t_bck, t_bcv))
                    bk.rd = ev
                    mm2 = mm
                    ck_toks.append(ev)
                kw_rd[s] = mm2
    cvt_toks = []
    for h in range(16):
        hb = 64 * (h % 2)
        hp = h // 2
        for nc4 in range(4):
            nblk = 128 if nc4 < 3 else 127
            pb = g.PT[(h * 4 + nc4) % 2]
            tt = p.op("tensor", lambda e, h=h, hb=hb, hp=hp, nc4=nc4, nblk=nblk, pb=pb: e.transpose(
                pb.t[0:nblk, 0:64], CVF[hb:hb + 64, hp, nc4 * 128: nc4 * 128 + nblk], g.IDN[hb:hb + 64, hb:hb + 64]),
                waits=tk(ck_toks[-1] if (h == 0 and nc4 == 0) else None, pb.rd, g.tok_idn))
            cp = p.op("vector", lambda e, h=h, nc4=nc4, nblk=nblk, pb=pb: e.tensor_copy(
                CVT[0:nblk, nc4, h * 64:(h + 1) * 64], pb.t[0:nblk, 0:64]), waits=tk(tt))
            pb.rd = cp
            cvt_toks.append(cp)

    BIB = g.PS[5]
    EW = g.SC[:, 2568:3592]
    bi_last = None
    first = True
    for qt in range(16):
        for h in range(16):
            hb = 64 * (h % 2)
            hp = h // 2
            sb_ = g.PS[(qt + h) % 4]
            mm = p.op("tensor", lambda e, qt=qt, h=h, hb=hb, hp=hp, sb_=sb_: e.matmul(
                sb_.t[:, 0:511], QV[hb:hb + 64, hp, qt * 128:(qt + 1) * 128], CK[hb:hb + 64, hp, 0:511],
                start=True, stop=True),
                waits=tk(ck_toks[-1] if first else None, q_toks[hp][qt // 4] if first else None, sb_.rd))
            ex = p.op("scalar", lambda e, sb_=sb_: e.activation(
                EW[:, 0:511], sb_.t[:, 0:511], AF.Exp, scale=SCL), waits=tk(mm, bi_last))
            rd_ = p.op("vector", lambda e, sb_=sb_: e.reduce_sum(
                g.MISC[:, 378:379], EW[:, 0:511], axis=AX.X), waits=tk(ex))
            rc = p.op("vector", lambda e: e.reciprocal(g.MISC[:, 379:380], g.MISC[:, 378:379]), waits=tk(rd_))
            ts_ = p.op("vector", lambda e: e.tensor_scalar_mul(EW[:, 0:511], EW[:, 0:511], g.MISC[:, 379:380]),
                       waits=tk(rc))
            sb_.rd = ts_
            bi_last = p.add("tensor", lambda e, qt=qt, h=h: e.matmul(
                BIB.t[0:1, 0:511], g.ONESB[:], EW[:, 0:511],
                start=(qt == 0 and h == 0), stop=(qt == 15 and h == 15)),
                waits=tk(ts_, BIB.rd if first else None),
                inc=p.esem["tensor"])
            first = False
    bi_ev = p.op("vector", lambda e: e.tensor_copy(g.STAT[0:1, 0:511], BIB.t[0:1, 0:511]), waits=tk(bi_last))
    BIB.rd = bi_ev
    t_bst = p.dma("sync", g.bi_in[:], g.STAT[0:1, 0:511], waits=tk(bi_ev))
    t_bar = p.add("gpsimd", lambda e: e.collective_compute(
        "AllReduce", ALU.add, replica_groups=GROUPS, ins=[g.bi_in[:]], outs=[g.bi_out[:]]),
        waits=tk(t_bst), inc=p.csem, amt=1)
    t_bld = p.dma("sync", g.STAT[0:1, 0:511], g.bi_out[:], waits=tk(t_bar))

    TOPV = g.STAT[0:1, 512:528]
    STATU = g.STAT.bitcast(mybir.dt.uint32)
    IDXU = STATU[0:1, 528:544]
    BI2 = g.STAT[0:1, 544:1055]
    t_dbi = None
    if "d_bi" in g.dbg:
        t_dbi = p.dma("sync", g.dbg["d_bi"][:], g.STAT[0:1, 0:511], waits=tk(t_bld))
    tt1 = p.op("vector", lambda e: e.max(TOPV[0:1, 0:8], g.STAT[0:1, 0:511]), waits=tk(t_bld))
    tt2 = p.op("vector", lambda e: e.match_replace(BI2[0:1, 0:511], TOPV[0:1, 0:8], g.STAT[0:1, 0:511], -1.0),
               waits=tk(tt1))
    tt3 = p.op("vector", lambda e: e.max(TOPV[0:1, 8:16], BI2[0:1, 0:511]), waits=tk(tt2))
    it1 = p.op("vector", lambda e: e.max_index(IDXU[0:1, 0:8], TOPV[0:1, 0:8], g.STAT[0:1, 0:511]), waits=tk(tt3))
    cv2 = p.op("vector", lambda e: e.max_index(IDXU[0:1, 8:16], TOPV[0:1, 8:16], BI2[0:1, 0:511]), waits=tk(it1))
    IDXF2 = g.STAT[0:1, 1056:1072]
    cf1 = p.op("vector", lambda e: e.tensor_copy(IDXF2, IDXU), waits=tk(cv2))
    if "d_idx" in g.dbg:
        p.dma("sync", g.dbg["d_idx"][:], IDXF2, waits=tk(cf1))
    bbx = g.PS[4]
    bx1 = p.op("tensor", lambda e: e.matmul(bbx.t[:, 0:16], g.ONESR[:], IDXF2, start=True, stop=True),
               waits=tk(cf1, bbx.rd))
    BLKC = g.MISC[:, 380:396]
    bx2 = p.op("vector", lambda e: e.tensor_copy(BLKC, bbx.t[:, 0:16]), waits=tk(bx1))
    bbx.rd = bx2
    t_io = p.dma("sync", g.MISC[:, 396:397], dr["iota64"][:])
    SELF_ = g.MISC[:, 397:405]
    st_toks = []
    for ci in range(8):
        for half in (0, 64):
            bidx = 2 * ci + (half // 64)
            s1_ = p.op("vector", lambda e, ci=ci, half=half, bidx=bidx: e.scalar_tensor_tensor(
                SELF_[half:half + 64, ci:ci + 1], BLKC[half:half + 64, bidx:bidx + 1], 16.0,
                g.MISC[half:half + 64, 396:397], ALU.mult, ALU.add), waits=tk(bx2, t_io))
            s2_ = p.op("vector", lambda e, ci=ci, half=half: e.tensor_scalar_min(
                SELF_[half:half + 64, ci:ci + 1], SELF_[half:half + 64, ci:ci + 1], float(S - 1)),
                waits=tk(s1_))
            st_toks.append(s2_)
    SELTI = g.MISCI[:, 405:413]
    cvt_ = p.op("vector", lambda e: e.tensor_copy(SELTI, SELF_), waits=tk(*st_toks))
    from concourse.bass_core_rust import IndirectOffsetOnAxis as _IOA
    gather_toks = []
    KTCH = g.SC[:, 0:1024]
    KSF = g.SC[:, 1024:2048].rearrange("p (c t) -> p c t", c=8)
    ktch_rd = None
    for ci in range(8):
        gk = p.add("gpsimd", lambda e, ci=ci: e.indirect_dma_start(
            KTCH, None, g.ktm_out[0:128, :], _IOA(SELTI[:, ci:ci + 1], 0)),
            waits=tk(cvt_, t_ktag, ktch_rd, kw_rd[0], kw_rd[1]), inc=p.qsem["gpsimd"], amt=16)
        trs = []
        for fc in range(8):
            pb = g.PT[fc % 2]
            tt = p.op("tensor", lambda e, fc=fc, pb=pb: e.transpose(
                pb.t[:, 0:128], KTCH.rearrange("p (c t) -> p c t", c=8)[:, fc, :], g.IDN[:]),
                waits=tk(gk, pb.rd, g.tok_idn))
            cp = p.op("vector", lambda e, fc=fc, pb=pb: e.tensor_copy(KSF[:, fc, :], pb.t[:, 0:128]),
                      waits=tk(tt))
            pb.rd = cp
            trs.append(cp)
        ktch_rd = trs[-1]
        gks = p.dma("sync", g.ksel[:, ci * 128:(ci + 1) * 128].rearrange("(c p) t -> p c t", p=128),
                    KSF[:], waits=tk(*trs))
        VTCH = g.SC[:, 2048:3072]
        gv1 = p.add("gpsimd", lambda e, ci=ci, VTCH=VTCH: e.indirect_dma_start(
            VTCH, None, g.vag_out[0:128, :], _IOA(SELTI[:, ci:ci + 1], 0)),
            waits=tk(cvt_, t_vag, gather_toks[-1] if gather_toks else None), inc=p.qsem["gpsimd"], amt=16)
        gv = p.dma("sync", g.vsel[ci * 128:(ci + 1) * 128, :], VTCH, waits=tk(gv1))
        gather_toks += [gks, gv]

    t_qld = p.dma("sync", g.MISC[:, 2:10], g.qs_out[:], waits=tk(t_qar))
    t_wgb = p.dma("gpsimd", g.SC[:, 3592:3595], dr["l2_wgb"][:])
    t_bgr = p.dma("sync", g.STAT[0:1, 1100:1148], dr["l2_bgr"][:])
    QSB = g.SC[:, 3596:3604]
    cq = p.op("vector", lambda e: e.tensor_copy(QSB, g.MISC[:, 2:10]), waits=tk(t_qld))
    GROW = g.STAT[0:1, 1148:1196]
    g_toks = []
    for h in range(16):
        hb = 64 * (h % 2)
        bb = g.PS[4 + (h % 2)]
        m1 = p.op("tensor", lambda e, h=h, hb=hb, bb=bb: e.matmul(
            bb.t[0:1, 0:3], QSB[hb:hb + 64, h // 2: h // 2 + 1], g.SC[hb:hb + 64, 3592:3595],
            start=True, stop=True), waits=tk(cq, t_wgb, bb.rd))
        c1 = p.op("vector", lambda e, h=h, bb=bb: e.tensor_copy(GROW[0:1, 3 * h:3 * h + 3], bb.t[0:1, 0:3]),
                  waits=tk(m1))
        bb.rd = c1
        g_toks.append(c1)
    gs1 = p.op("vector", lambda e: e.tensor_scalar_mul(GROW, GROW, 1.0 / S), waits=tk(*g_toks))
    gs2 = p.op("vector", lambda e: e.tensor_add(GROW, GROW, g.STAT[0:1, 1100:1148]), waits=tk(gs1, t_bgr))
    gs3 = p.op("scalar", lambda e: e.activation(GROW, GROW, AF.Sigmoid), waits=tk(gs2))
    GC = g.MISC[0:64, 412:460]
    gc_toks = []
    for j in range(48):
        bb = g.PS[4 + (j % 2)]
        m1 = p.op("tensor", lambda e, j=j, bb=bb: e.matmul(
            bb.t[0:64, 0:1], g.ONESR[0:1, 0:64], GROW[0:1, j:j + 1], start=True, stop=True),
            waits=tk(gs3, bb.rd))
        c1 = p.op("vector", lambda e, j=j, bb=bb: e.tensor_copy(GC[:, j:j + 1], bb.t[0:64, 0:1]), waits=tk(m1))
        bb.rd = c1
        gc_toks.append(c1)
    gate_done = gc_toks[-1]

    ATT_A = g.SBIG[:, 20480:24576].rearrange("p (c t) -> p c t", c=4)
    ATT_B = g.SBIG[:, 28672:32768].rearrange("p (c t) -> p c t", c=4)
    def att_sl(c, qs_):
        return ATT_A[:, c, qs_] if c < 4 else ATT_B[:, c - 4, qs_]
    KCH = lambda s: g.SC[:, s * 1024: (s + 1) * 1024].rearrange("p (c t) -> p c t", c=8)
    VCH = lambda s: g.SC[:, 2048 + s * 1024: 2048 + (s + 1) * 1024]
    ECH = lambda s: g.SC[:, 4096 + s * 512: 4096 + (s + 1) * 512]
    RBC = g.SCf[:, 2592:3104]
    DENR = g.STAT[0:1, 0:512]
    TS_ = g.SCf[0:64, 3104:3616]
    kch_rd = [cvt_toks[-1], cvt_toks[-1]]
    vch_rd = [None, None]
    ech_rd = [None, None]
    x_toks = []
    for qh in range(2):
        att_toks = []
        for qt2 in range(2):
            q0 = qh * 1024 + qt2 * 512
            for hg in range(4):
                nbank = [g.PS[i] for i in range(4)]
                sbank = [g.PS[4], g.PS[5]]
                for src_i, nch in ((0, 4), (1, 8), (2, 4)):
                    den_end = [None] * 4
                    for kc in range(nch):
                        s2 = kc % 2
                        nblk = 128
                        if src_i == 0:
                            tkk = tkv = None
                            if kc == 3:
                                nblk = 127
                        elif src_i == 1:
                            tkk = p.dma("sync", KCH(s2)[:], g.ksel[:, kc * 128:(kc + 1) * 128].rearrange(
                                "(c p) t -> p c t", p=128), waits=tk(*gather_toks, kch_rd[s2]))
                            tkv = p.dma("sync", VCH(s2), g.vsel[kc * 128:(kc + 1) * 128, :],
                                        waits=tk(gather_toks[-1], vch_rd[s2]))
                        else:
                            w0 = T - 512 + kc * 128
                            tkk = p.dma("sync", KCH(s2)[:], g.kag_out[3 * D:4 * D, w0:w0 + 128].rearrange(
                                "(c p) t -> p c t", p=128), waits=tk(t_kag, kch_rd[s2]))
                            tkv = p.dma("sync", VCH(s2), g.vag_out[S - 512 + kc * 128: S - 512 + (kc + 1) * 128, :],
                                        waits=tk(t_vag, vch_rd[s2]))
                        for h4 in range(4):
                            h = hg * 4 + h4
                            hb = 64 * (h % 2)
                            hp = h // 2
                            sb_ = sbank[h4 % 2]
                            if src_i == 0:
                                mm = p.op("tensor", lambda e, hb=hb, hp=hp, kc=kc, nblk=nblk, sb_=sb_, q0=q0: e.matmul(
                                    sb_.t[0:nblk, 0:512], CK[hb:hb + 64, hp, kc * 128: kc * 128 + nblk],
                                    QV[hb:hb + 64, hp, q0:q0 + 512], start=True, stop=True),
                                    waits=tk(sb_.rd, *( (gate_done, t_dbi) if (qh == 0 and qt2 == 0 and hg == 0 and kc == 0 and h4 == 0) else ())))
                            else:
                                mm = p.op("tensor", lambda e, hb=hb, hp=hp, s2=s2, sb_=sb_, q0=q0: e.matmul(
                                    sb_.t[:, 0:512], KCH(s2)[hb:hb + 64, hp, :],
                                    QV[hb:hb + 64, hp, q0:q0 + 512], start=True, stop=True),
                                    waits=tk(tkk, sb_.rd))
                            ex = p.op("scalar", lambda e, s2=s2, sb_=sb_, nblk=nblk: e.activation(
                                ECH(s2)[0:nblk, :], sb_.t[0:nblk, 0:512], AF.Exp, scale=SCL),
                                waits=tk(mm, ech_rd[s2]))
                            sb_.rd = ex
                            nb = nbank[h4]
                            if src_i == 0:
                                lhsV = (lambda e, h=h, kc=kc, nblk=nblk: CVT[0:nblk, kc, 64 * h:64 * h + 64])
                            else:
                                lhsV = (lambda e, h=h, s2=s2: VCH(s2)[:, 64 * h:64 * h + 64])
                            mmn = p.add("tensor", lambda e, lhsV=lhsV, s2=s2, nb=nb, kc=kc, nblk=nblk, nch=nch: e.matmul(
                                nb.t[0:64, 0:512], lhsV(e), ECH(s2)[0:nblk, :],
                                start=(kc == 0), stop=(kc == nch - 1)),
                                waits=tk(ex, tkv, nb.rd if kc == 0 else None), inc=None)
                            mmd = p.add("tensor", lambda e, s2=s2, nb=nb, kc=kc, nblk=nblk, nch=nch: e.matmul(
                                nb.t[64:65, 0:512], g.ONESB[0:nblk, :], ECH(s2)[0:nblk, :],
                                start=(kc == 0), stop=(kc == nch - 1), tile_position=(0, 64)),
                                waits=tk(), inc=p.esem["tensor"])
                            if kc == nch - 1:
                                den_end[h4] = mmd
                            ech_rd[s2] = mmd
                            if h4 == 3:
                                kch_rd[s2] = mm
                                vch_rd[s2] = mmd
                    for h4 in range(4):
                        h = hg * 4 + h4
                        hb = 64 * (h % 2)
                        hc = h // 2
                        nb = nbank[h4]
                        r1 = p.op("vector", lambda e, nb=nb: e.reciprocal(DENR[0:1, 0:512], nb.t[64:65, 0:512]),
                                  waits=tk(den_end[h4]))
                        bb = sbank[h4 % 2]
                        r2 = p.op("tensor", lambda e, bb=bb: e.matmul(
                            bb.t[0:64, 0:512], g.ONESR[0:1, 0:64], DENR[0:1, 0:512],
                            start=True, stop=True), waits=tk(r1, bb.rd))
                        r3 = p.op("vector", lambda e, bb=bb: e.tensor_copy(RBC[0:64, :], bb.t[0:64, 0:512]), waits=tk(r2))
                        bb.rd = r3
                        qsl = slice(qt2 * 512, qt2 * 512 + 512)
                        TSH = g.SCf[hb:hb + 64, 3104:3616]
                        r4 = p.op("vector", lambda e, h=h, src_i=src_i, nb=nb, TSH=TSH: e.scalar_tensor_tensor(
                            TSH, nb.t[0:64, 0:512], GC[0:64, 3 * h + src_i: 3 * h + src_i + 1], RBC[0:64, :],
                            ALU.mult, ALU.mult), waits=tk(r3))
                        nb.rd = r4
                        if src_i == 0:
                            r5 = p.op("vector", lambda e, hb=hb, hc=hc, qsl=qsl, TSH=TSH: e.tensor_copy(
                                att_sl(hc, qsl)[hb:hb + 64, :], TSH), waits=tk(r4))
                        else:
                            r5 = p.op("vector", lambda e, hb=hb, hc=hc, qsl=qsl, TSH=TSH: e.tensor_add(
                                att_sl(hc, qsl)[hb:hb + 64, :], att_sl(hc, qsl)[hb:hb + 64, :], TSH), waits=tk(r4))
                        att_toks.append(r5)
        for og in range(2):
            b01 = og % 2
            wdst = g.WB[:, b01, :].rearrange("p (k o) -> p k o", k=8)
            tw = p.dma("gpsimd", wdst, dr["l2_wo"][:, og], waits=tk(wb_rd[b01]))
            for oc in range(4):
                c_out = og * 4 + oc
                for t2 in range(2):
                    qsl = slice(t2 * 512, t2 * 512 + 512)
                    bk = g.PS[(oc * 2 + t2) % 4]
                    mm = None
                    for c in range(FC):
                        mm = p.add("tensor", lambda e, c=c, oc=oc, b01=b01, qsl=qsl, bk=bk: e.matmul(
                            bk.t[:, 0:512], g.WB[:, b01, c * 512 + oc * 128: c * 512 + oc * 128 + 128],
                            att_sl(c, qsl), start=(c == 0), stop=(c == FC - 1)),
                            waits=tk(att_toks[-1] if (c == 0 and og == 0 and oc == 0 and t2 == 0) else None,
                                     *( (tw, bk.rd) if c == 0 else ())),
                            inc=p.esem["tensor"] if c == FC - 1 else None)
                    xa = p.op("vector", lambda e, c_out=c_out, qh=qh, qsl=qsl, bk=bk: e.tensor_add(
                        g.X[:, c_out, qh * 1024 + qsl.start: qh * 1024 + qsl.stop],
                        g.X[:, c_out, qh * 1024 + qsl.start: qh * 1024 + qsl.stop], bk.t[:, 0:512]),
                        waits=tk(mm))
                    bk.rd = xa
                    x_toks.append(xa)
            wb_rd[b01] = x_toks[-1]

    WRB2 = g.SC[:, 0:64].rearrange("p (k o) -> p k o", k=8)
    t_wr2 = p.dma("gpsimd", WRB2[:], dr["l2_wr"][:, 0], waits=tk(x_toks[-1] if x_toks else None, kch_rd[0], kch_rd[1]))
    t_br2 = p.dma("gpsimd", g.BROW[0:1, 0:8], dr["l2_br"][:], waits=tk())
    LG = g.SCf[:, 512:640].rearrange("p (tc e) -> p tc e", tc=16)
    lg_toks = []
    for ti in range(16):
        bk = g.PS[ti % 4]
        mm = None
        for c in range(FC):
            mm = p.add("tensor", lambda e, c=c, ti=ti, bk=bk: e.matmul(
                bk.t[0:128, 0:8], g.XN[:, c, ti * 128:(ti + 1) * 128], WRB2[:, c, :],
                start=(c == 0), stop=False),
                waits=tk(t_wr2 if c == 0 else None, bk.rd if c == 0 else None), inc=None)
        mm = p.add("tensor", lambda e, bk=bk: e.matmul(
            bk.t[0:128, 0:8], g.ONESRB[:], g.BROW[0:1, 0:8], start=False, stop=True),
            waits=tk(t_br2), inc=p.esem["tensor"])
        ev = p.op("vector", lambda e, ti=ti, bk=bk: e.tensor_copy(LG[:, ti, :], bk.t[0:128, 0:8]), waits=tk(mm))
        bk.rd = ev
        lg_toks.append(ev)
    WEX = g.SC[:, 1024:1152].rearrange("p (tc e) -> p tc e", tc=16)
    TC1 = g.MISC[:, 378:379]; TC2 = g.MISC[:, 379:380]
    EQ1 = g.SCf[:, 640:648]; EQ2 = g.SCf[:, 648:656]
    EXR = g.MISC[:, 396:397]; WA = g.MISC[:, 397:398]; WBC2 = g.MISC[:, 398:399]
    wex_toks = []
    for ti in range(16):
        m1 = p.op("vector", lambda e, ti=ti: e.reduce_max(TC1, LG[:, ti, :], axis=AX.X), waits=tk(lg_toks[ti], *wex_toks[-1:]))
        e1 = p.op("vector", lambda e, ti=ti: e.tensor_scalar(EQ1, LG[:, ti, :], TC1, None, ALU.is_ge), waits=tk(m1))
        s1_ = p.op("vector", lambda e, ti=ti: e.scalar_tensor_tensor(EQ2, EQ1, -1e30, LG[:, ti, :], ALU.mult, ALU.add), waits=tk(e1))
        m2 = p.op("vector", lambda e: e.reduce_max(TC2, EQ2, axis=AX.X), waits=tk(s1_))
        e2 = p.op("vector", lambda e, ti=ti: e.tensor_scalar(EQ2, LG[:, ti, :], TC2, None, ALU.is_ge), waits=tk(m2))
        d1 = p.op("vector", lambda e: e.tensor_sub(EXR, TC2, TC1), waits=tk(m2, e2))
        sg = p.op("scalar", lambda e: e.activation(WA, EXR, AF.Sigmoid), waits=tk(d1))
        w1_ = p.op("vector", lambda e: e.tensor_scalar(WBC2, WA, -1.0, 1.0, ALU.mult, ALU.add), waits=tk(sg))
        c1 = p.op("vector", lambda e, ti=ti: e.tensor_scalar_mul(EQ1, EQ1, WBC2), waits=tk(w1_, e1))
        c2 = p.op("vector", lambda e, ti=ti: e.tensor_scalar_mul(EQ2, EQ2, WA), waits=tk(sg, e2, c1))
        c3 = p.op("vector", lambda e, ti=ti: e.tensor_add(WEX[:, ti, :], EQ1, EQ2), waits=tk(c2))
        wex_toks.append(c3)
    wf_toks = []
    WFM = g.SCf[0:8, 656:2704]
    for ti in range(16):
        pb = g.PS[4 + ti % 2]
        tt = p.op("tensor", lambda e, ti=ti, pb=pb: e.matmul(
            pb.t[0:8, 0:128], WEX[:, ti, :], g.IDN[:], start=True, stop=True),
            waits=tk(wex_toks[ti], pb.rd, g.tok_idn))
        cp = p.op("vector", lambda e, ti=ti, pb=pb: e.tensor_copy(
            WFM[0:8, ti * 128:(ti + 1) * 128], pb.t[0:8, 0:128]), waits=tk(tt))
        pb.rd = cp
        wf_toks.append(cp)
    t_wfd = p.dma("sync", g.wrow_d[:], WFM, waits=tk(*wf_toks))

    HMO = g.SBIG[:, 0:16384].rearrange("p (o t) -> p o t", o=32)
    moe_x_toks = []
    wrow_rd = None
    for e_ in range(8):
        t_wr = p.dma("sync", g.STAT[0:1, 0:2048], g.wrow_d[e_:e_ + 1, :], waits=tk(t_wfd, wrow_rd))
        for t in range(4):
            ts = slice(t * 512, (t + 1) * 512)
            mb = g.PS[5]
            bcm = p.op("tensor", lambda e, t=t, mb=mb: e.matmul(
                mb.t[:, 0:512], g.ONESR[:], g.STAT[0:1, t * 512:(t + 1) * 512], start=True, stop=True),
                waits=tk(t_wr, mb.rd))
            WBC = g.SCf[:, 2704 + (t % 2) * 512: 2704 + (t % 2) * 512 + 512]
            mcp = p.op("vector", lambda e, mb=mb, WBC=WBC: e.tensor_copy(WBC, mb.t[:, 0:512]),
                       waits=tk(bcm, moe_x_toks[-15] if len(moe_x_toks) >= 15 else None))
            mb.rd = mcp
            wrow_rd = mcp
            hlast = None
            for og in range(8):
                b01 = og % 2
                wdst = g.WB[:, b01, :].rearrange("p (k o) -> p k o", k=8)
                tw = p.dma("gpsimd", wdst, dr["l2_W1"][e_][:, og], waits=tk(wb_rd[b01]))
                for oc in range(4):
                    oabs = og * 4 + oc
                    bk = g.PS[oabs % 4]
                    mm = None
                    for c in range(FC):
                        mm = p.add("tensor", lambda e, c=c, oc=oc, b01=b01, ts=ts, bk=bk: e.matmul(
                            bk.t[:, 0:512], g.WB[:, b01, c * 512 + oc * 128: c * 512 + oc * 128 + 128],
                            g.XN[:, c, ts], start=(c == 0), stop=(c == FC - 1)),
                            waits=tk(ln[c][t], *( (tw, bk.rd) if c == 0 else ())),
                            inc=p.esem["tensor"] if c == FC - 1 else None)
                    wb_rd[b01] = mm
                    gv = p.op("scalar", lambda e, e_=e_, oabs=oabs, bk=bk: e.activation(
                        HMO[:, oabs, 0:512], bk.t[:, 0:512], AF.Gelu, bias=g.MISC[:, 10 + e_ * 32 + oabs: 11 + e_ * 32 + oabs]),
                        waits=tk(mm, t_b1))
                    bk.rd = gv
                    hlast = gv
            for og in range(FC):
                b01 = og % 2
                wdst = g.WB[:, b01, :].rearrange("p (k o) -> p k o", k=32)
                tw = p.dma("gpsimd", wdst, dr["l2_W2"][e_][:, og], waits=tk(wb_rd[b01]))
                bk = g.PS[og % 4]
                mm = None
                for k in range(32):
                    mm = p.add("tensor", lambda e, k=k, b01=b01, bk=bk: e.matmul(
                        bk.t[:, 0:512], g.WB[:, b01, k * 128:(k + 1) * 128],
                        HMO[:, k, 0:512], start=(k == 0), stop=(k == 31)),
                        waits=tk(*( (tw, bk.rd, hlast) if k == 0 else ())),
                        inc=p.esem["tensor"] if k == 31 else None)
                wb_rd[b01] = mm
                x1_ = p.op("vector", lambda e, e_=e_, og=og, bk=bk, WBC=WBC: e.scalar_tensor_tensor(
                    g.SCf[:, 656 + (og % 2) * 512: 656 + (og % 2) * 512 + 512],
                    bk.t[:, 0:512], g.MISC[:, 266 + e_ * 8 + og: 267 + e_ * 8 + og], WBC, ALU.add, ALU.mult),
                    waits=tk(mm, t_b2, mcp, moe_x_toks[-2] if len(moe_x_toks) >= 2 else None))
                bk.rd = x1_
                x2_ = p.op("vector", lambda e, og=og, ts=ts: e.tensor_add(
                    g.X[:, og, ts], g.X[:, og, ts],
                    g.SCf[:, 656 + (og % 2) * 512: 656 + (og % 2) * 512 + 512]), waits=tk(x1_))
                moe_x_toks.append(x2_)
    g.x_toks = x_toks + moe_x_toks

def emit_l3(g, ln):
    p = g.p
    dr = g.dram
    EPS = 1e-5
    ZX = g.SBIG[:, 0:17408].rearrange("p (c t) -> p c t", c=34)
    XBCC = g.SBIG[:, 17408:26112].rearrange("p (c t) -> p c t", c=17)
    XT = g.SBIG[:, 26112:28160].rearrange("p (c f) -> p c f", c=16)
    YV = g.SBIG[:, 8192:16384].rearrange("p (c t) -> p c t", c=16)
    SSTB = g.SBIG[:, 28160:30208].rearrange("p (h n) -> p h n", h=32)
    CSAVE = g.SBIG[:, 30208:32256].rearrange("p (c t) -> p c t", c=16)
    sst_sl = lambda h: SSTB[0:64, h, :]
    TAIL = g.SC[:, 0:51].rearrange("p (c t) -> p c t", c=17)
    CALN = g.SC[:, 64:576]
    BTM = g.SC[:, 576:640]
    SMT = lambda s: g.SC[:, 640 + s * 128: 640 + (s + 1) * 128]
    GSB = g.SC[:, 896:1024]
    CESL = lambda s: g.SC[:, 1024 + s * 128: 1024 + (s + 1) * 128]
    BESL = g.SC[:, 1280:1344]
    AIH = g.SC[:, 1344:1408]
    DTT = g.SCf[:, 704:736]
    CAT = g.SCf[:, 736:768]
    EBT = g.SCf[:, 768:800]
    ACOL = g.SCf[:, 800:832]
    CA = [g.SCf[0:32, 832:960], g.SCf[0:32, 960:1088]]
    RUNC = g.SCf[0:32, 1088:1089]
    CAEND = g.SCf[0:1, 1089:1121]
    DTQ = g.SCf[0:32, 1152:1664]
    LDA = g.SCf[0:32, 1664:2176]
    CACC = g.SCf[:, 2176:2688]
    NROW = lambda i: g.SCf[0:1, 2688 + i * 128: 2688 + (i + 1) * 128]
    EROW = g.SCf[0:1, 2944:3072]

    t_convw = p.dma("sync", g.MISC[:, 51:119], dr["l3_convw"].rearrange("p c f -> p (c f)"))
    t_convb = p.dma("sync", g.MISC[:, 34:51], dr["l3_convb"][:])
    t_dtb = p.dma("sync", g.MISC[0:32, 0:1], dr["l3_dtb"][:])
    t_alog = p.dma("sync", g.MISC[0:32, 1:2], dr["l3_alog"][:])
    t_dcol = p.dma("sync", g.MISC[:, 2:18], dr["l3_dcol"][:])
    t_nw = p.dma("sync", g.MISC[:, 18:34], dr["l3_nw"][:])
    t_seg = p.dma("sync", g.MISC[:, 119:122], dr["l3_seg"][:])
    t_hm = p.dma("sync", g.MISC[:, 122:126], dr["l3_hmask"][:])
    t_trin = p.dma("sync", g.TRINEG[:], dr["trineg"][:])
    t_triu = t_trin
    SMTF_ = lambda s: g.SCf[:, 3232 + s * 128: 3232 + (s + 1) * 128]
    a1 = p.op("scalar", lambda e: e.activation(g.MISC[0:32, 1:2], g.MISC[0:32, 1:2], AF.Exp), waits=tk(t_alog))
    t_negA = p.op("vector", lambda e: e.tensor_scalar_mul(g.MISC[0:32, 1:2], g.MISC[0:32, 1:2], -1.0), waits=tk(a1))

    t_hst = p.dma("sync", g.hag_in.rearrange("(c p) t -> p c t", p=128), g.XN[:, :, T - 3:T],
                  waits=tk(*[ln[c][3] for c in range(FC)]))
    t_hag = p.add("gpsimd", lambda e: e.collective_compute(
        "AllGather", ALU.bypass, replica_groups=GROUPS, ins=[g.hag_in[:]], outs=[g.hag_out[:]]),
        waits=tk(t_hst), inc=p.csem, amt=1)
    HALL = g.SCf[:, 3072:3120].bitcast(BF16)
    t_hld = p.dma("sync", HALL.rearrange("p (r c t) -> p r c t", r=4, c=8),
                  g.hag_out.rearrange("(r c p) t -> p r c t", p=128, r=4), waits=tk(t_hag))
    HSUM = g.SCf[:, 3120:3132].bitcast(BF16)
    hz = p.op("vector", lambda e: e.memset(HSUM[:, 0:24], 0.0), waits=tk())
    hs_toks = []
    for r in range(4):
        hv = p.op("vector", lambda e, r=r: e.scalar_tensor_tensor(
            HSUM[:, 0:24], HALL[:, r * 24:(r + 1) * 24], g.MISC[:, 122 + r:123 + r],
            HSUM[:, 0:24], ALU.mult, ALU.add), waits=tk(t_hld, hz, t_hm))
        hs_toks.append(hv)

    wb_rd = [None, None]
    tail_toks = [None] * 17
    sst_tok = [p.op("vector", lambda e: e.memset(SSTB[0:64, :, :], 0.0), waits=tk())] * 32
    runc_tok = p.op("vector", lambda e: e.memset(RUNC[:], 0.0), waits=tk())
    ca_rd = None
    ca_bd = None
    row_rd = None
    erow_rd = None
    sm_rd = [None, None]
    smtf_rd = [None, None]
    nrow_rd = [None, None]
    ce_rd = [None, None]
    be_rd = None
    ai_rd = None
    ty = tz = None
    t_acore = None
    conv_toks = [None] * 17

    for q in range(4):
        ts = slice(q * 512, (q + 1) * 512)
        zx_toks = [None] * 34
        first_ev = True
        for og in range(17):
            b01 = og % 2
            wdst = g.WB[:, b01, 0:2048].rearrange("p (k o) -> p k o", k=8)
            tw = p.dma("gpsimd", wdst, dr["l3_win"][:, og], waits=tk(wb_rd[b01]))
            last_mm = None
            for oc in range(2):
                c_out = og * 2 + oc
                bk = g.PS[(og * 2 + oc) % 4]
                mm = None
                for c in range(FC):
                    mm = p.add("tensor", lambda e, c=c, oc=oc, b01=b01, ts=ts, bk=bk: e.matmul(
                        bk.t[:, 0:512], g.WB[:, b01, c * 256 + oc * 128: c * 256 + oc * 128 + 128],
                        g.XN[:, c, ts], start=(c == 0), stop=(c == FC - 1)),
                        waits=tk(ln[c][q], *( (tw, bk.rd) if c == 0 else ())),
                        inc=p.esem["tensor"] if c == FC - 1 else None)
                ev = p.op("vector", lambda e, c_out=c_out, bk=bk: e.tensor_copy(
                    ZX[:, c_out, 0:512], bk.t[:, 0:512]),
                    waits=tk(mm, *( (ty, tz) if (first_ev and q > 0) else ())))
                first_ev = False
                bk.rd = ev
                zx_toks[c_out] = ev
                last_mm = mm
            if q == 0 and 8 <= og <= 16:
                for oc in range(2):
                    c_out = og * 2 + oc
                    if not (16 <= c_out <= 32):
                        continue
                    ch = c_out - 16
                    bk = g.PS[4 + (ch % 2)]
                    mm = None
                    for c in range(FC):
                        mm = p.add("tensor", lambda e, c=c, oc=oc, b01=b01, bk=bk: e.matmul(
                            bk.t[:, 0:3], g.WB[:, b01, c * 256 + oc * 128: c * 256 + oc * 128 + 128],
                            HSUM[:, 0:24].rearrange("p (c t) -> p c t", c=8)[:, c, :],
                            start=(c == 0), stop=(c == FC - 1)),
                            waits=tk(*( (hs_toks[-1], bk.rd) if c == 0 else ())),
                            inc=p.esem["tensor"] if c == FC - 1 else None)
                    tv = p.op("vector", lambda e, ch=ch, bk=bk: e.tensor_copy(
                        TAIL[:, ch, :], bk.t[:, 0:3]), waits=tk(mm))
                    bk.rd = tv
                    tail_toks[ch] = tv
            wb_rd[b01] = last_mm
        d1 = p.op("vector", lambda e: e.tensor_scalar_add(DTQ, ZX[0:32, 33, 0:512], g.MISC[0:32, 0:1]),
                  waits=tk(zx_toks[33], t_dtb))
        d2 = p.op("scalar", lambda e: e.activation(LDA, DTQ, AF.Abs), waits=tk(d1))
        d3 = p.op("scalar", lambda e: e.activation(LDA, LDA, AF.Exp, scale=-1.0), waits=tk(d2))
        d4 = p.op("scalar", lambda e: e.activation(LDA, LDA, AF.Ln, bias=1.0), waits=tk(d3))
        d5 = p.op("vector", lambda e: e.tensor_scalar_max(DTQ, DTQ, 0.0), waits=tk(d2))
        d6 = p.op("vector", lambda e: e.tensor_add(DTQ, DTQ, LDA), waits=tk(d4, d5))
        d7 = p.op("vector", lambda e: e.tensor_scalar_mul(LDA, DTQ, g.MISC[0:32, 1:2]), waits=tk(d6, t_negA))
        conv_prev_q = conv_toks[16] if q else None
        conv_toks = [None] * 17
        for ch in range(17):
            wc = lambda d_, ch=ch: g.MISC[:, 51 + ch * 4 + d_: 52 + ch * 4 + d_]
            raw = ZX[:, 16 + ch, 0:512]
            cc = p.op("vector", lambda e, raw=raw, wc=wc: e.tensor_scalar_mul(CACC, raw, wc(3)),
                      waits=tk(zx_toks[16 + ch], t_convw, conv_toks[ch - 1] if ch else (conv_prev_q if q else None)))
            for s in range(1, 4):
                cc = p.op("vector", lambda e, s=s, raw=raw, wc=wc: e.scalar_tensor_tensor(
                    CACC[:, s:512], raw[:, 0:512 - s], wc(3 - s), CACC[:, s:512], ALU.mult, ALU.add),
                    waits=tk(cc))
                cc = p.op("vector", lambda e, s=s, ch=ch, wc=wc: e.scalar_tensor_tensor(
                    CACC[:, 0:s], TAIL[:, ch, 3 - s:3], wc(3 - s), CACC[:, 0:s], ALU.mult, ALU.add),
                    waits=tk(cc, tail_toks[ch]))
            tu = p.op("vector", lambda e, ch=ch, raw=raw: e.tensor_copy(TAIL[:, ch, :], raw[:, 509:512]),
                      waits=tk(cc))
            tail_toks[ch] = tu
            cs_ = p.op("scalar", lambda e, ch=ch: e.activation(
                XBCC[:, ch, 0:512], CACC, AF.Silu, bias=g.MISC[:, 34 + ch:35 + ch]),
                waits=tk(cc, t_convb))
            conv_toks[ch] = cs_
        t_cal = p.dma("sync", CALN[0:64, :], XBCC[64:128, 16, 0:512], waits=tk(conv_toks[16], ce_rd[0], ce_rd[1]))
        for c4 in range(4):
            p.op("vector", lambda e, q=q, c4=c4: e.tensor_copy(
                CSAVE[0:64, q * 4 + c4, :], CALN.rearrange("p (a t) -> p a t", a=4)[0:64, c4, :]),
                waits=tk(t_cal))
        for c4 in range(4):
            gidx = q * 4 + c4
            cs_sl = slice(c4 * 128, (c4 + 1) * 128)
            cur = 0
            lt = p.op("vector", lambda e, c4=c4: e.tensor_copy(CA[0], LDA[:, c4 * 128:(c4 + 1) * 128]),
                      waits=tk(d7, ca_rd, ca_bd))
            for s in (1, 2, 4, 8, 16, 32, 64):
                nxt = 1 - cur
                l1 = p.op("vector", lambda e, s=s, cur=cur, nxt=nxt: e.tensor_copy(
                    CA[nxt][:, 0:s], CA[cur][:, 0:s]), waits=tk(lt))
                lt = p.op("vector", lambda e, s=s, cur=cur, nxt=nxt: e.tensor_add(
                    CA[nxt][:, s:128], CA[cur][:, s:128], CA[cur][:, 0:128 - s]), waits=tk(l1))
                cur = nxt
            lt = p.op("vector", lambda e, cur=cur: e.tensor_scalar_add(CA[cur], CA[cur], RUNC),
                      waits=tk(lt, runc_tok))
            runc_tok = p.op("vector", lambda e, cur=cur: e.tensor_copy(RUNC, CA[cur][:, 127:128]), waits=tk(lt))
            bd = p.dma("sync", g.cag[:, gidx * 128:(gidx + 1) * 128], CA[cur], waits=tk(lt))
            ca_bd = bd
            pb = g.PS[4]
            ct1 = p.op("tensor", lambda e, cur=cur, pb=pb: e.transpose(
                pb.t[:, 0:32], CA[cur], g.IDNF[0:32, 0:32]), waits=tk(lt, pb.rd, g.tok_idnf))
            ct2 = p.op("vector", lambda e, pb=pb: e.tensor_copy(CAT, pb.t[:, 0:32]), waits=tk(ct1))
            pb.rd = ct2
            ca_rd = ct1
            pb2 = g.PS[5]
            dt1 = p.op("tensor", lambda e, c4=c4, pb2=pb2: e.transpose(
                pb2.t[:, 0:32], DTQ[:, c4 * 128:(c4 + 1) * 128], g.IDNF[0:32, 0:32]),
                waits=tk(d6, pb2.rd, g.tok_idnf))
            dt2 = p.op("vector", lambda e, pb2=pb2: e.tensor_copy(DTT, pb2.t[:, 0:32]), waits=tk(dt1))
            pb2.rd = dt2
            gb = g.PS[4]
            g1 = p.op("tensor", lambda e, c4=c4, gb=gb, cs_sl=cs_sl: e.matmul(
                gb.t[:, 0:128], XBCC[0:64, 16, cs_sl], CALN[0:64, cs_sl], start=True, stop=True),
                waits=tk(t_cal, conv_toks[16], gb.rd))
            g2 = p.op("vector", lambda e, gb=gb: e.tensor_copy(GSB, gb.t[:, 0:128]), waits=tk(g1))
            gb.rd = g2
            bb_ = g.PT[0]
            b1 = p.op("tensor", lambda e, c4=c4, bb_=bb_, cs_sl=cs_sl: e.transpose(
                bb_.t[:, 0:64], XBCC[0:64, 16, cs_sl], g.IDN[0:64, 0:64]),
                waits=tk(conv_toks[16], bb_.rd, g.tok_idn, be_rd))
            b2 = p.op("vector", lambda e, bb_=bb_: e.tensor_copy(BTM, bb_.t[:, 0:64]), waits=tk(b1))
            bb_.rd = b2
            xt_toks = []
            for zc in range(16):
                pbx = g.PT[zc % 2]
                x1 = p.op("tensor", lambda e, zc=zc, cs_sl=cs_sl, pbx=pbx: e.transpose(
                    pbx.t[:, 0:128], XBCC[:, zc, cs_sl], g.IDN[:]),
                    waits=tk(conv_toks[zc], pbx.rd, g.tok_idn))
                x2 = p.op("vector", lambda e, zc=zc, pbx=pbx: e.tensor_copy(
                    XT[:, zc, :], pbx.t[:, 0:128]), waits=tk(x1))
                pbx.rd = x2
                xt_toks.append(x2)
            for half in range(2):
                bl = p.dma("sync", g.STAT[0:1, 0:2048].rearrange("o (h t) -> o h t", h=16),
                           g.cag[half * 16:(half + 1) * 16, gidx * 128:(gidx + 1) * 128]
                           .rearrange("h t -> () h t" if False else "h t -> h t")[None, :, :] if False else
                           g.cag[half * 16:(half + 1) * 16, gidx * 128:(gidx + 1) * 128].rearrange("h t -> o h t", o=1) if False else
                           g.cag[half * 16:(half + 1) * 16, gidx * 128:(gidx + 1) * 128],
                           waits=tk(bd, row_rd))
                hsl = slice(half * 16, half * 16 + 16)
                ce1 = p.op("vector", lambda e, hsl=hsl: e.tensor_copy(
                    CAEND[0:1, hsl], g.STAT[0:1, 127:2048:128]), waits=tk(bl))
                pb3 = g.PS[4]
                eb1 = p.op("tensor", lambda e, hsl=hsl, pb3=pb3: e.matmul(
                    pb3.t[:, 0:16], g.ONESR[:], CAEND[0:1, hsl], start=True, stop=True),
                    waits=tk(ce1, pb3.rd))
                eb2 = p.op("vector", lambda e, hsl=hsl, pb3=pb3: e.tensor_sub(
                    EBT[:, hsl], pb3.t[:, 0:16], CAT[:, hsl]), waits=tk(eb1, ct2))
                pb3.rd = eb2
                eb3 = p.op("scalar", lambda e, hsl=hsl: e.activation(EBT[:, hsl], EBT[:, hsl], AF.Exp),
                           waits=tk(eb2))
                eb4 = p.op("vector", lambda e, hsl=hsl: e.tensor_mul(EBT[:, hsl], EBT[:, hsl], DTT[:, hsl]),
                           waits=tk(eb3, dt2))
                av1 = p.op("scalar", lambda e, hsl=hsl: e.activation(CAEND[0:1, hsl], CAEND[0:1, hsl], AF.Exp),
                           waits=tk(eb2))
                pb4 = g.PS[5]
                av2 = p.op("tensor", lambda e, hsl=hsl, pb4=pb4: e.matmul(
                    pb4.t[:, 0:16], g.ONESR[:], CAEND[0:1, hsl], start=True, stop=True),
                    waits=tk(av1, pb4.rd))
                av3 = p.op("vector", lambda e, hsl=hsl, pb4=pb4: e.tensor_copy(
                    ACOL[:, hsl], pb4.t[:, 0:16]), waits=tk(av2))
                pb4.rd = av3
                if gidx == 15 and half == 1:
                    t_acore = p.dma("sync", g.a_in[:], CAEND, waits=tk(av1, ce1))
                for hh in range(16):
                    h = half * 16 + hh
                    hb = 64 * (h % 2)
                    hc = h // 2
                    zc = hc
                    carow = g.STAT[0:1, hh * 128:(hh + 1) * 128]
                    nr = NROW(h % 2)
                    n1 = p.op("vector", lambda e, nr=nr, carow=carow: e.tensor_scalar_mul(nr, carow, -1.0),
                              waits=tk(bl, nrow_rd[h % 2]))
                    db = g.PS[h % 2]
                    m1 = p.op("tensor", lambda e, db=db, carow=carow: e.matmul(
                        db.t[:, 0:128], g.ONESR[:], carow, start=True, stop=False),
                        waits=tk(bl, db.rd))
                    m2 = p.add("tensor", lambda e, db=db, nr=nr: e.matmul(
                        db.t[:, 0:128], nr, g.ONESR[:], start=False, stop=True),
                        waits=tk(n1), inc=p.esem["tensor"])
                    nrow_rd[h % 2] = m2
                    sm = SMT(h % 2)
                    smf = SMTF_(h % 2)
                    s0 = p.op("vector", lambda e, db=db, smf=smf: e.tensor_add(smf, db.t[:, 0:128], g.TRINEG[:]),
                              waits=tk(m2, t_trin, smtf_rd[h % 2]))
                    db.rd = s0
                    s1 = p.op("scalar", lambda e, sm=sm, smf=smf: e.activation(sm, smf, AF.Exp),
                              waits=tk(s0, sm_rd[h % 2]))
                    smtf_rd[h % 2] = s1
                    s3 = p.op("vector", lambda e, sm=sm: e.tensor_mul(sm, sm, GSB), waits=tk(s1, g2))
                    s4 = p.op("vector", lambda e, sm=sm, h=h: e.tensor_scalar_mul(sm, sm, DTT[:, h:h + 1]),
                              waits=tk(s3, dt2))
                    er1 = p.op("scalar", lambda e, carow=carow: e.activation(EROW, carow, AF.Exp),
                               waits=tk(bl, erow_rd))
                    eb_ = g.PS[4 + (h % 2)]
                    er2 = p.op("tensor", lambda e, eb_=eb_: e.matmul(
                        eb_.t[0:64, 0:128], g.ONESR[0:1, 0:64], EROW, start=True, stop=True),
                        waits=tk(er1, eb_.rd))
                    erow_rd = er2
                    ce_ = CESL(h % 2)
                    er3 = p.op("vector", lambda e, eb_=eb_, ce_=ce_, cs_sl=cs_sl: e.tensor_mul(
                        ce_[0:64, :], CALN[0:64, cs_sl], eb_.t[0:64, 0:128]),
                        waits=tk(er2, t_cal, ce_rd[h % 2]))
                    eb_.rd = er3
                    py = g.PS[2 + (h % 2)]
                    y1 = p.op("tensor", lambda e, py=py, zc=zc, hb=hb, sm=sm: e.matmul(
                        py.t[0:64, 0:128], XT[:, zc, hb:hb + 64], sm, start=True, stop=False),
                        waits=tk(s4, xt_toks[zc], py.rd))
                    sm_rd[h % 2] = y1
                    y2 = p.add("tensor", lambda e, py=py, h=h, ce_=ce_: e.matmul(
                        py.t[0:64, 0:128], sst_sl(h), ce_[0:64, :], start=False, stop=True),
                        waits=tk(er3, sst_tok[h]), inc=p.esem["tensor"])
                    ce_rd[h % 2] = y2
                    yv = p.op("vector", lambda e, py=py, hb=hb, hc=hc, cs_sl=cs_sl: e.scalar_tensor_tensor(
                        YV[hb:hb + 64, hc, cs_sl], XBCC[hb:hb + 64, hc, cs_sl],
                        g.MISC[hb:hb + 64, 2 + hc:3 + hc], py.t[0:64, 0:128], ALU.mult, ALU.add),
                        waits=tk(y2, t_dcol, conv_toks[hc]))
                    py.rd = yv
                    ai1 = p.op("vector", lambda e, h=h: e.tensor_scalar_mul(
                        AIH[0:64, :], g.IDN[0:64, 0:64], ACOL[0:64, h:h + 1]), waits=tk(av3, ai_rd))
                    be1 = p.op("vector", lambda e, h=h: e.tensor_scalar_mul(BESL, BTM, EBT[:, h:h + 1]),
                               waits=tk(b2, eb4, be_rd))
                    st1 = p.op("tensor", lambda e, db=db, zc=zc, hb=hb: e.matmul(
                        db.t[0:64, 0:64], BESL, XT[:, zc, hb:hb + 64], start=True, stop=False),
                        waits=tk(be1, xt_toks[zc], s1))
                    be_rd = st1
                    st2 = p.add("tensor", lambda e, db=db, h=h: e.matmul(
                        db.t[0:64, 0:64], AIH[0:64, :], sst_sl(h), start=False, stop=True),
                        waits=tk(ai1, sst_tok[h]), inc=p.esem["tensor"])
                    ai_rd = st2
                    sv = p.op("vector", lambda e, db=db, h=h: e.tensor_copy(sst_sl(h), db.t[0:64, 0:64]),
                              waits=tk(st2))
                    db.rd = sv
                    sst_tok[h] = sv
                row_rd = er2
        sz_toks = []
        last_yv_tok = sst_tok[31]
        for zc in range(16):
            z1 = p.op("scalar", lambda e, zc=zc: e.activation(
                ZX[:, zc, 0:512], ZX[:, zc, 0:512], AF.Silu), waits=tk(zx_toks[zc]))
            w1 = p.op("vector", lambda e, zc=zc: e.tensor_mul(
                YV[:, zc, 0:512], YV[:, zc, 0:512], ZX[:, zc, 0:512]), waits=tk(z1, last_yv_tok))
            sz_toks.append(w1)
        ty = p.dma("sync", g.ysc[:, ts].rearrange("(c p) t -> p c t", p=128), YV[:, :, 0:512],
                   waits=tk(*sz_toks))
        tz = p.dma("sync", g.zsc[:, ts].rearrange("(c p) t -> p c t", p=128), ZX[:, 0:16, 0:512],
                   waits=tk(*sz_toks))

    STAGE = g.SBIGf[0:64, 6144:8192]
    t_sst_c = p.op("vector", lambda e: e.tensor_copy(
        STAGE.rearrange("p (h n) -> p h n", h=32), SSTB[0:64, :, :]), waits=tk(*sst_tok, ty, tz))
    t_sstst = p.dma("sync", g.sst_in[:], STAGE, waits=tk(t_sst_c))
    t_sag2 = p.add("gpsimd", lambda e: e.collective_compute(
        "AllGather", ALU.bypass, replica_groups=GROUPS, ins=[g.sst_in[:]], outs=[g.sst_out[:]]),
        waits=tk(t_sstst), inc=p.csem, amt=1)
    t_aag = p.add("gpsimd", lambda e: e.collective_compute(
        "AllGather", ALU.bypass, replica_groups=GROUPS, ins=[g.a_in[:]], outs=[g.a_out[:]]),
        waits=tk(t_acore), inc=p.csem, amt=1)

    SALL = g.SBIGf[0:64, 0:8192].rearrange("p (r h n) -> p r h n", r=4, h=32)
    t_sld2 = p.dma("sync", g.SBIGf[0:64, 0:8192].rearrange("p (r x) -> p r x", r=4),
                   g.sst_out.rearrange("(r p) x -> p r x", p=64), waits=tk(t_sag2, ty, tz))
    AROW4 = g.SCf[0:1, 3072:3200]
    t_ald = p.dma("sync", AROW4,
                  g.a_out.rearrange("r h -> (r h)").rearrange("(o x) -> o x", o=1), waits=tk(t_aag))
    TCUR = g.SBIGf[0:64, 8192:10240].rearrange("p (h n) -> p h n", h=32)
    SIN = g.SBIGf[0:64, 10240:12288].rearrange("p (h n) -> p h n", h=32)
    SINB = g.SBIG[0:64, 24576:26624].rearrange("p (h n) -> p h n", h=32)
    ACM = g.SCf[:, 3200:3232]
    tz1 = p.op("vector", lambda e: e.memset(TCUR[:, :, :], 0.0), waits=tk(t_sld2))
    tz2 = p.op("vector", lambda e: e.memset(SIN[:, :, :], 0.0), waits=tk(t_sld2, t_sst_c))
    tprev = tz1
    sprev = tz2
    for r in range(3):
        pb5 = g.PS[4]
        am1 = p.op("tensor", lambda e, r=r, pb5=pb5: e.matmul(
            pb5.t[:, 0:32], g.ONESR[:], AROW4[0:1, r * 32:(r + 1) * 32], start=True, stop=True),
            waits=tk(t_ald, pb5.rd))
        am2 = p.op("vector", lambda e, pb5=pb5: e.tensor_copy(ACM, pb5.t[:, 0:32]), waits=tk(am1))
        pb5.rd = am2
        for h in range(32):
            t1 = p.op("vector", lambda e, r=r, h=h: e.scalar_tensor_tensor(
                TCUR[:, h, :], TCUR[:, h, :], ACM[0:64, h:h + 1], SALL[:, r, h, :], ALU.mult, ALU.add),
                waits=tk(am2, tprev))
            t2 = p.op("vector", lambda e, r=r, h=h: e.scalar_tensor_tensor(
                SIN[:, h, :], TCUR[:, h, :], g.MISC[0:64, 119 + r:120 + r], SIN[:, h, :], ALU.mult, ALU.add),
                waits=tk(t1, t_seg, sprev))
            tprev = t1
            sprev = t2
    sinb = p.op("vector", lambda e: e.tensor_copy(SINB[:, :, :], SIN[:, :, :]), waits=tk(sprev))

    WQ = g.SBIG[:, 16384:24576].rearrange("p (c t) -> p c t", c=16)
    SZQ = g.SBIG[:, 24576:32768 - 2560].rearrange("p (c t) -> p c t", c=16) if False else None
    SZQ = g.SBIG[:, 0:8192].rearrange("p (c t) -> p c t", c=16)
    x_toks = []
    wq_rd = None
    sq_rd = None
    r4_prev = None
    for q in range(4):
        ts = slice(q * 512, (q + 1) * 512)
        t_wld = p.dma("sync", WQ[:, :, 0:512], g.ysc[:, ts].rearrange("(c p) t -> p c t", p=128),
                      waits=tk(wq_rd, sinb))
        t_szld = p.dma("sync", SZQ[:, :, 0:512], g.zsc[:, ts].rearrange("(c p) t -> p c t", p=128),
                       waits=tk(sq_rd, sinb, tz))
        corr_toks = []
        for half in range(2):
            for hh in range(16):
                if getattr(emit_l3, "NO_CORR", False):
                    continue
                h = half * 16 + hh
                hb = 64 * (h % 2)
                hc = h // 2
                erw = p.dma("sync", g.STAT[0:1, 0:512],
                            g.cag[h:h + 1, q * 512:(q + 1) * 512], waits=tk(row_rd))
                ex1 = p.op("scalar", lambda e: e.activation(g.STAT[0:1, 0:512], g.STAT[0:1, 0:512], AF.Exp),
                           waits=tk(erw))
                row_rd = ex1
                for c4 in range(4):
                    gidx = q * 4 + c4
                    cs_sl = slice(c4 * 128, (c4 + 1) * 128)
                    eb_ = g.PS[4 + (c4 % 2)]
                    er2 = p.op("tensor", lambda e, c4=c4, eb_=eb_: e.matmul(
                        eb_.t[0:64, 0:128], g.ONESR[0:1, 0:64], g.STAT[0:1, c4 * 128:(c4 + 1) * 128],
                        start=True, stop=True), waits=tk(ex1, eb_.rd))
                    row_rd = er2
                    ce_ = CESL(c4 % 2)
                    er3 = p.op("vector", lambda e, gidx=gidx, eb_=eb_, ce_=ce_: e.tensor_mul(
                        ce_[0:64, :], CSAVE[0:64, gidx, :], eb_.t[0:64, 0:128]),
                        waits=tk(er2, ce_rd[c4 % 2]))
                    eb_.rd = er3
                    pc = g.PS[c4 % 2]
                    yc = p.op("tensor", lambda e, h=h, pc=pc, ce_=ce_: e.matmul(
                        pc.t[0:64, 0:128], SINB[0:64, h, :], ce_[0:64, :],
                        start=True, stop=True), waits=tk(er3, sinb, pc.rd))
                    ce_rd[c4 % 2] = yc
                    TMPH = g.SC[hb:hb + 64, 1408 + (c4 % 2) * 128: 1408 + (c4 % 2) * 128 + 128]
                    c1 = p.op("vector", lambda e, hb=hb, hc=hc, cs_sl=cs_sl, pc=pc, TMPH=TMPH: e.tensor_mul(
                        TMPH, pc.t[0:64, 0:128], SZQ[hb:hb + 64, hc, cs_sl]),
                        waits=tk(yc, t_szld))
                    pc.rd = c1
                    c2 = p.op("vector", lambda e, hb=hb, hc=hc, cs_sl=cs_sl, TMPH=TMPH: e.tensor_add(
                        WQ[hb:hb + 64, hc, cs_sl], WQ[hb:hb + 64, hc, cs_sl], TMPH),
                        waits=tk(c1, t_wld))
                    corr_toks.append(c2)
        if q == 0 and "d_wq0" in g.dbg:
            dd1 = p.dma("sync", g.dbg["d_wq0"][:], g.SBIG[:, 16384:24576],
                        waits=tk(corr_toks[-1] if corr_toks else t_wld))
            p.wait("sync", [dd1])
        nb = g.PS[4]
        mm = None
        sqt_toks = []
        for chh in range(16):
            slot = g.SC[:, 1664 + (chh % 4) * 512: 1664 + (chh % 4) * 512 + 512]
            sq = p.op("vector", lambda e, chh=chh, slot=slot: e.tensor_mul(
                slot, WQ[:, chh, 0:512], WQ[:, chh, 0:512]),
                waits=tk(corr_toks[-1] if corr_toks else t_wld, sq_rd if chh < 4 else sqt_toks[chh - 4]))
            mm = p.add("tensor", lambda e, chh=chh, slot=slot, nb=nb: e.matmul(
                nb.t[0:1, 0:512], g.ONESB[:], slot, start=(chh == 0), stop=(chh == 15)),
                waits=tk(sq, nb.rd if chh == 0 else None),
                inc=p.esem["tensor"])
            sqt_toks.append(mm)
        r1 = p.op("vector", lambda e, nb=nb: e.tensor_scalar(
            g.SCf[0:1, 2944:3456][0:1, 0:512], nb.t[0:1, 0:512], 1.0 / (2 * D), EPS, ALU.mult, ALU.add),
            waits=tk(mm, r4_prev))
        nb.rd = r1
        r2 = p.op("vector", lambda e: e.reciprocal(g.SCf[0:1, 2944:3456][0:1, 0:512],
                                                   g.SCf[0:1, 2944:3456][0:1, 0:512]), waits=tk(r1))
        r3 = p.op("scalar", lambda e: e.activation(g.SCf[0:1, 2944:3456][0:1, 0:512],
                                                   g.SCf[0:1, 2944:3456][0:1, 0:512], AF.Sqrt), waits=tk(r2))
        nb2 = g.PS[5]
        r4 = p.op("tensor", lambda e, nb2=nb2: e.matmul(
            nb2.t[:, 0:512], g.ONESR[:], g.SCf[0:1, 2944:3456][0:1, 0:512], start=True, stop=True),
            waits=tk(r3, nb2.rd))
        r4_prev = r4
        appl_toks = []
        for chh in range(16):
            ap1 = p.op("vector", lambda e, chh=chh, nb2=nb2: e.scalar_tensor_tensor(
                WQ[:, chh, 0:512], WQ[:, chh, 0:512], g.MISC[:, 18 + chh:19 + chh], nb2.t[:, 0:512],
                ALU.mult, ALU.mult), waits=tk(r4, t_nw, corr_toks[-1] if corr_toks else t_wld))
            appl_toks.append(ap1)
        nb2.rd = appl_toks[-1]
        sq_rd = appl_toks[-1]

        for og in range(4):
            b01 = og % 2
            wdst = g.WB[:, b01, :].rearrange("p (k o) -> p k o", k=16)
            tww = p.dma("gpsimd", wdst, dr["l3_wout"][:, og], waits=tk(wb_rd[b01]))
            for oc in range(2):
                c_out = og * 2 + oc
                bk = g.PS[(og * 2 + oc) % 4]
                mm2 = None
                for k in range(16):
                    mm2 = p.add("tensor", lambda e, k=k, oc=oc, b01=b01, bk=bk: e.matmul(
                        bk.t[:, 0:512], g.WB[:, b01, k * 256 + oc * 128: k * 256 + oc * 128 + 128],
                        WQ[:, k, 0:512], start=(k == 0), stop=(k == 15)),
                        waits=tk(appl_toks[k], *( (tww, bk.rd) if k == 0 else ())),
                        inc=p.esem["tensor"] if k == 15 else None)
                xa = p.op("vector", lambda e, c_out=c_out, ts=ts, bk=bk: e.tensor_add(
                    g.X[:, c_out, ts], g.X[:, c_out, ts], bk.t[:, 0:512]), waits=tk(mm2))
                bk.rd = xa
                x_toks.append(xa)
            wb_rd[b01] = x_toks[-1]
        wq_rd = x_toks[-1]

    if getattr(emit_l3, "NO_PEER", False):
        g.x_toks = x_toks
        return
    VALS = g.SBIG[:, 0:16384].rearrange("p (e d) -> p e d", e=16)
    SELB = g.SBIG[:, 16384:16512].rearrange("p (c e) -> p c e", c=8)
    last_x = x_toks[-1]
    t_vals = p.dma("gpsimd", VALS[:], dr["l3_vals"].rearrange("e p d -> p e d"), waits=tk(last_x))
    t_selt = p.dma("gpsimd", SELB[:], dr["l3_selt"][:], waits=tk(last_x))
    SCW = g.SC.rearrange("p (e t) -> p e t", e=16)
    peer_tile_done = None
    selrow_rd = last_x
    scw_rd = [None] * 16
    for t in range(4):
        ts = slice(t * 512, (t + 1) * 512)
        for e_ in range(16):
            kb01 = e_ % 2
            wdst = g.WB[:, kb01, 0:1024].rearrange("p (k o) -> p k o", k=8)
            twk = p.dma("gpsimd", wdst, dr["l3_keys"][e_][:, 0], waits=tk(wb_rd[kb01]))
            bk = g.PS[e_ % 2]
            mm = None
            for c in range(FC):
                mm = p.add("tensor", lambda e, c=c, kb01=kb01, ts=ts, bk=bk: e.matmul(
                    bk.t[:, 0:512], g.WB[:, kb01, c * 128:(c + 1) * 128],
                    g.XN[:, c, ts], start=(c == 0), stop=(c == FC - 1)),
                    waits=tk(ln[c][t], *( (twk, bk.rd) if c == 0 else ())),
                    inc=p.esem["tensor"] if c == FC - 1 else None)
            wb_rd[kb01] = mm
            nbs = g.PS[4 + (e_ % 2)]
            mms = None
            for c in range(FC):
                mms = p.add("tensor", lambda e, c=c, e_=e_, ts=ts, nbs=nbs: e.matmul(
                    nbs.t[0:1, 0:512], SELB[:, c, e_:e_ + 1], g.XN[:, c, ts],
                    start=(c == 0), stop=(c == FC - 1)),
                    waits=tk(*( (t_selt, nbs.rd) if c == 0 else ())),
                    inc=p.esem["tensor"] if c == FC - 1 else None)
            srow2 = g.STAT[0:1, 0:512]
            sg = p.op("scalar", lambda e, nbs=nbs, srow2=srow2: e.activation(
                srow2, nbs.t[0:1, 0:512], AF.Sigmoid), waits=tk(mms, selrow_rd))
            nbs.rd = sg
            bcb = g.PS[2 + (e_ % 2)]
            sb1 = p.op("tensor", lambda e, bcb=bcb, srow2=srow2: e.matmul(
                bcb.t[:, 0:512], g.ONESR[:], srow2, start=True, stop=True),
                waits=tk(sg, bcb.rd))
            selrow_rd = sb1
            sc1 = p.op("vector", lambda e, e_=e_, bk=bk: e.tensor_scalar_max(
                SCW[:, e_, 0:512], bk.t[:, 0:512], 0.0), waits=tk(mm, scw_rd[e_], peer_tile_done))
            bk.rd = sc1
            if getattr(emit_l3, "NO_SEL", False):
                sc2 = p.op("vector", lambda e, e_=e_, bcb=bcb: e.tensor_copy(
                    SCW[:, e_, 0:512], SCW[:, e_, 0:512]), waits=tk(sc1, sb1))
            else:
                sc2 = p.op("vector", lambda e, e_=e_, bcb=bcb: e.tensor_mul(
                    SCW[:, e_, 0:512], SCW[:, e_, 0:512], bcb.t[:, 0:512]), waits=tk(sc1, sb1))
            bcb.rd = sc2
            scw_rd[e_] = sc2
        for dch in range(FC):
            bk = g.PS[dch % 2]
            mm = None
            for e_ in range(16):
                mm = p.add("tensor", lambda e, e_=e_, dch=dch, bk=bk: e.matmul(
                    bk.t[:, 0:512], VALS[:, e_, dch * 128:(dch + 1) * 128], SCW[:, e_, 0:512],
                    start=(e_ == 0), stop=(e_ == 15)),
                    waits=tk(scw_rd[e_], bk.rd if e_ == 0 else None,
                             t_vals if (dch == 0 and t == 0 and e_ == 0) else None),
                    inc=p.esem["tensor"] if e_ == 15 else None)
            xa = p.op("vector", lambda e, dch=dch, ts=ts, bk=bk: e.tensor_add(
                g.X[:, dch, ts], g.X[:, dch, ts], bk.t[:, 0:512]), waits=tk(mm))
            bk.rd = xa
            x_toks.append(xa)
            if dch == FC - 1:
                peer_tile_done = mm
    g.x_toks = x_toks



def tile_w(w, og):
    K, O = w.shape
    return np.ascontiguousarray(
        w.reshape(K // 128, 128, O // og, og).transpose(1, 2, 0, 3))


def col128(v):
    return np.ascontiguousarray(v.reshape(-1, 128).T)


def prep_weights(inputs, layers):
    d = {}
    f32 = lambda a: np.ascontiguousarray(a, dtype=np.float32)
    if 0 in layers:
        d["l0_qp"] = tile_w(f32(inputs["L0_attn_q_proj"]), 256)
        d["l0_kp"] = tile_w(f32(inputs["L0_attn_k_proj"]), 256)
        d["l0_w1"] = tile_w(f32(inputs["L0_ffn_w1_w"]).T, 256)
        d["l0_w2"] = tile_w(f32(inputs["L0_ffn_w2_w"]).T, 256)
        d["l0_h1"] = tile_w(f32(inputs["L0_ffn_h1_w"]).T, 128)
        d["l0_h2"] = tile_w(f32(inputs["L0_ffn_h2_w"]).T, 128)
        d["l0_w3"] = tile_w(f32(inputs["L0_ffn_w3_w"]).T, 128)
        d["l0_s1"] = col128(f32(inputs["L0_ffn_w1_s"]))
        d["l0_s2"] = col128(f32(inputs["L0_ffn_w2_s"]))
        d["l0_sh1"] = col128(f32(inputs["L0_ffn_h1_s"]))
        d["l0_sh2"] = col128(f32(inputs["L0_ffn_h2_s"]))
        d["l0_s3"] = col128(f32(inputs["L0_ffn_w3_s"]))
    if 2 in layers:
        d["l2_wq"] = tile_w(f32(inputs["L2_attn_Wq"]), 512)
        d["l2_wk"] = tile_w(f32(inputs["L2_attn_Wk"]), 512)
        d["l2_wv"] = tile_w(f32(inputs["L2_attn_Wv"]), 512)
        d["l2_wo"] = tile_w(f32(inputs["L2_attn_Wo"]), 512)
        wck = f32(inputs["L2_attn_Wck"]).reshape(32, 64, 64).transpose(1, 0, 2)
        d["l2_wck"] = np.ascontiguousarray(np.concatenate([wck, wck], axis=0))
        wcv = f32(inputs["L2_attn_Wcv"]).reshape(32, 64, 64).transpose(1, 0, 2)
        d["l2_wcv"] = np.ascontiguousarray(np.concatenate([wcv, wcv], axis=0))
        d["l2_bck"] = f32(inputs["L2_attn_bck"]).reshape(64, 1)
        d["l2_bcv"] = f32(inputs["L2_attn_bcv"]).reshape(64, 1)
        wg = f32(inputs["L2_attn_Wg"])
        d["l2_wgb"] = np.ascontiguousarray(np.concatenate([wg, wg], axis=0))
        d["l2_bgr"] = np.ascontiguousarray(np.tile(f32(inputs["L2_attn_bg"]), 16).reshape(1, 48))
        d["l2_wr"] = tile_w(f32(inputs["L2_ffn_Wr"]), 8)
        d["l2_br"] = f32(inputs["L2_ffn_br"]).reshape(1, 8)
        d["l2_W1"] = np.ascontiguousarray(np.stack([tile_w(f32(inputs["L2_ffn_W1"][e]), 512) for e in range(8)]))
        d["l2_b1"] = np.ascontiguousarray(np.concatenate([col128(f32(inputs["L2_ffn_b1"][e])) for e in range(8)], axis=1))
        d["l2_W2"] = np.ascontiguousarray(np.stack([tile_w(f32(inputs["L2_ffn_W2"][e]), 128) for e in range(8)]))
        d["l2_b2"] = np.ascontiguousarray(np.concatenate([col128(f32(inputs["L2_ffn_b2"][e])) for e in range(8)], axis=1))
        d["iota64"] = np.ascontiguousarray((np.arange(128) % 64).astype(np.float32).reshape(128, 1))
    if 3 in layers:
        win = np.zeros((1024, 4352), np.float32)
        win[:, 0:4256] = f32(inputs["L3_attn_W_in"])
        d["l3_win"] = tile_w(win, 256)
        d["l3_wout"] = tile_w(f32(inputs["L3_attn_W_out"]), 256)
        cw = f32(inputs["L3_attn_conv_w"]).reshape(17, 128, 4).transpose(1, 0, 2)
        d["l3_convw"] = np.ascontiguousarray(cw)
        d["l3_convb"] = col128(f32(inputs["L3_attn_conv_b"]))
        d["l3_dtb"] = f32(inputs["L3_attn_dt_bias"]).reshape(32, 1)
        d["l3_alog"] = f32(inputs["L3_attn_A_log"]).reshape(32, 1)
        dd = f32(inputs["L3_attn_D"])
        d["l3_dcol"] = np.ascontiguousarray(
            dd[2 * np.arange(16)[None, :] + (np.arange(128)[:, None] >= 64)].astype(np.float32))
        d["l3_nw"] = col128(f32(inputs["L3_attn_norm_weight"]))
        d["l3_keys"] = np.ascontiguousarray(np.stack([tile_w(f32(inputs["L3_ffn_keys"][e]), 128) for e in range(16)]))
        d["l3_vals"] = np.ascontiguousarray(f32(inputs["L3_ffn_values"]))
        d["l3_selt"] = tile_w(np.ascontiguousarray(f32(inputs["L3_ffn_expert_sel"]).T), 16)[:, 0]
        tri = np.zeros((128, 128), dtype=ml_dtypes.bfloat16)
        tri[np.tril_indices(128)] = 1
        d["trineg"] = np.ascontiguousarray(np.where(np.triu(np.ones((128, 128), np.bool_)), 0.0, -1e30).astype(np.float32))
    if 1 in layers:
        d["l1_wq"] = tile_w(f32(inputs["L1_attn_wq"]).T, 512)
        d["l1_wk"] = tile_w(f32(inputs["L1_attn_wk"]).T, 512)
        d["l1_wv"] = tile_w(f32(inputs["L1_attn_wv"]).T, 512)
        d["l1_wo"] = tile_w(f32(inputs["L1_attn_wo"]).T, 512)
        d["l1_bq"] = col128(f32(inputs["L1_attn_bq"]))
        d["l1_bk"] = col128(f32(inputs["L1_attn_bk"]))
        d["l1_bv"] = np.ascontiguousarray(f32(inputs["L1_attn_bv"]).reshape(1, 1024))
        d["l1_bo"] = col128(f32(inputs["L1_attn_bo"]))
        d["l1_g1"] = col128(f32(inputs["L1_attn_g1"]))
        d["l1_b1"] = col128(f32(inputs["L1_attn_b1"]))
        d["l1_g2"] = col128(f32(inputs["L1_attn_g2"]))
        d["l1_b2"] = col128(f32(inputs["L1_attn_b2"]))
        d["l1_w1"] = tile_w(f32(inputs["L1_attn_w1"]).T, 512)
        d["l1_bf1"] = col128(f32(inputs["L1_attn_bf1"]))
        d["l1_w2"] = tile_w(f32(inputs["L1_attn_w2"]).T, 128)
        d["l1_bf2"] = col128(f32(inputs["L1_attn_bf2"]))
        d["l1_wr"] = col128(f32(inputs["L1_ffn_w_router"]))
        d["l1_W1"] = tile_w(f32(inputs["L1_ffn_W1"]), 512)
        d["l1_b1m"] = col128(f32(inputs["L1_ffn_b1"]))
        d["l1_W2"] = tile_w(f32(inputs["L1_ffn_W2"]), 128)
        d["l1_b2m"] = col128(f32(inputs["L1_ffn_b2"]))
        hm = np.zeros((128, 8, 16), dtype=ml_dtypes.bfloat16)
        for p_ in range(128):
            for c_ in range(8):
                hm[p_, c_, 2 * c_ + p_ // 64] = 1
        d["hmask"] = hm
        sc_ = np.zeros((16, 8, 128), dtype=ml_dtypes.bfloat16)
        for h_ in range(16):
            for p_ in range(128):
                if h_ % 2 == p_ // 64:
                    sc_[h_, h_ // 2, p_] = 1
        d["selc"] = sc_
    d["identity"] = np.eye(128, dtype=ml_dtypes.bfloat16)
    d["identityf"] = np.eye(64, dtype=np.float32)
    return d


def build_nc(layers):
    Prog.STRICT_T = False
    g = make_ctx(layers)
    emit_prelude(g)
    for i, L in enumerate(sorted(layers)):
        if i > 0:
            g.p.bar = {e: tk(*g.x_toks) for e in ENGS}
        Prog.STRICT_T = (L == 3)
        Prog.STRICT_MODE = (L >= 2)
        ln = emit_ln(g)
        if L == 0:
            emit_l0(g, ln)
        elif L == 1:
            emit_l1(g, ln)
        elif L == 2:
            emit_l2(g, ln)
        elif L == 3:
            emit_l3(g, ln)
        else:
            raise NotImplementedError(f"layer {L}")
    Prog.STRICT_MODE = True
    p = g.p
    if "d_stat" in g.dbg:
        w = tk(*g.x_toks)
        s1 = p.dma("sync", g.dbg["d_stat"][:], g.STAT[:], waits=w)
        s2 = p.dma("sync", g.dbg["d_misc"][:], g.MISC[:], waits=w)
        s3 = p.dma("sync", g.dbg["d_sbig"][:], g.SBIGf[:, 8192:16384], waits=w)
        s4 = p.dma("sync", g.dbg["d_sc"][:], g.SCf[:], waits=w)
        p.wait("sync", [s1, s2, s3, s4])
    if "d_ysc" in g.dbg:
        w = tk(*g.x_toks)
        s1 = p.dma("sync", g.dbg["d_ysc"][:], g.ysc[:], waits=w)
        s2 = p.dma("sync", g.dbg["d_zsc"][:], g.zsc[:], waits=w)
        s3 = p.dma("sync", g.dbg["d_cag"][:], g.cag[:], waits=w)
        s4 = p.dma("sync", g.dbg["d_sst"][:], g.sst_out[:], waits=w)
        p.wait("sync", [s1, s2, s3, s4])
    if "d_xn" in g.dbg:
        w = tk(*g.x_toks)
        st1 = p.dma("sync", g.dbg["d_xn"][:], g.XN[:], waits=w)
        st2 = p.dma("sync", g.dbg["d_qp"][:], g.SC[:, 0:4096], waits=w)
        st3 = p.dma("sync", g.dbg["d_kvb"][:], g.SC[:, 4096:6144], waits=w)
        st4 = p.dma("sync", g.dbg["d_z"][:], g.STAT[0:1, 0:2048], waits=w)
        p.wait("sync", [st1, st2, st3, st4])
    emit_store_x(g)
    g.p.emit()
    return g


def run_layers(x_full, inputs, layers):
    g = build_nc(layers)
    wd = prep_weights(inputs, layers)
    in_maps = []
    for c in range(NCORE):
        b, q = c // 4, c % 4
        m = dict(wd)
        m["x_in"] = np.ascontiguousarray(x_full[b, :, q * T:(q + 1) * T])
        if 3 in layers:
            seg = np.zeros((128, 3), np.float32)
            if q >= 1:
                seg[:, q - 1] = 1.0
            m["l3_seg"] = seg
            hm = np.zeros((128, 4), np.float32)
            if q >= 1:
                hm[:, q - 1] = 1.0
            m["l3_hmask"] = hm
        in_maps.append(m)
    res = run_bass_kernel_spmd(g.nc, in_maps, core_ids=list(range(NCORE)))
    out = np.empty_like(x_full)
    for c in range(NCORE):
        b, q = c // 4, c % 4
        out[b, :, q * T:(q + 1) * T] = res.results[c]["y_out"]
    if g.dbg:
        return out, res
    return out


def kernel(**inputs):
    x = np.ascontiguousarray(
        np.transpose(np.asarray(inputs["L0_attn_x"], dtype=np.float32), (0, 2, 1)))
    x = run_layers(x, inputs, [0, 1, 2, 3])
    return np.ascontiguousarray(np.transpose(x, (0, 2, 1)))
